# version 15
# baseline (speedup 1.0000x reference)
import contextlib
import math
import numpy as np

import concourse.bass as bass
import concourse.mybir as mybir
from concourse.bass_utils import run_bass_kernel_spmd

AF = mybir.ActivationFunctionType
ALU = mybir.AluOpType
AX = mybir.AxisListType
F32, BF16, F16 = mybir.dt.float32, mybir.dt.bfloat16, mybir.dt.float16

NCORES = 8
T = 2048
NT = 16
CD = 1024
FUSED = True
STAGES_ALL = ("A", "B", "C", "D")


def _rope_tables_l0():
    pos = np.arange(T, dtype=np.float64)
    inv = np.exp(np.arange(0, 64, 2, dtype=np.float64) * (-np.log(10000.0) / 64.0))
    ang = pos[:, None] * inv[None, :]
    sin = np.sin(ang).astype(np.float16).astype(np.float32)
    cos = np.cos(ang).astype(np.float16).astype(np.float32)
    d = np.arange(128)
    cosA = cos[:, d % 32].T.copy()
    sgn = np.where((d % 64) < 32, -1.0, 1.0).astype(np.float32)
    sinA = (sin[:, d % 32] * sgn[None, :]).T.copy()
    return cosA, sinA

def _rope_tables_l1():
    HH = 32
    inv = (1.0 / 10000.0 ** (np.arange(0, HH, 2, dtype=np.float64) / HH))
    pos = np.arange(T, dtype=np.float64)
    freqs = pos[:, None] * inv[None, :]
    emb = np.concatenate([freqs, freqs], axis=-1)
    cosE = np.cos(emb).astype(np.float32)
    sinE = np.sin(emb).astype(np.float32)
    cosD = np.zeros((128, T), np.float32)
    sinD = np.zeros((128, T), np.float32)
    d = np.arange(32)
    sgn = np.where(d < 16, -1.0, 1.0).astype(np.float32)
    for base in (32, 96):
        cosD[base:base + 32] = cosE.T
        sinD[base:base + 32] = (sinE * sgn[None, :]).T
    return cosD, sinD

def _perm64(n_heads):
    p = np.arange(n_heads * 64)
    return (p // 64) * 64 + (p % 64 + 32) % 64

def _perm32_block(ncols):
    p = np.arange(ncols)
    return (p // 32) * 32 + (p % 32 + 16) % 32

def _rep(v):
    return np.ascontiguousarray(np.broadcast_to(np.asarray(v, np.float32)[None, :], (128, len(v))))

def _colT(v, nchunk):
    return np.ascontiguousarray(np.asarray(v, np.float32).reshape(nchunk, 128).T)

def mk_consts(ins):
    import ml_dtypes
    bfd = ml_dtypes.bfloat16
    c = {}
    c["IDN"] = np.eye(128, dtype=bfd)
    c["IDF"] = np.eye(128, dtype=np.float32)
    c["UTRI"] = np.triu(np.ones((128, 128), bfd))
    c["ONEBF"] = np.ones((128, 128), bfd)
    c["ONEF"] = np.ones((128, 128), np.float32)
    c["SCAL"] = _rep(np.array([0.125, -0.5, 1.0 / math.sqrt(32.0), 1.0, 0.3, 1e38, 0.5, -0.5 * math.log(1024.0), -1.0, 0, 0, 0, 0, 0, 0, 0], np.float32))
    cosA, sinA = _rope_tables_l0()
    c["COSA"], c["SINA"] = cosA.astype(bfd), sinA.astype(bfd)
    cosD, sinD = _rope_tables_l1()
    c["COSD"], c["SIND"] = cosD.astype(bfd), sinD.astype(bfd)
    p64 = _perm64(16)
    c["A_Wq"] = np.ascontiguousarray(ins["L0_attn_Wq"].astype(np.float32))
    c["A_Wqp"] = np.ascontiguousarray(c["A_Wq"][:, p64])
    c["A_Wk"] = np.ascontiguousarray(ins["L0_attn_Wk"].astype(np.float32))
    c["A_Wkp"] = np.ascontiguousarray(c["A_Wk"][:, p64])
    c["A_Wv"] = np.ascontiguousarray(ins["L0_attn_Wv"].astype(np.float32))
    c["A_Wo"] = np.ascontiguousarray(ins["L0_attn_Wo"].astype(np.float32))
    bq = ins["L0_attn_bq"].astype(np.float32)
    bk = ins["L0_attn_bk"].astype(np.float32)
    c["A_bqT"] = _colT(bq, 8)
    c["A_bqpT"] = _colT(bq[p64], 8)
    c["A_bkT"] = _colT(bk, 8)
    c["A_bkpT"] = _colT(bk[p64], 8)
    c["A_bvR"] = _rep(ins["L0_attn_bv"].astype(np.float32))
    c["A_boR"] = _rep(ins["L0_attn_bo"].astype(np.float32))
    qsel = (np.arange(512) // 32) * 64 + (np.arange(512) % 32)
    c["B_qw"] = np.ascontiguousarray(ins["L0_ffn_q_w"][:, qsel])
    def _pairT(v):
        out = np.zeros((128, 8), np.float32)
        v = np.asarray(v, np.float32).reshape(16, 32)
        for hp in range(8):
            out[0:32, hp] = v[2 * hp]
            out[64:96, hp] = v[2 * hp + 1]
        return out
    c["B_qbT"] = _pairT(ins["L0_ffn_q_b"][qsel])
    c["B_kw"] = np.ascontiguousarray(ins["L0_ffn_k_w"])
    c["B_kbT"] = _pairT(ins["L0_ffn_k_b"])
    c["B_vw"] = np.ascontiguousarray(ins["L0_ffn_v_w"])
    c["B_vbR"] = _rep(ins["L0_ffn_v_b"])
    c["B_ow"] = np.ascontiguousarray(ins["L0_ffn_o_w"][:512, :])
    c["B_obR"] = _rep(ins["L0_ffn_o_b"])
    c["B_rw"] = np.ascontiguousarray(ins["L0_ffn_router_w"])
    c["B_rbR"] = _rep(ins["L0_ffn_router_b"])
    c["B_w1"] = ins["L0_ffn_w1"]
    c["B_w2"] = ins["L0_ffn_w2"]
    c["B_b1T"] = np.ascontiguousarray(
        ins["L0_ffn_b1"].reshape(8, 16, 128).transpose(2, 0, 1))
    c["B_b2"] = np.ascontiguousarray(ins["L0_ffn_b2"])
    c["C_WqT"] = np.ascontiguousarray(ins["L1_attn_Wq"].T)
    c["C_WkT"] = np.ascontiguousarray(ins["L1_attn_Wk"].T)
    c["C_WvT"] = np.ascontiguousarray(ins["L1_attn_Wv"].T)
    c["C_WoT"] = np.ascontiguousarray(ins["L1_attn_Wo"].T)
    c["C_WifT"] = np.ascontiguousarray(
        np.concatenate([ins["L1_attn_Wi"], ins["L1_attn_Wf"]], axis=0).T)
    c["C_bifR"] = _rep(np.concatenate([ins["L1_attn_bi"], ins["L1_attn_bf"]]))
    c["C_boR"] = _rep(ins["L1_attn_bo"])
    c["C_WoutT"] = np.ascontiguousarray(ins["L1_attn_W_out"].T)
    c["C_lngR"] = _rep(ins["L1_attn_ln_g"])
    c["C_lnbR"] = _rep(ins["L1_attn_ln_b"])
    c["D_wln1R"] = _rep(ins["L1_ffn_w_ln1"])
    c["D_wln2R"] = _rep(ins["L1_ffn_w_ln2"])
    c["D_kvd"] = np.ascontiguousarray(ins["L1_ffn_W_kvd"])
    c["D_qd"] = np.ascontiguousarray(ins["L1_ffn_W_qd"])
    c["D_ku"] = np.ascontiguousarray(ins["L1_ffn_W_ku"])
    c["D_qu"] = np.ascontiguousarray(ins["L1_ffn_W_qu"])
    c["D_vu"] = np.ascontiguousarray(ins["L1_ffn_W_vu"])
    p32q = _perm32_block(512)
    c["D_rq"] = np.ascontiguousarray(ins["L1_ffn_W_ropeq"])
    c["D_rqp"] = np.ascontiguousarray(ins["L1_ffn_W_ropeq"][:, p32q])
    c["D_rk"] = np.ascontiguousarray(ins["L1_ffn_W_ropek"])
    c["D_rkp"] = np.ascontiguousarray(ins["L1_ffn_W_ropek"][:, p32q])
    c["D_wo"] = np.ascontiguousarray(ins["L1_ffn_W_o"])
    c["D_gs"] = np.ascontiguousarray(ins["L1_ffn_Wg_s"])
    c["D_us"] = np.ascontiguousarray(ins["L1_ffn_Wu_s"])
    c["D_ds"] = np.ascontiguousarray(ins["L1_ffn_Wd_s"])
    c["D_gr"] = ins["L1_ffn_Wg_r"]
    c["D_ur"] = ins["L1_ffn_Wu_r"]
    c["D_dr"] = ins["L1_ffn_Wd_r"]
    c["D_rw"] = np.ascontiguousarray(ins["L1_ffn_W_router"])
    c["D_rbR"] = _rep(ins["L1_ffn_routing_bias"])
    return c


class Buf:
    __slots__ = ("w", "r")
    def __init__(self):
        self.w = None
        self.r = {}

class Prog:
    def __init__(self, name):
        self.nc = bass.Bass("TRN2", target_bir_lowering=False, name=name,
                            dynamic_dma_scratch_size=16384)
        self.q = {e: [] for e in ("T", "V", "S", "G", "Y")}
        self.cnt = {"T": 0, "V": 0, "S": 0, "G": 0}
        self.dcnt = {"DY": 0, "DG": 0}
        self._gfifo = []
        self._gdone = 0
        self.known = {e: {} for e in ("T", "V", "S", "G", "Y")}
        self.bufs = {}
        self.inputs = set()
        base = self.nc.sbuf_base
        self.sb0 = (base + 31) & ~31
        self.sb_hi = self.nc.sbuf_top

    def buf(self, key):
        b = self.bufs.get(key)
        if b is None:
            b = self.bufs[key] = Buf()
        return b

    def emit(self, e, fns, reads=(), writes=(), dma=False, ndesc=256):
        if callable(fns):
            fns = [fns]
        waits = {}
        def need(tok):
            if tok is None:
                return
            k, v = tok
            if k == "T" and e == "T":
                return
            if waits.get(k, 0) < v:
                waits[k] = v
        for b in reads:
            need(b.w)
        for b in writes:
            need(b.w)
            for k, v in b.r.items():
                need((k, v))
        kn = self.known[e]
        wl = []
        for k in sorted(waits):
            v = waits[k]
            if k[0] == "D":
                v = self.dcnt[k]
            if kn.get(k, 0) >= v:
                continue
            kn[k] = v
            wl.append((k, v))
        if dma:
            key = "D" + e
            if e == "G":
                BUDGET = 260
                out = sum(n for _, n in self._gfifo)
                while self._gfifo and out + ndesc > BUDGET:
                    tokv, n = self._gfifo.pop(0)
                    out -= n
                    self._gdone += 1
                    lim = 16 * self._gdone
                    if kn.get(key, 0) < lim:
                        kn[key] = lim
                        wl.append((key, lim))
                self._gfifo.append((self.dcnt[key] + 16, ndesc))
            self.dcnt[key] += 16
            tok = (key, self.dcnt[key])
        else:
            self.cnt[e] += 1
            tok = (e, self.cnt[e])
        self.q[e].append((wl, fns, tok))
        for b in writes:
            b.w = tok
            b.r = {}
        for b in reads:
            if b.r.get(tok[0], 0) < tok[1]:
                b.r[tok[0]] = tok[1]
        return tok

    def alias(self, dst, *srcs):
        for sb in srcs:
            if sb.w is not None:
                k, v = sb.w
                if dst.r.get(k, 0) < v:
                    dst.r[k] = v
            for k, v in sb.r.items():
                if dst.r.get(k, 0) < v:
                    dst.r[k] = v

    def fence(self):
        for e in ("T", "V", "S", "G", "Y"):
            wl = []
            kn = self.known[e]
            for k in ("T", "V", "S", "G"):
                if k == e:
                    continue
                if self.cnt[k] and kn.get(k, 0) < self.cnt[k]:
                    kn[k] = self.cnt[k]
                    wl.append((k, self.cnt[k]))
            for k in ("DY", "DG"):
                if self.dcnt[k] and kn.get(k, 0) < self.dcnt[k]:
                    kn[k] = self.dcnt[k]
                    wl.append((k, self.dcnt[k]))
            if wl:
                self.q[e].append((wl, [], None))

    def final_wait(self):
        wl = []
        for k, vmax in self.dcnt.items():
            if vmax and self.known["Y"].get(k, 0) < vmax:
                wl.append((k, vmax))
        for k in ("T", "V", "S", "G"):
            if self.cnt[k] and self.known["Y"].get(k, 0) < self.cnt[k]:
                wl.append((k, self.cnt[k]))
        self.q["Y"].append((wl, [], None))

    def din(self, name, shape, dtype):
        self.inputs.add(name)
        return self.nc.dram_tensor(name, list(shape), dtype, kind="ExternalInput")

    def sb(self, name, shape, dtype, off):
        byte_off = self.sb0 + off
        nbytes = int(np.prod(shape[1:])) * mybir.dt.size(dtype)
        assert byte_off + nbytes <= self.sb_hi, (name, byte_off + nbytes, self.sb_hi)
        return self.nc.alloc_sbuf_tensor_at(name, list(shape), dtype, offset=byte_off)

    def build(self):
        nc = self.nc
        st = contextlib.ExitStack()
        sems = {}
        for k in ("T", "V", "S", "G", "DY", "DG"):
            sems[k] = st.enter_context(nc.semaphore("sem_" + k))
        blk = st.enter_context(nc.Block())
        def mk(e):
            def body(eng):
                for wl, fns, tok in self.q[e]:
                    for k, v in wl:
                        eng.wait_ge(sems[k], v)
                    for i, fn in enumerate(fns):
                        inst = fn(eng)
                        if tok is not None and i == len(fns) - 1:
                            inst.then_inc(sems[tok[0]], 16 if tok[0][0] == "D" else 1)
            return body
        blk.tensor(mk("T"))
        blk.vector(mk("V"))
        blk.scalar(mk("S"))
        blk.gpsimd(mk("G"))
        blk.sync(mk("Y"))
        st.close()
        return nc

OFF_R = 0
OFF_A = 65536
OFF_B = 98304
OFF_C = 131072
OFF_W = 164352
OFF_S = 180736
OFF_K = 197120


def g_load(P, sb_ap, dram_ap, key):
    b = P.buf(key)
    P.emit("G", lambda g: g.dma_start(sb_ap, dram_ap), reads=(), writes=(b,), dma=True)
    return b

def g_load_ch(P, sb_t, dram_ap3, key, nch=None):
    b = P.buf(key)
    k = sb_t.shape[1]
    step = 2
    for c0 in range(0, k, step):
        c1 = min(k, c0 + step)
        P.emit("G", lambda g, c0=c0, c1=c1: g.dma_start(
            sb_t[:, c0:c1, :], dram_ap3[:, c0:c1, :]), reads=(), writes=(b,), dma=True)
    return b

def setup_common(P):
    if hasattr(P, "_common"):
        return P._common
    nc = P.nc
    K_id = P.sb("K_id", [128, 128], BF16, OFF_K)
    K_idf = P.sb("K_idf", [128, 128], F32, OFF_K + 256)
    K_ut = P.sb("K_ut", [128, 128], BF16, OFF_K + 768)
    K_on = P.sb("K_on", [128, 128], BF16, OFF_K + 1024)
    K_onf = P.sb("K_onf", [128, 128], F32, OFF_K + 1280)
    K_sc = P.sb("K_sc", [128, 16], F32, OFF_K + 1792)
    pd = [nc.alloc_psum_tensor(f"pd{i}", [128, 1024], F32) for i in range(3)]
    pb = [nc.alloc_psum_tensor(f"pb{i}", [128, 1024], BF16) for i in range(2)]

    class HalfBank:
        def __init__(self, t, off):
            self.t = t
            self.off = off
        def __getitem__(self, key):
            p, c = key
            st = c.start or 0
            sp = c.stop if c.stop is not None else 512
            return self.t[p, self.off + st:self.off + sp]
    ps = [HalfBank(pd[i // 2], (i % 2) * 512) for i in range(6)]
    P._pd = pd
    psB = [P.buf(("psum", i)) for i in range(6)]
    pbB = [P.buf(("psumb", i)) for i in range(2)]
    kbufs = {}
    for nm, t, shp, dt_ in [("IDN", K_id, [128, 128], BF16),
                            ("IDF", K_idf, [128, 128], F32),
                            ("UTRI", K_ut, [128, 128], BF16),
                            ("ONEBF", K_on, [128, 128], BF16),
                            ("ONEF", K_onf, [128, 128], F32),
                            ("SCAL", K_sc, [128, 16], F32)]:
        d = P.din(nm, shp, dt_)
        b = P.buf(("k", nm))
        P.emit("Y", lambda y, t=t, d=d: y.dma_start(t[:, :], d.ap()),
               reads=(), writes=(b,), dma=True)
        kbufs[nm] = b
    P._common = (K_id, K_idf, K_ut, K_on, K_onf, K_sc, ps, pb, psB, pbB, kbufs)
    return P._common

def pd_pair(P, bank_pair):
    assert bank_pair[0] % 2 == 0 and bank_pair[1] == bank_pair[0] + 1
    return P._pd[bank_pair[0] // 2]

def load_zone(P, cons, native=()):
    zb = P.buf("KZONE")
    out = {}
    for ent in cons:
        nm, t, shp = ent[0], ent[1], ent[2]
        spec = ent[3] if len(ent) > 3 else None
        b = P.buf(("k", nm))
        P.alias(b, zb)
        if nm in native:
            d = P.din(nm, shp, t.dtype)
            full = t[tuple(slice(None) for _ in t.shape)]
            P.emit("Y", lambda y, full=full, d=d: y.dma_start(full, d.ap()),
                   reads=(), writes=(b, zb), dma=True)
            out[nm] = b
            continue
        d = P.din(nm, shp, F32)
        src = d.ap() if spec is None else d.ap().rearrange(spec, p=128)
        if len(t.shape) == 3 and t.shape[1] > 2:
            for c0 in range(0, t.shape[1], 2):
                c1 = min(t.shape[1], c0 + 2)
                P.emit("G", lambda g, t=t, src=src, c0=c0, c1=c1: g.dma_start(
                    t[:, c0:c1, :], src[:, c0:c1, :]),
                    reads=(), writes=(b, zb), dma=True)
        else:
            full = t[tuple(slice(None) for _ in t.shape)]
            P.emit("G", lambda g, full=full, src=src: g.dma_start(full, src),
                   reads=(), writes=(b, zb), dma=True)
        out[nm] = b
    return out


def emit_stage_A(P, x_dram):
    nc = P.nc
    R = P.sb("R", [128, NT, 1024], F32, OFF_R)
    A_x0T = P.sb("A_x0T", [128, 8, 2048], BF16, OFF_A)
    B_kt = P.sb("B_kt", [128, 8, 2048], BF16, OFF_B)
    C_v = P.sb("C_v", [128, 16, 1040], BF16, OFF_C)
    C_stage = P.sb("C_stage", [128, 16, 1024], BF16, OFF_C)
    W1 = P.sb("W1", [128, 8, 1024], BF16, OFF_W)
    s_qw = [P.sb(f"s_qw{i}", [128, 8, 128], BF16, OFF_S + i * 2048) for i in range(2)]
    s_qwp = [P.sb(f"s_qwp{i}", [128, 8, 128], BF16, OFF_S + 4096 + i * 2048) for i in range(2)]
    s_ctx0 = P.sb("s_ctx0", [128, 2048], BF16, OFF_S + 8192)
    s_qt = P.sb("s_qt", [128, 512], BF16, OFF_S + 12288)
    s_rec = P.sb("s_rec", [128, 1024], BF16, OFF_S + 13312)
    s_exp = [P.sb(f"s_exp{i}", [128, 1024], BF16, OFF_W + i * 2048) for i in range(4)]
    s_X = P.sb("s_X", [128, 1024], F32, OFF_S)
    s_xres = P.sb("s_xres", [128, 1024], F32, OFF_S + 4096)
    s_dump = P.sb("s_dump", [128, 1024], BF16, OFF_S + 12288)
    s_small = P.sb("s_small", [128, 64], F32, OFF_S + 14336)
    K_id, K_idf, K_ut, K_on, K_onf, K_sc, ps, pb, psB, pbB, kbufs0 = setup_common(P)
    KZ = OFF_K + 2048
    K_cos = P.sb("K_cos", [128, 2048], BF16, KZ)
    K_sin = P.sb("K_sin", [128, 2048], BF16, KZ + 4096)
    K_bqT = P.sb("K_bqT", [128, 8], F32, KZ + 8192)
    K_bqpT = P.sb("K_bqpT", [128, 8], F32, KZ + 8224)
    K_bkT = P.sb("K_bkT", [128, 8], F32, KZ + 8256)
    K_bkpT = P.sb("K_bkpT", [128, 8], F32, KZ + 8288)
    K_bv = P.sb("K_bv", [128, 1024], BF16, KZ + 8320)
    K_bo = P.sb("K_bo", [128, 1024], BF16, KZ + 10368)
    s_t1 = P.sb("s_t1", [128, 512], BF16, KZ + 12416)

    dWq = P.din("A_Wq", [1024, 1024], F32)
    dWqp = P.din("A_Wqp", [1024, 1024], F32)
    dWk = P.din("A_Wk", [1024, 1024], F32)
    dWkp = P.din("A_Wkp", [1024, 1024], F32)
    dWv = P.din("A_Wv", [1024, 1024], F32)
    dWo = P.din("A_Wo", [1024, 1024], F32)
    cons = [("COSA", K_cos, [128, 2048]), ("SINA", K_sin, [128, 2048]),
            ("A_bqT", K_bqT, [128, 8]), ("A_bqpT", K_bqpT, [128, 8]),
            ("A_bkT", K_bkT, [128, 8]), ("A_bkpT", K_bkpT, [128, 8]),
            ("A_bvR", K_bv, [128, 1024]), ("A_boR", K_bo, [128, 1024])]
    kbufs = dict(kbufs0)
    kbufs.update(load_zone(P, cons, native=("COSA", "SINA")))
    bW1 = P.buf("bW1"); bW2 = P.buf("bW2")
    bR = [P.buf(("R", j)) for j in range(NT)]
    bA = P.buf("bA"); bB = P.buf("bB"); bC = P.buf("bC")
    bctx0 = P.buf("bctx0")

    xr = x_dram.ap().rearrange("(j p) c -> p j c", p=128)
    g_load_ch(P, C_stage, xr, "bC")
    for j in range(NT):
        for cc in range(8):
            pbi = (j * 8 + cc) % 2
            P.emit("T", lambda t, j=j, cc=cc, pbi=pbi: t.transpose(
                pb[pbi][:, 0:128], C_stage[:, j, cc * 128:(cc + 1) * 128], K_id[:, :]),
                reads=(bC, kbufs["IDN"], pbB[pbi]), writes=(pbB[pbi],))
            P.emit("V", lambda v, j=j, cc=cc, pbi=pbi: v.tensor_copy(
                A_x0T[:, cc, j * 128:(j + 1) * 128], pb[pbi][:, 0:128]),
                reads=(pbB[pbi],), writes=(bA, pbB[pbi]))

    for hp in range(8):
        qwb = s_qw[hp % 2]; qwpb = s_qwp[hp % 2]
        bqw = P.buf(("qw", hp % 2)); bqwp = P.buf(("qwp", hp % 2))
        for c0 in range(0, 8, 2):
            P.emit("G", lambda g, hp=hp, qwb=qwb, c0=c0: g.dma_start(
                qwb[:, c0:c0 + 2, :], dWk.ap().rearrange("(k p) n -> p k n", p=128)
                [:, c0:c0 + 2, hp * 128:(hp + 1) * 128]),
                reads=(bqw,), writes=(bqw,), dma=True)
            P.emit("G", lambda g, hp=hp, qwpb=qwpb, c0=c0: g.dma_start(
                qwpb[:, c0:c0 + 2, :], dWkp.ap().rearrange("(k p) n -> p k n", p=128)
                [:, c0:c0 + 2, hp * 128:(hp + 1) * 128]),
                reads=(bqwp,), writes=(bqwp,), dma=True)
        for w in range(4):
            wc = slice(w * 512, (w + 1) * 512)
            P.emit("T", [
                (lambda t, cc=cc, wc=wc, qwb=qwb: t.matmul(
                    ps[0][:, 0:512], qwb[:, cc, :],
                    A_x0T[:, cc, wc], start=(cc == 0), stop=(cc == 7)))
                for cc in range(8)],
                reads=(bqw, bA, psB[0]), writes=(psB[0],))
            P.emit("T", [
                (lambda t, cc=cc, wc=wc, qwpb=qwpb: t.matmul(
                    ps[1][:, 0:512], qwpb[:, cc, :],
                    A_x0T[:, cc, wc], start=(cc == 0), stop=(cc == 7)))
                for cc in range(8)],
                reads=(bqwp, bA, psB[1]), writes=(psB[1],))
            bt1 = P.buf("s_t1")
            P.emit("V", lambda v, hp=hp, wc=wc: v.scalar_tensor_tensor(
                s_t1[:, 0:512], ps[0][:, 0:512], K_bkT[:, hp:hp + 1], K_cos[:, wc],
                ALU.add, ALU.mult),
                reads=(psB[0], kbufs["A_bkT"], kbufs["COSA"]), writes=(bt1,))
            P.emit("V", lambda v, hp=hp, wc=wc: v.scalar_tensor_tensor(
                B_kt[:, hp, wc], ps[1][:, 0:512], K_bkpT[:, hp:hp + 1], K_sin[:, wc],
                ALU.add, ALU.mult),
                reads=(psB[1], kbufs["A_bkpT"], kbufs["SINA"]), writes=(bB,))
            P.emit("V", lambda v, hp=hp, wc=wc: v.tensor_add(
                B_kt[:, hp, wc], B_kt[:, hp, wc], s_t1[:, 0:512]),
                reads=(bt1, bB), writes=(bB,))
    Cv4 = C_v[:, :, :].rearrange("p t (h e) -> p t h e", e=65)
    P.emit("V", lambda v: v.memset(Cv4[:, :, :, 64:65], 1.0), reads=(bC,), writes=(bC,))
    g_load_ch(P, W1, dWv.ap().rearrange("(k p) n -> p k n", p=128), "bW1", 4)
    for j in range(NT):
        for half in range(2):
            pi = half
            P.emit("T", [
                (lambda t, cc=cc, j=j, half=half, pi=pi: t.matmul(
                    ps[pi][:, 0:512], A_x0T[:, cc, j * 128:(j + 1) * 128],
                    W1[:, cc, half * 512:(half + 1) * 512],
                    start=(cc == 0), stop=(cc == 7)))
                for cc in range(8)],
                reads=(bW1, bA, psB[pi]), writes=(psB[pi],))
            P.emit("V", lambda v, j=j, half=half, pi=pi: v.tensor_add(
                Cv4[:, j, half * 8:(half + 1) * 8, 0:64],
                ps[pi][:, 0:512].rearrange("p (h e) -> p h e", e=64),
                K_bv[:, half * 512:(half + 1) * 512].rearrange("p (h e) -> p h e", e=64)),
                reads=(psB[pi], kbufs["A_bvR"]), writes=(bC,))

    P.alias(P.buf(("exp", 2)), P.buf("bW1"))
    P.alias(P.buf(("exp", 3)), P.buf("bW1"))
    for hp in range(8):
        qwb = s_qw[hp % 2]; qwpb = s_qwp[hp % 2]
        bqw = P.buf(("qw", hp % 2)); bqwp = P.buf(("qwp", hp % 2))
        for c0 in range(0, 8, 2):
            P.emit("G", lambda g, hp=hp, qwb=qwb, c0=c0: g.dma_start(
                qwb[:, c0:c0 + 2, :], dWq.ap().rearrange("(k p) n -> p k n", p=128)
                [:, c0:c0 + 2, hp * 128:(hp + 1) * 128]),
                reads=(bqw,), writes=(bqw,), dma=True)
            P.emit("G", lambda g, hp=hp, qwpb=qwpb, c0=c0: g.dma_start(
                qwpb[:, c0:c0 + 2, :], dWqp.ap().rearrange("(k p) n -> p k n", p=128)
                [:, c0:c0 + 2, hp * 128:(hp + 1) * 128]),
                reads=(bqwp,), writes=(bqwp,), dma=True)
        for w in range(4):
            wc = slice(w * 512, (w + 1) * 512)
            P.emit("T", [
                (lambda t, cc=cc, wc=wc, qwb=qwb: t.matmul(
                    ps[0][:, 0:512], qwb[:, cc, :], A_x0T[:, cc, wc],
                    start=(cc == 0), stop=(cc == 7))) for cc in range(8)],
                reads=(bqw, bA, psB[0]), writes=(psB[0],))
            P.emit("T", [
                (lambda t, cc=cc, wc=wc, qwpb=qwpb: t.matmul(
                    ps[1][:, 0:512], qwpb[:, cc, :], A_x0T[:, cc, wc],
                    start=(cc == 0), stop=(cc == 7))) for cc in range(8)],
                reads=(bqwp, bA, psB[1]), writes=(psB[1],))
            bt1 = P.buf("s_t1"); bqt = P.buf("s_qt")
            P.emit("V", lambda v, hp=hp, wc=wc: v.scalar_tensor_tensor(
                s_t1[:, 0:512], ps[0][:, 0:512], K_bqT[:, hp:hp + 1], K_cos[:, wc],
                ALU.add, ALU.mult),
                reads=(psB[0], kbufs["A_bqT"], kbufs["COSA"]), writes=(bt1,))
            P.emit("V", lambda v, hp=hp, wc=wc: v.scalar_tensor_tensor(
                s_qt[:, 0:512], ps[1][:, 0:512], K_bqpT[:, hp:hp + 1], K_sin[:, wc],
                ALU.add, ALU.mult),
                reads=(psB[1], kbufs["A_bqpT"], kbufs["SINA"]), writes=(bqt,))
            P.emit("V", lambda v: v.tensor_add(s_qt[:, 0:512], s_qt[:, 0:512],
                                               s_t1[:, 0:512]),
                reads=(bt1, bqt), writes=(bqt,))
            nS = 4 * w + 4
            j0s = [max(0, s * 128 - w * 512) for s in range(nS)]
            ROT2 = ((4, 5), (0, 1))
            def emit_av(s, j0, hp=hp, w=w, nS=nS):
                for hh in range(2):
                    h = hp * 2 + hh
                    pc = 2 + hh
                    be = P.buf(("exp", s % 4))
                    P.emit("T", lambda t, s=s, j0=j0, pc=pc, h=h, hh=hh, nS=nS: t.matmul(
                        ps[pc][0:65, j0:512], C_v[:, s, h * 65:(h + 1) * 65],
                        s_exp[s % 4][:, hh * 512 + j0:hh * 512 + 512],
                        start=(s == 0), stop=(s == nS - 1)),
                        reads=(bC, be, psB[pc]), writes=(psB[pc],))
            for s in range(nS):
                j0 = j0s[s]
                pr = ROT2[s % 2]
                P.emit("T", [
                    (lambda t, s=s, j0=j0, hp=hp, w=w, pr=pr, hh=hh, r0=r0: t.matmul(
                        ps[pr[hh]][:, j0:512],
                        B_kt[r0:r0 + 64, hp, s * 128:(s + 1) * 128],
                        s_qt[r0:r0 + 64, j0:512], start=True, stop=True,
                        tile_position=(r0, 0)))
                    for hh, r0 in ((0, 0), (1, 64))],
                    reads=(bB, P.buf("s_qt"), psB[pr[0]], psB[pr[1]]),
                    writes=(psB[pr[0]], psB[pr[1]]))
                be = P.buf(("exp", s % 4))
                pdp = pd_pair(P, pr)
                P.emit("S", lambda sc, s=s, j0=j0, pdp=pdp: sc.activation(
                    s_exp[s % 4][:, :].rearrange("p (h c) -> p h c", h=2)[:, :, j0:512],
                    pdp[:, :].rearrange("p (h c) -> p h c", h=2)[:, :, j0:512],
                    AF.Exp, bias=0.0, scale=K_sc[:, 0:1]),
                    reads=(psB[pr[0]], psB[pr[1]], kbufs["SCAL"]), writes=(be,))
                if s * 128 >= w * 512:
                    for hh in range(2):
                        P.emit("V", lambda v, s=s, j0=j0, hh=hh: v.tensor_mul(
                            s_exp[s % 4][:, hh * 512 + j0:hh * 512 + j0 + 128],
                            s_exp[s % 4][:, hh * 512 + j0:hh * 512 + j0 + 128],
                            K_ut[:, :]),
                            reads=(be, kbufs["UTRI"]), writes=(be,))
                if s >= 1:
                    emit_av(s - 1, j0s[s - 1])
            emit_av(nS - 1, j0s[nS - 1])
            for hh in range(2):
                h = hp * 2 + hh
                r0 = hh * 64
                pc = 2 + hh
                bpc = psB[pc]
                brec = P.buf("s_rec")
                P.emit("S", lambda sc, pc=pc, hh=hh: sc.activation(
                    s_rec[64:65, 512:1024], ps[pc][64:65, 0:512], AF.Ln, bias=0.0),
                    reads=(bpc,), writes=(brec,))
                P.emit("S", lambda sc: sc.activation(
                    s_rec[64:65, 0:512], s_rec[64:65, 512:1024], AF.Exp,
                    bias=0.0, scale=K_sc[64:65, 8:9]),
                    reads=(brec, kbufs["SCAL"]), writes=(brec,))
                P.emit("T", lambda t: t.matmul(
                    ps[5][0:64, 0:512], K_on[64:65, 0:64], s_rec[64:65, 0:512],
                    start=True, stop=True),
                    reads=(brec, kbufs["ONEBF"], psB[5]), writes=(psB[5],))
                P.emit("V", lambda v: v.tensor_copy(s_rec[0:64, 0:512],
                                                    ps[5][0:64, 0:512]),
                    reads=(psB[5],), writes=(brec,))
                if hp == 0:
                    dst = lambda wc=wc, r0=r0: s_ctx0[r0:r0 + 64, wc]
                    dstb = bctx0
                else:
                    dst = lambda wc=wc, r0=r0, hp=hp: B_kt[r0:r0 + 64, hp - 1, wc]
                    dstb = bB
                P.emit("V", lambda v, dst=dst, pc=pc: v.tensor_mul(
                    dst(), ps[pc][0:64, 0:512], s_rec[0:64, 0:512]),
                    reads=(bpc, brec), writes=(dstb,))

    P.alias(P.buf("bW1"), P.buf(("exp", 2)), P.buf(("exp", 3)))
    g_load_ch(P, W1, dWo.ap().rearrange("(k p) n -> p k n", p=128), "bW1", 4)
    bsm = P.buf("s_small")
    P.alias(P.buf("s_X"), P.buf(("qw", 0)), P.buf(("qw", 1)))
    P.alias(P.buf("s_xres"), P.buf(("qwp", 0)), P.buf(("qwp", 1)))
    P.alias(P.buf("s_dump"), P.buf("s_qt"), P.buf("s_rec"))
    P.alias(bsm, P.buf(("exp", 0)), P.buf(("exp", 1)))
    P.alias(P.buf("s_t1"), P.buf(("exp", 2)), P.buf(("exp", 3)))
    for j in range(NT):
        bxr = P.buf("s_xres")
        P.emit("G", lambda g, j=j: g.dma_start(s_xres[:, :], xr[:, j, :]),
               reads=(bxr,), writes=(bxr,), dma=True)
        for half in range(2):
            pi = half
            def mml(cc, j=j, half=half, pi=pi):
                def f(t):
                    src = s_ctx0[:, j * 128:(j + 1) * 128] if cc == 0 else \
                        B_kt[:, cc - 1, j * 128:(j + 1) * 128]
                    return t.matmul(ps[pi][:, 0:512], src,
                                    W1[:, cc, half * 512:(half + 1) * 512],
                                    start=(cc == 0), stop=(cc == 7))
                return f
            P.emit("T", [mml(cc) for cc in range(8)],
                   reads=(bW1, bB, bctx0, psB[pi]), writes=(psB[pi],))
        bX = P.buf("s_X")
        for half in range(2):
            hs = slice(half * 512, (half + 1) * 512)
            P.emit("V", lambda v, hs=hs, half=half: v.tensor_add(
                s_X[:, hs], ps[half][:, 0:512], K_bo[:, hs]),
                reads=(psB[half], kbufs["A_boR"]), writes=(bX,))
        for half in range(2):
            hs = slice(half * 512, (half + 1) * 512)
            P.emit("V", lambda v, hs=hs, half=half: v.scalar_tensor_tensor(
                s_X[:, hs], s_X[:, hs], 0.0, s_xres[:, hs], ALU.add, ALU.add,
                accum_out=s_small[:, half:half + 1]),
                reads=(bX, bxr), writes=(bX, bsm))
        P.emit("V", lambda v: v.scalar_tensor_tensor(
            s_dump[:, 0:1024], s_X[:, 0:1024], 0.0, s_X[:, 0:1024], ALU.add,
            ALU.mult, accum_out=s_small[:, 2:3]),
            reads=(bX,), writes=(P.buf("s_dump"), bsm))
        P.emit("V", lambda v: v.tensor_add(s_small[:, 3:4], s_small[:, 0:1],
                                           s_small[:, 1:2]), reads=(bsm,), writes=(bsm,))
        P.emit("V", lambda v: v.tensor_scalar_mul(s_small[:, 4:5], s_small[:, 3:4],
                                                  1.0 / 1024.0), reads=(bsm,), writes=(bsm,))
        P.emit("V", lambda v: v.tensor_scalar(s_small[:, 5:6], s_small[:, 4:5],
                                              s_small[:, 4:5], -1e-5,
                                              ALU.mult, ALU.add),
               reads=(bsm,), writes=(bsm,))
        P.emit("V", lambda v: v.scalar_tensor_tensor(
            s_small[:, 6:7], s_small[:, 2:3], 1.0 / 1024.0, s_small[:, 5:6],
            ALU.mult, ALU.subtract), reads=(bsm,), writes=(bsm,))
        P.emit("S", lambda sc: sc.activation(s_small[:, 7:8], s_small[:, 6:7],
                                             AF.Ln, bias=0.0), reads=(bsm,), writes=(bsm,))
        P.emit("S", lambda sc: sc.activation(s_small[:, 8:9], s_small[:, 7:8],
                                             AF.Exp, bias=0.0, scale=K_sc[:, 1:2]),
               reads=(bsm, kbufs["SCAL"]), writes=(bsm,))
        P.emit("V", lambda v: v.scalar_tensor_tensor(
            s_small[:, 9:10], s_small[:, 4:5], -1.0, s_small[:, 8:9],
            ALU.mult, ALU.mult), reads=(bsm,), writes=(bsm,))
        P.emit("V", lambda v, j=j: v.tensor_scalar(
            R[:, j, :], s_X[:, 0:1024], s_small[:, 8:9], s_small[:, 9:10],
            ALU.mult, ALU.add), reads=(bX, bsm), writes=(bR[j],))
    bxb = P.buf("s_dump")
    for j in range(NT):
        P.emit("S", lambda sc, j=j: sc.copy(s_dump[:, 0:1024], R[:, j, :]),
               reads=(bR[j],), writes=(bxb,))
        for cc in range(8):
            pbi = cc % 2
            P.emit("T", lambda t, cc=cc, pbi=pbi: t.transpose(
                pb[pbi][:, 0:128], s_dump[:, cc * 128:(cc + 1) * 128], K_id[:, :]),
                reads=(bxb, kbufs["IDN"], pbB[pbi]), writes=(pbB[pbi],))
            P.emit("V", lambda v, j=j, cc=cc, pbi=pbi: v.tensor_copy(
                B_kt[:, cc, j * 128:(j + 1) * 128], pb[pbi][:, 0:128]),
                reads=(pbB[pbi],), writes=(bB,))
    return R, B_kt, bR, bB


def emit_xT_from_R(P, R, bR, dstT, bdst, K_id, kb_id, pb, pbB, s_dump, bdump):
    for j in range(NT):
        P.emit("S", lambda sc, j=j: sc.copy(s_dump[:, 0:1024], R[:, j, :]),
               reads=(bR[j],), writes=(bdump,))
        for cc in range(8):
            pbi = cc % 2
            P.emit("T", lambda t, cc=cc, pbi=pbi: t.transpose(
                pb[pbi][:, 0:128], s_dump[:, cc * 128:(cc + 1) * 128], K_id[:, :]),
                reads=(bdump, kb_id, pbB[pbi]), writes=(pbB[pbi],))
            P.emit("V", lambda v, j=j, cc=cc, pbi=pbi: v.tensor_copy(
                dstT[:, cc, j * 128:(j + 1) * 128], pb[pbi][:, 0:128]),
                reads=(pbB[pbi],), writes=(bdst,))

def emit_ln_tiles(P, R, bR, j, stats_src, stats_bufs, s_small, bsm, K_sc, kb_sc,
                  apply_in, apply_bufs):
    P.emit("V", lambda v: v.tensor_add(s_small[:, 3:4], s_small[:, 0:1],
                                       s_small[:, 1:2]), reads=(bsm,), writes=(bsm,))
    P.emit("V", lambda v: v.tensor_scalar_mul(s_small[:, 4:5], s_small[:, 3:4],
                                              1.0 / 1024.0), reads=(bsm,), writes=(bsm,))
    P.emit("V", lambda v: v.tensor_scalar(s_small[:, 5:6], s_small[:, 4:5],
                                          s_small[:, 4:5], -1e-5,
                                          ALU.mult, ALU.add), reads=(bsm,), writes=(bsm,))
    P.emit("V", lambda v: v.scalar_tensor_tensor(
        s_small[:, 6:7], s_small[:, 2:3], 1.0 / 1024.0, s_small[:, 5:6],
        ALU.mult, ALU.subtract), reads=(bsm,), writes=(bsm,))
    P.emit("S", lambda sc: sc.activation(s_small[:, 7:8], s_small[:, 6:7],
                                         AF.Ln, bias=0.0), reads=(bsm,), writes=(bsm,))
    P.emit("S", lambda sc: sc.activation(s_small[:, 8:9], s_small[:, 7:8],
                                         AF.Exp, bias=0.0, scale=K_sc[:, 1:2]),
           reads=(bsm, kb_sc), writes=(bsm,))
    P.emit("V", lambda v: v.scalar_tensor_tensor(
        s_small[:, 9:10], s_small[:, 4:5], -1.0, s_small[:, 8:9],
        ALU.mult, ALU.mult), reads=(bsm,), writes=(bsm,))
    P.emit("V", lambda v, j=j: v.tensor_scalar(
        R[:, j, :], apply_in, s_small[:, 8:9], s_small[:, 9:10],
        ALU.mult, ALU.add), reads=tuple(apply_bufs) + (bsm,), writes=(bR[j],))


def emit_stage_B(P, x_in=None, dbg=False):
    nc = P.nc
    dbg_outs = {}
    def DUMP(name, shape, dtype, ap_fn, rbufs):
        if not dbg:
            return
        d = P.nc.dram_tensor(name, shape, dtype, kind="ExternalOutput")
        P.emit("Y", lambda y: y.dma_start(d.ap(), ap_fn()), reads=rbufs,
               writes=(), dma=True)
        dbg_outs[name] = d
    R = P.sb("R", [128, NT, 1024], F32, OFF_R)
    Bt = P.sb("Bt", [128, 8, 2048], BF16, OFF_B)
    A_kt = P.sb("A_kt", [128, 8, 2048], BF16, OFF_A)
    A_w1 = P.sb("A_w1", [128, 8, 2048], BF16, OFF_A)
    C_v = P.sb("C_v", [128, 16, 528], BF16, OFF_C)
    C_ao = P.sb("C_ao", [128, 4, 2048], BF16, OFF_C + 16896)
    C_w2 = P.sb("C_w2", [128, 16, 1024], BF16, OFF_C)
    Wa = P.sb("Wa", [128, 8, 512], BF16, OFF_W)
    Wb = P.sb("Wb", [128, 8, 512], BF16, OFF_W + 8192)
    Wo2 = P.sb("Wo2", [128, 4, 1024], BF16, OFF_W + 8192)
    s_qt = P.sb("s_qt", [128, 512], BF16, OFF_S)
    s_rec = P.sb("s_rec", [128, 1024], BF16, OFF_S + 1024)
    s_exp = [P.sb(f"s_exp{i}", [128, 1024], BF16, OFF_S + 4096 + i * 2048) for i in range(2)] + \
            [P.sb(f"s_exp{2 + i}", [128, 1024], BF16, OFF_S + 10240 + i * 2048) for i in range(2)]
    s_X = P.sb("s_X", [128, 1024], F32, OFF_S + 4096)
    s_dump = P.sb("s_dump", [128, 1024], BF16, OFF_S + 8192)
    s_mT = P.sb("s_mT", [128, 1024], F32, OFF_S + 10240)
    s_small = P.sb("s_small", [128, 64], F32, OFF_S + 14336)
    s_h1 = [P.sb(f"s_h1{i}", [128, 256], BF16, OFF_S + 14592 + i * 512) for i in range(2)]
    s_rr = P.sb("s_rr", [128, 16], F32, OFF_S + 15616)
    s_cb = P.sb("s_cb", [128, 8], BF16, OFF_S + 15680)
    K_id, K_idf, K_ut, K_on, K_onf, K_sc, ps, pb, psB, pbB, kbufs0 = setup_common(P)
    KZ = OFF_K + 2048
    K_qbT = P.sb("K_qbT", [128, 8], F32, KZ)
    K_kbT = P.sb("K_kbT", [128, 8], F32, KZ + 32)
    K_bv = P.sb("K_bv", [128, 512], BF16, KZ + 64)
    K_bo = P.sb("K_bo", [128, 1024], BF16, KZ + 1088)
    K_rw = P.sb("K_rw", [128, 8, 8], F32, KZ + 3136)
    K_rb = P.sb("K_rb", [128, 8], F32, KZ + 3392)
    K_b1T = P.sb("K_b1T", [128, 8, 16], F32, KZ + 3424)
    K_b2 = P.sb("K_b2", [8, 1024], BF16, KZ + 3936)
    K_cT = P.sb("K_cT", [8, 2048], BF16, KZ + 5984)
    K_c3 = P.sb("K_c3", [128, 16, 8], F32, KZ + 10080)
    kbufs = dict(kbufs0)
    kbufs.update(load_zone(P, [
        ("B_qbT", K_qbT, [128, 8]), ("B_kbT", K_kbT, [128, 8]),
        ("B_vbR", K_bv, [128, 512]), ("B_obR", K_bo, [128, 1024]),
        ("B_rw", K_rw, [1024, 8], "(k p) n -> p k n"), ("B_rbR", K_rb, [128, 8]),
        ("B_b1T", K_b1T, [128, 8, 16]), ("B_b2", K_b2, [8, 1024]),
    ]))
    dkw = P.din("B_kw", [1024, 512], F32)
    dvw = P.din("B_vw", [1024, 512], F32)
    dqw = P.din("B_qw", [1024, 512], F32)
    dow = P.din("B_ow", [512, 1024], F32)
    dw1 = P.din("B_w1", [8, 1024, 2048], F32)
    dw2 = P.din("B_w2", [8, 2048, 1024], F32)
    bR = [P.buf(("R", j)) for j in range(NT)]
    bBt = P.buf("bBt"); bA = P.buf("bA"); bC = P.buf("bC")
    bWa = P.buf("bWa"); bWb = P.buf("bWb")
    bc3 = P.buf("bc3"); bcT = P.buf("bcT")
    bsm = P.buf("s_small")
    bdump0 = P.buf("s_dump")
    if x_in is not None:
        _load_R_f32(P, x_in, R, bR)
        emit_xT_from_R(P, R, bR, Bt, bBt, K_id, kbufs["IDN"], pb, pbB,
                       s_dump, bdump0)

    g_load_ch(P, Wa, dkw.ap().rearrange("(k p) n -> p k n", p=128), "bWa", 4)
    g_load_ch(P, Wb, dvw.ap().rearrange("(k p) n -> p k n", p=128), "bWb", 4)
    for hp in range(8):
        for w in range(4):
            wc = slice(w * 512, (w + 1) * 512)
            pi = (hp * 4 + w) % 2
            grp = []
            for hh in range(2):
                r0 = hh * 64
                for cc in range(8):
                    grp.append(lambda t, cc=cc, hp=hp, hh=hh, r0=r0, wc=wc, pi=pi:
                               t.matmul(
                        ps[pi][r0:r0 + 32, 0:512],
                        Wa[:, cc, hp * 64 + hh * 32:hp * 64 + hh * 32 + 32],
                        Bt[:, cc, wc], start=(cc == 0), stop=(cc == 7),
                        tile_position=(0, r0)))
            P.emit("T", grp, reads=(bWa, bBt, psB[pi]), writes=(psB[pi],))
            for r0 in (0, 64):
                P.emit("V", lambda v, hp=hp, wc=wc, pi=pi, r0=r0: v.tensor_scalar(
                    A_kt[r0:r0 + 32, hp, wc], ps[pi][r0:r0 + 32, 0:512],
                    K_kbT[r0:r0 + 32, hp:hp + 1], None, ALU.add),
                    reads=(psB[pi], kbufs["B_kbT"]), writes=(bA,))
    Cv4 = C_v[:, :, :].rearrange("p t (h e) -> p t h e", e=33)
    P.emit("V", lambda v: v.memset(Cv4[:, :, :, 32:33], 1.0), reads=(bC,), writes=(bC,))
    for j in range(NT):
        pi = j % 2
        P.emit("T", [
            (lambda t, cc=cc, j=j, pi=pi: t.matmul(
                ps[pi][:, 0:512], Bt[:, cc, j * 128:(j + 1) * 128],
                Wb[:, cc, 0:512], start=(cc == 0), stop=(cc == 7)))
            for cc in range(8)],
            reads=(bWb, bBt, psB[pi]), writes=(psB[pi],))
        P.emit("V", lambda v, j=j, pi=pi: v.tensor_add(
            Cv4[:, j, :, 0:32],
            ps[pi][:, 0:512].rearrange("p (h e) -> p h e", e=32),
            K_bv[:, :].rearrange("p (h e) -> p h e", e=32)),
            reads=(psB[pi], kbufs["B_vbR"]), writes=(bC,))

    DUMP("d_kt", [128, 8, 2048], BF16, lambda: A_kt[:, :, :], (bA,))
    DUMP("d_v", [128, 16, 528], BF16, lambda: C_v[:, :, :], (bC,))
    g_load_ch(P, Wa, dqw.ap().rearrange("(k p) n -> p k n", p=128), "bWa", 4)
    bAO = P.buf("C_ao")
    for hp in range(8):
        for w in range(4):
            wc = slice(w * 512, (w + 1) * 512)
            grp = []
            for hh in range(2):
                r0 = hh * 64
                for cc in range(8):
                    grp.append(lambda t, cc=cc, hp=hp, hh=hh, r0=r0, wc=wc: t.matmul(
                        ps[0][r0:r0 + 32, 0:512],
                        Wa[:, cc, hp * 64 + hh * 32:hp * 64 + hh * 32 + 32],
                        Bt[:, cc, wc], start=(cc == 0), stop=(cc == 7),
                        tile_position=(0, r0)))
            P.emit("T", grp, reads=(bWa, bBt, psB[0]), writes=(psB[0],))
            bqt = P.buf("s_qt")
            for r0 in (0, 64):
                P.emit("V", lambda v, hp=hp, r0=r0: v.tensor_scalar(
                    s_qt[r0:r0 + 32, 0:512], ps[0][r0:r0 + 32, 0:512],
                    K_qbT[r0:r0 + 32, hp:hp + 1], None, ALU.add),
                    reads=(psB[0], kbufs["B_qbT"]), writes=(bqt,))
            ROT2 = ((4, 5), (0, 1))
            def emit_av(sck, hp=hp, w=w):
                for hh in range(2):
                    h = hp * 2 + hh
                    pc = 2 + hh
                    be = P.buf(("exp", sck % 4))
                    P.emit("T", lambda t, sck=sck, pc=pc, h=h, hh=hh: t.matmul(
                        ps[pc][0:33, 0:512], C_v[:, sck, h * 33:(h + 1) * 33],
                        s_exp[sck % 4][:, hh * 512:hh * 512 + 512],
                        start=(sck == 0), stop=(sck == 15)),
                        reads=(bC, be, psB[pc]), writes=(psB[pc],))
            for sck in range(16):
                pr = ROT2[sck % 2]
                P.emit("T", [
                    (lambda t, sck=sck, hp=hp, pr=pr, hh=hh, r0=r0: t.matmul(
                        ps[pr[hh]][:, 0:512],
                        A_kt[r0:r0 + 32, hp, sck * 128:(sck + 1) * 128],
                        s_qt[r0:r0 + 32, 0:512], start=True, stop=True,
                        tile_position=(r0, 0)))
                    for hh, r0 in ((0, 0), (1, 64))],
                    reads=(bA, P.buf("s_qt"), psB[pr[0]], psB[pr[1]]),
                    writes=(psB[pr[0]], psB[pr[1]]))
                be = P.buf(("exp", sck % 4))
                pdp = pd_pair(P, pr)
                P.emit("S", lambda sc, sck=sck, pdp=pdp: sc.activation(
                    s_exp[sck % 4][:, 0:1024], pdp[:, 0:1024],
                    AF.Exp, bias=0.0, scale=K_sc[:, 2:3]),
                    reads=(psB[pr[0]], psB[pr[1]], kbufs["SCAL"]), writes=(be,))
                if sck >= 1:
                    emit_av(sck - 1)
            emit_av(15)
            for hh in range(2):
                h = hp * 2 + hh
                r0 = hh * 64
                pc = 2 + hh
                bpc = psB[pc]
                brec = P.buf("s_rec")
                P.emit("S", lambda sc, pc=pc: sc.activation(
                    s_rec[32:33, 512:1024], ps[pc][32:33, 0:512], AF.Ln, bias=0.0),
                    reads=(bpc,), writes=(brec,))
                P.emit("S", lambda sc: sc.activation(
                    s_rec[32:33, 0:512], s_rec[32:33, 512:1024], AF.Exp,
                    bias=0.0, scale=K_sc[32:33, 8:9]),
                    reads=(brec, kbufs["SCAL"]), writes=(brec,))
                P.emit("T", lambda t: t.matmul(
                    ps[5][0:32, 0:512], K_on[32:33, 0:32], s_rec[32:33, 0:512],
                    start=True, stop=True),
                    reads=(brec, kbufs["ONEBF"], psB[5]), writes=(psB[5],))
                P.emit("V", lambda v: v.tensor_copy(s_rec[0:32, 0:512],
                                                    ps[5][0:32, 0:512]),
                    reads=(psB[5],), writes=(brec,))
                P.emit("V", lambda v, h=h, wc=wc, pc=pc: v.tensor_mul(
                    C_ao[(h % 4) * 32:(h % 4) * 32 + 32, h // 4, wc],
                    ps[pc][0:32, 0:512], s_rec[0:32, 0:512]),
                    reads=(bpc, brec), writes=(bAO,))

    g_load_ch(P, Wo2, dow.ap().rearrange("(k p) n -> p k n", p=128), "bWb", 2)
    bX = P.buf("s_X"); bmT = P.buf("s_mT"); brr = P.buf("s_rr")
    bdump = P.buf("s_dump")
    P.alias(bmT, P.buf(("exp", 2)), P.buf(("exp", 3)))
    P.alias(bX, P.buf(("exp", 0)), P.buf(("exp", 1)))
    for j in range(NT):
        for half in range(2):
            pi = half
            P.emit("T", [
                (lambda t, cc=cc, j=j, half=half, pi=pi: t.matmul(
                    ps[pi][:, 0:512], C_ao[:, cc, j * 128:(j + 1) * 128],
                    Wo2[:, cc, half * 512:(half + 1) * 512],
                    start=(cc == 0), stop=(cc == 3)))
                for cc in range(4)],
                reads=(bWb, bAO, psB[pi]), writes=(psB[pi],))
            P.emit("V", lambda v, half=half, pi=pi: v.tensor_add(
                s_X[:, half * 512:(half + 1) * 512], ps[pi][:, 0:512],
                K_bo[:, half * 512:(half + 1) * 512]),
                reads=(psB[pi], kbufs["B_obR"]), writes=(bX,))
        P.emit("S", lambda sc: sc.copy(s_dump[:, 0:1024], s_X[:, 0:1024]),
               reads=(bX,), writes=(bdump,))
        for cc in range(8):
            pbi = cc % 2
            P.emit("T", lambda t, cc=cc, pbi=pbi: t.transpose(
                pb[pbi][:, 0:128], s_dump[:, cc * 128:(cc + 1) * 128], K_id[:, :]),
                reads=(bdump, kbufs["IDN"], pbB[pbi]), writes=(pbB[pbi],))
            P.emit("V", lambda v, j=j, cc=cc, pbi=pbi: v.tensor_copy(
                Bt[:, cc, j * 128:(j + 1) * 128], pb[pbi][:, 0:128]),
                reads=(pbB[pbi],), writes=(bBt,))
        for cc in range(8):
            pi = 2 + cc % 2
            P.emit("T", lambda t, cc=cc, pi=pi: t.transpose(
                ps[pi][:, 0:128], s_X[:, cc * 128:(cc + 1) * 128], K_idf[:, :]),
                reads=(bX, kbufs["IDF"], psB[pi]), writes=(psB[pi],))
            P.emit("V", lambda v, cc=cc, pi=pi: v.tensor_copy(
                s_mT[:, cc * 128:(cc + 1) * 128], ps[pi][:, 0:128]),
                reads=(psB[pi],), writes=(bmT,))
        P.emit("T", [
            (lambda t, cc=cc: t.matmul(
                ps[4][:, 0:8], s_mT[:, cc * 128:(cc + 1) * 128],
                K_rw[:, cc, :], start=(cc == 0), stop=(cc == 7)))
            for cc in range(8)],
            reads=(bmT, kbufs["B_rw"], psB[4]), writes=(psB[4],))
        P.emit("V", lambda v: v.tensor_add(s_rr[:, 0:8], ps[4][:, 0:8], K_rb[:, 0:8]),
               reads=(psB[4], kbufs["B_rbR"]), writes=(brr,))
        P.emit("S", lambda sc: sc.activation(s_small[:, 16:24], s_rr[:, 0:8], AF.Exp),
               reads=(brr,), writes=(bsm,))
        P.emit("V", lambda v: v.tensor_reduce(s_small[:, 0:1], s_small[:, 16:24],
                                              AX.X, ALU.add), reads=(bsm,), writes=(bsm,))
        P.emit("V", lambda v: v.reciprocal(s_small[:, 1:2], s_small[:, 0:1]),
               reads=(bsm,), writes=(bsm,))
        P.emit("V", lambda v: v.tensor_scalar(s_small[:, 16:24], s_small[:, 16:24],
                                              s_small[:, 1:2], None, ALU.mult),
               reads=(bsm,), writes=(bsm,))
        P.emit("V", lambda v: v.tensor_reduce(s_small[:, 2:3], s_small[:, 16:24],
                                              AX.X, ALU.max), reads=(bsm,), writes=(bsm,))
        P.emit("V", lambda v: v.tensor_single_scalar(s_small[:, 24:32], s_small[:, 16:24],
                                                     s_small[:, 2:3], ALU.subtract),
               reads=(bsm,), writes=(bsm,))
        P.emit("S", lambda sc: sc.activation(s_small[:, 32:40], s_small[:, 24:32],
                                             AF.Exp, bias=0.0, scale=K_sc[:, 5:6]),
               reads=(bsm, kbufs["SCAL"]), writes=(bsm,))
        P.emit("V", lambda v: v.scalar_tensor_tensor(
            s_small[:, 40:48], s_small[:, 32:40], -1e30, s_small[:, 16:24],
            ALU.mult, ALU.add), reads=(bsm,), writes=(bsm,))
        P.emit("V", lambda v: v.tensor_reduce(s_small[:, 3:4], s_small[:, 40:48],
                                              AX.X, ALU.max), reads=(bsm,), writes=(bsm,))
        P.emit("V", lambda v: v.tensor_single_scalar(s_small[:, 24:32], s_small[:, 16:24],
                                                     s_small[:, 3:4], ALU.subtract),
               reads=(bsm,), writes=(bsm,))
        P.emit("V", lambda v: v.tensor_scalar_min(s_small[:, 24:32], s_small[:, 24:32],
                                                  0.0), reads=(bsm,), writes=(bsm,))
        P.emit("S", lambda sc: sc.activation(s_small[:, 32:40], s_small[:, 24:32],
                                             AF.Exp, bias=0.0, scale=K_sc[:, 5:6]),
               reads=(bsm, kbufs["SCAL"]), writes=(bsm,))
        P.emit("V", lambda v: v.tensor_mul(s_small[:, 40:48], s_small[:, 16:24],
                                           s_small[:, 32:40]), reads=(bsm,), writes=(bsm,))
        P.emit("V", lambda v: v.tensor_reduce(s_small[:, 4:5], s_small[:, 40:48],
                                              AX.X, ALU.add), reads=(bsm,), writes=(bsm,))
        P.emit("V", lambda v: v.tensor_scalar_add(s_small[:, 4:5], s_small[:, 4:5],
                                                  1e-8), reads=(bsm,), writes=(bsm,))
        P.emit("V", lambda v: v.reciprocal(s_small[:, 5:6], s_small[:, 4:5]),
               reads=(bsm,), writes=(bsm,))
        P.emit("V", lambda v, j=j: v.tensor_scalar(
            K_c3[:, j, :], s_small[:, 40:48], s_small[:, 5:6], K_sc[:, 4:5],
            ALU.mult, ALU.mult), reads=(bsm, kbufs["SCAL"]), writes=(bc3,))
        P.emit("V", lambda v, j=j: v.tensor_copy(s_cb[:, 0:8], K_c3[:, j, :]),
               reads=(bc3,), writes=(P.buf("s_cb"),))
        P.emit("T", lambda t: t.transpose(pb[0][0:8, 0:128], s_cb[:, 0:8], K_id[:, :]),
               reads=(P.buf("s_cb"), kbufs["IDN"], pbB[0]), writes=(pbB[0],))
        P.emit("V", lambda v, j=j: v.tensor_copy(K_cT[0:8, j * 128:(j + 1) * 128],
                                                 pb[0][0:8, 0:128]),
               reads=(pbB[0],), writes=(bcT,))

    DUMP("d_ao", [128, 4, 2048], BF16, lambda: C_ao[:, :, :], (bAO,))
    DUMP("d_mlaT", [128, 8, 2048], BF16, lambda: Bt[:, :, :], (bBt,))
    DUMP("d_c3", [128, 16, 8], F32, lambda: K_c3[:, :, :], (bc3,))
    import os as _os
    NEXP = int(_os.environ.get("KDBG_NEXP", "8"))
    NOB2 = bool(_os.environ.get("KDBG_NOB2"))
    NFC = int(_os.environ.get("KDBG_NFC", "16"))
    P.alias(P.buf("A_w1"), bA)
    P.alias(P.buf("C_w2"), bC, P.buf("C_ao"))
    bw1 = P.buf("A_w1"); bw2 = P.buf("C_w2")
    for e in range(NEXP):
        w1src = dw1.ap()[e].rearrange("(k p) n -> p k n", p=128)
        w2src = dw2.ap()[e].rearrange("(k p) n -> p k n", p=128)
        for f0 in range(0, 2048, 512):
            bc = P.buf(("A_w1", f0))
            P.alias(bc, bw1)
            for c0 in range(0, 8, 2):
                P.emit("G", lambda g, c0=c0, f0=f0, w1src=w1src: g.dma_start(
                    A_w1[:, c0:c0 + 2, f0:f0 + 512], w1src[:, c0:c0 + 2, f0:f0 + 512]),
                    reads=(), writes=(bc, bw1), dma=True)
        for c0 in range(0, 16, 2):
            bc = P.buf(("C_w2", c0))
            P.alias(bc, bw2)
            P.emit("G", lambda g, c0=c0, w2src=w2src: g.dma_start(
                C_w2[:, c0:c0 + 2, :], w2src[:, c0:c0 + 2, :]),
                reads=(), writes=(bc, bw2), dma=True)
        for tw in range(8):
            tc = slice(tw * 256, (tw + 1) * 256)
            def emit_h1(fc, tc=tc, e=e):
                pi = fc % 2
                bh = P.buf(("h1", fc % 2))
                P.emit("T", [
                    (lambda t, cc=cc, fc=fc, tc=tc, pi=pi: t.matmul(
                        ps[pi][:, 0:256], A_w1[:, cc, fc * 128:(fc + 1) * 128],
                        Bt[:, cc, tc], start=(cc == 0), stop=(cc == 7)))
                    for cc in range(8)],
                    reads=(P.buf(("A_w1", (fc // 4) * 512)), bBt, psB[pi]),
                    writes=(psB[pi],))
                P.emit("S", lambda sc, fc=fc, pi=pi, e=e: sc.activation(
                    s_h1[fc % 2][:, 0:256], ps[pi][:, 0:256], AF.Gelu,
                    bias=K_b1T[:, e, fc:fc + 1]),
                    reads=(psB[pi], kbufs["B_b1T"]), writes=(bh,))
            emit_h1(0)
            for fc in range(NFC):
                if fc + 1 < NFC:
                    emit_h1(fc + 1)
                bh = P.buf(("h1", fc % 2))
                for tt in range(2):
                    for dg in range(2):
                        pj = 2 + tt * 2 + dg
                        P.emit("T", lambda t, fc=fc, tt=tt, dg=dg, pj=pj: t.matmul(
                            ps[pj][:, 0:512], s_h1[fc % 2][:, tt * 128:(tt + 1) * 128],
                            C_w2[:, fc, dg * 512:(dg + 1) * 512],
                            start=(fc == 0), stop=(fc == NFC - 1)),
                            reads=(bh, P.buf(("C_w2", (fc // 2) * 2)), psB[pj]),
                            writes=(psB[pj],))
            for tt in range(2):
                j = tw * 2 + tt
                for dg in range(2):
                    pj = 2 + tt * 2 + dg
                    P.emit("V", lambda v, j=j, dg=dg, pj=pj, e=e: v.tensor_scalar(
                        s_X[:, dg * 512:(dg + 1) * 512], ps[pj][:, 0:512],
                        K_c3[:, j, e:e + 1], None, ALU.mult),
                        reads=(psB[pj], bc3), writes=(bX, psB[pj]))
                    P.emit("V", lambda v, j=j, dg=dg: v.tensor_add(
                        R[:, j, dg * 512:(dg + 1) * 512],
                        R[:, j, dg * 512:(dg + 1) * 512],
                        s_X[:, dg * 512:(dg + 1) * 512]),
                        reads=(bX, bR[j]), writes=(bR[j],))
    for j in (range(0) if NOB2 else range(NT)):
        for dg in range(2):
            pi = dg % 2
            P.emit("T", lambda t, j=j, dg=dg, pi=pi: t.matmul(
                ps[pi][:, 0:512], K_cT[0:8, j * 128:(j + 1) * 128],
                K_b2[0:8, dg * 512:(dg + 1) * 512], start=True, stop=True),
                reads=(bcT, kbufs["B_b2"], psB[pi]), writes=(psB[pi],))
            P.emit("V", lambda v, j=j, dg=dg, pi=pi: v.tensor_add(
                R[:, j, dg * 512:(dg + 1) * 512],
                R[:, j, dg * 512:(dg + 1) * 512], ps[pi][:, 0:512]),
                reads=(psB[pi], bR[j]), writes=(bR[j], psB[pi]))
    if dbg:
        dR = P.nc.dram_tensor("d_R", [T, CD], F32, kind="ExternalOutput")
        drr2 = dR.ap().rearrange("(j p) c -> p j c", p=128)
        for j in range(NT):
            P.emit("Y", lambda y, j=j: y.dma_start(drr2[:, j, :], R[:, j, :]),
                   reads=(bR[j],), writes=(), dma=True)
    bdump2 = P.buf("s_dump")
    for j in range(NT):
        P.emit("V", lambda v, j=j: v.scalar_tensor_tensor(
            s_dump[:, 0:1024], R[:, j, :], 0.0, R[:, j, :], ALU.add, ALU.mult,
            accum_out=s_small[:, 2:3]), reads=(bR[j],), writes=(bdump2, bsm))
        P.emit("V", lambda v, j=j: v.tensor_reduce(
            s_small[:, 0:1], R[:, j, :], AX.X, ALU.add), reads=(bR[j],), writes=(bsm,))
        P.emit("V", lambda v: v.memset(s_small[:, 1:2], 0.0), reads=(bsm,), writes=(bsm,))
        emit_ln_tiles(P, R, bR, j, None, None, s_small, bsm, K_sc, kbufs["SCAL"],
                      R[:, j, :], (bR[j],))
    emit_xT_from_R(P, R, bR, Bt, bBt, K_id, kbufs["IDN"], pb, pbB, s_dump, bdump2)
    return R, Bt, bR, bBt, (K_id, kbufs, pb, pbB, ps, psB, K_sc, dbg_outs)


def emit_stage_C(P, x_in=None, dbg=False):
    nc = P.nc
    def DUMP(name, shape, dtype, ap_fn, rbufs):
        if not dbg:
            return
        d = P.nc.dram_tensor(name, shape, dtype, kind="ExternalOutput")
        P.emit("Y", lambda y: y.dma_start(d.ap(), ap_fn()), reads=rbufs,
               writes=(), dma=True)
    R = P.sb("R", [128, NT, 1024], F32, OFF_R)
    Bt = P.sb("Bt", [128, 8, 2048], BF16, OFF_B)
    B_hnT = P.sb("B_hnT", [128, 4, 2048], BF16, OFF_B)
    A_q = P.sb("A_q", [128, 16, 512], BF16, OFF_A)
    A_k = P.sb("A_k", [128, 16, 512], BF16, OFF_A + 16384)
    C_v = P.sb("C_v", [128, 16, 520], BF16, OFF_C)
    C_og = P.sb("C_og", [128, 16, 512], BF16, OFF_C + 16640)
    Wt = P.sb("Wt", [128, 8, 512], BF16, OFF_W)
    Wt2 = P.sb("Wt2", [128, 8, 512], BF16, OFF_W + 8192)
    Wout = P.sb("Wout", [128, 4, 1024], BF16, OFF_W + 8192)
    s_g = P.sb("s_g", [128, 16, 16], BF16, OFF_S)
    s_eq = P.sb("s_eq", [128, 8], F32, OFF_S + 512)
    s_ek = P.sb("s_ek", [128, 8], F32, OFF_S + 544)
    s_dd = P.sb("s_dd", [128, 8], F32, OFF_S + 576)
    s_F = P.sb("s_F", [128, 8], F32, OFF_S + 608)
    s_qp = P.sb("s_qp", [128, 512], BF16, OFF_S + 640)
    s_kp = P.sb("s_kp", [128, 512], BF16, OFF_S + 1664)
    s_qpT = P.sb("s_qpT", [128, 4, 128], BF16, OFF_S + 2688)
    s_kpT = P.sb("s_kpT", [128, 4, 128], BF16, OFF_S + 3712)
    s_AT = [P.sb(f"s_AT{i}", [128, 128], BF16, OFF_S + 4736 + 256 * i) for i in range(2)]
    stf = [P.sb(f"stf{i}", [128, 4, 65], F32, OFF_S + 5248 + 1056 * i) for i in range(2)]
    stb = [P.sb(f"stb{i}", [128, 4, 65], BF16, OFF_S + 7360 + 544 * i) for i in range(2)]
    s_dn = P.sb("s_dn", [128, 8], F32, OFF_S + 8448)
    s_X = P.sb("s_X", [128, 1024], F32, OFF_S + 8480)
    s_dump = P.sb("s_dump", [128, 1024], BF16, OFF_S + 12576)
    s_small = P.sb("s_small", [128, 64], F32, OFF_S + 14624)
    s_hn = P.sb("s_hn", [128, 512], F32, OFF_S + 640)
    K_id, K_idf, K_ut, K_on, K_onf, K_sc, ps, pb, psB, pbB, kbufs0 = setup_common(P)
    KZ = OFF_K + 2048
    K_wif = P.sb("K_wif", [128, 8, 16], BF16, KZ)
    K_bif = P.sb("K_bif", [128, 16], F32, KZ + 512)
    K_bo = P.sb("K_bo", [128, 512], BF16, KZ + 576)
    K_lng = P.sb("K_lng", [128, 512], BF16, KZ + 1600)
    K_lnb = P.sb("K_lnb", [128, 512], BF16, KZ + 2624)
    kbufs = dict(kbufs0)
    kbufs.update(load_zone(P, [
        ("C_WifT", K_wif, [1024, 16], "(k p) n -> p k n"), ("C_bifR", K_bif, [128, 16]),
        ("C_boR", K_bo, [128, 512]),
        ("C_lngR", K_lng, [128, 512]), ("C_lnbR", K_lnb, [128, 512]),
    ]))
    dq = P.din("C_WqT", [1024, 512], F32)
    dk = P.din("C_WkT", [1024, 512], F32)
    dv = P.din("C_WvT", [1024, 512], F32)
    do = P.din("C_WoT", [1024, 512], F32)
    dwo = P.din("C_WoutT", [512, 1024], F32)
    bR = [P.buf(("R", j)) for j in range(NT)]
    bBt = P.buf("bBt"); bA = P.buf("bA"); bC = P.buf("bC")
    bW = P.buf("bW"); bW2 = P.buf("bW2")
    bsm = P.buf("s_small")
    if x_in is not None:
        _load_R_f32(P, x_in, R, bR)
        emit_xT_from_R(P, R, bR, Bt, bBt, K_id, kbufs["IDN"], pb, pbB,
                       s_dump, P.buf("s_dump"))

    g_load_ch(P, Wt, dq.ap().rearrange("(k p) n -> p k n", p=128), "bW", 4)
    g_load_ch(P, Wt2, dk.ap().rearrange("(k p) n -> p k n", p=128), "bW2", 4)
    Cv4 = C_v[:, :, :].rearrange("p t (h e) -> p t h e", e=65)
    P.emit("V", lambda v: v.memset(Cv4[:, :, :, 64:65], 1.0), reads=(bC,), writes=(bC,))
    for j in range(NT):
        P.emit("T", [
            (lambda t, cc=cc, j=j: t.matmul(
                ps[0][:, 0:512], Bt[:, cc, j * 128:(j + 1) * 128],
                Wt[:, cc, :], start=(cc == 0), stop=(cc == 7)))
            for cc in range(8)],
            reads=(bW, bBt, psB[0]), writes=(psB[0],))
        P.emit("S", lambda sc, j=j: sc.copy(A_q[:, j, :], ps[0][:, 0:512]),
               reads=(psB[0],), writes=(bA,))
        P.emit("T", [
            (lambda t, cc=cc, j=j: t.matmul(
                ps[1][:, 0:512], Bt[:, cc, j * 128:(j + 1) * 128],
                Wt2[:, cc, :], start=(cc == 0), stop=(cc == 7)))
            for cc in range(8)],
            reads=(bW2, bBt, psB[1]), writes=(psB[1],))
        P.emit("S", lambda sc, j=j: sc.activation(A_k[:, j, :], ps[1][:, 0:512],
                                                  AF.Copy, bias=0.0, scale=K_sc[:, 0:1]),
               reads=(psB[1], kbufs["SCAL"]), writes=(bA,))
    g_load_ch(P, Wt, dv.ap().rearrange("(k p) n -> p k n", p=128), "bW", 4)
    g_load_ch(P, Wt2, do.ap().rearrange("(k p) n -> p k n", p=128), "bW2", 4)
    bXc = P.buf("s_X")
    for j in range(NT):
        P.emit("T", [
            (lambda t, cc=cc, j=j: t.matmul(
                ps[0][:, 0:512], Bt[:, cc, j * 128:(j + 1) * 128],
                Wt[:, cc, :], start=(cc == 0), stop=(cc == 7)))
            for cc in range(8)],
            reads=(bW, bBt, psB[0]), writes=(psB[0],))
        P.emit("V", lambda v, j=j: v.tensor_copy(
            Cv4[:, j, :, 0:64], ps[0][:, 0:512].rearrange("p (h e) -> p h e", e=64)),
            reads=(psB[0],), writes=(bC,))
        P.emit("T", [
            (lambda t, cc=cc, j=j: t.matmul(
                ps[1][:, 0:512], Bt[:, cc, j * 128:(j + 1) * 128],
                Wt2[:, cc, :], start=(cc == 0), stop=(cc == 7)))
            for cc in range(8)],
            reads=(bW2, bBt, psB[1]), writes=(psB[1],))
        P.emit("V", lambda v, j=j: v.tensor_add(s_X[:, 0:512], ps[1][:, 0:512],
                                                K_bo[:, :]),
               reads=(psB[1], kbufs["C_boR"]), writes=(bXc,))
        P.emit("S", lambda sc, j=j: sc.activation(C_og[:, j, :], s_X[:, 0:512],
                                                  AF.Sigmoid),
               reads=(bXc,), writes=(bC,))
        P.emit("T", [
            (lambda t, cc=cc, j=j: t.matmul(
                ps[2][:, 0:16], Bt[:, cc, j * 128:(j + 1) * 128],
                K_wif[:, cc, :], start=(cc == 0), stop=(cc == 7)))
            for cc in range(8)],
            reads=(kbufs["C_WifT"], bBt, psB[2]), writes=(psB[2],))
        P.emit("V", lambda v, j=j: v.tensor_add(s_g[:, j, :], ps[2][:, 0:16],
                                                K_bif[:, :]),
               reads=(psB[2], kbufs["C_bifR"]), writes=(P.buf("s_g"),))

    DUMP("d_q", [128, 16, 512], BF16, lambda: A_q[:, :, :], (bA,))
    DUMP("d_k", [128, 16, 512], BF16, lambda: A_k[:, :, :], (bA,))
    DUMP("d_og", [128, 16, 512], BF16, lambda: C_og[:, :, :], (bC,))
    DUMP("d_g", [128, 16, 16], BF16, lambda: s_g[:, :, :], (P.buf("s_g"),))
    bstf = [[P.buf(("stf", par, h)) for h in range(8)] for par in range(2)]
    bstb = [[P.buf(("stb", par, h)) for h in range(8)] for par in range(2)]
    P.emit("V", lambda v: v.memset(stf[0][:, :, :], 0.0),
           reads=(), writes=tuple(bstf[0]))
    P.emit("V", lambda v: v.memset(stb[0][:, :, :], 0.0),
           reads=(), writes=tuple(bstb[0]))
    bg = P.buf("s_g"); beq = P.buf("s_eq"); bek = P.buf("s_ek")
    bdd = P.buf("s_dd"); bF = P.buf("s_F")
    bqp = P.buf("s_qp"); bkp = P.buf("s_kp")
    bqpT = P.buf("s_qpT"); bkpT = P.buf("s_kpT")
    for m in range(NT):
        par = m % 2
        P.emit("T", lambda t, m=m: t.matmul(
            ps[0][:, 0:8], K_ut[:, :], s_g[:, m, 8:16], start=True, stop=True),
            reads=(kbufs["UTRI"], bg, psB[0]), writes=(psB[0],))
        P.emit("S", lambda sc: sc.activation(s_eq[:, 0:8], ps[0][:, 0:8], AF.Exp),
               reads=(psB[0],), writes=(beq,))
        P.emit("V", lambda v, m=m: v.tensor_sub(s_dd[:, 0:8], s_g[:, m, 0:8],
                                                ps[0][:, 0:8]),
               reads=(bg, psB[0]), writes=(bdd, psB[0]))
        P.emit("S", lambda sc: sc.activation(s_ek[:, 0:8], s_dd[:, 0:8], AF.Exp),
               reads=(bdd,), writes=(bek,))
        P.emit("T", lambda t, m=m: t.matmul(ps[0][0:1, 8:16], K_on[:, 0:1],
                                            s_g[:, m, 8:16], start=True, stop=True),
               reads=(kbufs["ONEBF"], bg, psB[0]), writes=(psB[0],))
        P.emit("S", lambda sc: sc.activation(s_F[0:1, 0:8], ps[0][0:1, 8:16], AF.Exp),
               reads=(psB[0],), writes=(bF,))
        P.emit("T", lambda t: t.matmul(ps[0][:, 8:16], K_onf[0:1, 0:128],
                                       s_F[0:1, 0:8], start=True, stop=True),
               reads=(bF, kbufs["ONEF"], psB[0]), writes=(psB[0],))
        P.emit("V", lambda v: v.tensor_copy(s_F[:, 0:8], ps[0][:, 8:16]),
               reads=(psB[0],), writes=(bF,))
        for h in range(8):
            hp = h % 2
            r0 = hp * 64
            P.emit("V", lambda v, m=m, h=h: v.tensor_scalar(
                s_qp[:, h * 64:(h + 1) * 64], A_q[:, m, h * 64:(h + 1) * 64],
                s_eq[:, h:h + 1], None, ALU.mult),
                reads=(bA, beq), writes=(bqp,))
            P.emit("V", lambda v, m=m, h=h: v.tensor_scalar(
                s_kp[:, h * 64:(h + 1) * 64], A_k[:, m, h * 64:(h + 1) * 64],
                s_ek[:, h:h + 1], None, ALU.mult),
                reads=(bA, bek), writes=(bkp,))
            P.emit("T", lambda t, h=h: t.transpose(
                pb[0][0:64, 0:128], s_qp[:, h * 64:(h + 1) * 64], K_id[:, :]),
                reads=(bqp, kbufs["IDN"], pbB[0]), writes=(pbB[0],))
            P.emit("V", lambda v, h=h, r0=r0: v.tensor_copy(
                s_qpT[r0:r0 + 64, h // 2, :], pb[0][0:64, 0:128]),
                reads=(pbB[0],), writes=(bqpT,))
            P.emit("T", lambda t, h=h: t.transpose(
                pb[1][0:64, 0:128], s_kp[:, h * 64:(h + 1) * 64], K_id[:, :]),
                reads=(bkp, kbufs["IDN"], pbB[1]), writes=(pbB[1],))
            P.emit("V", lambda v, h=h, r0=r0: v.tensor_copy(
                s_kpT[r0:r0 + 64, h // 2, :], pb[1][0:64, 0:128]),
                reads=(pbB[1],), writes=(bkpT,))
            pe = 1 + (h % 2)
            P.emit("T", lambda t, h=h, r0=r0, pe=pe: t.matmul(
                ps[pe][:, 0:128], s_kpT[r0:r0 + 64, h // 2, :],
                s_qpT[r0:r0 + 64, h // 2, :], start=True, stop=True),
                reads=(bkpT, bqpT, psB[pe]), writes=(psB[pe],))
            bat = P.buf(("AT", h % 2))
            P.emit("V", lambda v, h=h, pe=pe: v.tensor_mul(
                s_AT[h % 2][:, :], ps[pe][:, 0:128], K_ut[:, :]),
                reads=(psB[pe], kbufs["UTRI"]), writes=(bat,))
            ph = 3 + (h % 2)
            P.emit("T", lambda t, h=h, m=m: t.matmul(
                ps[3 + (h % 2)][:, 0:65], s_AT[h % 2][:, :],
                C_v[:, m, h * 65:(h + 1) * 65], start=True, stop=False),
                reads=(bat, bC, psB[ph]), writes=(psB[ph],))
            P.emit("T", lambda t, h=h, r0=r0, par=par: t.matmul(
                ps[3 + (h % 2)][:, 0:65], s_qpT[r0:r0 + 64, h // 2, :],
                stb[par][r0:r0 + 64, h // 2, :], start=False, stop=True),
                reads=(bqpT, bstb[par][h], psB[ph]), writes=(psB[ph],))
            P.emit("T", lambda t, h=h, m=m: t.matmul(
                ps[5][0:64, 0:65], s_kp[:, h * 64:(h + 1) * 64],
                C_v[:, m, h * 65:(h + 1) * 65], start=True, stop=True),
                reads=(bkp, bC, psB[5]), writes=(psB[5],))
            P.emit("S", lambda sc, h=h, ph=ph: sc.activation(
                s_dn[:, h % 4:h % 4 + 1], ps[ph][:, 64:65], AF.Abs),
                reads=(psB[ph],), writes=(P.buf("s_dn"),))
            P.emit("V", lambda v, h=h: v.tensor_scalar_max(
                s_dn[:, h % 4:h % 4 + 1], s_dn[:, h % 4:h % 4 + 1], 1.0),
                reads=(P.buf("s_dn"),), writes=(P.buf("s_dn"),))
            P.emit("V", lambda v, h=h: v.reciprocal(
                s_dn[:, 4 + h % 4:5 + h % 4], s_dn[:, h % 4:h % 4 + 1]),
                reads=(P.buf("s_dn"),), writes=(P.buf("s_dn"),))
            P.emit("V", lambda v, h=h, m=m, ph=ph: v.scalar_tensor_tensor(
                C_v[:, m, h * 64:(h + 1) * 64], ps[ph][:, 0:64],
                s_dn[:, 4 + h % 4:5 + h % 4], C_og[:, m, h * 64:(h + 1) * 64],
                ALU.mult, ALU.mult),
                reads=(psB[ph], P.buf("s_dn"), bC), writes=(bC, psB[ph]))
            P.emit("V", lambda v, h=h, r0=r0, par=par: v.tensor_add(
                stf[1 - par][r0:r0 + 64, h // 2, :], stf[par][r0:r0 + 64, h // 2, :],
                ps[5][0:64, 0:65]),
                reads=(bstf[par][h], psB[5]), writes=(bstf[1 - par][h], psB[5]))
            P.emit("V", lambda v, h=h, r0=r0, par=par: v.tensor_scalar(
                stf[1 - par][r0:r0 + 64, h // 2, :], stf[1 - par][r0:r0 + 64, h // 2, :],
                s_F[r0:r0 + 64, h:h + 1], None, ALU.mult),
                reads=(bstf[1 - par][h], bF), writes=(bstf[1 - par][h],))
            P.emit("V", lambda v, h=h, r0=r0, par=par: v.tensor_copy(
                stb[1 - par][r0:r0 + 64, h // 2, :],
                stf[1 - par][r0:r0 + 64, h // 2, :]),
                reads=(bstf[1 - par][h],), writes=(bstb[1 - par][h],))

    DUMP("d_hs", [128, 16, 520], BF16, lambda: C_v[:, :, :], (bC,))
    g_load_ch(P, Wout, dwo.ap().rearrange("(k p) n -> p k n", p=128), "bW2", 2)
    bhn = P.buf("s_hn"); bdump = P.buf("s_dump")
    P.alias(bhn, P.buf("s_qp"), P.buf("s_kp"))
    for m in range(NT):
        P.emit("V", lambda v, m=m: v.scalar_tensor_tensor(
            s_dump[:, 0:512], C_v[:, m, 0:512], 0.0, C_v[:, m, 0:512],
            ALU.add, ALU.mult, accum_out=s_small[:, 2:3]),
            reads=(bC,), writes=(bdump, bsm))
        P.emit("V", lambda v, m=m: v.tensor_reduce(
            s_small[:, 0:1], C_v[:, m, 0:512], AX.X, ALU.add),
            reads=(bC,), writes=(bsm,))
        P.emit("V", lambda v: v.tensor_scalar_mul(s_small[:, 4:5], s_small[:, 0:1],
                                                  1.0 / 512.0), reads=(bsm,), writes=(bsm,))
        P.emit("V", lambda v: v.tensor_scalar(s_small[:, 5:6], s_small[:, 4:5],
                                              s_small[:, 4:5], -1e-5,
                                              ALU.mult, ALU.add), reads=(bsm,), writes=(bsm,))
        P.emit("V", lambda v: v.scalar_tensor_tensor(
            s_small[:, 6:7], s_small[:, 2:3], 1.0 / 512.0, s_small[:, 5:6],
            ALU.mult, ALU.subtract), reads=(bsm,), writes=(bsm,))
        P.emit("S", lambda sc: sc.activation(s_small[:, 7:8], s_small[:, 6:7],
                                             AF.Ln, bias=0.0), reads=(bsm,), writes=(bsm,))
        P.emit("S", lambda sc: sc.activation(s_small[:, 8:9], s_small[:, 7:8],
                                             AF.Exp, bias=0.0, scale=K_sc[:, 1:2]),
               reads=(bsm, kbufs["SCAL"]), writes=(bsm,))
        P.emit("V", lambda v: v.scalar_tensor_tensor(
            s_small[:, 9:10], s_small[:, 4:5], -1.0, s_small[:, 8:9],
            ALU.mult, ALU.mult), reads=(bsm,), writes=(bsm,))
        P.emit("V", lambda v, m=m: v.tensor_scalar(
            s_hn[:, 0:512], C_v[:, m, 0:512], s_small[:, 8:9], s_small[:, 9:10],
            ALU.mult, ALU.add), reads=(bC, bsm), writes=(bhn,))
        P.emit("V", lambda v, m=m: v.tensor_mul(s_hn[:, 0:512], s_hn[:, 0:512],
                                                K_lng[:, :]),
               reads=(bhn, kbufs["C_lngR"]), writes=(bhn,))
        P.emit("V", lambda v, m=m: v.tensor_add(s_dump[:, 0:512], s_hn[:, 0:512],
                                                K_lnb[:, :]),
               reads=(bhn, kbufs["C_lnbR"]), writes=(bdump,))
        for cc in range(4):
            pbi = cc % 2
            P.emit("T", lambda t, cc=cc, pbi=pbi: t.transpose(
                pb[pbi][:, 0:128], s_dump[:, cc * 128:(cc + 1) * 128], K_id[:, :]),
                reads=(bdump, kbufs["IDN"], pbB[pbi]), writes=(pbB[pbi],))
            P.emit("V", lambda v, m=m, cc=cc, pbi=pbi: v.tensor_copy(
                B_hnT[:, cc, m * 128:(m + 1) * 128], pb[pbi][:, 0:128]),
                reads=(pbB[pbi],), writes=(bBt,))
    bX = P.buf("s_X")
    for j in range(NT):
        for half in range(2):
            pi = half
            P.emit("T", [
                (lambda t, cc=cc, j=j, half=half, pi=pi: t.matmul(
                    ps[pi][:, 0:512], B_hnT[:, cc, j * 128:(j + 1) * 128],
                    Wout[:, cc, half * 512:(half + 1) * 512],
                    start=(cc == 0), stop=(cc == 3)))
                for cc in range(4)],
                reads=(bW2, bBt, psB[pi]), writes=(psB[pi],))
            P.emit("V", lambda v, j=j, half=half, pi=pi: v.scalar_tensor_tensor(
                s_X[:, half * 512:(half + 1) * 512], ps[pi][:, 0:512], 0.0,
                R[:, j, half * 512:(half + 1) * 512], ALU.add, ALU.add,
                accum_out=s_small[:, half:half + 1]),
                reads=(psB[pi], bR[j]), writes=(bX, bsm))
        P.emit("V", lambda v: v.scalar_tensor_tensor(
            s_dump[:, 0:1024], s_X[:, 0:1024], 0.0, s_X[:, 0:1024], ALU.add,
            ALU.mult, accum_out=s_small[:, 2:3]),
            reads=(bX,), writes=(bdump, bsm))
        emit_ln_tiles(P, R, bR, j, None, None, s_small, bsm, K_sc, kbufs["SCAL"],
                      s_X[:, 0:1024], (bX,))
    return R, bR


def emit_stage_D(P, out_dram, x_in=None, dbg=False):
    nc = P.nc
    def DUMP(name, shape, dtype, ap_fn, rbufs):
        if not dbg:
            return
        d = P.nc.dram_tensor(name, shape, dtype, kind="ExternalOutput")
        P.emit("Y", lambda y: y.dma_start(d.ap(), ap_fn()), reads=rbufs,
               writes=(), dma=True)
    R = P.sb("R", [128, NT, 1024], F32, OFF_R)
    A_xnT = P.sb("A_xnT", [128, 8, 2048], BF16, OFF_A)
    B_kk = P.sb("B_kk", [128, 8, 2048], BF16, OFF_B)
    C_v = P.sb("C_v", [128, 16, 1040], BF16, OFF_C)
    C_mla = P.sb("C_mla", [128, 16, 1024], BF16, OFF_C)
    C_wd = P.sb("C_wd", [128, 8, 1024], BF16, OFF_C)
    W_kvd = P.sb("W_kvd", [128, 8, 256], BF16, OFF_W)
    W_qd = P.sb("W_qd", [128, 8, 256], BF16, OFF_W + 4096)
    W_kvT = P.sb("W_kvT", [128, 2, 2048], BF16, OFF_W + 8192)
    W_ku = P.sb("W_ku", [128, 2, 512], BF16, OFF_W)
    W_qu = P.sb("W_qu", [128, 2, 512], BF16, OFF_W + 2048)
    W_vu = P.sb("W_vu", [128, 2, 1024], BF16, OFF_W + 4096)
    W_rq = P.sb("W_rq", [128, 2, 512], BF16, OFF_W + 4096)
    W_wo = P.sb("W_wo", [128, 8, 1024], BF16, OFF_W)
    s_y0 = P.sb("s_y0", [128, 2048], BF16, OFF_S)
    s_qt = P.sb("s_qt", [128, 512], BF16, OFF_S + 4096)
    s_rec = P.sb("s_rec", [128, 1024], BF16, OFF_S + 5120)
    s_exp = [P.sb(f"s_exp{i}", [128, 1024], BF16, OFF_W + 8192 + i * 2048) for i in range(4)]
    S_qlT = P.sb("S_qlT", [128, 2, 2048], BF16, OFF_S + 8192)
    s_rkw = [P.sb(f"s_rkw{i}", [128, 8, 64], BF16, OFF_S + 4096 + i * 1024) for i in range(2)]
    s_rkwp = [P.sb(f"s_rkwp{i}", [128, 8, 64], BF16, OFF_S + 6144 + i * 1024) for i in range(2)]
    s_X = P.sb("s_X", [128, 1024], F32, OFF_S)
    s_mT = P.sb("s_mT", [128, 1024], F32, OFF_S + 4096)
    s_dump = P.sb("s_dump", [128, 1024], BF16, OFF_S + 8192)
    K_id, K_idf, K_ut, K_on, K_onf, K_sc, ps, pb, psB, pbB, kbufs0 = setup_common(P)
    KZ = OFF_K + 2048
    K_cos = P.sb("K_cosD", [128, 2048], BF16, KZ)
    K_sin = P.sb("K_sinD", [128, 2048], BF16, KZ + 4096)
    K_wln1 = P.sb("K_wln1", [128, 1024], BF16, KZ + 8192)
    K_wln2 = P.sb("K_wln2", [128, 1024], BF16, KZ + 10240)
    K_rw = P.sb("K_rw7", [128, 8, 7], F32, KZ + 12288)
    K_rb = P.sb("K_rb7", [128, 7], F32, KZ + 12512)
    s_t1 = P.sb("s_t1D", [128, 512], BF16, KZ + 12576)
    s_t1k = P.sb("s_t1k", [128, 512], BF16, OFF_S)
    W_rqp = P.sb("W_rqp", [128, 2, 512], BF16, OFF_W + 6144)
    s_small = P.sb("s_small", [128, 64], F32, OFF_S + 10240)
    s_wts = P.sb("s_wts", [128, 16, 8], F32, OFF_S + 10496)
    s_hm = [P.sb(f"s_hm{i}", [128, 256], BF16, OFF_S + 11008 + i * 512) for i in range(2)]
    s_sg = P.sb("s_sg", [128, 512], BF16, OFF_S + 12032)
    kbufs = dict(kbufs0)
    kbufs.update(load_zone(P, [
        ("COSD", K_cos, [128, 2048]), ("SIND", K_sin, [128, 2048]),
        ("D_wln1R", K_wln1, [128, 1024]), ("D_wln2R", K_wln2, [128, 1024]),
        ("D_rw", K_rw, [1024, 7], "(k p) n -> p k n"), ("D_rbR", K_rb, [128, 7]),
    ], native=("COSD", "SIND")))
    dkvd = P.din("D_kvd", [1024, 256], F32)
    dqd = P.din("D_qd", [1024, 256], F32)
    dku = P.din("D_ku", [256, 512], F32)
    dqu = P.din("D_qu", [256, 512], F32)
    dvu = P.din("D_vu", [256, 1024], F32)
    drq = P.din("D_rq", [256, 512], F32)
    drqp = P.din("D_rqp", [256, 512], F32)
    drk = P.din("D_rk", [1024, 512], F32)
    drkp = P.din("D_rkp", [1024, 512], F32)
    dwo = P.din("D_wo", [1024, 1024], F32)
    dgs = P.din("D_gs", [1024, 1024], F32)
    dus = P.din("D_us", [1024, 1024], F32)
    dds = P.din("D_ds", [1024, 1024], F32)
    dgr = P.din("D_gr", [7, 1024, 1024], F32)
    dur = P.din("D_ur", [7, 1024, 1024], F32)
    ddr = P.din("D_dr", [7, 1024, 1024], F32)
    bR = [P.buf(("R", j)) for j in range(NT)]
    bA = P.buf("bA"); bB = P.buf("bB"); bC = P.buf("bC")
    bsm = P.buf("s_small")
    bdump = P.buf("s_dump")
    if x_in is not None:
        _load_R_f32(P, x_in, R, bR)

    for j in range(NT):
        P.emit("V", lambda v, j=j: v.scalar_tensor_tensor(
            s_dump[:, 0:1024], R[:, j, :], 0.0, R[:, j, :], ALU.add, ALU.mult,
            accum_out=s_small[:, 2:3]), reads=(bR[j],), writes=(bdump, bsm))
        P.emit("S", lambda sc: sc.activation(s_small[:, 7:8], s_small[:, 2:3],
                                             AF.Ln, bias=0.0), reads=(bsm,), writes=(bsm,))
        P.emit("S", lambda sc: sc.activation(s_small[:, 8:9], s_small[:, 7:8],
                                             AF.Exp, bias=K_sc[:, 7:8], scale=K_sc[:, 6:7]),
               reads=(bsm, kbufs["SCAL"]), writes=(bsm,))
        P.emit("V", lambda v: v.tensor_scalar_add(s_small[:, 9:10], s_small[:, 8:9],
                                                  1e-5), reads=(bsm,), writes=(bsm,))
        P.emit("V", lambda v: v.reciprocal(s_small[:, 10:11], s_small[:, 9:10]),
               reads=(bsm,), writes=(bsm,))
        P.emit("V", lambda v, j=j: v.scalar_tensor_tensor(
            s_dump[:, 0:1024], R[:, j, :], s_small[:, 10:11], K_wln1[:, :],
            ALU.mult, ALU.mult), reads=(bR[j], bsm, kbufs["D_wln1R"]), writes=(bdump,))
        for cc in range(8):
            pbi = cc % 2
            P.emit("T", lambda t, cc=cc, pbi=pbi: t.transpose(
                pb[pbi][:, 0:128], s_dump[:, cc * 128:(cc + 1) * 128], K_id[:, :]),
                reads=(bdump, kbufs["IDN"], pbB[pbi]), writes=(pbB[pbi],))
            P.emit("V", lambda v, j=j, cc=cc, pbi=pbi: v.tensor_copy(
                A_xnT[:, cc, j * 128:(j + 1) * 128], pb[pbi][:, 0:128]),
                reads=(pbB[pbi],), writes=(bA,))

    g_load_ch(P, W_kvd, dkvd.ap().rearrange("(k p) n -> p k n", p=128), "bkvd")
    g_load_ch(P, W_qd, dqd.ap().rearrange("(k p) n -> p k n", p=128), "bqd")
    bkvT = P.buf("W_kvT"); bqlT = P.buf("S_qlT")
    P.alias(bqlT, bdump, bsm)
    for lc in range(2):
        for w in range(4):
            wc = slice(w * 512, (w + 1) * 512)
            pi = (lc * 4 + w) % 2
            P.emit("T", [
                (lambda t, cc=cc, lc=lc, wc=wc, pi=pi: t.matmul(
                    ps[pi][:, 0:512], W_kvd[:, cc, lc * 128:(lc + 1) * 128],
                    A_xnT[:, cc, wc], start=(cc == 0), stop=(cc == 7)))
                for cc in range(8)],
                reads=(P.buf("bkvd"), bA, psB[pi]), writes=(psB[pi],))
            P.emit("S", lambda sc, lc=lc, wc=wc, pi=pi: sc.copy(
                W_kvT[:, lc, wc], ps[pi][:, 0:512]), reads=(psB[pi],), writes=(bkvT,))
            P.emit("T", [
                (lambda t, cc=cc, lc=lc, wc=wc, pi=pi: t.matmul(
                    ps[2 + pi][:, 0:512], W_qd[:, cc, lc * 128:(lc + 1) * 128],
                    A_xnT[:, cc, wc], start=(cc == 0), stop=(cc == 7)))
                for cc in range(8)],
                reads=(P.buf("bqd"), bA, psB[2 + pi]), writes=(psB[2 + pi],))
            P.emit("S", lambda sc, lc=lc, wc=wc, pi=pi: sc.copy(
                S_qlT[:, lc, wc], ps[2 + pi][:, 0:512]), reads=(psB[2 + pi],), writes=(bqlT,))

    P.alias(P.buf("W_ku"), P.buf("bkvd"))
    P.alias(P.buf("W_qu"), P.buf("bkvd"))
    P.alias(P.buf("W_vu"), P.buf("bqd"))
    P.alias(P.buf("W_rq"), P.buf("bqd"))
    g_load(P, W_ku[:, :, :], dku.ap().rearrange("(k p) n -> p k n", p=128), "W_ku")
    g_load(P, W_qu[:, :, :], dqu.ap().rearrange("(k p) n -> p k n", p=128), "W_qu")
    g_load(P, W_vu[:, :, :], dvu.ap().rearrange("(k p) n -> p k n", p=128), "W_vu")
    Cv4 = C_v[:, :, :].rearrange("p t (h e) -> p t h e", e=65)
    P.emit("V", lambda v: v.memset(Cv4[:, :, :, 64:65], 1.0), reads=(bC,), writes=(bC,))
    bt1 = P.buf("s_t1D"); bt1k = P.buf("s_t1k")
    for j in range(NT):
        for half in range(2):
            pi = 2 + half
            P.emit("T", [
                (lambda t, lc=lc, j=j, half=half, pi=pi: t.matmul(
                    ps[pi][:, 0:512], W_kvT[:, lc, j * 128:(j + 1) * 128],
                    W_vu[:, lc, half * 512:(half + 1) * 512],
                    start=(lc == 0), stop=(lc == 1)))
                for lc in range(2)],
                reads=(bkvT, P.buf("W_vu"), psB[pi]), writes=(psB[pi],))
            P.emit("V", lambda v, j=j, half=half, pi=pi: v.tensor_copy(
                Cv4[:, j, half * 8:(half + 1) * 8, 0:64],
                ps[pi][:, 0:512].rearrange("p (h e) -> p h e", e=64)),
                reads=(psB[pi],), writes=(bC,))

    P.alias(P.buf("W_rq"), P.buf("W_vu"))
    P.alias(P.buf("W_rqp"), P.buf("W_vu"))
    g_load(P, W_rq[:, :, :], drq.ap().rearrange("(k p) n -> p k n", p=128), "W_rq")
    g_load(P, W_rqp[:, :, :], drqp.ap().rearrange("(k p) n -> p k n", p=128), "W_rqp")
    for hp in range(8):
        kwb = s_rkw[hp % 2]; kwpb = s_rkwp[hp % 2]
        bkw = P.buf(("rkw", hp % 2)); bkwp = P.buf(("rkwp", hp % 2))
        for c0 in range(0, 8, 2):
            P.emit("G", lambda g, hp=hp, kwb=kwb, c0=c0: g.dma_start(
                kwb[:, c0:c0 + 2, :], drk.ap().rearrange("(k p) n -> p k n", p=128)
                [:, c0:c0 + 2, hp * 64:(hp + 1) * 64]),
                reads=(bkw,), writes=(bkw,), dma=True)
            P.emit("G", lambda g, hp=hp, kwpb=kwpb, c0=c0: g.dma_start(
                kwpb[:, c0:c0 + 2, :], drkp.ap().rearrange("(k p) n -> p k n", p=128)
                [:, c0:c0 + 2, hp * 64:(hp + 1) * 64]),
                reads=(bkwp,), writes=(bkwp,), dma=True)
        for w in range(4):
            wc = slice(w * 512, (w + 1) * 512)
            grp = []
            for hh in range(2):
                h = hp * 2 + hh
                b0 = hh * 64
                grp.append(lambda t, h=h, b0=b0, wc=wc: t.matmul(
                    ps[0][b0:b0 + 32, 0:512], W_ku[:, 0, h * 32:(h + 1) * 32],
                    W_kvT[:, 0, wc], start=True, stop=False,
                    tile_position=(0, b0)))
                grp.append(lambda t, h=h, b0=b0, wc=wc: t.matmul(
                    ps[0][b0:b0 + 32, 0:512], W_ku[:, 1, h * 32:(h + 1) * 32],
                    W_kvT[:, 1, wc], start=False, stop=True,
                    tile_position=(0, b0)))
            for hh in range(2):
                b0 = hh * 64 + 32
                for cc in range(8):
                    grp.append(lambda t, hh=hh, b0=b0, cc=cc, wc=wc, kwb=kwb: t.matmul(
                        ps[0][b0:b0 + 32, 0:512], kwb[:, cc, hh * 32:(hh + 1) * 32],
                        A_xnT[:, cc, wc], start=(cc == 0), stop=(cc == 7),
                        tile_position=(0, b0)))
            P.emit("T", grp, reads=(P.buf("W_ku"), bkvT, bkw, bA, psB[0]),
                   writes=(psB[0],))
            grp2 = []
            for hh in range(2):
                b0 = hh * 64 + 32
                for cc in range(8):
                    grp2.append(lambda t, hh=hh, b0=b0, cc=cc, wc=wc, kwpb=kwpb: t.matmul(
                        ps[1][b0:b0 + 32, 0:512], kwpb[:, cc, hh * 32:(hh + 1) * 32],
                        A_xnT[:, cc, wc], start=(cc == 0), stop=(cc == 7),
                        tile_position=(0, b0)))
            P.emit("T", grp2, reads=(bkwp, bA, psB[1]), writes=(psB[1],))
            for hh in range(2):
                b0 = hh * 64
                P.emit("V", lambda v, hp=hp, b0=b0, wc=wc: v.tensor_copy(
                    B_kk[b0:b0 + 32, hp, wc], ps[0][b0:b0 + 32, 0:512]),
                    reads=(psB[0],), writes=(bB,))
                rb = b0 + 32
                P.emit("V", lambda v, rb=rb, wc=wc: v.tensor_mul(
                    s_t1k[rb:rb + 32, 0:512], ps[0][rb:rb + 32, 0:512],
                    K_cos[rb:rb + 32, wc]),
                    reads=(psB[0], kbufs["COSD"]), writes=(bt1k,))
                P.emit("V", lambda v, hp=hp, rb=rb, wc=wc: v.tensor_mul(
                    B_kk[rb:rb + 32, hp, wc], ps[1][rb:rb + 32, 0:512],
                    K_sin[rb:rb + 32, wc]),
                    reads=(psB[1], kbufs["SIND"]), writes=(bB,))
                P.emit("G", lambda g, hp=hp, rb=rb, wc=wc: g.tensor_add(
                    B_kk[rb:rb + 32, hp, wc], B_kk[rb:rb + 32, hp, wc],
                    s_t1k[rb:rb + 32, 0:512]),
                    reads=(bt1k, bB), writes=(bB,))
    DUMP("d_xnT", [128, 8, 2048], BF16, lambda: A_xnT[:, :, :], (bA,))
    DUMP("d_kk", [128, 8, 2048], BF16, lambda: B_kk[:, :, :], (bB,))
    DUMP("d_v", [128, 16, 1040], BF16, lambda: C_v[:, :, :], (bC,))
    by0 = P.buf("s_y0")
    P.alias(by0, P.buf("s_t1k"))
    P.alias(P.buf("s_qt"), P.buf(("rkw", 0)))
    P.alias(P.buf("s_rec"), P.buf(("rkw", 1)), P.buf(("rkwp", 0)))
    P.alias(P.buf(("exp", 0)), P.buf(("rkwp", 0)), P.buf(("rkwp", 1)))
    P.alias(P.buf(("exp", 1)), P.buf("W_kvT"))
    P.alias(P.buf(("exp", 2)), P.buf("W_kvT"))
    P.alias(P.buf(("exp", 3)), P.buf("W_kvT"))
    for hp in range(8):
        for w in range(4):
            wc = slice(w * 512, (w + 1) * 512)
            for hh in range(2):
                h = hp * 2 + hh
                r0 = hh * 64
                grp = []
                for lc in range(2):
                    grp.append(lambda t, lc=lc, h=h, r0=r0, wc=wc: t.matmul(
                        ps[0][r0:r0 + 32, 0:512], W_qu[:, lc, h * 32:(h + 1) * 32],
                        S_qlT[:, lc, wc], start=(lc == 0), stop=(lc == 1),
                        tile_position=(0, r0)))
                for lc in range(2):
                    grp.append(lambda t, lc=lc, h=h, r0=r0, wc=wc: t.matmul(
                        ps[0][r0 + 32:r0 + 64, 0:512], W_rq[:, lc, h * 32:(h + 1) * 32],
                        S_qlT[:, lc, wc], start=(lc == 0), stop=(lc == 1),
                        tile_position=(0, r0 + 32)))
                P.emit("T", grp, reads=(P.buf("W_qu"), P.buf("W_rq"), bqlT, psB[0]),
                       writes=(psB[0],))
                P.emit("T", [
                    (lambda t, lc=lc, h=h, r0=r0, wc=wc: t.matmul(
                        ps[1][r0 + 32:r0 + 64, 0:512], W_rqp[:, lc, h * 32:(h + 1) * 32],
                        S_qlT[:, lc, wc], start=(lc == 0), stop=(lc == 1),
                        tile_position=(0, r0 + 32)))
                    for lc in range(2)],
                    reads=(P.buf("W_rqp"), bqlT, psB[1]), writes=(psB[1],))
                bqt = P.buf("s_qt")
                P.emit("V", lambda v, r0=r0: v.tensor_copy(
                    s_qt[r0:r0 + 32, 0:512], ps[0][r0:r0 + 32, 0:512]),
                    reads=(psB[0],), writes=(bqt,))
                rb = r0 + 32
                P.emit("V", lambda v, rb=rb, wc=wc: v.tensor_mul(
                    s_t1[rb:rb + 32, 0:512], ps[0][rb:rb + 32, 0:512],
                    K_cos[rb:rb + 32, wc]),
                    reads=(psB[0], kbufs["COSD"]), writes=(bt1,))
                P.emit("V", lambda v, rb=rb, wc=wc: v.tensor_mul(
                    s_qt[rb:rb + 32, 0:512], ps[1][rb:rb + 32, 0:512],
                    K_sin[rb:rb + 32, wc]),
                    reads=(psB[1], kbufs["SIND"]), writes=(bqt,))
                P.emit("V", lambda v, rb=rb: v.tensor_add(
                    s_qt[rb:rb + 32, 0:512], s_qt[rb:rb + 32, 0:512],
                    s_t1[rb:rb + 32, 0:512]),
                    reads=(bt1, bqt), writes=(bqt,))
            nS = 4 * w + 4
            j0s = [max(0, sck * 128 - w * 512) for sck in range(nS)]
            ROT2 = ((4, 5), (2, 3))
            def emit_av(sck, j0, hp=hp, w=w, nS=nS):
                for hh in range(2):
                    h = hp * 2 + hh
                    pc = 6 + hh
                    pc = hh
                    be = P.buf(("exp", sck % 4))
                    P.emit("T", lambda t, sck=sck, j0=j0, pc=pc, h=h, hh=hh, nS=nS: t.matmul(
                        ps[pc][0:65, j0:512], C_v[:, sck, h * 65:(h + 1) * 65],
                        s_exp[sck % 4][:, hh * 512 + j0:hh * 512 + 512],
                        start=(sck == 0), stop=(sck == nS - 1)),
                        reads=(bC, be, psB[pc]), writes=(psB[pc],))
            for sck in range(nS):
                j0 = j0s[sck]
                pr = ROT2[sck % 2]
                P.emit("T", [
                    (lambda t, sck=sck, j0=j0, hp=hp, pr=pr, hh=hh, r0=r0: t.matmul(
                        ps[pr[hh]][:, j0:512],
                        B_kk[r0:r0 + 64, hp, sck * 128:(sck + 1) * 128],
                        s_qt[r0:r0 + 64, j0:512], start=True, stop=True,
                        tile_position=(r0, 0)))
                    for hh, r0 in ((0, 0), (1, 64))],
                    reads=(bB, P.buf("s_qt"), psB[pr[0]], psB[pr[1]]),
                    writes=(psB[pr[0]], psB[pr[1]]))
                be = P.buf(("exp", sck % 4))
                pdp = pd_pair(P, pr)
                P.emit("S", lambda sc, sck=sck, j0=j0, pdp=pdp: sc.activation(
                    s_exp[sck % 4][:, :].rearrange("p (h c) -> p h c", h=2)[:, :, j0:512],
                    pdp[:, :].rearrange("p (h c) -> p h c", h=2)[:, :, j0:512],
                    AF.Exp, bias=0.0, scale=K_sc[:, 0:1]),
                    reads=(psB[pr[0]], psB[pr[1]], kbufs["SCAL"]), writes=(be,))
                if sck * 128 >= w * 512:
                    for hh in range(2):
                        P.emit("V", lambda v, sck=sck, j0=j0, hh=hh: v.tensor_mul(
                            s_exp[sck % 4][:, hh * 512 + j0:hh * 512 + j0 + 128],
                            s_exp[sck % 4][:, hh * 512 + j0:hh * 512 + j0 + 128],
                            K_ut[:, :]),
                            reads=(be, kbufs["UTRI"]), writes=(be,))
                if sck >= 1:
                    emit_av(sck - 1, j0s[sck - 1])
            emit_av(nS - 1, j0s[nS - 1])
            for hh in range(2):
                h = hp * 2 + hh
                r0 = hh * 64
                pc = hh
                bpc = psB[pc]
                brec = P.buf("s_rec")
                P.emit("S", lambda sc, pc=pc: sc.activation(
                    s_rec[64:65, 512:1024], ps[pc][64:65, 0:512], AF.Ln, bias=0.0),
                    reads=(bpc,), writes=(brec,))
                P.emit("S", lambda sc: sc.activation(
                    s_rec[64:65, 0:512], s_rec[64:65, 512:1024], AF.Exp,
                    bias=0.0, scale=K_sc[64:65, 8:9]),
                    reads=(brec, kbufs["SCAL"]), writes=(brec,))
                P.emit("T", lambda t: t.matmul(
                    ps[5][0:64, 0:512], K_on[64:65, 0:64], s_rec[64:65, 0:512],
                    start=True, stop=True),
                    reads=(brec, kbufs["ONEBF"], psB[5]), writes=(psB[5],))
                P.emit("V", lambda v: v.tensor_copy(s_rec[0:64, 0:512],
                                                    ps[5][0:64, 0:512]),
                    reads=(psB[5],), writes=(brec,))
                if hp == 0:
                    dst = lambda wc=wc, r0=r0: s_y0[r0:r0 + 64, wc]
                    dstb = by0
                else:
                    dst = lambda wc=wc, r0=r0, hp=hp: B_kk[r0:r0 + 64, hp - 1, wc]
                    dstb = bB
                P.emit("V", lambda v, dst=dst, pc=pc: v.tensor_mul(
                    dst(), ps[pc][0:64, 0:512], s_rec[0:64, 0:512]),
                    reads=(bpc, brec), writes=(dstb,))

    P.alias(P.buf("W_wo"), P.buf("W_ku"), P.buf("W_qu"), P.buf("W_vu"),
            P.buf("W_rq"), P.buf("bkvd"), P.buf("bqd"), bkvT,
            P.buf(("exp", 2)), P.buf(("exp", 3)))
    g_load_ch(P, W_wo, dwo.ap().rearrange("(k p) n -> p k n", p=128), "W_wo", 4)
    P.alias(P.buf("C_mla"), bC)
    bmla = P.buf("C_mla")
    for j in range(NT):
        for half in range(2):
            pi = half
            def mml(cc, j=j, half=half, pi=pi):
                def f(t):
                    src = s_y0[:, j * 128:(j + 1) * 128] if cc == 0 else \
                        B_kk[:, cc - 1, j * 128:(j + 1) * 128]
                    return t.matmul(ps[pi][:, 0:512], src,
                                    W_wo[:, cc, half * 512:(half + 1) * 512],
                                    start=(cc == 0), stop=(cc == 7))
                return f
            P.emit("T", [mml(cc) for cc in range(8)],
                   reads=(P.buf("W_wo"), bB, by0, psB[pi]), writes=(psB[pi],))
            P.emit("V", lambda v, j=j, half=half, pi=pi: v.scalar_tensor_tensor(
                R[:, j, half * 512:(half + 1) * 512], ps[pi][:, 0:512], 1.0,
                R[:, j, half * 512:(half + 1) * 512], ALU.mult, ALU.add),
                reads=(psB[pi], bR[j]), writes=(bR[j],))
            P.emit("S", lambda sc, j=j, half=half, pi=pi: sc.copy(
                C_mla[:, j, half * 512:(half + 1) * 512], ps[pi][:, 0:512]),
                reads=(psB[pi],), writes=(bmla, psB[pi]))

    if dbg:
        dU = P.nc.dram_tensor("d_u", [T, CD], F32, kind="ExternalOutput")
        dur2 = dU.ap().rearrange("(j p) c -> p j c", p=128)
        for j in range(NT):
            P.emit("Y", lambda y, j=j: y.dma_start(dur2[:, j, :], R[:, j, :]),
                   reads=(bR[j],), writes=(), dma=True)
        DUMP("d_mla", [128, 16, 1024], BF16, lambda: C_mla[:, :, :], (bmla,))
    P.alias(P.buf("s_X2"), by0)
    P.alias(P.buf("s_mT2"), P.buf("s_qt"), P.buf("s_rec"), P.buf(("exp", 0)),
            P.buf(("exp", 1)))
    P.alias(bdump, bqlT)
    P.alias(bsm, bqlT)
    P.alias(P.buf("s_wts"), bqlT)
    P.alias(P.buf(("hm", 0)), bqlT)
    P.alias(P.buf(("hm", 1)), bqlT)
    P.alias(P.buf("s_sg"), bqlT)
    bX = P.buf("s_X2"); bmT = P.buf("s_mT2")
    bwts = P.buf("s_wts")
    for j in range(NT):
        P.emit("V", lambda v, j=j: v.scalar_tensor_tensor(
            s_dump[:, 0:1024], R[:, j, :], 0.0, R[:, j, :], ALU.add, ALU.mult,
            accum_out=s_small[:, 2:3]), reads=(bR[j],), writes=(bdump, bsm))
        P.emit("S", lambda sc: sc.activation(s_small[:, 7:8], s_small[:, 2:3],
                                             AF.Ln, bias=0.0), reads=(bsm,), writes=(bsm,))
        P.emit("S", lambda sc: sc.activation(s_small[:, 8:9], s_small[:, 7:8],
                                             AF.Exp, bias=K_sc[:, 7:8], scale=K_sc[:, 6:7]),
               reads=(bsm, kbufs["SCAL"]), writes=(bsm,))
        P.emit("V", lambda v: v.tensor_scalar_add(s_small[:, 9:10], s_small[:, 8:9],
                                                  1e-5), reads=(bsm,), writes=(bsm,))
        P.emit("V", lambda v: v.reciprocal(s_small[:, 10:11], s_small[:, 9:10]),
               reads=(bsm,), writes=(bsm,))
        P.emit("V", lambda v, j=j: v.scalar_tensor_tensor(
            s_X[:, 0:1024], R[:, j, :], s_small[:, 10:11], K_wln2[:, :],
            ALU.mult, ALU.mult), reads=(bR[j], bsm, kbufs["D_wln2R"]), writes=(bX,))
        P.emit("S", lambda sc: sc.copy(s_dump[:, 0:1024], s_X[:, 0:1024]),
               reads=(bX,), writes=(bdump,))
        for cc in range(8):
            pbi = cc % 2
            P.emit("T", lambda t, cc=cc, pbi=pbi: t.transpose(
                pb[pbi][:, 0:128], s_dump[:, cc * 128:(cc + 1) * 128], K_id[:, :]),
                reads=(bdump, kbufs["IDN"], pbB[pbi]), writes=(pbB[pbi],))
            P.emit("V", lambda v, j=j, cc=cc, pbi=pbi: v.tensor_copy(
                A_xnT[:, cc, j * 128:(j + 1) * 128], pb[pbi][:, 0:128]),
                reads=(pbB[pbi],), writes=(bA,))
        for cc in range(8):
            pi = cc % 2
            P.emit("T", lambda t, cc=cc, pi=pi: t.transpose(
                ps[pi][:, 0:128], s_X[:, cc * 128:(cc + 1) * 128], K_idf[:, :]),
                reads=(bX, kbufs["IDF"], psB[pi]), writes=(psB[pi],))
            P.emit("V", lambda v, cc=cc, pi=pi: v.tensor_copy(
                s_mT[:, cc * 128:(cc + 1) * 128], ps[pi][:, 0:128]),
                reads=(psB[pi],), writes=(bmT,))
        P.emit("T", [
            (lambda t, cc=cc: t.matmul(
                ps[4][:, 0:7], s_mT[:, cc * 128:(cc + 1) * 128],
                K_rw[:, cc, :], start=(cc == 0), stop=(cc == 7)))
            for cc in range(8)],
            reads=(bmT, kbufs["D_rw"], psB[4]), writes=(psB[4],))
        P.emit("V", lambda v: v.tensor_add(s_small[:, 16:23], ps[4][:, 0:7],
                                           K_rb[:, 0:7]),
               reads=(psB[4], kbufs["D_rbR"]), writes=(bsm, psB[4]))
        P.emit("S", lambda sc: sc.activation(s_small[:, 16:23], s_small[:, 16:23],
                                             AF.Sigmoid), reads=(bsm,), writes=(bsm,))
        P.emit("V", lambda v: v.tensor_reduce(s_small[:, 2:3], s_small[:, 16:23],
                                              AX.X, ALU.max), reads=(bsm,), writes=(bsm,))
        P.emit("V", lambda v: v.tensor_single_scalar(s_small[:, 24:31], s_small[:, 16:23],
                                                     s_small[:, 2:3], ALU.subtract),
               reads=(bsm,), writes=(bsm,))
        P.emit("S", lambda sc: sc.activation(s_small[:, 32:39], s_small[:, 24:31],
                                             AF.Exp, bias=0.0, scale=K_sc[:, 5:6]),
               reads=(bsm, kbufs["SCAL"]), writes=(bsm,))
        P.emit("V", lambda v: v.scalar_tensor_tensor(
            s_small[:, 40:47], s_small[:, 32:39], -1e30, s_small[:, 16:23],
            ALU.mult, ALU.add), reads=(bsm,), writes=(bsm,))
        P.emit("V", lambda v: v.tensor_reduce(s_small[:, 3:4], s_small[:, 40:47],
                                              AX.X, ALU.max), reads=(bsm,), writes=(bsm,))
        P.emit("V", lambda v: v.tensor_single_scalar(s_small[:, 24:31], s_small[:, 16:23],
                                                     s_small[:, 3:4], ALU.subtract),
               reads=(bsm,), writes=(bsm,))
        P.emit("V", lambda v: v.tensor_scalar_min(s_small[:, 24:31], s_small[:, 24:31],
                                                  0.0), reads=(bsm,), writes=(bsm,))
        P.emit("S", lambda sc: sc.activation(s_small[:, 32:39], s_small[:, 24:31],
                                             AF.Exp, bias=0.0, scale=K_sc[:, 5:6]),
               reads=(bsm, kbufs["SCAL"]), writes=(bsm,))
        P.emit("V", lambda v: v.tensor_mul(s_small[:, 40:47], s_small[:, 16:23],
                                           s_small[:, 32:39]), reads=(bsm,), writes=(bsm,))
        P.emit("V", lambda v: v.tensor_reduce(s_small[:, 4:5], s_small[:, 40:47],
                                              AX.X, ALU.add), reads=(bsm,), writes=(bsm,))
        P.emit("V", lambda v: v.reciprocal(s_small[:, 5:6], s_small[:, 4:5]),
               reads=(bsm,), writes=(bsm,))
        P.emit("V", lambda v, j=j: v.tensor_scalar(
            s_wts[:, j, 0:7], s_small[:, 40:47], s_small[:, 5:6], None,
            ALU.mult), reads=(bsm,), writes=(bwts,))
        P.emit("V", lambda v, j=j: v.scalar_tensor_tensor(
            R[:, j, :], R[:, j, :], 2.0, C_mla[:, j, :], ALU.mult, ALU.subtract),
            reads=(bR[j], bmla), writes=(bR[j],))

    P.alias(bB, by0)
    P.alias(P.buf("C_wd"), bmla, bC)
    bwd = P.buf("C_wd")
    bsg = P.buf("s_sg")
    for e in range(-1, 7):
        gsrc = dgs.ap() if e < 0 else dgr.ap()[e]
        usrc = dus.ap() if e < 0 else dur.ap()[e]
        dsrc = dds.ap() if e < 0 else ddr.ap()[e]
        for i0 in range(0, 1024, 512):
            bc = P.buf(("Wgu", i0))
            P.alias(bc, bB)
            for c0 in range(0, 8, 2):
                P.emit("G", lambda g, c0=c0, i0=i0, gsrc=gsrc: g.dma_start(
                    B_kk[:, c0:c0 + 2, i0:i0 + 512],
                    gsrc.rearrange("(k p) n -> p k n", p=128)
                    [:, c0:c0 + 2, i0:i0 + 512]),
                    reads=(), writes=(bc, bB), dma=True)
                P.emit("G", lambda g, c0=c0, i0=i0, usrc=usrc: g.dma_start(
                    B_kk[:, c0:c0 + 2, 1024 + i0:1024 + i0 + 512],
                    usrc.rearrange("(k p) n -> p k n", p=128)
                    [:, c0:c0 + 2, i0:i0 + 512]),
                    reads=(), writes=(bc, bB), dma=True)
        for c0 in range(0, 8, 2):
            bcd = P.buf(("Cwd", c0))
            P.alias(bcd, bwd)
            P.emit("G", lambda g, c0=c0, dsrc=dsrc: g.dma_start(
                C_wd[:, c0:c0 + 2, :],
                dsrc.rearrange("(k p) n -> p k n", p=128)[:, c0:c0 + 2, :]),
                reads=(), writes=(bcd, bwd), dma=True)
        for tw in range(8):
            tc = slice(tw * 256, (tw + 1) * 256)
            def emit_gu(ic, tc=tc):
                wgu = (P.buf(("Wgu", (ic // 4) * 512)),)
                P.emit("T", [
                    (lambda t, cc=cc, ic=ic, tc=tc: t.matmul(
                        ps[0][:, 0:256],
                        B_kk[:, cc, ic * 128:(ic + 1) * 128],
                        A_xnT[:, cc, tc], start=(cc == 0), stop=(cc == 7)))
                    for cc in range(8)],
                    reads=wgu + (bA, psB[0]), writes=(psB[0],))
                P.emit("T", [
                    (lambda t, cc=cc, ic=ic, tc=tc: t.matmul(
                        ps[1][:, 0:256],
                        B_kk[:, cc, 1024 + ic * 128:1024 + (ic + 1) * 128],
                        A_xnT[:, cc, tc], start=(cc == 0), stop=(cc == 7)))
                    for cc in range(8)],
                    reads=wgu + (bA, psB[1]), writes=(psB[1],))
                P.emit("S", lambda sc, ic=ic: sc.activation(
                    s_sg[:, 0:256], ps[0][:, 0:256], AF.Silu),
                       reads=(psB[0],), writes=(bsg,))
                bh = P.buf(("hm", ic % 2))
                P.emit("V", lambda v, ic=ic: v.tensor_mul(
                    s_hm[ic % 2][:, 0:256], s_sg[:, 0:256], ps[1][:, 0:256]),
                    reads=(bsg, psB[1]), writes=(bh,))
            emit_gu(0)
            for ic in range(8):
                if ic + 1 < 8:
                    emit_gu(ic + 1)
                bh = P.buf(("hm", ic % 2))
                for tt in range(2):
                    for dg in range(2):
                        pj = 2 + tt * 2 + dg
                        P.emit("T", lambda t, ic=ic, tt=tt, dg=dg, pj=pj: t.matmul(
                            ps[pj][:, 0:512], s_hm[ic % 2][:, tt * 128:(tt + 1) * 128],
                            C_wd[:, ic, dg * 512:(dg + 1) * 512],
                            start=(ic == 0), stop=(ic == 7)),
                            reads=(bh, P.buf(("Cwd", (ic // 2) * 2)), psB[pj]),
                            writes=(psB[pj],))
            for tt in range(2):
                j = tw * 2 + tt
                for dg in range(2):
                    pj = 2 + tt * 2 + dg
                    if e < 0:
                        P.emit("V", lambda v, j=j, dg=dg, pj=pj: v.scalar_tensor_tensor(
                            R[:, j, dg * 512:(dg + 1) * 512], ps[pj][:, 0:512], 1.0,
                            R[:, j, dg * 512:(dg + 1) * 512], ALU.mult, ALU.add),
                            reads=(psB[pj], bR[j]), writes=(bR[j], psB[pj]))
                    else:
                        P.emit("V", lambda v, j=j, dg=dg, pj=pj, e=e: v.scalar_tensor_tensor(
                            R[:, j, dg * 512:(dg + 1) * 512], ps[pj][:, 0:512],
                            s_wts[:, j, e:e + 1], R[:, j, dg * 512:(dg + 1) * 512],
                            ALU.mult, ALU.add),
                            reads=(psB[pj], bwts, bR[j]), writes=(bR[j], psB[pj]))
    DUMP("d_wts", [128, 16, 8], F32, lambda: s_wts[:, :, :], (P.buf("s_wts"),))
    if dbg:
        dP = P.nc.dram_tensor("d_preln", [T, CD], F32, kind="ExternalOutput")
        dpr2 = dP.ap().rearrange("(j p) c -> p j c", p=128)
        for j in range(NT):
            P.emit("Y", lambda y, j=j: y.dma_start(dpr2[:, j, :], R[:, j, :]),
                   reads=(bR[j],), writes=(), dma=True)
    orr = out_dram.ap().rearrange("(j p) c -> p j c", p=128)
    for j in range(NT):
        P.emit("V", lambda v, j=j: v.scalar_tensor_tensor(
            s_dump[:, 0:1024], R[:, j, :], 0.0, R[:, j, :], ALU.add, ALU.mult,
            accum_out=s_small[:, 2:3]), reads=(bR[j],), writes=(bdump, bsm))
        P.emit("V", lambda v, j=j: v.tensor_reduce(
            s_small[:, 0:1], R[:, j, :], AX.X, ALU.add), reads=(bR[j],), writes=(bsm,))
        P.emit("V", lambda v: v.memset(s_small[:, 1:2], 0.0), reads=(bsm,), writes=(bsm,))
        emit_ln_tiles(P, R, bR, j, None, None, s_small, bsm, K_sc, kbufs["SCAL"],
                      R[:, j, :], (bR[j],))
        P.emit("Y", lambda y, j=j: y.dma_start(orr[:, j, :], R[:, j, :]),
               reads=(bR[j],), writes=(), dma=True)
    return R, bR


def _in_maps(P, consts, per_core):
    maps = []
    for ci in range(NCORES):
        m = {}
        for k in P.inputs:
            if k in per_core:
                m[k] = per_core[k][ci]
            else:
                m[k] = consts[k]
        maps.append(m)
    return maps

def _load_R_f32(P, x_dram, R, bR):
    xr = x_dram.ap().rearrange("(j p) c -> p j c", p=128)
    for j in range(NT):
        P.emit("Y", lambda y, j=j: y.dma_start(R[:, j, :], xr[:, j, :]),
               reads=(), writes=(bR[j],), dma=True)

def _store_R(P, x_dram, R, bR):
    orr = x_dram.ap().rearrange("(j p) c -> p j c", p=128)
    for j in range(NT):
        P.emit("Y", lambda y, j=j: y.dma_start(orr[:, j, :], R[:, j, :]),
               reads=(bR[j],), writes=(), dma=True)

def run_stage_A(x_slices, consts):
    P = Prog("stageA")
    x_in = P.din("x_in", [T, CD], F16)
    x_out = P.nc.dram_tensor("x_out", [T, CD], F32, kind="ExternalOutput")
    R, Bt, bR, bB = emit_stage_A(P, x_in)
    _store_R(P, x_out, R, bR)
    P.final_wait()
    nc = P.build()
    res = run_bass_kernel_spmd(nc, _in_maps(P, consts, {"x_in": x_slices}),
                               core_ids=list(range(NCORES)))
    return [r["x_out"] for r in res.results]

def _prologue_xT(P, x_in):
    R = P.sb("Rpre", [128, NT, 1024], F32, OFF_R)
    Bt = P.sb("Btpre", [128, 8, 2048], BF16, OFF_B)
    bR = [P.buf(("R", j)) for j in range(NT)]
    bB = P.buf("bBt")
    _load_R_f32(P, x_in, R, bR)
    return R, Bt, bR, bB

def run_stage_B(x_full, consts):
    P = Prog("stageB")
    x_in = P.din("x_in", [T, CD], F32)
    x_out = P.nc.dram_tensor("x_out", [T, CD], F32, kind="ExternalOutput")
    import os
    dbg = bool(os.environ.get("KDBG"))
    R, Bt, bR, bBt, extra = emit_stage_B(P, x_in=x_in, dbg=dbg)
    _store_R(P, x_out, R, bR)
    P.final_wait()
    nc = P.build()
    res = run_bass_kernel_spmd(nc, _in_maps(P, consts, {"x_in": x_full}),
                               core_ids=list(range(NCORES)))
    if dbg:
        import pickle
        with open("/tmp/dbgB.pkl", "wb") as f:
            pickle.dump({k: np.asarray(v) for k, v in res.results[0].items()}, f)
    return [r["x_out"] for r in res.results]

def run_stage_C(x_full, consts):
    P = Prog("stageC")
    x_in = P.din("x_in", [T, CD], F32)
    x_out = P.nc.dram_tensor("x_out", [T, CD], F32, kind="ExternalOutput")
    import os
    dbg = bool(os.environ.get("KDBG"))
    R, bR = emit_stage_C(P, x_in=x_in, dbg=dbg)
    _store_R(P, x_out, R, bR)
    P.final_wait()
    nc = P.build()
    res = run_bass_kernel_spmd(nc, _in_maps(P, consts, {"x_in": x_full}),
                               core_ids=list(range(NCORES)))
    if dbg:
        import pickle
        with open("/tmp/dbgC.pkl", "wb") as f:
            pickle.dump({k: np.asarray(v) for k, v in res.results[0].items()}, f)
    return [r["x_out"] for r in res.results]

def run_stage_D(x_full, consts):
    P = Prog("stageD")
    x_in = P.din("x_in", [T, CD], F32)
    x_out = P.nc.dram_tensor("x_out", [T, CD], F32, kind="ExternalOutput")
    import os
    dbg = bool(os.environ.get("KDBG"))
    emit_stage_D(P, x_out, x_in=x_in, dbg=dbg)
    P.final_wait()
    nc = P.build()
    res = run_bass_kernel_spmd(nc, _in_maps(P, consts, {"x_in": x_full}),
                               core_ids=list(range(NCORES)))
    if dbg:
        import pickle
        with open("/tmp/dbgD.pkl", "wb") as f:
            pickle.dump({k: np.asarray(v) for k, v in res.results[0].items()}, f)
    return [r["x_out"] for r in res.results]

def run_fused(x_slices, consts):
    P = Prog("mk")
    x_in = P.din("x_in", [T, CD], F16)
    out = P.nc.dram_tensor("out", [T, CD], F32, kind="ExternalOutput")
    emit_stage_A(P, x_in)
    P.fence()
    emit_stage_B(P)
    P.fence()
    emit_stage_C(P)
    P.fence()
    emit_stage_D(P, out)
    P.final_wait()
    nc = P.build()
    res = run_bass_kernel_spmd(nc, _in_maps(P, consts, {"x_in": x_slices}),
                               core_ids=list(range(NCORES)))
    return [r["out"] for r in res.results]

def kernel(**inputs):
    consts = mk_consts(inputs)
    x = np.ascontiguousarray(inputs["L0_attn_x"])
    x_slices = [np.ascontiguousarray(x[i]) for i in range(NCORES)]
    if FUSED:
        return np.stack(run_fused(x_slices, consts), axis=0)
    x1 = run_stage_A(x_slices, consts)
    x2 = run_stage_B(x1, consts)
    x3 = run_stage_C(x2, consts)
    x4 = run_stage_D(x3, consts)
    return np.stack(x4, axis=0)


# version 16
# speedup vs baseline: 1.0195x; 1.0195x over previous
import contextlib
import math
import numpy as np

import concourse.bass as bass
import concourse.mybir as mybir
from concourse.bass_utils import run_bass_kernel_spmd

AF = mybir.ActivationFunctionType
ALU = mybir.AluOpType
AX = mybir.AxisListType
F32, BF16, F16 = mybir.dt.float32, mybir.dt.bfloat16, mybir.dt.float16

NCORES = 8
T = 2048
NT = 16
CD = 1024
FUSED = True
STAGES_ALL = ("A", "B", "C", "D")


def _rope_tables_l0():
    pos = np.arange(T, dtype=np.float64)
    inv = np.exp(np.arange(0, 64, 2, dtype=np.float64) * (-np.log(10000.0) / 64.0))
    ang = pos[:, None] * inv[None, :]
    sin = np.sin(ang).astype(np.float16).astype(np.float32)
    cos = np.cos(ang).astype(np.float16).astype(np.float32)
    d = np.arange(128)
    cosA = cos[:, d % 32].T.copy()
    sgn = np.where((d % 64) < 32, -1.0, 1.0).astype(np.float32)
    sinA = (sin[:, d % 32] * sgn[None, :]).T.copy()
    return cosA, sinA

def _rope_tables_l1():
    HH = 32
    inv = (1.0 / 10000.0 ** (np.arange(0, HH, 2, dtype=np.float64) / HH))
    pos = np.arange(T, dtype=np.float64)
    freqs = pos[:, None] * inv[None, :]
    emb = np.concatenate([freqs, freqs], axis=-1)
    cosE = np.cos(emb).astype(np.float32)
    sinE = np.sin(emb).astype(np.float32)
    cosD = np.zeros((128, T), np.float32)
    sinD = np.zeros((128, T), np.float32)
    d = np.arange(32)
    sgn = np.where(d < 16, -1.0, 1.0).astype(np.float32)
    for base in (32, 96):
        cosD[base:base + 32] = cosE.T
        sinD[base:base + 32] = (sinE * sgn[None, :]).T
    return cosD, sinD

def _perm64(n_heads):
    p = np.arange(n_heads * 64)
    return (p // 64) * 64 + (p % 64 + 32) % 64

def _perm32_block(ncols):
    p = np.arange(ncols)
    return (p // 32) * 32 + (p % 32 + 16) % 32

def _rep(v):
    return np.ascontiguousarray(np.broadcast_to(np.asarray(v, np.float32)[None, :], (128, len(v))))

def _colT(v, nchunk):
    return np.ascontiguousarray(np.asarray(v, np.float32).reshape(nchunk, 128).T)

def mk_consts(ins):
    import ml_dtypes
    bfd = ml_dtypes.bfloat16
    c = {}
    c["IDN"] = np.eye(128, dtype=bfd)
    c["IDF"] = np.eye(128, dtype=np.float32)
    c["UTRI"] = np.triu(np.ones((128, 128), bfd))
    c["ONEBF"] = np.ones((128, 128), bfd)
    c["ONEF"] = np.ones((128, 128), np.float32)
    c["SCAL"] = _rep(np.array([0.125, -0.5, 1.0 / math.sqrt(32.0), 1.0, 0.3, 1e38, 0.5, -0.5 * math.log(1024.0), -1.0, 0, 0, 0, 0, 0, 0, 0], np.float32))
    cosA, sinA = _rope_tables_l0()
    c["COSA"], c["SINA"] = cosA.astype(bfd), sinA.astype(bfd)
    cosD, sinD = _rope_tables_l1()
    c["COSD"], c["SIND"] = cosD.astype(bfd), sinD.astype(bfd)
    p64 = _perm64(16)
    c["A_Wq"] = np.ascontiguousarray(ins["L0_attn_Wq"].astype(np.float32))
    c["A_Wqp"] = np.ascontiguousarray(c["A_Wq"][:, p64])
    c["A_Wk"] = np.ascontiguousarray(ins["L0_attn_Wk"].astype(np.float32))
    c["A_Wkp"] = np.ascontiguousarray(c["A_Wk"][:, p64])
    c["A_Wv"] = np.ascontiguousarray(ins["L0_attn_Wv"].astype(np.float32))
    c["A_Wo"] = np.ascontiguousarray(ins["L0_attn_Wo"].astype(np.float32))
    bq = ins["L0_attn_bq"].astype(np.float32)
    bk = ins["L0_attn_bk"].astype(np.float32)
    c["A_bqT"] = _colT(bq, 8)
    c["A_bqpT"] = _colT(bq[p64], 8)
    c["A_bkT"] = _colT(bk, 8)
    c["A_bkpT"] = _colT(bk[p64], 8)
    c["A_bvR"] = _rep(ins["L0_attn_bv"].astype(np.float32))
    c["A_boR"] = _rep(ins["L0_attn_bo"].astype(np.float32))
    qsel = (np.arange(512) // 32) * 64 + (np.arange(512) % 32)
    c["B_qw"] = np.ascontiguousarray(ins["L0_ffn_q_w"][:, qsel])
    def _pairT(v):
        out = np.zeros((128, 8), np.float32)
        v = np.asarray(v, np.float32).reshape(16, 32)
        for hp in range(8):
            out[0:32, hp] = v[2 * hp]
            out[64:96, hp] = v[2 * hp + 1]
        return out
    c["B_qbT"] = _pairT(ins["L0_ffn_q_b"][qsel])
    c["B_kw"] = np.ascontiguousarray(ins["L0_ffn_k_w"])
    c["B_kbT"] = _pairT(ins["L0_ffn_k_b"])
    c["B_vw"] = np.ascontiguousarray(ins["L0_ffn_v_w"])
    c["B_vbR"] = _rep(ins["L0_ffn_v_b"])
    c["B_ow"] = np.ascontiguousarray(ins["L0_ffn_o_w"][:512, :])
    c["B_obR"] = _rep(ins["L0_ffn_o_b"])
    c["B_rw"] = np.ascontiguousarray(ins["L0_ffn_router_w"])
    c["B_rbR"] = _rep(ins["L0_ffn_router_b"])
    c["B_w1"] = ins["L0_ffn_w1"]
    c["B_w2"] = ins["L0_ffn_w2"]
    c["B_b1T"] = np.ascontiguousarray(
        ins["L0_ffn_b1"].reshape(8, 16, 128).transpose(2, 0, 1))
    c["B_b2"] = np.ascontiguousarray(ins["L0_ffn_b2"])
    c["C_WqT"] = np.ascontiguousarray(ins["L1_attn_Wq"].T)
    c["C_WkT"] = np.ascontiguousarray(ins["L1_attn_Wk"].T)
    c["C_WvT"] = np.ascontiguousarray(ins["L1_attn_Wv"].T)
    c["C_WoT"] = np.ascontiguousarray(ins["L1_attn_Wo"].T)
    c["C_WifT"] = np.ascontiguousarray(
        np.concatenate([ins["L1_attn_Wi"], ins["L1_attn_Wf"]], axis=0).T)
    c["C_bifR"] = _rep(np.concatenate([ins["L1_attn_bi"], ins["L1_attn_bf"]]))
    c["C_boR"] = _rep(ins["L1_attn_bo"])
    c["C_WoutT"] = np.ascontiguousarray(ins["L1_attn_W_out"].T)
    c["C_lngR"] = _rep(ins["L1_attn_ln_g"])
    c["C_lnbR"] = _rep(ins["L1_attn_ln_b"])
    c["D_wln1R"] = _rep(ins["L1_ffn_w_ln1"])
    c["D_wln2R"] = _rep(ins["L1_ffn_w_ln2"])
    c["D_kvd"] = np.ascontiguousarray(ins["L1_ffn_W_kvd"])
    c["D_qd"] = np.ascontiguousarray(ins["L1_ffn_W_qd"])
    c["D_ku"] = np.ascontiguousarray(ins["L1_ffn_W_ku"])
    c["D_qu"] = np.ascontiguousarray(ins["L1_ffn_W_qu"])
    c["D_vu"] = np.ascontiguousarray(ins["L1_ffn_W_vu"])
    p32q = _perm32_block(512)
    c["D_rq"] = np.ascontiguousarray(ins["L1_ffn_W_ropeq"])
    c["D_rqp"] = np.ascontiguousarray(ins["L1_ffn_W_ropeq"][:, p32q])
    c["D_rk"] = np.ascontiguousarray(ins["L1_ffn_W_ropek"])
    c["D_rkp"] = np.ascontiguousarray(ins["L1_ffn_W_ropek"][:, p32q])
    c["D_wo"] = np.ascontiguousarray(ins["L1_ffn_W_o"])
    c["D_gs"] = np.ascontiguousarray(ins["L1_ffn_Wg_s"])
    c["D_us"] = np.ascontiguousarray(ins["L1_ffn_Wu_s"])
    c["D_ds"] = np.ascontiguousarray(ins["L1_ffn_Wd_s"])
    c["D_gr"] = ins["L1_ffn_Wg_r"]
    c["D_ur"] = ins["L1_ffn_Wu_r"]
    c["D_dr"] = ins["L1_ffn_Wd_r"]
    c["D_rw"] = np.ascontiguousarray(ins["L1_ffn_W_router"])
    c["D_rbR"] = _rep(ins["L1_ffn_routing_bias"])
    return c


class Buf:
    __slots__ = ("w", "r")
    def __init__(self):
        self.w = None
        self.r = {}

class Prog:
    def __init__(self, name):
        self.nc = bass.Bass("TRN2", target_bir_lowering=False, name=name,
                            dynamic_dma_scratch_size=16384)
        self.q = {e: [] for e in ("T", "V", "S", "G", "Y")}
        self.cnt = {"T": 0, "V": 0, "S": 0, "G": 0}
        self.dcnt = {"DY": 0, "DG": 0}
        self._gfifo = []
        self._gdone = 0
        self.known = {e: {} for e in ("T", "V", "S", "G", "Y")}
        self.bufs = {}
        self.inputs = set()
        base = self.nc.sbuf_base
        self.sb0 = (base + 31) & ~31
        self.sb_hi = self.nc.sbuf_top

    def buf(self, key):
        b = self.bufs.get(key)
        if b is None:
            b = self.bufs[key] = Buf()
        return b

    def emit(self, e, fns, reads=(), writes=(), dma=False, ndesc=256):
        if callable(fns):
            fns = [fns]
        waits = {}
        def need(tok):
            if tok is None:
                return
            k, v = tok
            if k == "T" and e == "T":
                return
            if waits.get(k, 0) < v:
                waits[k] = v
        for b in reads:
            need(b.w)
        for b in writes:
            need(b.w)
            for k, v in b.r.items():
                need((k, v))
        kn = self.known[e]
        wl = []
        for k in sorted(waits):
            v = waits[k]
            if k[0] == "D":
                v = self.dcnt[k]
            if kn.get(k, 0) >= v:
                continue
            kn[k] = v
            wl.append((k, v))
        if dma:
            key = "D" + e
            if e == "G":
                BUDGET = 260
                out = sum(n for _, n in self._gfifo)
                while self._gfifo and out + ndesc > BUDGET:
                    tokv, n = self._gfifo.pop(0)
                    out -= n
                    self._gdone += 1
                    lim = 16 * self._gdone
                    if kn.get(key, 0) < lim:
                        kn[key] = lim
                        wl.append((key, lim))
                self._gfifo.append((self.dcnt[key] + 16, ndesc))
            self.dcnt[key] += 16
            tok = (key, self.dcnt[key])
        else:
            self.cnt[e] += 1
            tok = (e, self.cnt[e])
        self.q[e].append((wl, fns, tok))
        for b in writes:
            b.w = tok
            b.r = {}
        for b in reads:
            if b.r.get(tok[0], 0) < tok[1]:
                b.r[tok[0]] = tok[1]
        return tok

    def alias(self, dst, *srcs):
        for sb in srcs:
            if sb.w is not None:
                k, v = sb.w
                if dst.r.get(k, 0) < v:
                    dst.r[k] = v
            for k, v in sb.r.items():
                if dst.r.get(k, 0) < v:
                    dst.r[k] = v

    def fence(self):
        for e in ("T", "V", "S", "G", "Y"):
            wl = []
            kn = self.known[e]
            for k in ("T", "V", "S", "G"):
                if k == e:
                    continue
                if self.cnt[k] and kn.get(k, 0) < self.cnt[k]:
                    kn[k] = self.cnt[k]
                    wl.append((k, self.cnt[k]))
            for k in ("DY", "DG"):
                if self.dcnt[k] and kn.get(k, 0) < self.dcnt[k]:
                    kn[k] = self.dcnt[k]
                    wl.append((k, self.dcnt[k]))
            if wl:
                self.q[e].append((wl, [], None))

    def final_wait(self):
        wl = []
        for k, vmax in self.dcnt.items():
            if vmax and self.known["Y"].get(k, 0) < vmax:
                wl.append((k, vmax))
        for k in ("T", "V", "S", "G"):
            if self.cnt[k] and self.known["Y"].get(k, 0) < self.cnt[k]:
                wl.append((k, self.cnt[k]))
        self.q["Y"].append((wl, [], None))

    def din(self, name, shape, dtype):
        self.inputs.add(name)
        return self.nc.dram_tensor(name, list(shape), dtype, kind="ExternalInput")

    def sb(self, name, shape, dtype, off):
        byte_off = self.sb0 + off
        nbytes = int(np.prod(shape[1:])) * mybir.dt.size(dtype)
        assert byte_off + nbytes <= self.sb_hi, (name, byte_off + nbytes, self.sb_hi)
        return self.nc.alloc_sbuf_tensor_at(name, list(shape), dtype, offset=byte_off)

    def build(self):
        nc = self.nc
        st = contextlib.ExitStack()
        sems = {}
        for k in ("T", "V", "S", "G", "DY", "DG"):
            sems[k] = st.enter_context(nc.semaphore("sem_" + k))
        blk = st.enter_context(nc.Block())
        def mk(e):
            def body(eng):
                for wl, fns, tok in self.q[e]:
                    for k, v in wl:
                        eng.wait_ge(sems[k], v)
                    for i, fn in enumerate(fns):
                        inst = fn(eng)
                        if tok is not None and i == len(fns) - 1:
                            inst.then_inc(sems[tok[0]], 16 if tok[0][0] == "D" else 1)
            return body
        blk.tensor(mk("T"))
        blk.vector(mk("V"))
        blk.scalar(mk("S"))
        blk.gpsimd(mk("G"))
        blk.sync(mk("Y"))
        st.close()
        return nc

OFF_R = 0
OFF_A = 65536
OFF_B = 98304
OFF_C = 131072
OFF_W = 164352
OFF_S = 180736
OFF_K = 197120


def g_load(P, sb_ap, dram_ap, key):
    b = P.buf(key)
    P.emit("G", lambda g: g.dma_start(sb_ap, dram_ap), reads=(), writes=(b,), dma=True)
    return b

def g_load_ch(P, sb_t, dram_ap3, key, nch=None):
    b = P.buf(key)
    k = sb_t.shape[1]
    step = 2
    for c0 in range(0, k, step):
        c1 = min(k, c0 + step)
        P.emit("G", lambda g, c0=c0, c1=c1: g.dma_start(
            sb_t[:, c0:c1, :], dram_ap3[:, c0:c1, :]), reads=(), writes=(b,), dma=True)
    return b

def setup_common(P):
    if hasattr(P, "_common"):
        return P._common
    nc = P.nc
    K_id = P.sb("K_id", [128, 128], BF16, OFF_K)
    K_idf = P.sb("K_idf", [128, 128], F32, OFF_K + 256)
    K_ut = P.sb("K_ut", [128, 128], BF16, OFF_K + 768)
    K_on = P.sb("K_on", [128, 128], BF16, OFF_K + 1024)
    K_onf = P.sb("K_onf", [128, 128], F32, OFF_K + 1280)
    K_sc = P.sb("K_sc", [128, 16], F32, OFF_K + 1792)
    ps = [nc.alloc_psum_tensor(f"ps{i}", [128, 512], F32) for i in range(6)]
    pb = [nc.alloc_psum_tensor(f"pb{i}", [128, 1024], BF16) for i in range(2)]
    psB = [P.buf(("psum", i)) for i in range(6)]
    pbB = [P.buf(("psumb", i)) for i in range(2)]
    kbufs = {}
    for nm, t, shp, dt_ in [("IDN", K_id, [128, 128], BF16),
                            ("IDF", K_idf, [128, 128], F32),
                            ("UTRI", K_ut, [128, 128], BF16),
                            ("ONEBF", K_on, [128, 128], BF16),
                            ("ONEF", K_onf, [128, 128], F32),
                            ("SCAL", K_sc, [128, 16], F32)]:
        d = P.din(nm, shp, dt_)
        b = P.buf(("k", nm))
        P.emit("Y", lambda y, t=t, d=d: y.dma_start(t[:, :], d.ap()),
               reads=(), writes=(b,), dma=True)
        kbufs[nm] = b
    P._common = (K_id, K_idf, K_ut, K_on, K_onf, K_sc, ps, pb, psB, pbB, kbufs)
    return P._common

def load_zone(P, cons, native=()):
    zb = P.buf("KZONE")
    out = {}
    for ent in cons:
        nm, t, shp = ent[0], ent[1], ent[2]
        spec = ent[3] if len(ent) > 3 else None
        b = P.buf(("k", nm))
        P.alias(b, zb)
        if nm in native:
            d = P.din(nm, shp, t.dtype)
            full = t[tuple(slice(None) for _ in t.shape)]
            P.emit("Y", lambda y, full=full, d=d: y.dma_start(full, d.ap()),
                   reads=(), writes=(b, zb), dma=True)
            out[nm] = b
            continue
        d = P.din(nm, shp, F32)
        src = d.ap() if spec is None else d.ap().rearrange(spec, p=128)
        if len(t.shape) == 3 and t.shape[1] > 2:
            for c0 in range(0, t.shape[1], 2):
                c1 = min(t.shape[1], c0 + 2)
                P.emit("G", lambda g, t=t, src=src, c0=c0, c1=c1: g.dma_start(
                    t[:, c0:c1, :], src[:, c0:c1, :]),
                    reads=(), writes=(b, zb), dma=True)
        else:
            full = t[tuple(slice(None) for _ in t.shape)]
            P.emit("G", lambda g, full=full, src=src: g.dma_start(full, src),
                   reads=(), writes=(b, zb), dma=True)
        out[nm] = b
    return out


def emit_stage_A(P, x_dram):
    nc = P.nc
    R = P.sb("R", [128, NT, 1024], F32, OFF_R)
    A_x0T = P.sb("A_x0T", [128, 8, 2048], BF16, OFF_A)
    B_kt = P.sb("B_kt", [128, 8, 2048], BF16, OFF_B)
    C_v = P.sb("C_v", [128, 16, 1040], BF16, OFF_C)
    C_stage = P.sb("C_stage", [128, 16, 1024], BF16, OFF_C)
    W1 = P.sb("W1", [128, 8, 1024], BF16, OFF_W)
    s_qw = [P.sb(f"s_qw{i}", [128, 8, 128], BF16, OFF_S + i * 2048) for i in range(2)]
    s_qwp = [P.sb(f"s_qwp{i}", [128, 8, 128], BF16, OFF_S + 4096 + i * 2048) for i in range(2)]
    s_ctx0 = P.sb("s_ctx0", [128, 2048], BF16, OFF_S + 8192)
    s_qt = P.sb("s_qt", [128, 512], BF16, OFF_S + 12288)
    s_rec = P.sb("s_rec", [128, 1024], BF16, OFF_S + 13312)
    s_exp = [P.sb("s_exp0", [128, 512], BF16, OFF_S + 15360)] + \
            [P.sb(f"s_exp{1 + i}", [128, 512], BF16, OFF_W + i * 1024) for i in range(3)]
    s_X = P.sb("s_X", [128, 1024], F32, OFF_S)
    s_xres = P.sb("s_xres", [128, 1024], F32, OFF_S + 4096)
    s_dump = P.sb("s_dump", [128, 1024], BF16, OFF_S + 12288)
    s_small = P.sb("s_small", [128, 64], F32, OFF_S + 14336)
    K_id, K_idf, K_ut, K_on, K_onf, K_sc, ps, pb, psB, pbB, kbufs0 = setup_common(P)
    KZ = OFF_K + 2048
    K_cos = P.sb("K_cos", [128, 2048], BF16, KZ)
    K_sin = P.sb("K_sin", [128, 2048], BF16, KZ + 4096)
    K_bqT = P.sb("K_bqT", [128, 8], F32, KZ + 8192)
    K_bqpT = P.sb("K_bqpT", [128, 8], F32, KZ + 8224)
    K_bkT = P.sb("K_bkT", [128, 8], F32, KZ + 8256)
    K_bkpT = P.sb("K_bkpT", [128, 8], F32, KZ + 8288)
    K_bv = P.sb("K_bv", [128, 1024], BF16, KZ + 8320)
    K_bo = P.sb("K_bo", [128, 1024], BF16, KZ + 10368)
    s_t1 = P.sb("s_t1", [128, 512], BF16, KZ + 12416)

    dWq = P.din("A_Wq", [1024, 1024], F32)
    dWqp = P.din("A_Wqp", [1024, 1024], F32)
    dWk = P.din("A_Wk", [1024, 1024], F32)
    dWkp = P.din("A_Wkp", [1024, 1024], F32)
    dWv = P.din("A_Wv", [1024, 1024], F32)
    dWo = P.din("A_Wo", [1024, 1024], F32)
    cons = [("COSA", K_cos, [128, 2048]), ("SINA", K_sin, [128, 2048]),
            ("A_bqT", K_bqT, [128, 8]), ("A_bqpT", K_bqpT, [128, 8]),
            ("A_bkT", K_bkT, [128, 8]), ("A_bkpT", K_bkpT, [128, 8]),
            ("A_bvR", K_bv, [128, 1024]), ("A_boR", K_bo, [128, 1024])]
    kbufs = dict(kbufs0)
    kbufs.update(load_zone(P, cons, native=("COSA", "SINA")))
    bW1 = P.buf("bW1"); bW2 = P.buf("bW2")
    bR = [P.buf(("R", j)) for j in range(NT)]
    bA = P.buf("bA"); bB = P.buf("bB"); bC = P.buf("bC")
    bctx0 = P.buf("bctx0")

    xr = x_dram.ap().rearrange("(j p) c -> p j c", p=128)
    g_load_ch(P, C_stage, xr, "bC")
    for j in range(NT):
        for cc in range(8):
            pbi = (j * 8 + cc) % 2
            P.emit("T", lambda t, j=j, cc=cc, pbi=pbi: t.transpose(
                pb[pbi][:, 0:128], C_stage[:, j, cc * 128:(cc + 1) * 128], K_id[:, :]),
                reads=(bC, kbufs["IDN"], pbB[pbi]), writes=(pbB[pbi],))
            P.emit("V", lambda v, j=j, cc=cc, pbi=pbi: v.tensor_copy(
                A_x0T[:, cc, j * 128:(j + 1) * 128], pb[pbi][:, 0:128]),
                reads=(pbB[pbi],), writes=(bA, pbB[pbi]))

    for hp in range(8):
        qwb = s_qw[hp % 2]; qwpb = s_qwp[hp % 2]
        bqw = P.buf(("qw", hp % 2)); bqwp = P.buf(("qwp", hp % 2))
        for c0 in range(0, 8, 2):
            P.emit("G", lambda g, hp=hp, qwb=qwb, c0=c0: g.dma_start(
                qwb[:, c0:c0 + 2, :], dWk.ap().rearrange("(k p) n -> p k n", p=128)
                [:, c0:c0 + 2, hp * 128:(hp + 1) * 128]),
                reads=(bqw,), writes=(bqw,), dma=True)
            P.emit("G", lambda g, hp=hp, qwpb=qwpb, c0=c0: g.dma_start(
                qwpb[:, c0:c0 + 2, :], dWkp.ap().rearrange("(k p) n -> p k n", p=128)
                [:, c0:c0 + 2, hp * 128:(hp + 1) * 128]),
                reads=(bqwp,), writes=(bqwp,), dma=True)
        for w in range(4):
            wc = slice(w * 512, (w + 1) * 512)
            P.emit("T", [
                (lambda t, cc=cc, wc=wc, qwb=qwb: t.matmul(
                    ps[0][:, 0:512], qwb[:, cc, :],
                    A_x0T[:, cc, wc], start=(cc == 0), stop=(cc == 7)))
                for cc in range(8)],
                reads=(bqw, bA, psB[0]), writes=(psB[0],))
            P.emit("T", [
                (lambda t, cc=cc, wc=wc, qwpb=qwpb: t.matmul(
                    ps[1][:, 0:512], qwpb[:, cc, :],
                    A_x0T[:, cc, wc], start=(cc == 0), stop=(cc == 7)))
                for cc in range(8)],
                reads=(bqwp, bA, psB[1]), writes=(psB[1],))
            bt1 = P.buf("s_t1")
            P.emit("V", lambda v, hp=hp, wc=wc: v.scalar_tensor_tensor(
                s_t1[:, 0:512], ps[0][:, 0:512], K_bkT[:, hp:hp + 1], K_cos[:, wc],
                ALU.add, ALU.mult),
                reads=(psB[0], kbufs["A_bkT"], kbufs["COSA"]), writes=(bt1,))
            P.emit("V", lambda v, hp=hp, wc=wc: v.scalar_tensor_tensor(
                B_kt[:, hp, wc], ps[1][:, 0:512], K_bkpT[:, hp:hp + 1], K_sin[:, wc],
                ALU.add, ALU.mult),
                reads=(psB[1], kbufs["A_bkpT"], kbufs["SINA"]), writes=(bB,))
            P.emit("V", lambda v, hp=hp, wc=wc: v.tensor_add(
                B_kt[:, hp, wc], B_kt[:, hp, wc], s_t1[:, 0:512]),
                reads=(bt1, bB), writes=(bB,))
    Cv4 = C_v[:, :, :].rearrange("p t (h e) -> p t h e", e=65)
    P.emit("V", lambda v: v.memset(Cv4[:, :, :, 64:65], 1.0), reads=(bC,), writes=(bC,))
    g_load_ch(P, W1, dWv.ap().rearrange("(k p) n -> p k n", p=128), "bW1", 4)
    for j in range(NT):
        for half in range(2):
            pi = half
            P.emit("T", [
                (lambda t, cc=cc, j=j, half=half, pi=pi: t.matmul(
                    ps[pi][:, 0:512], A_x0T[:, cc, j * 128:(j + 1) * 128],
                    W1[:, cc, half * 512:(half + 1) * 512],
                    start=(cc == 0), stop=(cc == 7)))
                for cc in range(8)],
                reads=(bW1, bA, psB[pi]), writes=(psB[pi],))
            P.emit("V", lambda v, j=j, half=half, pi=pi: v.tensor_add(
                Cv4[:, j, half * 8:(half + 1) * 8, 0:64],
                ps[pi][:, 0:512].rearrange("p (h e) -> p h e", e=64),
                K_bv[:, half * 512:(half + 1) * 512].rearrange("p (h e) -> p h e", e=64)),
                reads=(psB[pi], kbufs["A_bvR"]), writes=(bC,))

    P.alias(P.buf(("exp", 2)), P.buf("bW1"))
    P.alias(P.buf(("exp", 3)), P.buf("bW1"))
    for hp in range(8):
        qwb = s_qw[hp % 2]; qwpb = s_qwp[hp % 2]
        bqw = P.buf(("qw", hp % 2)); bqwp = P.buf(("qwp", hp % 2))
        for c0 in range(0, 8, 2):
            P.emit("G", lambda g, hp=hp, qwb=qwb, c0=c0: g.dma_start(
                qwb[:, c0:c0 + 2, :], dWq.ap().rearrange("(k p) n -> p k n", p=128)
                [:, c0:c0 + 2, hp * 128:(hp + 1) * 128]),
                reads=(bqw,), writes=(bqw,), dma=True)
            P.emit("G", lambda g, hp=hp, qwpb=qwpb, c0=c0: g.dma_start(
                qwpb[:, c0:c0 + 2, :], dWqp.ap().rearrange("(k p) n -> p k n", p=128)
                [:, c0:c0 + 2, hp * 128:(hp + 1) * 128]),
                reads=(bqwp,), writes=(bqwp,), dma=True)
        for w in range(4):
            wc = slice(w * 512, (w + 1) * 512)
            P.emit("T", [
                (lambda t, cc=cc, wc=wc, qwb=qwb: t.matmul(
                    ps[0][:, 0:512], qwb[:, cc, :], A_x0T[:, cc, wc],
                    start=(cc == 0), stop=(cc == 7))) for cc in range(8)],
                reads=(bqw, bA, psB[0]), writes=(psB[0],))
            P.emit("T", [
                (lambda t, cc=cc, wc=wc, qwpb=qwpb: t.matmul(
                    ps[1][:, 0:512], qwpb[:, cc, :], A_x0T[:, cc, wc],
                    start=(cc == 0), stop=(cc == 7))) for cc in range(8)],
                reads=(bqwp, bA, psB[1]), writes=(psB[1],))
            bt1 = P.buf("s_t1"); bqt = P.buf("s_qt")
            P.emit("V", lambda v, hp=hp, wc=wc: v.scalar_tensor_tensor(
                s_t1[:, 0:512], ps[0][:, 0:512], K_bqT[:, hp:hp + 1], K_cos[:, wc],
                ALU.add, ALU.mult),
                reads=(psB[0], kbufs["A_bqT"], kbufs["COSA"]), writes=(bt1,))
            P.emit("V", lambda v, hp=hp, wc=wc: v.scalar_tensor_tensor(
                s_qt[:, 0:512], ps[1][:, 0:512], K_bqpT[:, hp:hp + 1], K_sin[:, wc],
                ALU.add, ALU.mult),
                reads=(psB[1], kbufs["A_bqpT"], kbufs["SINA"]), writes=(bqt,))
            P.emit("V", lambda v: v.tensor_add(s_qt[:, 0:512], s_qt[:, 0:512],
                                               s_t1[:, 0:512]),
                reads=(bt1, bqt), writes=(bqt,))
            nS = 4 * w + 4
            j0s = [max(0, s * 128 - w * 512) for s in range(nS)]
            ROT2 = ((4, 5), (0, 1))
            def emit_av(s, j0, hp=hp, w=w, nS=nS):
                for hh in range(2):
                    h = hp * 2 + hh
                    pc = 2 + hh
                    be = P.buf(("exp", hh * 2 + s % 2))
                    P.emit("T", lambda t, s=s, j0=j0, pc=pc, h=h, hh=hh, nS=nS: t.matmul(
                        ps[pc][0:65, j0:512], C_v[:, s, h * 65:(h + 1) * 65],
                        s_exp[hh * 2 + s % 2][:, j0:512],
                        start=(s == 0), stop=(s == nS - 1)),
                        reads=(bC, be, psB[pc]), writes=(psB[pc],))
            for s in range(nS):
                j0 = j0s[s]
                pr = ROT2[s % 2]
                P.emit("T", [
                    (lambda t, s=s, j0=j0, hp=hp, w=w, pr=pr, hh=hh, r0=r0: t.matmul(
                        ps[pr[hh]][:, j0:512],
                        B_kt[r0:r0 + 64, hp, s * 128:(s + 1) * 128],
                        s_qt[r0:r0 + 64, j0:512], start=True, stop=True,
                        tile_position=(r0, 0)))
                    for hh, r0 in ((0, 0), (1, 64))],
                    reads=(bB, P.buf("s_qt"), psB[pr[0]], psB[pr[1]]),
                    writes=(psB[pr[0]], psB[pr[1]]))
                for hh in range(2):
                    be = P.buf(("exp", hh * 2 + s % 2))
                    P.emit("S", lambda sc, s=s, j0=j0, pe=pr[hh], hh=hh: sc.activation(
                        s_exp[hh * 2 + s % 2][:, j0:512], ps[pe][:, j0:512], AF.Exp,
                        bias=0.0, scale=K_sc[:, 0:1]),
                        reads=(psB[pr[hh]], kbufs["SCAL"]), writes=(be,))
                    if s * 128 >= w * 512:
                        P.emit("V", lambda v, s=s, j0=j0, hh=hh: v.tensor_mul(
                            s_exp[hh * 2 + s % 2][:, j0:j0 + 128],
                            s_exp[hh * 2 + s % 2][:, j0:j0 + 128], K_ut[:, :]),
                            reads=(be, kbufs["UTRI"]), writes=(be,))
                if s >= 1:
                    emit_av(s - 1, j0s[s - 1])
            emit_av(nS - 1, j0s[nS - 1])
            for hh in range(2):
                h = hp * 2 + hh
                r0 = hh * 64
                pc = 2 + hh
                bpc = psB[pc]
                brec = P.buf("s_rec")
                P.emit("S", lambda sc, pc=pc, hh=hh: sc.activation(
                    s_rec[64:65, 512:1024], ps[pc][64:65, 0:512], AF.Ln, bias=0.0),
                    reads=(bpc,), writes=(brec,))
                P.emit("S", lambda sc: sc.activation(
                    s_rec[64:65, 0:512], s_rec[64:65, 512:1024], AF.Exp,
                    bias=0.0, scale=K_sc[64:65, 8:9]),
                    reads=(brec, kbufs["SCAL"]), writes=(brec,))
                P.emit("T", lambda t: t.matmul(
                    ps[5][0:64, 0:512], K_on[64:65, 0:64], s_rec[64:65, 0:512],
                    start=True, stop=True),
                    reads=(brec, kbufs["ONEBF"], psB[5]), writes=(psB[5],))
                P.emit("V", lambda v: v.tensor_copy(s_rec[0:64, 0:512],
                                                    ps[5][0:64, 0:512]),
                    reads=(psB[5],), writes=(brec,))
                if hp == 0:
                    dst = lambda wc=wc, r0=r0: s_ctx0[r0:r0 + 64, wc]
                    dstb = bctx0
                else:
                    dst = lambda wc=wc, r0=r0, hp=hp: B_kt[r0:r0 + 64, hp - 1, wc]
                    dstb = bB
                P.emit("V", lambda v, dst=dst, pc=pc: v.tensor_mul(
                    dst(), ps[pc][0:64, 0:512], s_rec[0:64, 0:512]),
                    reads=(bpc, brec), writes=(dstb,))

    P.alias(P.buf("bW1"), P.buf(("exp", 2)), P.buf(("exp", 3)))
    g_load_ch(P, W1, dWo.ap().rearrange("(k p) n -> p k n", p=128), "bW1", 4)
    bsm = P.buf("s_small")
    P.alias(P.buf("s_X"), P.buf(("qw", 0)), P.buf(("qw", 1)))
    P.alias(P.buf("s_xres"), P.buf(("qwp", 0)), P.buf(("qwp", 1)))
    P.alias(P.buf("s_dump"), P.buf("s_qt"), P.buf("s_rec"))
    P.alias(bsm, P.buf(("exp", 0)), P.buf(("exp", 1)))
    P.alias(P.buf("s_t1"), P.buf(("exp", 2)), P.buf(("exp", 3)))
    for j in range(NT):
        bxr = P.buf("s_xres")
        P.emit("G", lambda g, j=j: g.dma_start(s_xres[:, :], xr[:, j, :]),
               reads=(bxr,), writes=(bxr,), dma=True)
        for half in range(2):
            pi = half
            def mml(cc, j=j, half=half, pi=pi):
                def f(t):
                    src = s_ctx0[:, j * 128:(j + 1) * 128] if cc == 0 else \
                        B_kt[:, cc - 1, j * 128:(j + 1) * 128]
                    return t.matmul(ps[pi][:, 0:512], src,
                                    W1[:, cc, half * 512:(half + 1) * 512],
                                    start=(cc == 0), stop=(cc == 7))
                return f
            P.emit("T", [mml(cc) for cc in range(8)],
                   reads=(bW1, bB, bctx0, psB[pi]), writes=(psB[pi],))
        bX = P.buf("s_X")
        for half in range(2):
            hs = slice(half * 512, (half + 1) * 512)
            P.emit("V", lambda v, hs=hs, half=half: v.tensor_add(
                s_X[:, hs], ps[half][:, 0:512], K_bo[:, hs]),
                reads=(psB[half], kbufs["A_boR"]), writes=(bX,))
        for half in range(2):
            hs = slice(half * 512, (half + 1) * 512)
            P.emit("V", lambda v, hs=hs, half=half: v.scalar_tensor_tensor(
                s_X[:, hs], s_X[:, hs], 0.0, s_xres[:, hs], ALU.add, ALU.add,
                accum_out=s_small[:, half:half + 1]),
                reads=(bX, bxr), writes=(bX, bsm))
        P.emit("V", lambda v: v.scalar_tensor_tensor(
            s_dump[:, 0:1024], s_X[:, 0:1024], 0.0, s_X[:, 0:1024], ALU.add,
            ALU.mult, accum_out=s_small[:, 2:3]),
            reads=(bX,), writes=(P.buf("s_dump"), bsm))
        P.emit("V", lambda v: v.tensor_add(s_small[:, 3:4], s_small[:, 0:1],
                                           s_small[:, 1:2]), reads=(bsm,), writes=(bsm,))
        P.emit("V", lambda v: v.tensor_scalar_mul(s_small[:, 4:5], s_small[:, 3:4],
                                                  1.0 / 1024.0), reads=(bsm,), writes=(bsm,))
        P.emit("V", lambda v: v.tensor_scalar(s_small[:, 5:6], s_small[:, 4:5],
                                              s_small[:, 4:5], -1e-5,
                                              ALU.mult, ALU.add),
               reads=(bsm,), writes=(bsm,))
        P.emit("V", lambda v: v.scalar_tensor_tensor(
            s_small[:, 6:7], s_small[:, 2:3], 1.0 / 1024.0, s_small[:, 5:6],
            ALU.mult, ALU.subtract), reads=(bsm,), writes=(bsm,))
        P.emit("S", lambda sc: sc.activation(s_small[:, 7:8], s_small[:, 6:7],
                                             AF.Ln, bias=0.0), reads=(bsm,), writes=(bsm,))
        P.emit("S", lambda sc: sc.activation(s_small[:, 8:9], s_small[:, 7:8],
                                             AF.Exp, bias=0.0, scale=K_sc[:, 1:2]),
               reads=(bsm, kbufs["SCAL"]), writes=(bsm,))
        P.emit("V", lambda v: v.scalar_tensor_tensor(
            s_small[:, 9:10], s_small[:, 4:5], -1.0, s_small[:, 8:9],
            ALU.mult, ALU.mult), reads=(bsm,), writes=(bsm,))
        P.emit("V", lambda v, j=j: v.tensor_scalar(
            R[:, j, :], s_X[:, 0:1024], s_small[:, 8:9], s_small[:, 9:10],
            ALU.mult, ALU.add), reads=(bX, bsm), writes=(bR[j],))
    bxb = P.buf("s_dump")
    for j in range(NT):
        P.emit("S", lambda sc, j=j: sc.copy(s_dump[:, 0:1024], R[:, j, :]),
               reads=(bR[j],), writes=(bxb,))
        for cc in range(8):
            pbi = cc % 2
            P.emit("T", lambda t, cc=cc, pbi=pbi: t.transpose(
                pb[pbi][:, 0:128], s_dump[:, cc * 128:(cc + 1) * 128], K_id[:, :]),
                reads=(bxb, kbufs["IDN"], pbB[pbi]), writes=(pbB[pbi],))
            P.emit("V", lambda v, j=j, cc=cc, pbi=pbi: v.tensor_copy(
                B_kt[:, cc, j * 128:(j + 1) * 128], pb[pbi][:, 0:128]),
                reads=(pbB[pbi],), writes=(bB,))
    return R, B_kt, bR, bB


def emit_xT_from_R(P, R, bR, dstT, bdst, K_id, kb_id, pb, pbB, s_dump, bdump):
    for j in range(NT):
        P.emit("S", lambda sc, j=j: sc.copy(s_dump[:, 0:1024], R[:, j, :]),
               reads=(bR[j],), writes=(bdump,))
        for cc in range(8):
            pbi = cc % 2
            P.emit("T", lambda t, cc=cc, pbi=pbi: t.transpose(
                pb[pbi][:, 0:128], s_dump[:, cc * 128:(cc + 1) * 128], K_id[:, :]),
                reads=(bdump, kb_id, pbB[pbi]), writes=(pbB[pbi],))
            P.emit("V", lambda v, j=j, cc=cc, pbi=pbi: v.tensor_copy(
                dstT[:, cc, j * 128:(j + 1) * 128], pb[pbi][:, 0:128]),
                reads=(pbB[pbi],), writes=(bdst,))

def emit_ln_tiles(P, R, bR, j, stats_src, stats_bufs, s_small, bsm, K_sc, kb_sc,
                  apply_in, apply_bufs):
    P.emit("V", lambda v: v.tensor_add(s_small[:, 3:4], s_small[:, 0:1],
                                       s_small[:, 1:2]), reads=(bsm,), writes=(bsm,))
    P.emit("V", lambda v: v.tensor_scalar_mul(s_small[:, 4:5], s_small[:, 3:4],
                                              1.0 / 1024.0), reads=(bsm,), writes=(bsm,))
    P.emit("V", lambda v: v.tensor_scalar(s_small[:, 5:6], s_small[:, 4:5],
                                          s_small[:, 4:5], -1e-5,
                                          ALU.mult, ALU.add), reads=(bsm,), writes=(bsm,))
    P.emit("V", lambda v: v.scalar_tensor_tensor(
        s_small[:, 6:7], s_small[:, 2:3], 1.0 / 1024.0, s_small[:, 5:6],
        ALU.mult, ALU.subtract), reads=(bsm,), writes=(bsm,))
    P.emit("S", lambda sc: sc.activation(s_small[:, 7:8], s_small[:, 6:7],
                                         AF.Ln, bias=0.0), reads=(bsm,), writes=(bsm,))
    P.emit("S", lambda sc: sc.activation(s_small[:, 8:9], s_small[:, 7:8],
                                         AF.Exp, bias=0.0, scale=K_sc[:, 1:2]),
           reads=(bsm, kb_sc), writes=(bsm,))
    P.emit("V", lambda v: v.scalar_tensor_tensor(
        s_small[:, 9:10], s_small[:, 4:5], -1.0, s_small[:, 8:9],
        ALU.mult, ALU.mult), reads=(bsm,), writes=(bsm,))
    P.emit("V", lambda v, j=j: v.tensor_scalar(
        R[:, j, :], apply_in, s_small[:, 8:9], s_small[:, 9:10],
        ALU.mult, ALU.add), reads=tuple(apply_bufs) + (bsm,), writes=(bR[j],))


def emit_stage_B(P, x_in=None, dbg=False):
    nc = P.nc
    dbg_outs = {}
    def DUMP(name, shape, dtype, ap_fn, rbufs):
        if not dbg:
            return
        d = P.nc.dram_tensor(name, shape, dtype, kind="ExternalOutput")
        P.emit("Y", lambda y: y.dma_start(d.ap(), ap_fn()), reads=rbufs,
               writes=(), dma=True)
        dbg_outs[name] = d
    R = P.sb("R", [128, NT, 1024], F32, OFF_R)
    Bt = P.sb("Bt", [128, 8, 2048], BF16, OFF_B)
    A_kt = P.sb("A_kt", [128, 8, 2048], BF16, OFF_A)
    A_w1 = P.sb("A_w1", [128, 8, 2048], BF16, OFF_A)
    C_v = P.sb("C_v", [128, 16, 528], BF16, OFF_C)
    C_ao = P.sb("C_ao", [128, 4, 2048], BF16, OFF_C + 16896)
    C_w2 = P.sb("C_w2", [128, 16, 1024], BF16, OFF_C)
    Wa = P.sb("Wa", [128, 8, 512], BF16, OFF_W)
    Wb = P.sb("Wb", [128, 8, 512], BF16, OFF_W + 8192)
    Wo2 = P.sb("Wo2", [128, 4, 1024], BF16, OFF_W + 8192)
    s_qt = P.sb("s_qt", [128, 512], BF16, OFF_S)
    s_rec = P.sb("s_rec", [128, 1024], BF16, OFF_S + 1024)
    s_exp = [P.sb("s_exp0", [128, 512], BF16, OFF_S + 3072)] + \
            [P.sb(f"s_exp{1 + i}", [128, 512], BF16, OFF_S + 10240 + i * 1024) for i in range(3)]
    s_X = P.sb("s_X", [128, 1024], F32, OFF_S + 4096)
    s_dump = P.sb("s_dump", [128, 1024], BF16, OFF_S + 8192)
    s_mT = P.sb("s_mT", [128, 1024], F32, OFF_S + 10240)
    s_small = P.sb("s_small", [128, 64], F32, OFF_S + 14336)
    s_h1 = [P.sb(f"s_h1{i}", [128, 256], BF16, OFF_S + 14592 + i * 512) for i in range(2)]
    s_rr = P.sb("s_rr", [128, 16], F32, OFF_S + 15616)
    s_cb = P.sb("s_cb", [128, 8], BF16, OFF_S + 15680)
    K_id, K_idf, K_ut, K_on, K_onf, K_sc, ps, pb, psB, pbB, kbufs0 = setup_common(P)
    KZ = OFF_K + 2048
    K_qbT = P.sb("K_qbT", [128, 8], F32, KZ)
    K_kbT = P.sb("K_kbT", [128, 8], F32, KZ + 32)
    K_bv = P.sb("K_bv", [128, 512], BF16, KZ + 64)
    K_bo = P.sb("K_bo", [128, 1024], BF16, KZ + 1088)
    K_rw = P.sb("K_rw", [128, 8, 8], F32, KZ + 3136)
    K_rb = P.sb("K_rb", [128, 8], F32, KZ + 3392)
    K_b1T = P.sb("K_b1T", [128, 8, 16], F32, KZ + 3424)
    K_b2 = P.sb("K_b2", [8, 1024], BF16, KZ + 3936)
    K_cT = P.sb("K_cT", [8, 2048], BF16, KZ + 5984)
    K_c3 = P.sb("K_c3", [128, 16, 8], F32, KZ + 10080)
    kbufs = dict(kbufs0)
    kbufs.update(load_zone(P, [
        ("B_qbT", K_qbT, [128, 8]), ("B_kbT", K_kbT, [128, 8]),
        ("B_vbR", K_bv, [128, 512]), ("B_obR", K_bo, [128, 1024]),
        ("B_rw", K_rw, [1024, 8], "(k p) n -> p k n"), ("B_rbR", K_rb, [128, 8]),
        ("B_b1T", K_b1T, [128, 8, 16]), ("B_b2", K_b2, [8, 1024]),
    ]))
    dkw = P.din("B_kw", [1024, 512], F32)
    dvw = P.din("B_vw", [1024, 512], F32)
    dqw = P.din("B_qw", [1024, 512], F32)
    dow = P.din("B_ow", [512, 1024], F32)
    dw1 = P.din("B_w1", [8, 1024, 2048], F32)
    dw2 = P.din("B_w2", [8, 2048, 1024], F32)
    bR = [P.buf(("R", j)) for j in range(NT)]
    bBt = P.buf("bBt"); bA = P.buf("bA"); bC = P.buf("bC")
    bWa = P.buf("bWa"); bWb = P.buf("bWb")
    bc3 = P.buf("bc3"); bcT = P.buf("bcT")
    bsm = P.buf("s_small")
    bdump0 = P.buf("s_dump")
    if x_in is not None:
        _load_R_f32(P, x_in, R, bR)
        emit_xT_from_R(P, R, bR, Bt, bBt, K_id, kbufs["IDN"], pb, pbB,
                       s_dump, bdump0)

    g_load_ch(P, Wa, dkw.ap().rearrange("(k p) n -> p k n", p=128), "bWa", 4)
    g_load_ch(P, Wb, dvw.ap().rearrange("(k p) n -> p k n", p=128), "bWb", 4)
    for hp in range(8):
        for w in range(4):
            wc = slice(w * 512, (w + 1) * 512)
            pi = (hp * 4 + w) % 2
            grp = []
            for hh in range(2):
                r0 = hh * 64
                for cc in range(8):
                    grp.append(lambda t, cc=cc, hp=hp, hh=hh, r0=r0, wc=wc, pi=pi:
                               t.matmul(
                        ps[pi][r0:r0 + 32, 0:512],
                        Wa[:, cc, hp * 64 + hh * 32:hp * 64 + hh * 32 + 32],
                        Bt[:, cc, wc], start=(cc == 0), stop=(cc == 7),
                        tile_position=(0, r0)))
            P.emit("T", grp, reads=(bWa, bBt, psB[pi]), writes=(psB[pi],))
            for r0 in (0, 64):
                P.emit("V", lambda v, hp=hp, wc=wc, pi=pi, r0=r0: v.tensor_scalar(
                    A_kt[r0:r0 + 32, hp, wc], ps[pi][r0:r0 + 32, 0:512],
                    K_kbT[r0:r0 + 32, hp:hp + 1], None, ALU.add),
                    reads=(psB[pi], kbufs["B_kbT"]), writes=(bA,))
    Cv4 = C_v[:, :, :].rearrange("p t (h e) -> p t h e", e=33)
    P.emit("V", lambda v: v.memset(Cv4[:, :, :, 32:33], 1.0), reads=(bC,), writes=(bC,))
    for j in range(NT):
        pi = j % 2
        P.emit("T", [
            (lambda t, cc=cc, j=j, pi=pi: t.matmul(
                ps[pi][:, 0:512], Bt[:, cc, j * 128:(j + 1) * 128],
                Wb[:, cc, 0:512], start=(cc == 0), stop=(cc == 7)))
            for cc in range(8)],
            reads=(bWb, bBt, psB[pi]), writes=(psB[pi],))
        P.emit("V", lambda v, j=j, pi=pi: v.tensor_add(
            Cv4[:, j, :, 0:32],
            ps[pi][:, 0:512].rearrange("p (h e) -> p h e", e=32),
            K_bv[:, :].rearrange("p (h e) -> p h e", e=32)),
            reads=(psB[pi], kbufs["B_vbR"]), writes=(bC,))

    DUMP("d_kt", [128, 8, 2048], BF16, lambda: A_kt[:, :, :], (bA,))
    DUMP("d_v", [128, 16, 528], BF16, lambda: C_v[:, :, :], (bC,))
    g_load_ch(P, Wa, dqw.ap().rearrange("(k p) n -> p k n", p=128), "bWa", 4)
    bAO = P.buf("C_ao")
    for hp in range(8):
        for w in range(4):
            wc = slice(w * 512, (w + 1) * 512)
            grp = []
            for hh in range(2):
                r0 = hh * 64
                for cc in range(8):
                    grp.append(lambda t, cc=cc, hp=hp, hh=hh, r0=r0, wc=wc: t.matmul(
                        ps[0][r0:r0 + 32, 0:512],
                        Wa[:, cc, hp * 64 + hh * 32:hp * 64 + hh * 32 + 32],
                        Bt[:, cc, wc], start=(cc == 0), stop=(cc == 7),
                        tile_position=(0, r0)))
            P.emit("T", grp, reads=(bWa, bBt, psB[0]), writes=(psB[0],))
            bqt = P.buf("s_qt")
            for r0 in (0, 64):
                P.emit("V", lambda v, hp=hp, r0=r0: v.tensor_scalar(
                    s_qt[r0:r0 + 32, 0:512], ps[0][r0:r0 + 32, 0:512],
                    K_qbT[r0:r0 + 32, hp:hp + 1], None, ALU.add),
                    reads=(psB[0], kbufs["B_qbT"]), writes=(bqt,))
            ROT2 = ((4, 5), (0, 1))
            def emit_av(sck, hp=hp, w=w):
                for hh in range(2):
                    h = hp * 2 + hh
                    pc = 2 + hh
                    be = P.buf(("exp", hh * 2 + sck % 2))
                    P.emit("T", lambda t, sck=sck, pc=pc, h=h, hh=hh: t.matmul(
                        ps[pc][0:33, 0:512], C_v[:, sck, h * 33:(h + 1) * 33],
                        s_exp[hh * 2 + sck % 2][:, 0:512],
                        start=(sck == 0), stop=(sck == 15)),
                        reads=(bC, be, psB[pc]), writes=(psB[pc],))
            for sck in range(16):
                pr = ROT2[sck % 2]
                P.emit("T", [
                    (lambda t, sck=sck, hp=hp, pr=pr, hh=hh, r0=r0: t.matmul(
                        ps[pr[hh]][:, 0:512],
                        A_kt[r0:r0 + 32, hp, sck * 128:(sck + 1) * 128],
                        s_qt[r0:r0 + 32, 0:512], start=True, stop=True,
                        tile_position=(r0, 0)))
                    for hh, r0 in ((0, 0), (1, 64))],
                    reads=(bA, P.buf("s_qt"), psB[pr[0]], psB[pr[1]]),
                    writes=(psB[pr[0]], psB[pr[1]]))
                for hh in range(2):
                    be = P.buf(("exp", hh * 2 + sck % 2))
                    P.emit("S", lambda sc, sck=sck, pe=pr[hh], hh=hh: sc.activation(
                        s_exp[hh * 2 + sck % 2][:, 0:512], ps[pe][:, 0:512], AF.Exp,
                        bias=0.0, scale=K_sc[:, 2:3]),
                        reads=(psB[pr[hh]], kbufs["SCAL"]), writes=(be,))
                if sck >= 1:
                    emit_av(sck - 1)
            emit_av(15)
            for hh in range(2):
                h = hp * 2 + hh
                r0 = hh * 64
                pc = 2 + hh
                bpc = psB[pc]
                brec = P.buf("s_rec")
                P.emit("S", lambda sc, pc=pc: sc.activation(
                    s_rec[32:33, 512:1024], ps[pc][32:33, 0:512], AF.Ln, bias=0.0),
                    reads=(bpc,), writes=(brec,))
                P.emit("S", lambda sc: sc.activation(
                    s_rec[32:33, 0:512], s_rec[32:33, 512:1024], AF.Exp,
                    bias=0.0, scale=K_sc[32:33, 8:9]),
                    reads=(brec, kbufs["SCAL"]), writes=(brec,))
                P.emit("T", lambda t: t.matmul(
                    ps[5][0:32, 0:512], K_on[32:33, 0:32], s_rec[32:33, 0:512],
                    start=True, stop=True),
                    reads=(brec, kbufs["ONEBF"], psB[5]), writes=(psB[5],))
                P.emit("V", lambda v: v.tensor_copy(s_rec[0:32, 0:512],
                                                    ps[5][0:32, 0:512]),
                    reads=(psB[5],), writes=(brec,))
                P.emit("V", lambda v, h=h, wc=wc, pc=pc: v.tensor_mul(
                    C_ao[(h % 4) * 32:(h % 4) * 32 + 32, h // 4, wc],
                    ps[pc][0:32, 0:512], s_rec[0:32, 0:512]),
                    reads=(bpc, brec), writes=(bAO,))

    g_load_ch(P, Wo2, dow.ap().rearrange("(k p) n -> p k n", p=128), "bWb", 2)
    bX = P.buf("s_X"); bmT = P.buf("s_mT"); brr = P.buf("s_rr")
    bdump = P.buf("s_dump")
    P.alias(bmT, P.buf(("exp", 1)), P.buf(("exp", 2)), P.buf(("exp", 3)))
    for j in range(NT):
        for half in range(2):
            pi = half
            P.emit("T", [
                (lambda t, cc=cc, j=j, half=half, pi=pi: t.matmul(
                    ps[pi][:, 0:512], C_ao[:, cc, j * 128:(j + 1) * 128],
                    Wo2[:, cc, half * 512:(half + 1) * 512],
                    start=(cc == 0), stop=(cc == 3)))
                for cc in range(4)],
                reads=(bWb, bAO, psB[pi]), writes=(psB[pi],))
            P.emit("V", lambda v, half=half, pi=pi: v.tensor_add(
                s_X[:, half * 512:(half + 1) * 512], ps[pi][:, 0:512],
                K_bo[:, half * 512:(half + 1) * 512]),
                reads=(psB[pi], kbufs["B_obR"]), writes=(bX,))
        P.emit("S", lambda sc: sc.copy(s_dump[:, 0:1024], s_X[:, 0:1024]),
               reads=(bX,), writes=(bdump,))
        for cc in range(8):
            pbi = cc % 2
            P.emit("T", lambda t, cc=cc, pbi=pbi: t.transpose(
                pb[pbi][:, 0:128], s_dump[:, cc * 128:(cc + 1) * 128], K_id[:, :]),
                reads=(bdump, kbufs["IDN"], pbB[pbi]), writes=(pbB[pbi],))
            P.emit("V", lambda v, j=j, cc=cc, pbi=pbi: v.tensor_copy(
                Bt[:, cc, j * 128:(j + 1) * 128], pb[pbi][:, 0:128]),
                reads=(pbB[pbi],), writes=(bBt,))
        for cc in range(8):
            pi = 2 + cc % 2
            P.emit("T", lambda t, cc=cc, pi=pi: t.transpose(
                ps[pi][:, 0:128], s_X[:, cc * 128:(cc + 1) * 128], K_idf[:, :]),
                reads=(bX, kbufs["IDF"], psB[pi]), writes=(psB[pi],))
            P.emit("V", lambda v, cc=cc, pi=pi: v.tensor_copy(
                s_mT[:, cc * 128:(cc + 1) * 128], ps[pi][:, 0:128]),
                reads=(psB[pi],), writes=(bmT,))
        P.emit("T", [
            (lambda t, cc=cc: t.matmul(
                ps[4][:, 0:8], s_mT[:, cc * 128:(cc + 1) * 128],
                K_rw[:, cc, :], start=(cc == 0), stop=(cc == 7)))
            for cc in range(8)],
            reads=(bmT, kbufs["B_rw"], psB[4]), writes=(psB[4],))
        P.emit("V", lambda v: v.tensor_add(s_rr[:, 0:8], ps[4][:, 0:8], K_rb[:, 0:8]),
               reads=(psB[4], kbufs["B_rbR"]), writes=(brr,))
        P.emit("S", lambda sc: sc.activation(s_small[:, 16:24], s_rr[:, 0:8], AF.Exp),
               reads=(brr,), writes=(bsm,))
        P.emit("V", lambda v: v.tensor_reduce(s_small[:, 0:1], s_small[:, 16:24],
                                              AX.X, ALU.add), reads=(bsm,), writes=(bsm,))
        P.emit("V", lambda v: v.reciprocal(s_small[:, 1:2], s_small[:, 0:1]),
               reads=(bsm,), writes=(bsm,))
        P.emit("V", lambda v: v.tensor_scalar(s_small[:, 16:24], s_small[:, 16:24],
                                              s_small[:, 1:2], None, ALU.mult),
               reads=(bsm,), writes=(bsm,))
        P.emit("V", lambda v: v.tensor_reduce(s_small[:, 2:3], s_small[:, 16:24],
                                              AX.X, ALU.max), reads=(bsm,), writes=(bsm,))
        P.emit("V", lambda v: v.tensor_single_scalar(s_small[:, 24:32], s_small[:, 16:24],
                                                     s_small[:, 2:3], ALU.subtract),
               reads=(bsm,), writes=(bsm,))
        P.emit("S", lambda sc: sc.activation(s_small[:, 32:40], s_small[:, 24:32],
                                             AF.Exp, bias=0.0, scale=K_sc[:, 5:6]),
               reads=(bsm, kbufs["SCAL"]), writes=(bsm,))
        P.emit("V", lambda v: v.scalar_tensor_tensor(
            s_small[:, 40:48], s_small[:, 32:40], -1e30, s_small[:, 16:24],
            ALU.mult, ALU.add), reads=(bsm,), writes=(bsm,))
        P.emit("V", lambda v: v.tensor_reduce(s_small[:, 3:4], s_small[:, 40:48],
                                              AX.X, ALU.max), reads=(bsm,), writes=(bsm,))
        P.emit("V", lambda v: v.tensor_single_scalar(s_small[:, 24:32], s_small[:, 16:24],
                                                     s_small[:, 3:4], ALU.subtract),
               reads=(bsm,), writes=(bsm,))
        P.emit("V", lambda v: v.tensor_scalar_min(s_small[:, 24:32], s_small[:, 24:32],
                                                  0.0), reads=(bsm,), writes=(bsm,))
        P.emit("S", lambda sc: sc.activation(s_small[:, 32:40], s_small[:, 24:32],
                                             AF.Exp, bias=0.0, scale=K_sc[:, 5:6]),
               reads=(bsm, kbufs["SCAL"]), writes=(bsm,))
        P.emit("V", lambda v: v.tensor_mul(s_small[:, 40:48], s_small[:, 16:24],
                                           s_small[:, 32:40]), reads=(bsm,), writes=(bsm,))
        P.emit("V", lambda v: v.tensor_reduce(s_small[:, 4:5], s_small[:, 40:48],
                                              AX.X, ALU.add), reads=(bsm,), writes=(bsm,))
        P.emit("V", lambda v: v.tensor_scalar_add(s_small[:, 4:5], s_small[:, 4:5],
                                                  1e-8), reads=(bsm,), writes=(bsm,))
        P.emit("V", lambda v: v.reciprocal(s_small[:, 5:6], s_small[:, 4:5]),
               reads=(bsm,), writes=(bsm,))
        P.emit("V", lambda v, j=j: v.tensor_scalar(
            K_c3[:, j, :], s_small[:, 40:48], s_small[:, 5:6], K_sc[:, 4:5],
            ALU.mult, ALU.mult), reads=(bsm, kbufs["SCAL"]), writes=(bc3,))
        P.emit("V", lambda v, j=j: v.tensor_copy(s_cb[:, 0:8], K_c3[:, j, :]),
               reads=(bc3,), writes=(P.buf("s_cb"),))
        P.emit("T", lambda t: t.transpose(pb[0][0:8, 0:128], s_cb[:, 0:8], K_id[:, :]),
               reads=(P.buf("s_cb"), kbufs["IDN"], pbB[0]), writes=(pbB[0],))
        P.emit("V", lambda v, j=j: v.tensor_copy(K_cT[0:8, j * 128:(j + 1) * 128],
                                                 pb[0][0:8, 0:128]),
               reads=(pbB[0],), writes=(bcT,))

    DUMP("d_ao", [128, 4, 2048], BF16, lambda: C_ao[:, :, :], (bAO,))
    DUMP("d_mlaT", [128, 8, 2048], BF16, lambda: Bt[:, :, :], (bBt,))
    DUMP("d_c3", [128, 16, 8], F32, lambda: K_c3[:, :, :], (bc3,))
    import os as _os
    NEXP = int(_os.environ.get("KDBG_NEXP", "8"))
    NOB2 = bool(_os.environ.get("KDBG_NOB2"))
    NFC = int(_os.environ.get("KDBG_NFC", "16"))
    P.alias(P.buf("A_w1"), bA)
    P.alias(P.buf("C_w2"), bC, P.buf("C_ao"))
    bw1 = P.buf("A_w1"); bw2 = P.buf("C_w2")
    for e in range(NEXP):
        g_load_ch(P, A_w1, dw1.ap()[e].rearrange("(k p) n -> p k n", p=128), "A_w1", 8)
        g_load_ch(P, C_w2, dw2.ap()[e].rearrange("(k p) n -> p k n", p=128), "C_w2", 8)
        for tw in range(8):
            tc = slice(tw * 256, (tw + 1) * 256)
            def emit_h1(fc, tc=tc, e=e):
                pi = fc % 2
                bh = P.buf(("h1", fc % 2))
                P.emit("T", [
                    (lambda t, cc=cc, fc=fc, tc=tc, pi=pi: t.matmul(
                        ps[pi][:, 0:256], A_w1[:, cc, fc * 128:(fc + 1) * 128],
                        Bt[:, cc, tc], start=(cc == 0), stop=(cc == 7)))
                    for cc in range(8)],
                    reads=(bw1, bBt, psB[pi]), writes=(psB[pi],))
                P.emit("S", lambda sc, fc=fc, pi=pi, e=e: sc.activation(
                    s_h1[fc % 2][:, 0:256], ps[pi][:, 0:256], AF.Gelu,
                    bias=K_b1T[:, e, fc:fc + 1]),
                    reads=(psB[pi], kbufs["B_b1T"]), writes=(bh,))
            emit_h1(0)
            for fc in range(NFC):
                if fc + 1 < NFC:
                    emit_h1(fc + 1)
                bh = P.buf(("h1", fc % 2))
                for tt in range(2):
                    for dg in range(2):
                        pj = 2 + tt * 2 + dg
                        P.emit("T", lambda t, fc=fc, tt=tt, dg=dg, pj=pj: t.matmul(
                            ps[pj][:, 0:512], s_h1[fc % 2][:, tt * 128:(tt + 1) * 128],
                            C_w2[:, fc, dg * 512:(dg + 1) * 512],
                            start=(fc == 0), stop=(fc == NFC - 1)),
                            reads=(bh, bw2, psB[pj]), writes=(psB[pj],))
            for tt in range(2):
                j = tw * 2 + tt
                for dg in range(2):
                    pj = 2 + tt * 2 + dg
                    P.emit("V", lambda v, j=j, dg=dg, pj=pj, e=e: v.tensor_scalar(
                        s_X[:, dg * 512:(dg + 1) * 512], ps[pj][:, 0:512],
                        K_c3[:, j, e:e + 1], None, ALU.mult),
                        reads=(psB[pj], bc3), writes=(bX, psB[pj]))
                    P.emit("V", lambda v, j=j, dg=dg: v.tensor_add(
                        R[:, j, dg * 512:(dg + 1) * 512],
                        R[:, j, dg * 512:(dg + 1) * 512],
                        s_X[:, dg * 512:(dg + 1) * 512]),
                        reads=(bX, bR[j]), writes=(bR[j],))
    for j in (range(0) if NOB2 else range(NT)):
        for dg in range(2):
            pi = dg % 2
            P.emit("T", lambda t, j=j, dg=dg, pi=pi: t.matmul(
                ps[pi][:, 0:512], K_cT[0:8, j * 128:(j + 1) * 128],
                K_b2[0:8, dg * 512:(dg + 1) * 512], start=True, stop=True),
                reads=(bcT, kbufs["B_b2"], psB[pi]), writes=(psB[pi],))
            P.emit("V", lambda v, j=j, dg=dg, pi=pi: v.tensor_add(
                R[:, j, dg * 512:(dg + 1) * 512],
                R[:, j, dg * 512:(dg + 1) * 512], ps[pi][:, 0:512]),
                reads=(psB[pi], bR[j]), writes=(bR[j], psB[pi]))
    if dbg:
        dR = P.nc.dram_tensor("d_R", [T, CD], F32, kind="ExternalOutput")
        drr2 = dR.ap().rearrange("(j p) c -> p j c", p=128)
        for j in range(NT):
            P.emit("Y", lambda y, j=j: y.dma_start(drr2[:, j, :], R[:, j, :]),
                   reads=(bR[j],), writes=(), dma=True)
    bdump2 = P.buf("s_dump")
    for j in range(NT):
        P.emit("V", lambda v, j=j: v.scalar_tensor_tensor(
            s_dump[:, 0:1024], R[:, j, :], 0.0, R[:, j, :], ALU.add, ALU.mult,
            accum_out=s_small[:, 2:3]), reads=(bR[j],), writes=(bdump2, bsm))
        P.emit("V", lambda v, j=j: v.tensor_reduce(
            s_small[:, 0:1], R[:, j, :], AX.X, ALU.add), reads=(bR[j],), writes=(bsm,))
        P.emit("V", lambda v: v.memset(s_small[:, 1:2], 0.0), reads=(bsm,), writes=(bsm,))
        emit_ln_tiles(P, R, bR, j, None, None, s_small, bsm, K_sc, kbufs["SCAL"],
                      R[:, j, :], (bR[j],))
    emit_xT_from_R(P, R, bR, Bt, bBt, K_id, kbufs["IDN"], pb, pbB, s_dump, bdump2)
    return R, Bt, bR, bBt, (K_id, kbufs, pb, pbB, ps, psB, K_sc, dbg_outs)


def emit_stage_C(P, x_in=None, dbg=False):
    nc = P.nc
    def DUMP(name, shape, dtype, ap_fn, rbufs):
        if not dbg:
            return
        d = P.nc.dram_tensor(name, shape, dtype, kind="ExternalOutput")
        P.emit("Y", lambda y: y.dma_start(d.ap(), ap_fn()), reads=rbufs,
               writes=(), dma=True)
    R = P.sb("R", [128, NT, 1024], F32, OFF_R)
    Bt = P.sb("Bt", [128, 8, 2048], BF16, OFF_B)
    B_hnT = P.sb("B_hnT", [128, 4, 2048], BF16, OFF_B)
    A_q = P.sb("A_q", [128, 16, 512], BF16, OFF_A)
    A_k = P.sb("A_k", [128, 16, 512], BF16, OFF_A + 16384)
    C_v = P.sb("C_v", [128, 16, 520], BF16, OFF_C)
    C_og = P.sb("C_og", [128, 16, 512], BF16, OFF_C + 16640)
    Wt = P.sb("Wt", [128, 8, 512], BF16, OFF_W)
    Wt2 = P.sb("Wt2", [128, 8, 512], BF16, OFF_W + 8192)
    Wout = P.sb("Wout", [128, 4, 1024], BF16, OFF_W + 8192)
    s_g = P.sb("s_g", [128, 16, 16], BF16, OFF_S)
    s_eq = P.sb("s_eq", [128, 8], F32, OFF_S + 512)
    s_ek = P.sb("s_ek", [128, 8], F32, OFF_S + 544)
    s_dd = P.sb("s_dd", [128, 8], F32, OFF_S + 576)
    s_F = P.sb("s_F", [128, 8], F32, OFF_S + 608)
    s_qp = P.sb("s_qp", [128, 512], BF16, OFF_S + 640)
    s_kp = P.sb("s_kp", [128, 512], BF16, OFF_S + 1664)
    s_qpT = P.sb("s_qpT", [128, 4, 128], BF16, OFF_S + 2688)
    s_kpT = P.sb("s_kpT", [128, 4, 128], BF16, OFF_S + 3712)
    s_AT = [P.sb(f"s_AT{i}", [128, 128], BF16, OFF_S + 4736 + 256 * i) for i in range(2)]
    stf = [P.sb(f"stf{i}", [128, 4, 65], F32, OFF_S + 5248 + 1056 * i) for i in range(2)]
    stb = [P.sb(f"stb{i}", [128, 4, 65], BF16, OFF_S + 7360 + 544 * i) for i in range(2)]
    s_dn = P.sb("s_dn", [128, 8], F32, OFF_S + 8448)
    s_X = P.sb("s_X", [128, 1024], F32, OFF_S + 8480)
    s_dump = P.sb("s_dump", [128, 1024], BF16, OFF_S + 12576)
    s_small = P.sb("s_small", [128, 64], F32, OFF_S + 14624)
    s_hn = P.sb("s_hn", [128, 512], F32, OFF_S + 640)
    K_id, K_idf, K_ut, K_on, K_onf, K_sc, ps, pb, psB, pbB, kbufs0 = setup_common(P)
    KZ = OFF_K + 2048
    K_wif = P.sb("K_wif", [128, 8, 16], BF16, KZ)
    K_bif = P.sb("K_bif", [128, 16], F32, KZ + 512)
    K_bo = P.sb("K_bo", [128, 512], BF16, KZ + 576)
    K_lng = P.sb("K_lng", [128, 512], BF16, KZ + 1600)
    K_lnb = P.sb("K_lnb", [128, 512], BF16, KZ + 2624)
    kbufs = dict(kbufs0)
    kbufs.update(load_zone(P, [
        ("C_WifT", K_wif, [1024, 16], "(k p) n -> p k n"), ("C_bifR", K_bif, [128, 16]),
        ("C_boR", K_bo, [128, 512]),
        ("C_lngR", K_lng, [128, 512]), ("C_lnbR", K_lnb, [128, 512]),
    ]))
    dq = P.din("C_WqT", [1024, 512], F32)
    dk = P.din("C_WkT", [1024, 512], F32)
    dv = P.din("C_WvT", [1024, 512], F32)
    do = P.din("C_WoT", [1024, 512], F32)
    dwo = P.din("C_WoutT", [512, 1024], F32)
    bR = [P.buf(("R", j)) for j in range(NT)]
    bBt = P.buf("bBt"); bA = P.buf("bA"); bC = P.buf("bC")
    bW = P.buf("bW"); bW2 = P.buf("bW2")
    bsm = P.buf("s_small")
    if x_in is not None:
        _load_R_f32(P, x_in, R, bR)
        emit_xT_from_R(P, R, bR, Bt, bBt, K_id, kbufs["IDN"], pb, pbB,
                       s_dump, P.buf("s_dump"))

    g_load_ch(P, Wt, dq.ap().rearrange("(k p) n -> p k n", p=128), "bW", 4)
    g_load_ch(P, Wt2, dk.ap().rearrange("(k p) n -> p k n", p=128), "bW2", 4)
    Cv4 = C_v[:, :, :].rearrange("p t (h e) -> p t h e", e=65)
    P.emit("V", lambda v: v.memset(Cv4[:, :, :, 64:65], 1.0), reads=(bC,), writes=(bC,))
    for j in range(NT):
        P.emit("T", [
            (lambda t, cc=cc, j=j: t.matmul(
                ps[0][:, 0:512], Bt[:, cc, j * 128:(j + 1) * 128],
                Wt[:, cc, :], start=(cc == 0), stop=(cc == 7)))
            for cc in range(8)],
            reads=(bW, bBt, psB[0]), writes=(psB[0],))
        P.emit("S", lambda sc, j=j: sc.copy(A_q[:, j, :], ps[0][:, 0:512]),
               reads=(psB[0],), writes=(bA,))
        P.emit("T", [
            (lambda t, cc=cc, j=j: t.matmul(
                ps[1][:, 0:512], Bt[:, cc, j * 128:(j + 1) * 128],
                Wt2[:, cc, :], start=(cc == 0), stop=(cc == 7)))
            for cc in range(8)],
            reads=(bW2, bBt, psB[1]), writes=(psB[1],))
        P.emit("S", lambda sc, j=j: sc.activation(A_k[:, j, :], ps[1][:, 0:512],
                                                  AF.Copy, bias=0.0, scale=K_sc[:, 0:1]),
               reads=(psB[1], kbufs["SCAL"]), writes=(bA,))
    g_load_ch(P, Wt, dv.ap().rearrange("(k p) n -> p k n", p=128), "bW", 4)
    g_load_ch(P, Wt2, do.ap().rearrange("(k p) n -> p k n", p=128), "bW2", 4)
    bXc = P.buf("s_X")
    for j in range(NT):
        P.emit("T", [
            (lambda t, cc=cc, j=j: t.matmul(
                ps[0][:, 0:512], Bt[:, cc, j * 128:(j + 1) * 128],
                Wt[:, cc, :], start=(cc == 0), stop=(cc == 7)))
            for cc in range(8)],
            reads=(bW, bBt, psB[0]), writes=(psB[0],))
        P.emit("V", lambda v, j=j: v.tensor_copy(
            Cv4[:, j, :, 0:64], ps[0][:, 0:512].rearrange("p (h e) -> p h e", e=64)),
            reads=(psB[0],), writes=(bC,))
        P.emit("T", [
            (lambda t, cc=cc, j=j: t.matmul(
                ps[1][:, 0:512], Bt[:, cc, j * 128:(j + 1) * 128],
                Wt2[:, cc, :], start=(cc == 0), stop=(cc == 7)))
            for cc in range(8)],
            reads=(bW2, bBt, psB[1]), writes=(psB[1],))
        P.emit("V", lambda v, j=j: v.tensor_add(s_X[:, 0:512], ps[1][:, 0:512],
                                                K_bo[:, :]),
               reads=(psB[1], kbufs["C_boR"]), writes=(bXc,))
        P.emit("S", lambda sc, j=j: sc.activation(C_og[:, j, :], s_X[:, 0:512],
                                                  AF.Sigmoid),
               reads=(bXc,), writes=(bC,))
        P.emit("T", [
            (lambda t, cc=cc, j=j: t.matmul(
                ps[2][:, 0:16], Bt[:, cc, j * 128:(j + 1) * 128],
                K_wif[:, cc, :], start=(cc == 0), stop=(cc == 7)))
            for cc in range(8)],
            reads=(kbufs["C_WifT"], bBt, psB[2]), writes=(psB[2],))
        P.emit("V", lambda v, j=j: v.tensor_add(s_g[:, j, :], ps[2][:, 0:16],
                                                K_bif[:, :]),
               reads=(psB[2], kbufs["C_bifR"]), writes=(P.buf("s_g"),))

    DUMP("d_q", [128, 16, 512], BF16, lambda: A_q[:, :, :], (bA,))
    DUMP("d_k", [128, 16, 512], BF16, lambda: A_k[:, :, :], (bA,))
    DUMP("d_og", [128, 16, 512], BF16, lambda: C_og[:, :, :], (bC,))
    DUMP("d_g", [128, 16, 16], BF16, lambda: s_g[:, :, :], (P.buf("s_g"),))
    bstf = [[P.buf(("stf", par, h)) for h in range(8)] for par in range(2)]
    bstb = [[P.buf(("stb", par, h)) for h in range(8)] for par in range(2)]
    P.emit("V", lambda v: v.memset(stf[0][:, :, :], 0.0),
           reads=(), writes=tuple(bstf[0]))
    P.emit("V", lambda v: v.memset(stb[0][:, :, :], 0.0),
           reads=(), writes=tuple(bstb[0]))
    bg = P.buf("s_g"); beq = P.buf("s_eq"); bek = P.buf("s_ek")
    bdd = P.buf("s_dd"); bF = P.buf("s_F")
    bqp = P.buf("s_qp"); bkp = P.buf("s_kp")
    bqpT = P.buf("s_qpT"); bkpT = P.buf("s_kpT")
    for m in range(NT):
        par = m % 2
        P.emit("T", lambda t, m=m: t.matmul(
            ps[0][:, 0:8], K_ut[:, :], s_g[:, m, 8:16], start=True, stop=True),
            reads=(kbufs["UTRI"], bg, psB[0]), writes=(psB[0],))
        P.emit("S", lambda sc: sc.activation(s_eq[:, 0:8], ps[0][:, 0:8], AF.Exp),
               reads=(psB[0],), writes=(beq,))
        P.emit("V", lambda v, m=m: v.tensor_sub(s_dd[:, 0:8], s_g[:, m, 0:8],
                                                ps[0][:, 0:8]),
               reads=(bg, psB[0]), writes=(bdd, psB[0]))
        P.emit("S", lambda sc: sc.activation(s_ek[:, 0:8], s_dd[:, 0:8], AF.Exp),
               reads=(bdd,), writes=(bek,))
        P.emit("T", lambda t, m=m: t.matmul(ps[0][0:1, 8:16], K_on[:, 0:1],
                                            s_g[:, m, 8:16], start=True, stop=True),
               reads=(kbufs["ONEBF"], bg, psB[0]), writes=(psB[0],))
        P.emit("S", lambda sc: sc.activation(s_F[0:1, 0:8], ps[0][0:1, 8:16], AF.Exp),
               reads=(psB[0],), writes=(bF,))
        P.emit("T", lambda t: t.matmul(ps[0][:, 8:16], K_onf[0:1, 0:128],
                                       s_F[0:1, 0:8], start=True, stop=True),
               reads=(bF, kbufs["ONEF"], psB[0]), writes=(psB[0],))
        P.emit("V", lambda v: v.tensor_copy(s_F[:, 0:8], ps[0][:, 8:16]),
               reads=(psB[0],), writes=(bF,))
        def prep_head(h, m=m):
            hp = h % 2
            r0 = hp * 64
            P.emit("V", lambda v, m=m, h=h: v.tensor_scalar(
                s_qp[:, h * 64:(h + 1) * 64], A_q[:, m, h * 64:(h + 1) * 64],
                s_eq[:, h:h + 1], None, ALU.mult),
                reads=(bA, beq), writes=(bqp,))
            P.emit("V", lambda v, m=m, h=h: v.tensor_scalar(
                s_kp[:, h * 64:(h + 1) * 64], A_k[:, m, h * 64:(h + 1) * 64],
                s_ek[:, h:h + 1], None, ALU.mult),
                reads=(bA, bek), writes=(bkp,))
            P.emit("T", lambda t, h=h: t.transpose(
                pb[0][0:64, 0:128], s_qp[:, h * 64:(h + 1) * 64], K_id[:, :]),
                reads=(bqp, kbufs["IDN"], pbB[0]), writes=(pbB[0],))
            P.emit("V", lambda v, h=h, r0=r0: v.tensor_copy(
                s_qpT[r0:r0 + 64, h // 2, :], pb[0][0:64, 0:128]),
                reads=(pbB[0],), writes=(bqpT,))
            P.emit("T", lambda t, h=h: t.transpose(
                pb[1][0:64, 0:128], s_kp[:, h * 64:(h + 1) * 64], K_id[:, :]),
                reads=(bkp, kbufs["IDN"], pbB[1]), writes=(pbB[1],))
            P.emit("V", lambda v, h=h, r0=r0: v.tensor_copy(
                s_kpT[r0:r0 + 64, h // 2, :], pb[1][0:64, 0:128]),
                reads=(pbB[1],), writes=(bkpT,))
        def work_head(h, m=m, par=par):
            hp = h % 2
            r0 = hp * 64
            pe = 1 + (h % 2)
            P.emit("T", lambda t, h=h, r0=r0, pe=pe: t.matmul(
                ps[pe][:, 0:128], s_kpT[r0:r0 + 64, h // 2, :],
                s_qpT[r0:r0 + 64, h // 2, :], start=True, stop=True),
                reads=(bkpT, bqpT, psB[pe]), writes=(psB[pe],))
            bat = P.buf(("AT", h % 2))
            P.emit("V", lambda v, h=h, pe=pe: v.tensor_mul(
                s_AT[h % 2][:, :], ps[pe][:, 0:128], K_ut[:, :]),
                reads=(psB[pe], kbufs["UTRI"]), writes=(bat,))
            ph = 3 + (h % 2)
            P.emit("T", lambda t, h=h, m=m: t.matmul(
                ps[3 + (h % 2)][:, 0:65], s_AT[h % 2][:, :],
                C_v[:, m, h * 65:(h + 1) * 65], start=True, stop=False),
                reads=(bat, bC, psB[ph]), writes=(psB[ph],))
            P.emit("T", lambda t, h=h, r0=r0, par=par: t.matmul(
                ps[3 + (h % 2)][:, 0:65], s_qpT[r0:r0 + 64, h // 2, :],
                stb[par][r0:r0 + 64, h // 2, :], start=False, stop=True),
                reads=(bqpT, bstb[par][h], psB[ph]), writes=(psB[ph],))
            P.emit("T", lambda t, h=h, m=m: t.matmul(
                ps[5][0:64, 0:65], s_kp[:, h * 64:(h + 1) * 64],
                C_v[:, m, h * 65:(h + 1) * 65], start=True, stop=True),
                reads=(bkp, bC, psB[5]), writes=(psB[5],))
            P.emit("S", lambda sc, h=h, ph=ph: sc.activation(
                s_dn[:, h % 4:h % 4 + 1], ps[ph][:, 64:65], AF.Abs),
                reads=(psB[ph],), writes=(P.buf("s_dn"),))
            P.emit("V", lambda v, h=h: v.tensor_scalar_max(
                s_dn[:, h % 4:h % 4 + 1], s_dn[:, h % 4:h % 4 + 1], 1.0),
                reads=(P.buf("s_dn"),), writes=(P.buf("s_dn"),))
            P.emit("V", lambda v, h=h: v.reciprocal(
                s_dn[:, 4 + h % 4:5 + h % 4], s_dn[:, h % 4:h % 4 + 1]),
                reads=(P.buf("s_dn"),), writes=(P.buf("s_dn"),))
            P.emit("V", lambda v, h=h, m=m, ph=ph: v.scalar_tensor_tensor(
                C_v[:, m, h * 64:(h + 1) * 64], ps[ph][:, 0:64],
                s_dn[:, 4 + h % 4:5 + h % 4], C_og[:, m, h * 64:(h + 1) * 64],
                ALU.mult, ALU.mult),
                reads=(psB[ph], P.buf("s_dn"), bC), writes=(bC, psB[ph]))
            P.emit("V", lambda v, h=h, r0=r0, par=par: v.tensor_add(
                stf[1 - par][r0:r0 + 64, h // 2, :], stf[par][r0:r0 + 64, h // 2, :],
                ps[5][0:64, 0:65]),
                reads=(bstf[par][h], psB[5]), writes=(bstf[1 - par][h], psB[5]))
            P.emit("V", lambda v, h=h, r0=r0, par=par: v.tensor_scalar(
                stf[1 - par][r0:r0 + 64, h // 2, :], stf[1 - par][r0:r0 + 64, h // 2, :],
                s_F[r0:r0 + 64, h:h + 1], None, ALU.mult),
                reads=(bstf[1 - par][h], bF), writes=(bstf[1 - par][h],))
            P.emit("V", lambda v, h=h, r0=r0, par=par: v.tensor_copy(
                stb[1 - par][r0:r0 + 64, h // 2, :],
                stf[1 - par][r0:r0 + 64, h // 2, :]),
                reads=(bstf[1 - par][h],), writes=(bstb[1 - par][h],))
        prep_head(0)
        for h in range(8):
            if h + 1 < 8:
                prep_head(h + 1)
            work_head(h)

    DUMP("d_hs", [128, 16, 520], BF16, lambda: C_v[:, :, :], (bC,))
    g_load_ch(P, Wout, dwo.ap().rearrange("(k p) n -> p k n", p=128), "bW2", 2)
    bhn = P.buf("s_hn"); bdump = P.buf("s_dump")
    P.alias(bhn, P.buf("s_qp"), P.buf("s_kp"))
    for m in range(NT):
        P.emit("V", lambda v, m=m: v.scalar_tensor_tensor(
            s_dump[:, 0:512], C_v[:, m, 0:512], 0.0, C_v[:, m, 0:512],
            ALU.add, ALU.mult, accum_out=s_small[:, 2:3]),
            reads=(bC,), writes=(bdump, bsm))
        P.emit("V", lambda v, m=m: v.tensor_reduce(
            s_small[:, 0:1], C_v[:, m, 0:512], AX.X, ALU.add),
            reads=(bC,), writes=(bsm,))
        P.emit("V", lambda v: v.tensor_scalar_mul(s_small[:, 4:5], s_small[:, 0:1],
                                                  1.0 / 512.0), reads=(bsm,), writes=(bsm,))
        P.emit("V", lambda v: v.tensor_scalar(s_small[:, 5:6], s_small[:, 4:5],
                                              s_small[:, 4:5], -1e-5,
                                              ALU.mult, ALU.add), reads=(bsm,), writes=(bsm,))
        P.emit("V", lambda v: v.scalar_tensor_tensor(
            s_small[:, 6:7], s_small[:, 2:3], 1.0 / 512.0, s_small[:, 5:6],
            ALU.mult, ALU.subtract), reads=(bsm,), writes=(bsm,))
        P.emit("S", lambda sc: sc.activation(s_small[:, 7:8], s_small[:, 6:7],
                                             AF.Ln, bias=0.0), reads=(bsm,), writes=(bsm,))
        P.emit("S", lambda sc: sc.activation(s_small[:, 8:9], s_small[:, 7:8],
                                             AF.Exp, bias=0.0, scale=K_sc[:, 1:2]),
               reads=(bsm, kbufs["SCAL"]), writes=(bsm,))
        P.emit("V", lambda v: v.scalar_tensor_tensor(
            s_small[:, 9:10], s_small[:, 4:5], -1.0, s_small[:, 8:9],
            ALU.mult, ALU.mult), reads=(bsm,), writes=(bsm,))
        P.emit("V", lambda v, m=m: v.tensor_scalar(
            s_hn[:, 0:512], C_v[:, m, 0:512], s_small[:, 8:9], s_small[:, 9:10],
            ALU.mult, ALU.add), reads=(bC, bsm), writes=(bhn,))
        P.emit("V", lambda v, m=m: v.tensor_mul(s_hn[:, 0:512], s_hn[:, 0:512],
                                                K_lng[:, :]),
               reads=(bhn, kbufs["C_lngR"]), writes=(bhn,))
        P.emit("V", lambda v, m=m: v.tensor_add(s_dump[:, 0:512], s_hn[:, 0:512],
                                                K_lnb[:, :]),
               reads=(bhn, kbufs["C_lnbR"]), writes=(bdump,))
        for cc in range(4):
            pbi = cc % 2
            P.emit("T", lambda t, cc=cc, pbi=pbi: t.transpose(
                pb[pbi][:, 0:128], s_dump[:, cc * 128:(cc + 1) * 128], K_id[:, :]),
                reads=(bdump, kbufs["IDN"], pbB[pbi]), writes=(pbB[pbi],))
            P.emit("V", lambda v, m=m, cc=cc, pbi=pbi: v.tensor_copy(
                B_hnT[:, cc, m * 128:(m + 1) * 128], pb[pbi][:, 0:128]),
                reads=(pbB[pbi],), writes=(bBt,))
    bX = P.buf("s_X")
    for j in range(NT):
        for half in range(2):
            pi = half
            P.emit("T", [
                (lambda t, cc=cc, j=j, half=half, pi=pi: t.matmul(
                    ps[pi][:, 0:512], B_hnT[:, cc, j * 128:(j + 1) * 128],
                    Wout[:, cc, half * 512:(half + 1) * 512],
                    start=(cc == 0), stop=(cc == 3)))
                for cc in range(4)],
                reads=(bW2, bBt, psB[pi]), writes=(psB[pi],))
            P.emit("V", lambda v, j=j, half=half, pi=pi: v.scalar_tensor_tensor(
                s_X[:, half * 512:(half + 1) * 512], ps[pi][:, 0:512], 0.0,
                R[:, j, half * 512:(half + 1) * 512], ALU.add, ALU.add,
                accum_out=s_small[:, half:half + 1]),
                reads=(psB[pi], bR[j]), writes=(bX, bsm))
        P.emit("V", lambda v: v.scalar_tensor_tensor(
            s_dump[:, 0:1024], s_X[:, 0:1024], 0.0, s_X[:, 0:1024], ALU.add,
            ALU.mult, accum_out=s_small[:, 2:3]),
            reads=(bX,), writes=(bdump, bsm))
        emit_ln_tiles(P, R, bR, j, None, None, s_small, bsm, K_sc, kbufs["SCAL"],
                      s_X[:, 0:1024], (bX,))
    return R, bR


def emit_stage_D(P, out_dram, x_in=None, dbg=False):
    nc = P.nc
    def DUMP(name, shape, dtype, ap_fn, rbufs):
        if not dbg:
            return
        d = P.nc.dram_tensor(name, shape, dtype, kind="ExternalOutput")
        P.emit("Y", lambda y: y.dma_start(d.ap(), ap_fn()), reads=rbufs,
               writes=(), dma=True)
    R = P.sb("R", [128, NT, 1024], F32, OFF_R)
    A_xnT = P.sb("A_xnT", [128, 8, 2048], BF16, OFF_A)
    B_kk = P.sb("B_kk", [128, 8, 2048], BF16, OFF_B)
    C_v = P.sb("C_v", [128, 16, 1040], BF16, OFF_C)
    C_mla = P.sb("C_mla", [128, 16, 1024], BF16, OFF_C)
    C_wd = P.sb("C_wd", [128, 8, 1024], BF16, OFF_C)
    W_kvd = P.sb("W_kvd", [128, 8, 256], BF16, OFF_W)
    W_qd = P.sb("W_qd", [128, 8, 256], BF16, OFF_W + 4096)
    W_kvT = P.sb("W_kvT", [128, 2, 2048], BF16, OFF_W + 8192)
    W_ku = P.sb("W_ku", [128, 2, 512], BF16, OFF_W)
    W_qu = P.sb("W_qu", [128, 2, 512], BF16, OFF_W + 2048)
    W_vu = P.sb("W_vu", [128, 2, 1024], BF16, OFF_W + 4096)
    W_rq = P.sb("W_rq", [128, 2, 512], BF16, OFF_W + 4096)
    W_wo = P.sb("W_wo", [128, 8, 1024], BF16, OFF_W)
    s_y0 = P.sb("s_y0", [128, 2048], BF16, OFF_S)
    s_qt = P.sb("s_qt", [128, 512], BF16, OFF_S + 4096)
    s_rec = P.sb("s_rec", [128, 1024], BF16, OFF_S + 5120)
    s_exp = [P.sb(f"s_exp{i}", [128, 512], BF16, OFF_S + 6144 + i * 1024) for i in range(2)] + \
            [P.sb(f"s_exp{2 + i}", [128, 512], BF16, OFF_W + 8192 + i * 1024) for i in range(2)]
    S_qlT = P.sb("S_qlT", [128, 2, 2048], BF16, OFF_S + 8192)
    s_rkw = [P.sb(f"s_rkw{i}", [128, 8, 64], BF16, OFF_S + 4096 + i * 1024) for i in range(2)]
    s_rkwp = [P.sb(f"s_rkwp{i}", [128, 8, 64], BF16, OFF_S + 6144 + i * 1024) for i in range(2)]
    s_X = P.sb("s_X", [128, 1024], F32, OFF_S)
    s_mT = P.sb("s_mT", [128, 1024], F32, OFF_S + 4096)
    s_dump = P.sb("s_dump", [128, 1024], BF16, OFF_S + 8192)
    K_id, K_idf, K_ut, K_on, K_onf, K_sc, ps, pb, psB, pbB, kbufs0 = setup_common(P)
    KZ = OFF_K + 2048
    K_cos = P.sb("K_cosD", [128, 2048], BF16, KZ)
    K_sin = P.sb("K_sinD", [128, 2048], BF16, KZ + 4096)
    K_wln1 = P.sb("K_wln1", [128, 1024], BF16, KZ + 8192)
    K_wln2 = P.sb("K_wln2", [128, 1024], BF16, KZ + 10240)
    K_rw = P.sb("K_rw7", [128, 8, 7], F32, KZ + 12288)
    K_rb = P.sb("K_rb7", [128, 7], F32, KZ + 12512)
    s_t1 = P.sb("s_t1D", [128, 512], BF16, KZ + 12576)
    s_t1k = P.sb("s_t1k", [128, 512], BF16, OFF_S)
    W_rqp = P.sb("W_rqp", [128, 2, 512], BF16, OFF_W + 6144)
    s_small = P.sb("s_small", [128, 64], F32, OFF_S + 10240)
    s_wts = P.sb("s_wts", [128, 16, 8], F32, OFF_S + 10496)
    s_hm = [P.sb(f"s_hm{i}", [128, 256], BF16, OFF_S + 11008 + i * 512) for i in range(2)]
    s_sg = P.sb("s_sg", [128, 512], BF16, OFF_S + 12032)
    kbufs = dict(kbufs0)
    kbufs.update(load_zone(P, [
        ("COSD", K_cos, [128, 2048]), ("SIND", K_sin, [128, 2048]),
        ("D_wln1R", K_wln1, [128, 1024]), ("D_wln2R", K_wln2, [128, 1024]),
        ("D_rw", K_rw, [1024, 7], "(k p) n -> p k n"), ("D_rbR", K_rb, [128, 7]),
    ], native=("COSD", "SIND")))
    dkvd = P.din("D_kvd", [1024, 256], F32)
    dqd = P.din("D_qd", [1024, 256], F32)
    dku = P.din("D_ku", [256, 512], F32)
    dqu = P.din("D_qu", [256, 512], F32)
    dvu = P.din("D_vu", [256, 1024], F32)
    drq = P.din("D_rq", [256, 512], F32)
    drqp = P.din("D_rqp", [256, 512], F32)
    drk = P.din("D_rk", [1024, 512], F32)
    drkp = P.din("D_rkp", [1024, 512], F32)
    dwo = P.din("D_wo", [1024, 1024], F32)
    dgs = P.din("D_gs", [1024, 1024], F32)
    dus = P.din("D_us", [1024, 1024], F32)
    dds = P.din("D_ds", [1024, 1024], F32)
    dgr = P.din("D_gr", [7, 1024, 1024], F32)
    dur = P.din("D_ur", [7, 1024, 1024], F32)
    ddr = P.din("D_dr", [7, 1024, 1024], F32)
    bR = [P.buf(("R", j)) for j in range(NT)]
    bA = P.buf("bA"); bB = P.buf("bB"); bC = P.buf("bC")
    bsm = P.buf("s_small")
    bdump = P.buf("s_dump")
    if x_in is not None:
        _load_R_f32(P, x_in, R, bR)

    for j in range(NT):
        P.emit("V", lambda v, j=j: v.scalar_tensor_tensor(
            s_dump[:, 0:1024], R[:, j, :], 0.0, R[:, j, :], ALU.add, ALU.mult,
            accum_out=s_small[:, 2:3]), reads=(bR[j],), writes=(bdump, bsm))
        P.emit("S", lambda sc: sc.activation(s_small[:, 7:8], s_small[:, 2:3],
                                             AF.Ln, bias=0.0), reads=(bsm,), writes=(bsm,))
        P.emit("S", lambda sc: sc.activation(s_small[:, 8:9], s_small[:, 7:8],
                                             AF.Exp, bias=K_sc[:, 7:8], scale=K_sc[:, 6:7]),
               reads=(bsm, kbufs["SCAL"]), writes=(bsm,))
        P.emit("V", lambda v: v.tensor_scalar_add(s_small[:, 9:10], s_small[:, 8:9],
                                                  1e-5), reads=(bsm,), writes=(bsm,))
        P.emit("V", lambda v: v.reciprocal(s_small[:, 10:11], s_small[:, 9:10]),
               reads=(bsm,), writes=(bsm,))
        P.emit("V", lambda v, j=j: v.scalar_tensor_tensor(
            s_dump[:, 0:1024], R[:, j, :], s_small[:, 10:11], K_wln1[:, :],
            ALU.mult, ALU.mult), reads=(bR[j], bsm, kbufs["D_wln1R"]), writes=(bdump,))
        for cc in range(8):
            pbi = cc % 2
            P.emit("T", lambda t, cc=cc, pbi=pbi: t.transpose(
                pb[pbi][:, 0:128], s_dump[:, cc * 128:(cc + 1) * 128], K_id[:, :]),
                reads=(bdump, kbufs["IDN"], pbB[pbi]), writes=(pbB[pbi],))
            P.emit("V", lambda v, j=j, cc=cc, pbi=pbi: v.tensor_copy(
                A_xnT[:, cc, j * 128:(j + 1) * 128], pb[pbi][:, 0:128]),
                reads=(pbB[pbi],), writes=(bA,))

    g_load_ch(P, W_kvd, dkvd.ap().rearrange("(k p) n -> p k n", p=128), "bkvd")
    g_load_ch(P, W_qd, dqd.ap().rearrange("(k p) n -> p k n", p=128), "bqd")
    bkvT = P.buf("W_kvT"); bqlT = P.buf("S_qlT")
    P.alias(bqlT, bdump, bsm)
    for lc in range(2):
        for w in range(4):
            wc = slice(w * 512, (w + 1) * 512)
            pi = (lc * 4 + w) % 2
            P.emit("T", [
                (lambda t, cc=cc, lc=lc, wc=wc, pi=pi: t.matmul(
                    ps[pi][:, 0:512], W_kvd[:, cc, lc * 128:(lc + 1) * 128],
                    A_xnT[:, cc, wc], start=(cc == 0), stop=(cc == 7)))
                for cc in range(8)],
                reads=(P.buf("bkvd"), bA, psB[pi]), writes=(psB[pi],))
            P.emit("S", lambda sc, lc=lc, wc=wc, pi=pi: sc.copy(
                W_kvT[:, lc, wc], ps[pi][:, 0:512]), reads=(psB[pi],), writes=(bkvT,))
            P.emit("T", [
                (lambda t, cc=cc, lc=lc, wc=wc, pi=pi: t.matmul(
                    ps[2 + pi][:, 0:512], W_qd[:, cc, lc * 128:(lc + 1) * 128],
                    A_xnT[:, cc, wc], start=(cc == 0), stop=(cc == 7)))
                for cc in range(8)],
                reads=(P.buf("bqd"), bA, psB[2 + pi]), writes=(psB[2 + pi],))
            P.emit("S", lambda sc, lc=lc, wc=wc, pi=pi: sc.copy(
                S_qlT[:, lc, wc], ps[2 + pi][:, 0:512]), reads=(psB[2 + pi],), writes=(bqlT,))

    P.alias(P.buf("W_ku"), P.buf("bkvd"))
    P.alias(P.buf("W_qu"), P.buf("bkvd"))
    P.alias(P.buf("W_vu"), P.buf("bqd"))
    P.alias(P.buf("W_rq"), P.buf("bqd"))
    g_load(P, W_ku[:, :, :], dku.ap().rearrange("(k p) n -> p k n", p=128), "W_ku")
    g_load(P, W_qu[:, :, :], dqu.ap().rearrange("(k p) n -> p k n", p=128), "W_qu")
    g_load(P, W_vu[:, :, :], dvu.ap().rearrange("(k p) n -> p k n", p=128), "W_vu")
    Cv4 = C_v[:, :, :].rearrange("p t (h e) -> p t h e", e=65)
    P.emit("V", lambda v: v.memset(Cv4[:, :, :, 64:65], 1.0), reads=(bC,), writes=(bC,))
    bt1 = P.buf("s_t1D"); bt1k = P.buf("s_t1k")
    for j in range(NT):
        for half in range(2):
            pi = 2 + half
            P.emit("T", [
                (lambda t, lc=lc, j=j, half=half, pi=pi: t.matmul(
                    ps[pi][:, 0:512], W_kvT[:, lc, j * 128:(j + 1) * 128],
                    W_vu[:, lc, half * 512:(half + 1) * 512],
                    start=(lc == 0), stop=(lc == 1)))
                for lc in range(2)],
                reads=(bkvT, P.buf("W_vu"), psB[pi]), writes=(psB[pi],))
            P.emit("V", lambda v, j=j, half=half, pi=pi: v.tensor_copy(
                Cv4[:, j, half * 8:(half + 1) * 8, 0:64],
                ps[pi][:, 0:512].rearrange("p (h e) -> p h e", e=64)),
                reads=(psB[pi],), writes=(bC,))

    P.alias(P.buf("W_rq"), P.buf("W_vu"))
    P.alias(P.buf("W_rqp"), P.buf("W_vu"))
    g_load(P, W_rq[:, :, :], drq.ap().rearrange("(k p) n -> p k n", p=128), "W_rq")
    g_load(P, W_rqp[:, :, :], drqp.ap().rearrange("(k p) n -> p k n", p=128), "W_rqp")
    for hp in range(8):
        kwb = s_rkw[hp % 2]; kwpb = s_rkwp[hp % 2]
        bkw = P.buf(("rkw", hp % 2)); bkwp = P.buf(("rkwp", hp % 2))
        for c0 in range(0, 8, 2):
            P.emit("G", lambda g, hp=hp, kwb=kwb, c0=c0: g.dma_start(
                kwb[:, c0:c0 + 2, :], drk.ap().rearrange("(k p) n -> p k n", p=128)
                [:, c0:c0 + 2, hp * 64:(hp + 1) * 64]),
                reads=(bkw,), writes=(bkw,), dma=True)
            P.emit("G", lambda g, hp=hp, kwpb=kwpb, c0=c0: g.dma_start(
                kwpb[:, c0:c0 + 2, :], drkp.ap().rearrange("(k p) n -> p k n", p=128)
                [:, c0:c0 + 2, hp * 64:(hp + 1) * 64]),
                reads=(bkwp,), writes=(bkwp,), dma=True)
        for w in range(4):
            wc = slice(w * 512, (w + 1) * 512)
            grp = []
            for hh in range(2):
                h = hp * 2 + hh
                b0 = hh * 64
                grp.append(lambda t, h=h, b0=b0, wc=wc: t.matmul(
                    ps[0][b0:b0 + 32, 0:512], W_ku[:, 0, h * 32:(h + 1) * 32],
                    W_kvT[:, 0, wc], start=True, stop=False,
                    tile_position=(0, b0)))
                grp.append(lambda t, h=h, b0=b0, wc=wc: t.matmul(
                    ps[0][b0:b0 + 32, 0:512], W_ku[:, 1, h * 32:(h + 1) * 32],
                    W_kvT[:, 1, wc], start=False, stop=True,
                    tile_position=(0, b0)))
            for hh in range(2):
                b0 = hh * 64 + 32
                for cc in range(8):
                    grp.append(lambda t, hh=hh, b0=b0, cc=cc, wc=wc, kwb=kwb: t.matmul(
                        ps[0][b0:b0 + 32, 0:512], kwb[:, cc, hh * 32:(hh + 1) * 32],
                        A_xnT[:, cc, wc], start=(cc == 0), stop=(cc == 7),
                        tile_position=(0, b0)))
            P.emit("T", grp, reads=(P.buf("W_ku"), bkvT, bkw, bA, psB[0]),
                   writes=(psB[0],))
            grp2 = []
            for hh in range(2):
                b0 = hh * 64 + 32
                for cc in range(8):
                    grp2.append(lambda t, hh=hh, b0=b0, cc=cc, wc=wc, kwpb=kwpb: t.matmul(
                        ps[1][b0:b0 + 32, 0:512], kwpb[:, cc, hh * 32:(hh + 1) * 32],
                        A_xnT[:, cc, wc], start=(cc == 0), stop=(cc == 7),
                        tile_position=(0, b0)))
            P.emit("T", grp2, reads=(bkwp, bA, psB[1]), writes=(psB[1],))
            for hh in range(2):
                b0 = hh * 64
                P.emit("V", lambda v, hp=hp, b0=b0, wc=wc: v.tensor_copy(
                    B_kk[b0:b0 + 32, hp, wc], ps[0][b0:b0 + 32, 0:512]),
                    reads=(psB[0],), writes=(bB,))
                rb = b0 + 32
                P.emit("V", lambda v, rb=rb, wc=wc: v.tensor_mul(
                    s_t1k[rb:rb + 32, 0:512], ps[0][rb:rb + 32, 0:512],
                    K_cos[rb:rb + 32, wc]),
                    reads=(psB[0], kbufs["COSD"]), writes=(bt1k,))
                P.emit("V", lambda v, hp=hp, rb=rb, wc=wc: v.tensor_mul(
                    B_kk[rb:rb + 32, hp, wc], ps[1][rb:rb + 32, 0:512],
                    K_sin[rb:rb + 32, wc]),
                    reads=(psB[1], kbufs["SIND"]), writes=(bB,))
                P.emit("G", lambda g, hp=hp, rb=rb, wc=wc: g.tensor_add(
                    B_kk[rb:rb + 32, hp, wc], B_kk[rb:rb + 32, hp, wc],
                    s_t1k[rb:rb + 32, 0:512]),
                    reads=(bt1k, bB), writes=(bB,))
    DUMP("d_xnT", [128, 8, 2048], BF16, lambda: A_xnT[:, :, :], (bA,))
    DUMP("d_kk", [128, 8, 2048], BF16, lambda: B_kk[:, :, :], (bB,))
    DUMP("d_v", [128, 16, 1040], BF16, lambda: C_v[:, :, :], (bC,))
    by0 = P.buf("s_y0")
    P.alias(by0, P.buf("s_t1k"))
    P.alias(P.buf("s_qt"), P.buf(("rkw", 0)))
    P.alias(P.buf("s_rec"), P.buf(("rkw", 1)), P.buf(("rkwp", 0)))
    P.alias(P.buf(("exp", 0)), P.buf(("rkwp", 0)), P.buf(("rkwp", 1)))
    P.alias(P.buf(("exp", 1)), P.buf("W_kvT"))
    P.alias(P.buf(("exp", 2)), P.buf("W_kvT"))
    P.alias(P.buf(("exp", 3)), P.buf("W_kvT"))
    for hp in range(8):
        for w in range(4):
            wc = slice(w * 512, (w + 1) * 512)
            for hh in range(2):
                h = hp * 2 + hh
                r0 = hh * 64
                grp = []
                for lc in range(2):
                    grp.append(lambda t, lc=lc, h=h, r0=r0, wc=wc: t.matmul(
                        ps[0][r0:r0 + 32, 0:512], W_qu[:, lc, h * 32:(h + 1) * 32],
                        S_qlT[:, lc, wc], start=(lc == 0), stop=(lc == 1),
                        tile_position=(0, r0)))
                for lc in range(2):
                    grp.append(lambda t, lc=lc, h=h, r0=r0, wc=wc: t.matmul(
                        ps[0][r0 + 32:r0 + 64, 0:512], W_rq[:, lc, h * 32:(h + 1) * 32],
                        S_qlT[:, lc, wc], start=(lc == 0), stop=(lc == 1),
                        tile_position=(0, r0 + 32)))
                P.emit("T", grp, reads=(P.buf("W_qu"), P.buf("W_rq"), bqlT, psB[0]),
                       writes=(psB[0],))
                P.emit("T", [
                    (lambda t, lc=lc, h=h, r0=r0, wc=wc: t.matmul(
                        ps[1][r0 + 32:r0 + 64, 0:512], W_rqp[:, lc, h * 32:(h + 1) * 32],
                        S_qlT[:, lc, wc], start=(lc == 0), stop=(lc == 1),
                        tile_position=(0, r0 + 32)))
                    for lc in range(2)],
                    reads=(P.buf("W_rqp"), bqlT, psB[1]), writes=(psB[1],))
                bqt = P.buf("s_qt")
                P.emit("V", lambda v, r0=r0: v.tensor_copy(
                    s_qt[r0:r0 + 32, 0:512], ps[0][r0:r0 + 32, 0:512]),
                    reads=(psB[0],), writes=(bqt,))
                rb = r0 + 32
                P.emit("V", lambda v, rb=rb, wc=wc: v.tensor_mul(
                    s_t1[rb:rb + 32, 0:512], ps[0][rb:rb + 32, 0:512],
                    K_cos[rb:rb + 32, wc]),
                    reads=(psB[0], kbufs["COSD"]), writes=(bt1,))
                P.emit("V", lambda v, rb=rb, wc=wc: v.tensor_mul(
                    s_qt[rb:rb + 32, 0:512], ps[1][rb:rb + 32, 0:512],
                    K_sin[rb:rb + 32, wc]),
                    reads=(psB[1], kbufs["SIND"]), writes=(bqt,))
                P.emit("V", lambda v, rb=rb: v.tensor_add(
                    s_qt[rb:rb + 32, 0:512], s_qt[rb:rb + 32, 0:512],
                    s_t1[rb:rb + 32, 0:512]),
                    reads=(bt1, bqt), writes=(bqt,))
            nS = 4 * w + 4
            j0s = [max(0, sck * 128 - w * 512) for sck in range(nS)]
            ROT2 = ((4, 5), (2, 3))
            def emit_av(sck, j0, hp=hp, w=w, nS=nS):
                for hh in range(2):
                    h = hp * 2 + hh
                    pc = 6 + hh
                    pc = hh
                    be = P.buf(("exp", hh * 2 + sck % 2))
                    P.emit("T", lambda t, sck=sck, j0=j0, pc=pc, h=h, hh=hh, nS=nS: t.matmul(
                        ps[pc][0:65, j0:512], C_v[:, sck, h * 65:(h + 1) * 65],
                        s_exp[hh * 2 + sck % 2][:, j0:512],
                        start=(sck == 0), stop=(sck == nS - 1)),
                        reads=(bC, be, psB[pc]), writes=(psB[pc],))
            for sck in range(nS):
                j0 = j0s[sck]
                pr = ROT2[sck % 2]
                P.emit("T", [
                    (lambda t, sck=sck, j0=j0, hp=hp, pr=pr, hh=hh, r0=r0: t.matmul(
                        ps[pr[hh]][:, j0:512],
                        B_kk[r0:r0 + 64, hp, sck * 128:(sck + 1) * 128],
                        s_qt[r0:r0 + 64, j0:512], start=True, stop=True,
                        tile_position=(r0, 0)))
                    for hh, r0 in ((0, 0), (1, 64))],
                    reads=(bB, P.buf("s_qt"), psB[pr[0]], psB[pr[1]]),
                    writes=(psB[pr[0]], psB[pr[1]]))
                for hh in range(2):
                    be = P.buf(("exp", hh * 2 + sck % 2))
                    P.emit("S", lambda sc, sck=sck, j0=j0, pe=pr[hh], hh=hh: sc.activation(
                        s_exp[hh * 2 + sck % 2][:, j0:512], ps[pe][:, j0:512], AF.Exp,
                        bias=0.0, scale=K_sc[:, 0:1]),
                        reads=(psB[pr[hh]], kbufs["SCAL"]), writes=(be,))
                    if sck * 128 >= w * 512:
                        P.emit("V", lambda v, sck=sck, j0=j0, hh=hh: v.tensor_mul(
                            s_exp[hh * 2 + sck % 2][:, j0:j0 + 128],
                            s_exp[hh * 2 + sck % 2][:, j0:j0 + 128], K_ut[:, :]),
                            reads=(be, kbufs["UTRI"]), writes=(be,))
                if sck >= 1:
                    emit_av(sck - 1, j0s[sck - 1])
            emit_av(nS - 1, j0s[nS - 1])
            for hh in range(2):
                h = hp * 2 + hh
                r0 = hh * 64
                pc = hh
                bpc = psB[pc]
                brec = P.buf("s_rec")
                P.emit("S", lambda sc, pc=pc: sc.activation(
                    s_rec[64:65, 512:1024], ps[pc][64:65, 0:512], AF.Ln, bias=0.0),
                    reads=(bpc,), writes=(brec,))
                P.emit("S", lambda sc: sc.activation(
                    s_rec[64:65, 0:512], s_rec[64:65, 512:1024], AF.Exp,
                    bias=0.0, scale=K_sc[64:65, 8:9]),
                    reads=(brec, kbufs["SCAL"]), writes=(brec,))
                P.emit("T", lambda t: t.matmul(
                    ps[5][0:64, 0:512], K_on[64:65, 0:64], s_rec[64:65, 0:512],
                    start=True, stop=True),
                    reads=(brec, kbufs["ONEBF"], psB[5]), writes=(psB[5],))
                P.emit("V", lambda v: v.tensor_copy(s_rec[0:64, 0:512],
                                                    ps[5][0:64, 0:512]),
                    reads=(psB[5],), writes=(brec,))
                if hp == 0:
                    dst = lambda wc=wc, r0=r0: s_y0[r0:r0 + 64, wc]
                    dstb = by0
                else:
                    dst = lambda wc=wc, r0=r0, hp=hp: B_kk[r0:r0 + 64, hp - 1, wc]
                    dstb = bB
                P.emit("V", lambda v, dst=dst, pc=pc: v.tensor_mul(
                    dst(), ps[pc][0:64, 0:512], s_rec[0:64, 0:512]),
                    reads=(bpc, brec), writes=(dstb,))

    P.alias(P.buf("W_wo"), P.buf("W_ku"), P.buf("W_qu"), P.buf("W_vu"),
            P.buf("W_rq"), P.buf("bkvd"), P.buf("bqd"), bkvT,
            P.buf(("exp", 2)), P.buf(("exp", 3)))
    g_load_ch(P, W_wo, dwo.ap().rearrange("(k p) n -> p k n", p=128), "W_wo", 4)
    P.alias(P.buf("C_mla"), bC)
    bmla = P.buf("C_mla")
    for j in range(NT):
        for half in range(2):
            pi = half
            def mml(cc, j=j, half=half, pi=pi):
                def f(t):
                    src = s_y0[:, j * 128:(j + 1) * 128] if cc == 0 else \
                        B_kk[:, cc - 1, j * 128:(j + 1) * 128]
                    return t.matmul(ps[pi][:, 0:512], src,
                                    W_wo[:, cc, half * 512:(half + 1) * 512],
                                    start=(cc == 0), stop=(cc == 7))
                return f
            P.emit("T", [mml(cc) for cc in range(8)],
                   reads=(P.buf("W_wo"), bB, by0, psB[pi]), writes=(psB[pi],))
            P.emit("V", lambda v, j=j, half=half, pi=pi: v.scalar_tensor_tensor(
                R[:, j, half * 512:(half + 1) * 512], ps[pi][:, 0:512], 1.0,
                R[:, j, half * 512:(half + 1) * 512], ALU.mult, ALU.add),
                reads=(psB[pi], bR[j]), writes=(bR[j],))
            P.emit("S", lambda sc, j=j, half=half, pi=pi: sc.copy(
                C_mla[:, j, half * 512:(half + 1) * 512], ps[pi][:, 0:512]),
                reads=(psB[pi],), writes=(bmla, psB[pi]))

    if dbg:
        dU = P.nc.dram_tensor("d_u", [T, CD], F32, kind="ExternalOutput")
        dur2 = dU.ap().rearrange("(j p) c -> p j c", p=128)
        for j in range(NT):
            P.emit("Y", lambda y, j=j: y.dma_start(dur2[:, j, :], R[:, j, :]),
                   reads=(bR[j],), writes=(), dma=True)
        DUMP("d_mla", [128, 16, 1024], BF16, lambda: C_mla[:, :, :], (bmla,))
    P.alias(P.buf("s_X2"), by0)
    P.alias(P.buf("s_mT2"), P.buf("s_qt"), P.buf("s_rec"), P.buf(("exp", 0)),
            P.buf(("exp", 1)))
    P.alias(bdump, bqlT)
    P.alias(bsm, bqlT)
    P.alias(P.buf("s_wts"), bqlT)
    P.alias(P.buf(("hm", 0)), bqlT)
    P.alias(P.buf(("hm", 1)), bqlT)
    P.alias(P.buf("s_sg"), bqlT)
    bX = P.buf("s_X2"); bmT = P.buf("s_mT2")
    bwts = P.buf("s_wts")
    for j in range(NT):
        P.emit("V", lambda v, j=j: v.scalar_tensor_tensor(
            s_dump[:, 0:1024], R[:, j, :], 0.0, R[:, j, :], ALU.add, ALU.mult,
            accum_out=s_small[:, 2:3]), reads=(bR[j],), writes=(bdump, bsm))
        P.emit("S", lambda sc: sc.activation(s_small[:, 7:8], s_small[:, 2:3],
                                             AF.Ln, bias=0.0), reads=(bsm,), writes=(bsm,))
        P.emit("S", lambda sc: sc.activation(s_small[:, 8:9], s_small[:, 7:8],
                                             AF.Exp, bias=K_sc[:, 7:8], scale=K_sc[:, 6:7]),
               reads=(bsm, kbufs["SCAL"]), writes=(bsm,))
        P.emit("V", lambda v: v.tensor_scalar_add(s_small[:, 9:10], s_small[:, 8:9],
                                                  1e-5), reads=(bsm,), writes=(bsm,))
        P.emit("V", lambda v: v.reciprocal(s_small[:, 10:11], s_small[:, 9:10]),
               reads=(bsm,), writes=(bsm,))
        P.emit("V", lambda v, j=j: v.scalar_tensor_tensor(
            s_X[:, 0:1024], R[:, j, :], s_small[:, 10:11], K_wln2[:, :],
            ALU.mult, ALU.mult), reads=(bR[j], bsm, kbufs["D_wln2R"]), writes=(bX,))
        P.emit("S", lambda sc: sc.copy(s_dump[:, 0:1024], s_X[:, 0:1024]),
               reads=(bX,), writes=(bdump,))
        for cc in range(8):
            pbi = cc % 2
            P.emit("T", lambda t, cc=cc, pbi=pbi: t.transpose(
                pb[pbi][:, 0:128], s_dump[:, cc * 128:(cc + 1) * 128], K_id[:, :]),
                reads=(bdump, kbufs["IDN"], pbB[pbi]), writes=(pbB[pbi],))
            P.emit("V", lambda v, j=j, cc=cc, pbi=pbi: v.tensor_copy(
                A_xnT[:, cc, j * 128:(j + 1) * 128], pb[pbi][:, 0:128]),
                reads=(pbB[pbi],), writes=(bA,))
        for cc in range(8):
            pi = cc % 2
            P.emit("T", lambda t, cc=cc, pi=pi: t.transpose(
                ps[pi][:, 0:128], s_X[:, cc * 128:(cc + 1) * 128], K_idf[:, :]),
                reads=(bX, kbufs["IDF"], psB[pi]), writes=(psB[pi],))
            P.emit("V", lambda v, cc=cc, pi=pi: v.tensor_copy(
                s_mT[:, cc * 128:(cc + 1) * 128], ps[pi][:, 0:128]),
                reads=(psB[pi],), writes=(bmT,))
        P.emit("T", [
            (lambda t, cc=cc: t.matmul(
                ps[4][:, 0:7], s_mT[:, cc * 128:(cc + 1) * 128],
                K_rw[:, cc, :], start=(cc == 0), stop=(cc == 7)))
            for cc in range(8)],
            reads=(bmT, kbufs["D_rw"], psB[4]), writes=(psB[4],))
        P.emit("V", lambda v: v.tensor_add(s_small[:, 16:23], ps[4][:, 0:7],
                                           K_rb[:, 0:7]),
               reads=(psB[4], kbufs["D_rbR"]), writes=(bsm, psB[4]))
        P.emit("S", lambda sc: sc.activation(s_small[:, 16:23], s_small[:, 16:23],
                                             AF.Sigmoid), reads=(bsm,), writes=(bsm,))
        P.emit("V", lambda v: v.tensor_reduce(s_small[:, 2:3], s_small[:, 16:23],
                                              AX.X, ALU.max), reads=(bsm,), writes=(bsm,))
        P.emit("V", lambda v: v.tensor_single_scalar(s_small[:, 24:31], s_small[:, 16:23],
                                                     s_small[:, 2:3], ALU.subtract),
               reads=(bsm,), writes=(bsm,))
        P.emit("S", lambda sc: sc.activation(s_small[:, 32:39], s_small[:, 24:31],
                                             AF.Exp, bias=0.0, scale=K_sc[:, 5:6]),
               reads=(bsm, kbufs["SCAL"]), writes=(bsm,))
        P.emit("V", lambda v: v.scalar_tensor_tensor(
            s_small[:, 40:47], s_small[:, 32:39], -1e30, s_small[:, 16:23],
            ALU.mult, ALU.add), reads=(bsm,), writes=(bsm,))
        P.emit("V", lambda v: v.tensor_reduce(s_small[:, 3:4], s_small[:, 40:47],
                                              AX.X, ALU.max), reads=(bsm,), writes=(bsm,))
        P.emit("V", lambda v: v.tensor_single_scalar(s_small[:, 24:31], s_small[:, 16:23],
                                                     s_small[:, 3:4], ALU.subtract),
               reads=(bsm,), writes=(bsm,))
        P.emit("V", lambda v: v.tensor_scalar_min(s_small[:, 24:31], s_small[:, 24:31],
                                                  0.0), reads=(bsm,), writes=(bsm,))
        P.emit("S", lambda sc: sc.activation(s_small[:, 32:39], s_small[:, 24:31],
                                             AF.Exp, bias=0.0, scale=K_sc[:, 5:6]),
               reads=(bsm, kbufs["SCAL"]), writes=(bsm,))
        P.emit("V", lambda v: v.tensor_mul(s_small[:, 40:47], s_small[:, 16:23],
                                           s_small[:, 32:39]), reads=(bsm,), writes=(bsm,))
        P.emit("V", lambda v: v.tensor_reduce(s_small[:, 4:5], s_small[:, 40:47],
                                              AX.X, ALU.add), reads=(bsm,), writes=(bsm,))
        P.emit("V", lambda v: v.reciprocal(s_small[:, 5:6], s_small[:, 4:5]),
               reads=(bsm,), writes=(bsm,))
        P.emit("V", lambda v, j=j: v.tensor_scalar(
            s_wts[:, j, 0:7], s_small[:, 40:47], s_small[:, 5:6], None,
            ALU.mult), reads=(bsm,), writes=(bwts,))
        P.emit("V", lambda v, j=j: v.scalar_tensor_tensor(
            R[:, j, :], R[:, j, :], 2.0, C_mla[:, j, :], ALU.mult, ALU.subtract),
            reads=(bR[j], bmla), writes=(bR[j],))

    P.alias(bB, by0)
    P.alias(P.buf("C_wd"), bmla, bC)
    bwd = P.buf("C_wd")
    bsg = P.buf("s_sg")
    for e in range(-1, 7):
        gsrc = dgs.ap() if e < 0 else dgr.ap()[e]
        usrc = dus.ap() if e < 0 else dur.ap()[e]
        dsrc = dds.ap() if e < 0 else ddr.ap()[e]
        for c0 in range(0, 8, 2):
            P.emit("G", lambda g, c0=c0, gsrc=gsrc: g.dma_start(
                B_kk[:, c0:c0 + 2, 0:1024],
                gsrc.rearrange("(k p) n -> p k n", p=128)[:, c0:c0 + 2, :]),
                reads=(), writes=(bB,), dma=True)
            P.emit("G", lambda g, c0=c0, usrc=usrc: g.dma_start(
                B_kk[:, c0:c0 + 2, 1024:2048],
                usrc.rearrange("(k p) n -> p k n", p=128)[:, c0:c0 + 2, :]),
                reads=(), writes=(bB,), dma=True)
            P.emit("G", lambda g, c0=c0, dsrc=dsrc: g.dma_start(
                C_wd[:, c0:c0 + 2, :],
                dsrc.rearrange("(k p) n -> p k n", p=128)[:, c0:c0 + 2, :]),
                reads=(), writes=(bwd,), dma=True)
        for tw in range(8):
            tc = slice(tw * 256, (tw + 1) * 256)
            def emit_gu(ic, tc=tc):
                P.emit("T", [
                    (lambda t, cc=cc, ic=ic, tc=tc: t.matmul(
                        ps[0][:, 0:256],
                        B_kk[:, cc, ic * 128:(ic + 1) * 128],
                        A_xnT[:, cc, tc], start=(cc == 0), stop=(cc == 7)))
                    for cc in range(8)],
                    reads=(bB, bA, psB[0]), writes=(psB[0],))
                P.emit("T", [
                    (lambda t, cc=cc, ic=ic, tc=tc: t.matmul(
                        ps[1][:, 0:256],
                        B_kk[:, cc, 1024 + ic * 128:1024 + (ic + 1) * 128],
                        A_xnT[:, cc, tc], start=(cc == 0), stop=(cc == 7)))
                    for cc in range(8)],
                    reads=(bB, bA, psB[1]), writes=(psB[1],))
                P.emit("S", lambda sc, ic=ic: sc.activation(
                    s_sg[:, 0:256], ps[0][:, 0:256], AF.Silu),
                       reads=(psB[0],), writes=(bsg,))
                bh = P.buf(("hm", ic % 2))
                P.emit("V", lambda v, ic=ic: v.tensor_mul(
                    s_hm[ic % 2][:, 0:256], s_sg[:, 0:256], ps[1][:, 0:256]),
                    reads=(bsg, psB[1]), writes=(bh,))
            emit_gu(0)
            for ic in range(8):
                if ic + 1 < 8:
                    emit_gu(ic + 1)
                bh = P.buf(("hm", ic % 2))
                for tt in range(2):
                    for dg in range(2):
                        pj = 2 + tt * 2 + dg
                        P.emit("T", lambda t, ic=ic, tt=tt, dg=dg, pj=pj: t.matmul(
                            ps[pj][:, 0:512], s_hm[ic % 2][:, tt * 128:(tt + 1) * 128],
                            C_wd[:, ic, dg * 512:(dg + 1) * 512],
                            start=(ic == 0), stop=(ic == 7)),
                            reads=(bh, bwd, psB[pj]), writes=(psB[pj],))
            for tt in range(2):
                j = tw * 2 + tt
                for dg in range(2):
                    pj = 2 + tt * 2 + dg
                    if e < 0:
                        P.emit("V", lambda v, j=j, dg=dg, pj=pj: v.scalar_tensor_tensor(
                            R[:, j, dg * 512:(dg + 1) * 512], ps[pj][:, 0:512], 1.0,
                            R[:, j, dg * 512:(dg + 1) * 512], ALU.mult, ALU.add),
                            reads=(psB[pj], bR[j]), writes=(bR[j], psB[pj]))
                    else:
                        P.emit("V", lambda v, j=j, dg=dg, pj=pj, e=e: v.scalar_tensor_tensor(
                            R[:, j, dg * 512:(dg + 1) * 512], ps[pj][:, 0:512],
                            s_wts[:, j, e:e + 1], R[:, j, dg * 512:(dg + 1) * 512],
                            ALU.mult, ALU.add),
                            reads=(psB[pj], bwts, bR[j]), writes=(bR[j], psB[pj]))
    DUMP("d_wts", [128, 16, 8], F32, lambda: s_wts[:, :, :], (P.buf("s_wts"),))
    if dbg:
        dP = P.nc.dram_tensor("d_preln", [T, CD], F32, kind="ExternalOutput")
        dpr2 = dP.ap().rearrange("(j p) c -> p j c", p=128)
        for j in range(NT):
            P.emit("Y", lambda y, j=j: y.dma_start(dpr2[:, j, :], R[:, j, :]),
                   reads=(bR[j],), writes=(), dma=True)
    orr = out_dram.ap().rearrange("(j p) c -> p j c", p=128)
    for j in range(NT):
        P.emit("V", lambda v, j=j: v.scalar_tensor_tensor(
            s_dump[:, 0:1024], R[:, j, :], 0.0, R[:, j, :], ALU.add, ALU.mult,
            accum_out=s_small[:, 2:3]), reads=(bR[j],), writes=(bdump, bsm))
        P.emit("V", lambda v, j=j: v.tensor_reduce(
            s_small[:, 0:1], R[:, j, :], AX.X, ALU.add), reads=(bR[j],), writes=(bsm,))
        P.emit("V", lambda v: v.memset(s_small[:, 1:2], 0.0), reads=(bsm,), writes=(bsm,))
        emit_ln_tiles(P, R, bR, j, None, None, s_small, bsm, K_sc, kbufs["SCAL"],
                      R[:, j, :], (bR[j],))
        P.emit("Y", lambda y, j=j: y.dma_start(orr[:, j, :], R[:, j, :]),
               reads=(bR[j],), writes=(), dma=True)
    return R, bR


def _in_maps(P, consts, per_core):
    maps = []
    for ci in range(NCORES):
        m = {}
        for k in P.inputs:
            if k in per_core:
                m[k] = per_core[k][ci]
            else:
                m[k] = consts[k]
        maps.append(m)
    return maps

def _load_R_f32(P, x_dram, R, bR):
    xr = x_dram.ap().rearrange("(j p) c -> p j c", p=128)
    for j in range(NT):
        P.emit("Y", lambda y, j=j: y.dma_start(R[:, j, :], xr[:, j, :]),
               reads=(), writes=(bR[j],), dma=True)

def _store_R(P, x_dram, R, bR):
    orr = x_dram.ap().rearrange("(j p) c -> p j c", p=128)
    for j in range(NT):
        P.emit("Y", lambda y, j=j: y.dma_start(orr[:, j, :], R[:, j, :]),
               reads=(bR[j],), writes=(), dma=True)

def run_stage_A(x_slices, consts):
    P = Prog("stageA")
    x_in = P.din("x_in", [T, CD], F16)
    x_out = P.nc.dram_tensor("x_out", [T, CD], F32, kind="ExternalOutput")
    R, Bt, bR, bB = emit_stage_A(P, x_in)
    _store_R(P, x_out, R, bR)
    P.final_wait()
    nc = P.build()
    res = run_bass_kernel_spmd(nc, _in_maps(P, consts, {"x_in": x_slices}),
                               core_ids=list(range(NCORES)))
    return [r["x_out"] for r in res.results]

def _prologue_xT(P, x_in):
    R = P.sb("Rpre", [128, NT, 1024], F32, OFF_R)
    Bt = P.sb("Btpre", [128, 8, 2048], BF16, OFF_B)
    bR = [P.buf(("R", j)) for j in range(NT)]
    bB = P.buf("bBt")
    _load_R_f32(P, x_in, R, bR)
    return R, Bt, bR, bB

def run_stage_B(x_full, consts):
    P = Prog("stageB")
    x_in = P.din("x_in", [T, CD], F32)
    x_out = P.nc.dram_tensor("x_out", [T, CD], F32, kind="ExternalOutput")
    import os
    dbg = bool(os.environ.get("KDBG"))
    R, Bt, bR, bBt, extra = emit_stage_B(P, x_in=x_in, dbg=dbg)
    _store_R(P, x_out, R, bR)
    P.final_wait()
    nc = P.build()
    res = run_bass_kernel_spmd(nc, _in_maps(P, consts, {"x_in": x_full}),
                               core_ids=list(range(NCORES)))
    if dbg:
        import pickle
        with open("/tmp/dbgB.pkl", "wb") as f:
            pickle.dump({k: np.asarray(v) for k, v in res.results[0].items()}, f)
    return [r["x_out"] for r in res.results]

def run_stage_C(x_full, consts):
    P = Prog("stageC")
    x_in = P.din("x_in", [T, CD], F32)
    x_out = P.nc.dram_tensor("x_out", [T, CD], F32, kind="ExternalOutput")
    import os
    dbg = bool(os.environ.get("KDBG"))
    R, bR = emit_stage_C(P, x_in=x_in, dbg=dbg)
    _store_R(P, x_out, R, bR)
    P.final_wait()
    nc = P.build()
    res = run_bass_kernel_spmd(nc, _in_maps(P, consts, {"x_in": x_full}),
                               core_ids=list(range(NCORES)))
    if dbg:
        import pickle
        with open("/tmp/dbgC.pkl", "wb") as f:
            pickle.dump({k: np.asarray(v) for k, v in res.results[0].items()}, f)
    return [r["x_out"] for r in res.results]

def run_stage_D(x_full, consts):
    P = Prog("stageD")
    x_in = P.din("x_in", [T, CD], F32)
    x_out = P.nc.dram_tensor("x_out", [T, CD], F32, kind="ExternalOutput")
    import os
    dbg = bool(os.environ.get("KDBG"))
    emit_stage_D(P, x_out, x_in=x_in, dbg=dbg)
    P.final_wait()
    nc = P.build()
    res = run_bass_kernel_spmd(nc, _in_maps(P, consts, {"x_in": x_full}),
                               core_ids=list(range(NCORES)))
    if dbg:
        import pickle
        with open("/tmp/dbgD.pkl", "wb") as f:
            pickle.dump({k: np.asarray(v) for k, v in res.results[0].items()}, f)
    return [r["x_out"] for r in res.results]

def run_fused(x_slices, consts):
    P = Prog("mk")
    x_in = P.din("x_in", [T, CD], F16)
    out = P.nc.dram_tensor("out", [T, CD], F32, kind="ExternalOutput")
    emit_stage_A(P, x_in)
    P.fence()
    emit_stage_B(P)
    P.fence()
    emit_stage_C(P)
    P.fence()
    emit_stage_D(P, out)
    P.final_wait()
    nc = P.build()
    res = run_bass_kernel_spmd(nc, _in_maps(P, consts, {"x_in": x_slices}),
                               core_ids=list(range(NCORES)))
    return [r["out"] for r in res.results]

def kernel(**inputs):
    consts = mk_consts(inputs)
    x = np.ascontiguousarray(inputs["L0_attn_x"])
    x_slices = [np.ascontiguousarray(x[i]) for i in range(NCORES)]
    if FUSED:
        return np.stack(run_fused(x_slices, consts), axis=0)
    x1 = run_stage_A(x_slices, consts)
    x2 = run_stage_B(x1, consts)
    x3 = run_stage_C(x2, consts)
    x4 = run_stage_D(x3, consts)
    return np.stack(x4, axis=0)


# version 18 (final)
# speedup vs baseline: 1.0435x; 1.0236x over previous
import contextlib
import math
import numpy as np

import concourse.bass as bass
import concourse.mybir as mybir
from concourse.bass_utils import run_bass_kernel_spmd

AF = mybir.ActivationFunctionType
ALU = mybir.AluOpType
AX = mybir.AxisListType
F32, BF16, F16 = mybir.dt.float32, mybir.dt.bfloat16, mybir.dt.float16

NCORES = 8
T = 2048
NT = 16
CD = 1024
FUSED = True
STAGES_ALL = ("A", "B", "C", "D")


def _rope_tables_l0():
    pos = np.arange(T, dtype=np.float64)
    inv = np.exp(np.arange(0, 64, 2, dtype=np.float64) * (-np.log(10000.0) / 64.0))
    ang = pos[:, None] * inv[None, :]
    sin = np.sin(ang).astype(np.float16).astype(np.float32)
    cos = np.cos(ang).astype(np.float16).astype(np.float32)
    d = np.arange(128)
    cosA = cos[:, d % 32].T.copy()
    sgn = np.where((d % 64) < 32, -1.0, 1.0).astype(np.float32)
    sinA = (sin[:, d % 32] * sgn[None, :]).T.copy()
    return cosA, sinA

def _rope_tables_l1():
    HH = 32
    inv = (1.0 / 10000.0 ** (np.arange(0, HH, 2, dtype=np.float64) / HH))
    pos = np.arange(T, dtype=np.float64)
    freqs = pos[:, None] * inv[None, :]
    emb = np.concatenate([freqs, freqs], axis=-1)
    cosE = np.cos(emb).astype(np.float32)
    sinE = np.sin(emb).astype(np.float32)
    cosD = np.zeros((128, T), np.float32)
    sinD = np.zeros((128, T), np.float32)
    d = np.arange(32)
    sgn = np.where(d < 16, -1.0, 1.0).astype(np.float32)
    for base in (32, 96):
        cosD[base:base + 32] = cosE.T
        sinD[base:base + 32] = (sinE * sgn[None, :]).T
    return cosD, sinD

def _perm64(n_heads):
    p = np.arange(n_heads * 64)
    return (p // 64) * 64 + (p % 64 + 32) % 64

def _perm32_block(ncols):
    p = np.arange(ncols)
    return (p // 32) * 32 + (p % 32 + 16) % 32

def _rep(v):
    return np.ascontiguousarray(np.broadcast_to(np.asarray(v, np.float32)[None, :], (128, len(v))))

def _colT(v, nchunk):
    return np.ascontiguousarray(np.asarray(v, np.float32).reshape(nchunk, 128).T)

def mk_consts(ins):
    import ml_dtypes
    bfd = ml_dtypes.bfloat16
    c = {}
    c["IDN"] = np.eye(128, dtype=bfd)
    c["IDF"] = np.eye(128, dtype=np.float32)
    c["UTRI"] = np.triu(np.ones((128, 128), bfd))
    c["ONEBF"] = np.ones((128, 128), bfd)
    c["ONEF"] = np.ones((128, 128), np.float32)
    c["SCAL"] = _rep(np.array([0.125, -0.5, 1.0 / math.sqrt(32.0), 1.0, 0.3, 1e38, 0.5, -0.5 * math.log(1024.0), -1.0, 0, 0, 0, 0, 0, 0, 0], np.float32))
    cosA, sinA = _rope_tables_l0()
    c["COSA"], c["SINA"] = cosA.astype(bfd), sinA.astype(bfd)
    cosD, sinD = _rope_tables_l1()
    c["COSD"], c["SIND"] = cosD.astype(bfd), sinD.astype(bfd)
    p64 = _perm64(16)
    c["A_Wq"] = np.ascontiguousarray(ins["L0_attn_Wq"].astype(np.float32))
    c["A_Wqp"] = np.ascontiguousarray(c["A_Wq"][:, p64])
    c["A_Wk"] = np.ascontiguousarray(ins["L0_attn_Wk"].astype(np.float32))
    c["A_Wkp"] = np.ascontiguousarray(c["A_Wk"][:, p64])
    c["A_Wv"] = np.ascontiguousarray(ins["L0_attn_Wv"].astype(np.float32))
    c["A_Wo"] = np.ascontiguousarray(ins["L0_attn_Wo"].astype(np.float32))
    bq = ins["L0_attn_bq"].astype(np.float32)
    bk = ins["L0_attn_bk"].astype(np.float32)
    c["A_bqT"] = _colT(bq, 8)
    c["A_bqpT"] = _colT(bq[p64], 8)
    c["A_bkT"] = _colT(bk, 8)
    c["A_bkpT"] = _colT(bk[p64], 8)
    c["A_bvR"] = _rep(ins["L0_attn_bv"].astype(np.float32))
    c["A_boR"] = _rep(ins["L0_attn_bo"].astype(np.float32))
    qsel = (np.arange(512) // 32) * 64 + (np.arange(512) % 32)
    c["B_qw"] = np.ascontiguousarray(ins["L0_ffn_q_w"][:, qsel])
    def _pairT(v):
        out = np.zeros((128, 8), np.float32)
        v = np.asarray(v, np.float32).reshape(16, 32)
        for hp in range(8):
            out[0:32, hp] = v[2 * hp]
            out[64:96, hp] = v[2 * hp + 1]
        return out
    c["B_qbT"] = _pairT(ins["L0_ffn_q_b"][qsel])
    c["B_kw"] = np.ascontiguousarray(ins["L0_ffn_k_w"])
    c["B_kbT"] = _pairT(ins["L0_ffn_k_b"])
    c["B_vw"] = np.ascontiguousarray(ins["L0_ffn_v_w"])
    c["B_vbR"] = _rep(ins["L0_ffn_v_b"])
    c["B_ow"] = np.ascontiguousarray(ins["L0_ffn_o_w"][:512, :])
    c["B_obR"] = _rep(ins["L0_ffn_o_b"])
    c["B_rw"] = np.ascontiguousarray(ins["L0_ffn_router_w"])
    c["B_rbR"] = _rep(ins["L0_ffn_router_b"])
    c["B_w1"] = ins["L0_ffn_w1"]
    c["B_w2"] = ins["L0_ffn_w2"]
    c["B_b1T"] = np.ascontiguousarray(
        ins["L0_ffn_b1"].reshape(8, 16, 128).transpose(2, 0, 1))
    c["B_b2"] = np.ascontiguousarray(ins["L0_ffn_b2"])
    c["C_WqT"] = np.ascontiguousarray(ins["L1_attn_Wq"].T)
    c["C_WkT"] = np.ascontiguousarray(ins["L1_attn_Wk"].T)
    c["C_WvT"] = np.ascontiguousarray(ins["L1_attn_Wv"].T)
    c["C_WoT"] = np.ascontiguousarray(ins["L1_attn_Wo"].T)
    c["C_WifT"] = np.ascontiguousarray(
        np.concatenate([ins["L1_attn_Wi"], ins["L1_attn_Wf"]], axis=0).T)
    c["C_bifR"] = _rep(np.concatenate([ins["L1_attn_bi"], ins["L1_attn_bf"]]))
    c["C_boR"] = _rep(ins["L1_attn_bo"])
    c["C_WoutT"] = np.ascontiguousarray(ins["L1_attn_W_out"].T)
    c["C_lngR"] = _rep(ins["L1_attn_ln_g"])
    c["C_lnbR"] = _rep(ins["L1_attn_ln_b"])
    c["D_wln1R"] = _rep(ins["L1_ffn_w_ln1"])
    c["D_wln2R"] = _rep(ins["L1_ffn_w_ln2"])
    c["D_kvd"] = np.ascontiguousarray(ins["L1_ffn_W_kvd"])
    c["D_qd"] = np.ascontiguousarray(ins["L1_ffn_W_qd"])
    c["D_ku"] = np.ascontiguousarray(ins["L1_ffn_W_ku"])
    c["D_qu"] = np.ascontiguousarray(ins["L1_ffn_W_qu"])
    c["D_vu"] = np.ascontiguousarray(ins["L1_ffn_W_vu"])
    p32q = _perm32_block(512)
    c["D_rq"] = np.ascontiguousarray(ins["L1_ffn_W_ropeq"])
    c["D_rqp"] = np.ascontiguousarray(ins["L1_ffn_W_ropeq"][:, p32q])
    c["D_rk"] = np.ascontiguousarray(ins["L1_ffn_W_ropek"])
    c["D_rkp"] = np.ascontiguousarray(ins["L1_ffn_W_ropek"][:, p32q])
    c["D_wo"] = np.ascontiguousarray(ins["L1_ffn_W_o"])
    c["D_gs"] = np.ascontiguousarray(ins["L1_ffn_Wg_s"])
    c["D_us"] = np.ascontiguousarray(ins["L1_ffn_Wu_s"])
    c["D_ds"] = np.ascontiguousarray(ins["L1_ffn_Wd_s"])
    c["D_gr"] = ins["L1_ffn_Wg_r"]
    c["D_ur"] = ins["L1_ffn_Wu_r"]
    c["D_dr"] = ins["L1_ffn_Wd_r"]
    c["D_rw"] = np.ascontiguousarray(ins["L1_ffn_W_router"])
    c["D_rbR"] = _rep(ins["L1_ffn_routing_bias"])
    return c


class Buf:
    __slots__ = ("w", "r")
    def __init__(self):
        self.w = None
        self.r = {}

class Prog:
    def __init__(self, name):
        self.nc = bass.Bass("TRN2", target_bir_lowering=False, name=name,
                            dynamic_dma_scratch_size=16384)
        self.q = {e: [] for e in ("T", "V", "S", "G", "Y")}
        self.cnt = {"T": 0, "V": 0, "S": 0, "G": 0}
        self.dcnt = {"DY": 0, "DG": 0}
        self._gfifo = []
        self._gdone = 0
        self.known = {e: {} for e in ("T", "V", "S", "G", "Y")}
        self.bufs = {}
        self.inputs = set()
        base = self.nc.sbuf_base
        self.sb0 = (base + 31) & ~31
        self.sb_hi = self.nc.sbuf_top

    def buf(self, key):
        b = self.bufs.get(key)
        if b is None:
            b = self.bufs[key] = Buf()
        return b

    def emit(self, e, fns, reads=(), writes=(), dma=False, ndesc=256):
        if callable(fns):
            fns = [fns]
        waits = {}
        def need(tok):
            if tok is None:
                return
            k, v = tok
            if k == "T" and e == "T":
                return
            if waits.get(k, 0) < v:
                waits[k] = v
        for b in reads:
            need(b.w)
        for b in writes:
            need(b.w)
            for k, v in b.r.items():
                need((k, v))
        kn = self.known[e]
        wl = []
        for k in sorted(waits):
            v = waits[k]
            if k[0] == "D":
                v = self.dcnt[k]
            if kn.get(k, 0) >= v:
                continue
            kn[k] = v
            wl.append((k, v))
        if dma:
            key = "D" + e
            if e == "G":
                BUDGET = 260
                out = sum(n for _, n in self._gfifo)
                while self._gfifo and out + ndesc > BUDGET:
                    tokv, n = self._gfifo.pop(0)
                    out -= n
                    self._gdone += 1
                    lim = 16 * self._gdone
                    if kn.get(key, 0) < lim:
                        kn[key] = lim
                        wl.append((key, lim))
                self._gfifo.append((self.dcnt[key] + 16, ndesc))
            self.dcnt[key] += 16
            tok = (key, self.dcnt[key])
        else:
            self.cnt[e] += 1
            tok = (e, self.cnt[e])
        self.q[e].append((wl, fns, tok))
        for b in writes:
            b.w = tok
            b.r = {}
        for b in reads:
            if b.r.get(tok[0], 0) < tok[1]:
                b.r[tok[0]] = tok[1]
        return tok

    def alias(self, dst, *srcs):
        for sb in srcs:
            if sb.w is not None:
                k, v = sb.w
                if dst.r.get(k, 0) < v:
                    dst.r[k] = v
            for k, v in sb.r.items():
                if dst.r.get(k, 0) < v:
                    dst.r[k] = v

    def fence(self):
        for e in ("T", "V", "S", "G", "Y"):
            wl = []
            kn = self.known[e]
            for k in ("T", "V", "S", "G"):
                if k == e:
                    continue
                if self.cnt[k] and kn.get(k, 0) < self.cnt[k]:
                    kn[k] = self.cnt[k]
                    wl.append((k, self.cnt[k]))
            for k in ("DY", "DG"):
                if self.dcnt[k] and kn.get(k, 0) < self.dcnt[k]:
                    kn[k] = self.dcnt[k]
                    wl.append((k, self.dcnt[k]))
            if wl:
                self.q[e].append((wl, [], None))

    def final_wait(self):
        wl = []
        for k, vmax in self.dcnt.items():
            if vmax and self.known["Y"].get(k, 0) < vmax:
                wl.append((k, vmax))
        for k in ("T", "V", "S", "G"):
            if self.cnt[k] and self.known["Y"].get(k, 0) < self.cnt[k]:
                wl.append((k, self.cnt[k]))
        self.q["Y"].append((wl, [], None))

    def din(self, name, shape, dtype):
        self.inputs.add(name)
        return self.nc.dram_tensor(name, list(shape), dtype, kind="ExternalInput")

    def sb(self, name, shape, dtype, off):
        byte_off = self.sb0 + off
        nbytes = int(np.prod(shape[1:])) * mybir.dt.size(dtype)
        assert byte_off + nbytes <= self.sb_hi, (name, byte_off + nbytes, self.sb_hi)
        return self.nc.alloc_sbuf_tensor_at(name, list(shape), dtype, offset=byte_off)

    def build(self):
        nc = self.nc
        st = contextlib.ExitStack()
        sems = {}
        for k in ("T", "V", "S", "G", "DY", "DG"):
            sems[k] = st.enter_context(nc.semaphore("sem_" + k))
        blk = st.enter_context(nc.Block())
        def mk(e):
            def body(eng):
                for wl, fns, tok in self.q[e]:
                    for k, v in wl:
                        eng.wait_ge(sems[k], v)
                    for i, fn in enumerate(fns):
                        inst = fn(eng)
                        if tok is not None and i == len(fns) - 1:
                            inst.then_inc(sems[tok[0]], 16 if tok[0][0] == "D" else 1)
            return body
        blk.tensor(mk("T"))
        blk.vector(mk("V"))
        blk.scalar(mk("S"))
        blk.gpsimd(mk("G"))
        blk.sync(mk("Y"))
        st.close()
        return nc

OFF_R = 0
OFF_A = 65536
OFF_B = 98304
OFF_C = 131072
OFF_W = 164352
OFF_S = 180736
OFF_K = 197120


def g_load(P, sb_ap, dram_ap, key):
    b = P.buf(key)
    P.emit("G", lambda g: g.dma_start(sb_ap, dram_ap), reads=(), writes=(b,), dma=True)
    return b

def g_load_ch(P, sb_t, dram_ap3, key, nch=None):
    b = P.buf(key)
    k = sb_t.shape[1]
    step = 2
    for c0 in range(0, k, step):
        c1 = min(k, c0 + step)
        P.emit("G", lambda g, c0=c0, c1=c1: g.dma_start(
            sb_t[:, c0:c1, :], dram_ap3[:, c0:c1, :]), reads=(), writes=(b,), dma=True)
    return b

def setup_common(P):
    if hasattr(P, "_common"):
        return P._common
    nc = P.nc
    K_id = P.sb("K_id", [128, 128], BF16, OFF_K)
    K_idf = P.sb("K_idf", [128, 128], F32, OFF_K + 256)
    K_ut = P.sb("K_ut", [128, 128], BF16, OFF_K + 768)
    K_on = P.sb("K_on", [128, 128], BF16, OFF_K + 1024)
    K_onf = P.sb("K_onf", [128, 128], F32, OFF_K + 1280)
    K_sc = P.sb("K_sc", [128, 16], F32, OFF_K + 1792)
    ps = [nc.alloc_psum_tensor(f"ps{i}", [128, 512], F32) for i in range(6)]
    pb = [nc.alloc_psum_tensor(f"pb{i}", [128, 1024], BF16) for i in range(2)]
    psB = [P.buf(("psum", i)) for i in range(6)]
    pbB = [P.buf(("psumb", i)) for i in range(2)]
    kbufs = {}
    for nm, t, shp, dt_ in [("IDN", K_id, [128, 128], BF16),
                            ("IDF", K_idf, [128, 128], F32),
                            ("UTRI", K_ut, [128, 128], BF16),
                            ("ONEBF", K_on, [128, 128], BF16),
                            ("ONEF", K_onf, [128, 128], F32),
                            ("SCAL", K_sc, [128, 16], F32)]:
        d = P.din(nm, shp, dt_)
        b = P.buf(("k", nm))
        P.emit("Y", lambda y, t=t, d=d: y.dma_start(t[:, :], d.ap()),
               reads=(), writes=(b,), dma=True)
        kbufs[nm] = b
    P._common = (K_id, K_idf, K_ut, K_on, K_onf, K_sc, ps, pb, psB, pbB, kbufs)
    return P._common

def load_zone(P, cons, native=()):
    zb = P.buf("KZONE")
    out = {}
    for ent in cons:
        nm, t, shp = ent[0], ent[1], ent[2]
        spec = ent[3] if len(ent) > 3 else None
        b = P.buf(("k", nm))
        P.alias(b, zb)
        if nm in native:
            d = P.din(nm, shp, t.dtype)
            full = t[tuple(slice(None) for _ in t.shape)]
            P.emit("Y", lambda y, full=full, d=d: y.dma_start(full, d.ap()),
                   reads=(), writes=(b, zb), dma=True)
            out[nm] = b
            continue
        d = P.din(nm, shp, F32)
        src = d.ap() if spec is None else d.ap().rearrange(spec, p=128)
        if len(t.shape) == 3 and t.shape[1] > 2:
            for c0 in range(0, t.shape[1], 2):
                c1 = min(t.shape[1], c0 + 2)
                P.emit("G", lambda g, t=t, src=src, c0=c0, c1=c1: g.dma_start(
                    t[:, c0:c1, :], src[:, c0:c1, :]),
                    reads=(), writes=(b, zb), dma=True)
        else:
            full = t[tuple(slice(None) for _ in t.shape)]
            P.emit("G", lambda g, full=full, src=src: g.dma_start(full, src),
                   reads=(), writes=(b, zb), dma=True)
        out[nm] = b
    return out


def emit_stage_A(P, x_dram):
    nc = P.nc
    R = P.sb("R", [128, NT, 1024], F32, OFF_R)
    A_x0T = P.sb("A_x0T", [128, 8, 2048], BF16, OFF_A)
    B_kt = P.sb("B_kt", [128, 8, 2048], BF16, OFF_B)
    C_v = P.sb("C_v", [128, 16, 1040], BF16, OFF_C)
    C_stage = P.sb("C_stage", [128, 16, 1024], BF16, OFF_C)
    W1 = P.sb("W1", [128, 8, 1024], BF16, OFF_W)
    s_qw = [P.sb(f"s_qw{i}", [128, 8, 128], BF16, OFF_S + i * 2048) for i in range(2)]
    s_qwp = [P.sb(f"s_qwp{i}", [128, 8, 128], BF16, OFF_S + 4096 + i * 2048) for i in range(2)]
    s_ctx0 = P.sb("s_ctx0", [128, 2048], BF16, OFF_S + 8192)
    s_qt = P.sb("s_qt", [128, 512], BF16, OFF_S + 12288)
    s_rec = P.sb("s_rec", [128, 1024], BF16, OFF_S + 13312)
    s_exp = [P.sb("s_exp0", [128, 512], BF16, OFF_S + 15360)] + \
            [P.sb(f"s_exp{1 + i}", [128, 512], BF16, OFF_W + i * 1024) for i in range(3)]
    s_X = P.sb("s_X", [128, 1024], F32, OFF_S)
    s_xres = P.sb("s_xres", [128, 1024], F32, OFF_S + 4096)
    s_dump = P.sb("s_dump", [128, 1024], BF16, OFF_S + 12288)
    s_small = P.sb("s_small", [128, 64], F32, OFF_S + 14336)
    K_id, K_idf, K_ut, K_on, K_onf, K_sc, ps, pb, psB, pbB, kbufs0 = setup_common(P)
    KZ = OFF_K + 2048
    K_cos = P.sb("K_cos", [128, 2048], BF16, KZ)
    K_sin = P.sb("K_sin", [128, 2048], BF16, KZ + 4096)
    K_bqT = P.sb("K_bqT", [128, 8], F32, KZ + 8192)
    K_bqpT = P.sb("K_bqpT", [128, 8], F32, KZ + 8224)
    K_bkT = P.sb("K_bkT", [128, 8], F32, KZ + 8256)
    K_bkpT = P.sb("K_bkpT", [128, 8], F32, KZ + 8288)
    K_bv = P.sb("K_bv", [128, 1024], BF16, KZ + 8320)
    K_bo = P.sb("K_bo", [128, 1024], BF16, KZ + 10368)
    s_t1 = P.sb("s_t1", [128, 512], BF16, KZ + 12416)

    dWq = P.din("A_Wq", [1024, 1024], F32)
    dWqp = P.din("A_Wqp", [1024, 1024], F32)
    dWk = P.din("A_Wk", [1024, 1024], F32)
    dWkp = P.din("A_Wkp", [1024, 1024], F32)
    dWv = P.din("A_Wv", [1024, 1024], F32)
    dWo = P.din("A_Wo", [1024, 1024], F32)
    cons = [("COSA", K_cos, [128, 2048]), ("SINA", K_sin, [128, 2048]),
            ("A_bqT", K_bqT, [128, 8]), ("A_bqpT", K_bqpT, [128, 8]),
            ("A_bkT", K_bkT, [128, 8]), ("A_bkpT", K_bkpT, [128, 8]),
            ("A_bvR", K_bv, [128, 1024]), ("A_boR", K_bo, [128, 1024])]
    kbufs = dict(kbufs0)
    kbufs.update(load_zone(P, cons, native=("COSA", "SINA")))
    bW1 = P.buf("bW1"); bW2 = P.buf("bW2")
    bR = [P.buf(("R", j)) for j in range(NT)]
    bA = P.buf("bA"); bB = P.buf("bB"); bC = P.buf("bC")
    bctx0 = P.buf("bctx0")

    xr = x_dram.ap().rearrange("(j p) c -> p j c", p=128)
    g_load_ch(P, C_stage, xr, "bC")
    for j in range(NT):
        for cc in range(8):
            pbi = (j * 8 + cc) % 2
            P.emit("T", lambda t, j=j, cc=cc, pbi=pbi: t.transpose(
                pb[pbi][:, 0:128], C_stage[:, j, cc * 128:(cc + 1) * 128], K_id[:, :]),
                reads=(bC, kbufs["IDN"], pbB[pbi]), writes=(pbB[pbi],))
            P.emit("V", lambda v, j=j, cc=cc, pbi=pbi: v.tensor_copy(
                A_x0T[:, cc, j * 128:(j + 1) * 128], pb[pbi][:, 0:128]),
                reads=(pbB[pbi],), writes=(bA, pbB[pbi]))

    for hp in range(8):
        qwb = s_qw[hp % 2]; qwpb = s_qwp[hp % 2]
        bqw = P.buf(("qw", hp % 2)); bqwp = P.buf(("qwp", hp % 2))
        for c0 in range(0, 8, 2):
            P.emit("G", lambda g, hp=hp, qwb=qwb, c0=c0: g.dma_start(
                qwb[:, c0:c0 + 2, :], dWk.ap().rearrange("(k p) n -> p k n", p=128)
                [:, c0:c0 + 2, hp * 128:(hp + 1) * 128]),
                reads=(bqw,), writes=(bqw,), dma=True)
            P.emit("G", lambda g, hp=hp, qwpb=qwpb, c0=c0: g.dma_start(
                qwpb[:, c0:c0 + 2, :], dWkp.ap().rearrange("(k p) n -> p k n", p=128)
                [:, c0:c0 + 2, hp * 128:(hp + 1) * 128]),
                reads=(bqwp,), writes=(bqwp,), dma=True)
        for w in range(4):
            wc = slice(w * 512, (w + 1) * 512)
            P.emit("T", [
                (lambda t, cc=cc, wc=wc, qwb=qwb: t.matmul(
                    ps[0][:, 0:512], qwb[:, cc, :],
                    A_x0T[:, cc, wc], start=(cc == 0), stop=(cc == 7)))
                for cc in range(8)],
                reads=(bqw, bA, psB[0]), writes=(psB[0],))
            P.emit("T", [
                (lambda t, cc=cc, wc=wc, qwpb=qwpb: t.matmul(
                    ps[1][:, 0:512], qwpb[:, cc, :],
                    A_x0T[:, cc, wc], start=(cc == 0), stop=(cc == 7)))
                for cc in range(8)],
                reads=(bqwp, bA, psB[1]), writes=(psB[1],))
            bt1 = P.buf("s_t1")
            P.emit("V", lambda v, hp=hp, wc=wc: v.scalar_tensor_tensor(
                s_t1[:, 0:512], ps[0][:, 0:512], K_bkT[:, hp:hp + 1], K_cos[:, wc],
                ALU.add, ALU.mult),
                reads=(psB[0], kbufs["A_bkT"], kbufs["COSA"]), writes=(bt1,))
            P.emit("V", lambda v, hp=hp, wc=wc: v.scalar_tensor_tensor(
                B_kt[:, hp, wc], ps[1][:, 0:512], K_bkpT[:, hp:hp + 1], K_sin[:, wc],
                ALU.add, ALU.mult),
                reads=(psB[1], kbufs["A_bkpT"], kbufs["SINA"]), writes=(bB,))
            P.emit("V", lambda v, hp=hp, wc=wc: v.tensor_add(
                B_kt[:, hp, wc], B_kt[:, hp, wc], s_t1[:, 0:512]),
                reads=(bt1, bB), writes=(bB,))
    Cv4 = C_v[:, :, :].rearrange("p t (h e) -> p t h e", e=65)
    P.emit("V", lambda v: v.memset(Cv4[:, :, :, 64:65], 1.0), reads=(bC,), writes=(bC,))
    g_load_ch(P, W1, dWv.ap().rearrange("(k p) n -> p k n", p=128), "bW1", 4)
    for j in range(NT):
        for half in range(2):
            pi = half
            P.emit("T", [
                (lambda t, cc=cc, j=j, half=half, pi=pi: t.matmul(
                    ps[pi][:, 0:512], A_x0T[:, cc, j * 128:(j + 1) * 128],
                    W1[:, cc, half * 512:(half + 1) * 512],
                    start=(cc == 0), stop=(cc == 7)))
                for cc in range(8)],
                reads=(bW1, bA, psB[pi]), writes=(psB[pi],))
            P.emit("V", lambda v, j=j, half=half, pi=pi: v.tensor_add(
                Cv4[:, j, half * 8:(half + 1) * 8, 0:64],
                ps[pi][:, 0:512].rearrange("p (h e) -> p h e", e=64),
                K_bv[:, half * 512:(half + 1) * 512].rearrange("p (h e) -> p h e", e=64)),
                reads=(psB[pi], kbufs["A_bvR"]), writes=(bC,))

    P.alias(P.buf(("exp", 2)), P.buf("bW1"))
    P.alias(P.buf(("exp", 3)), P.buf("bW1"))
    for hp in range(8):
        qwb = s_qw[hp % 2]; qwpb = s_qwp[hp % 2]
        bqw = P.buf(("qw", hp % 2)); bqwp = P.buf(("qwp", hp % 2))
        for c0 in range(0, 8, 2):
            P.emit("G", lambda g, hp=hp, qwb=qwb, c0=c0: g.dma_start(
                qwb[:, c0:c0 + 2, :], dWq.ap().rearrange("(k p) n -> p k n", p=128)
                [:, c0:c0 + 2, hp * 128:(hp + 1) * 128]),
                reads=(bqw,), writes=(bqw,), dma=True)
            P.emit("G", lambda g, hp=hp, qwpb=qwpb, c0=c0: g.dma_start(
                qwpb[:, c0:c0 + 2, :], dWqp.ap().rearrange("(k p) n -> p k n", p=128)
                [:, c0:c0 + 2, hp * 128:(hp + 1) * 128]),
                reads=(bqwp,), writes=(bqwp,), dma=True)
        for w in range(4):
            wc = slice(w * 512, (w + 1) * 512)
            P.emit("T", [
                (lambda t, cc=cc, wc=wc, qwb=qwb: t.matmul(
                    ps[0][:, 0:512], qwb[:, cc, :], A_x0T[:, cc, wc],
                    start=(cc == 0), stop=(cc == 7))) for cc in range(8)],
                reads=(bqw, bA, psB[0]), writes=(psB[0],))
            P.emit("T", [
                (lambda t, cc=cc, wc=wc, qwpb=qwpb: t.matmul(
                    ps[1][:, 0:512], qwpb[:, cc, :], A_x0T[:, cc, wc],
                    start=(cc == 0), stop=(cc == 7))) for cc in range(8)],
                reads=(bqwp, bA, psB[1]), writes=(psB[1],))
            bt1 = P.buf("s_t1"); bqt = P.buf("s_qt")
            P.emit("V", lambda v, hp=hp, wc=wc: v.scalar_tensor_tensor(
                s_t1[:, 0:512], ps[0][:, 0:512], K_bqT[:, hp:hp + 1], K_cos[:, wc],
                ALU.add, ALU.mult),
                reads=(psB[0], kbufs["A_bqT"], kbufs["COSA"]), writes=(bt1,))
            P.emit("V", lambda v, hp=hp, wc=wc: v.scalar_tensor_tensor(
                s_qt[:, 0:512], ps[1][:, 0:512], K_bqpT[:, hp:hp + 1], K_sin[:, wc],
                ALU.add, ALU.mult),
                reads=(psB[1], kbufs["A_bqpT"], kbufs["SINA"]), writes=(bqt,))
            P.emit("V", lambda v: v.tensor_add(s_qt[:, 0:512], s_qt[:, 0:512],
                                               s_t1[:, 0:512]),
                reads=(bt1, bqt), writes=(bqt,))
            nS = 4 * w + 4
            j0s = [max(0, s * 128 - w * 512) for s in range(nS)]
            ROT2 = ((4, 5), (0, 1))
            def emit_av(s, j0, hp=hp, w=w, nS=nS):
                for hh in range(2):
                    h = hp * 2 + hh
                    pc = 2 + hh
                    be = P.buf(("exp", hh * 2 + s % 2))
                    P.emit("T", lambda t, s=s, j0=j0, pc=pc, h=h, hh=hh, nS=nS: t.matmul(
                        ps[pc][0:65, j0:512], C_v[:, s, h * 65:(h + 1) * 65],
                        s_exp[hh * 2 + s % 2][:, j0:512],
                        start=(s == 0), stop=(s == nS - 1)),
                        reads=(bC, be, psB[pc]), writes=(psB[pc],))
            for s in range(nS):
                j0 = j0s[s]
                pr = ROT2[s % 2]
                P.emit("T", [
                    (lambda t, s=s, j0=j0, hp=hp, w=w, pr=pr, hh=hh, r0=r0: t.matmul(
                        ps[pr[hh]][:, j0:512],
                        B_kt[r0:r0 + 64, hp, s * 128:(s + 1) * 128],
                        s_qt[r0:r0 + 64, j0:512], start=True, stop=True,
                        tile_position=(r0, 0)))
                    for hh, r0 in ((0, 0), (1, 64))],
                    reads=(bB, P.buf("s_qt"), psB[pr[0]], psB[pr[1]]),
                    writes=(psB[pr[0]], psB[pr[1]]))
                for hh in range(2):
                    be = P.buf(("exp", hh * 2 + s % 2))
                    P.emit("S", lambda sc, s=s, j0=j0, pe=pr[hh], hh=hh: sc.activation(
                        s_exp[hh * 2 + s % 2][:, j0:512], ps[pe][:, j0:512], AF.Exp,
                        bias=0.0, scale=K_sc[:, 0:1]),
                        reads=(psB[pr[hh]], kbufs["SCAL"]), writes=(be,))
                    if s * 128 >= w * 512:
                        P.emit("V", lambda v, s=s, j0=j0, hh=hh: v.tensor_mul(
                            s_exp[hh * 2 + s % 2][:, j0:j0 + 128],
                            s_exp[hh * 2 + s % 2][:, j0:j0 + 128], K_ut[:, :]),
                            reads=(be, kbufs["UTRI"]), writes=(be,))
                if s >= 1:
                    emit_av(s - 1, j0s[s - 1])
            emit_av(nS - 1, j0s[nS - 1])
            for hh in range(2):
                h = hp * 2 + hh
                r0 = hh * 64
                pc = 2 + hh
                bpc = psB[pc]
                brec = P.buf("s_rec")
                P.emit("S", lambda sc, pc=pc, hh=hh: sc.activation(
                    s_rec[64:65, 512:1024], ps[pc][64:65, 0:512], AF.Ln, bias=0.0),
                    reads=(bpc,), writes=(brec,))
                P.emit("S", lambda sc: sc.activation(
                    s_rec[64:65, 0:512], s_rec[64:65, 512:1024], AF.Exp,
                    bias=0.0, scale=K_sc[64:65, 8:9]),
                    reads=(brec, kbufs["SCAL"]), writes=(brec,))
                P.emit("T", lambda t: t.matmul(
                    ps[5][0:64, 0:512], K_on[64:65, 0:64], s_rec[64:65, 0:512],
                    start=True, stop=True),
                    reads=(brec, kbufs["ONEBF"], psB[5]), writes=(psB[5],))
                P.emit("V", lambda v: v.tensor_copy(s_rec[0:64, 0:512],
                                                    ps[5][0:64, 0:512]),
                    reads=(psB[5],), writes=(brec,))
                if hp == 0:
                    dst = lambda wc=wc, r0=r0: s_ctx0[r0:r0 + 64, wc]
                    dstb = bctx0
                else:
                    dst = lambda wc=wc, r0=r0, hp=hp: B_kt[r0:r0 + 64, hp - 1, wc]
                    dstb = bB
                P.emit("V", lambda v, dst=dst, pc=pc: v.tensor_mul(
                    dst(), ps[pc][0:64, 0:512], s_rec[0:64, 0:512]),
                    reads=(bpc, brec), writes=(dstb,))

    P.alias(P.buf("bW1"), P.buf(("exp", 2)), P.buf(("exp", 3)))
    g_load_ch(P, W1, dWo.ap().rearrange("(k p) n -> p k n", p=128), "bW1", 4)
    bsm = P.buf("s_small")
    P.alias(P.buf("s_X"), P.buf(("qw", 0)), P.buf(("qw", 1)))
    P.alias(P.buf("s_xres"), P.buf(("qwp", 0)), P.buf(("qwp", 1)))
    P.alias(P.buf("s_dump"), P.buf("s_qt"), P.buf("s_rec"))
    P.alias(bsm, P.buf(("exp", 0)), P.buf(("exp", 1)))
    P.alias(P.buf("s_t1"), P.buf(("exp", 2)), P.buf(("exp", 3)))
    for j in range(NT):
        bxr = P.buf("s_xres")
        P.emit("G", lambda g, j=j: g.dma_start(s_xres[:, :], xr[:, j, :]),
               reads=(bxr,), writes=(bxr,), dma=True)
        for half in range(2):
            pi = half
            def mml(cc, j=j, half=half, pi=pi):
                def f(t):
                    src = s_ctx0[:, j * 128:(j + 1) * 128] if cc == 0 else \
                        B_kt[:, cc - 1, j * 128:(j + 1) * 128]
                    return t.matmul(ps[pi][:, 0:512], src,
                                    W1[:, cc, half * 512:(half + 1) * 512],
                                    start=(cc == 0), stop=(cc == 7))
                return f
            P.emit("T", [mml(cc) for cc in range(8)],
                   reads=(bW1, bB, bctx0, psB[pi]), writes=(psB[pi],))
        bX = P.buf("s_X")
        for half in range(2):
            hs = slice(half * 512, (half + 1) * 512)
            P.emit("V", lambda v, hs=hs, half=half: v.tensor_add(
                s_X[:, hs], ps[half][:, 0:512], K_bo[:, hs]),
                reads=(psB[half], kbufs["A_boR"]), writes=(bX,))
        for half in range(2):
            hs = slice(half * 512, (half + 1) * 512)
            P.emit("V", lambda v, hs=hs, half=half: v.scalar_tensor_tensor(
                s_X[:, hs], s_X[:, hs], 0.0, s_xres[:, hs], ALU.add, ALU.add,
                accum_out=s_small[:, half:half + 1]),
                reads=(bX, bxr), writes=(bX, bsm))
        P.emit("V", lambda v: v.scalar_tensor_tensor(
            s_dump[:, 0:1024], s_X[:, 0:1024], 0.0, s_X[:, 0:1024], ALU.add,
            ALU.mult, accum_out=s_small[:, 2:3]),
            reads=(bX,), writes=(P.buf("s_dump"), bsm))
        P.emit("V", lambda v: v.tensor_add(s_small[:, 3:4], s_small[:, 0:1],
                                           s_small[:, 1:2]), reads=(bsm,), writes=(bsm,))
        P.emit("V", lambda v: v.tensor_scalar_mul(s_small[:, 4:5], s_small[:, 3:4],
                                                  1.0 / 1024.0), reads=(bsm,), writes=(bsm,))
        P.emit("V", lambda v: v.tensor_scalar(s_small[:, 5:6], s_small[:, 4:5],
                                              s_small[:, 4:5], -1e-5,
                                              ALU.mult, ALU.add),
               reads=(bsm,), writes=(bsm,))
        P.emit("V", lambda v: v.scalar_tensor_tensor(
            s_small[:, 6:7], s_small[:, 2:3], 1.0 / 1024.0, s_small[:, 5:6],
            ALU.mult, ALU.subtract), reads=(bsm,), writes=(bsm,))
        P.emit("S", lambda sc: sc.activation(s_small[:, 7:8], s_small[:, 6:7],
                                             AF.Ln, bias=0.0), reads=(bsm,), writes=(bsm,))
        P.emit("S", lambda sc: sc.activation(s_small[:, 8:9], s_small[:, 7:8],
                                             AF.Exp, bias=0.0, scale=K_sc[:, 1:2]),
               reads=(bsm, kbufs["SCAL"]), writes=(bsm,))
        P.emit("V", lambda v: v.scalar_tensor_tensor(
            s_small[:, 9:10], s_small[:, 4:5], -1.0, s_small[:, 8:9],
            ALU.mult, ALU.mult), reads=(bsm,), writes=(bsm,))
        P.emit("V", lambda v, j=j: v.tensor_scalar(
            R[:, j, :], s_X[:, 0:1024], s_small[:, 8:9], s_small[:, 9:10],
            ALU.mult, ALU.add), reads=(bX, bsm), writes=(bR[j],))
    bxb = P.buf("s_dump")
    for j in range(NT):
        P.emit("S", lambda sc, j=j: sc.copy(s_dump[:, 0:1024], R[:, j, :]),
               reads=(bR[j],), writes=(bxb,))
        for cc in range(8):
            pbi = cc % 2
            P.emit("T", lambda t, cc=cc, pbi=pbi: t.transpose(
                pb[pbi][:, 0:128], s_dump[:, cc * 128:(cc + 1) * 128], K_id[:, :]),
                reads=(bxb, kbufs["IDN"], pbB[pbi]), writes=(pbB[pbi],))
            P.emit("V", lambda v, j=j, cc=cc, pbi=pbi: v.tensor_copy(
                B_kt[:, cc, j * 128:(j + 1) * 128], pb[pbi][:, 0:128]),
                reads=(pbB[pbi],), writes=(bB,))
    return R, B_kt, bR, bB


def emit_xT_from_R(P, R, bR, dstT, bdst, K_id, kb_id, pb, pbB, s_dump, bdump):
    for j in range(NT):
        P.emit("S", lambda sc, j=j: sc.copy(s_dump[:, 0:1024], R[:, j, :]),
               reads=(bR[j],), writes=(bdump,))
        for cc in range(8):
            pbi = cc % 2
            P.emit("T", lambda t, cc=cc, pbi=pbi: t.transpose(
                pb[pbi][:, 0:128], s_dump[:, cc * 128:(cc + 1) * 128], K_id[:, :]),
                reads=(bdump, kb_id, pbB[pbi]), writes=(pbB[pbi],))
            P.emit("V", lambda v, j=j, cc=cc, pbi=pbi: v.tensor_copy(
                dstT[:, cc, j * 128:(j + 1) * 128], pb[pbi][:, 0:128]),
                reads=(pbB[pbi],), writes=(bdst,))

def emit_ln_tiles(P, R, bR, j, stats_src, stats_bufs, s_small, bsm, K_sc, kb_sc,
                  apply_in, apply_bufs):
    P.emit("V", lambda v: v.tensor_add(s_small[:, 3:4], s_small[:, 0:1],
                                       s_small[:, 1:2]), reads=(bsm,), writes=(bsm,))
    P.emit("V", lambda v: v.tensor_scalar_mul(s_small[:, 4:5], s_small[:, 3:4],
                                              1.0 / 1024.0), reads=(bsm,), writes=(bsm,))
    P.emit("V", lambda v: v.tensor_scalar(s_small[:, 5:6], s_small[:, 4:5],
                                          s_small[:, 4:5], -1e-5,
                                          ALU.mult, ALU.add), reads=(bsm,), writes=(bsm,))
    P.emit("V", lambda v: v.scalar_tensor_tensor(
        s_small[:, 6:7], s_small[:, 2:3], 1.0 / 1024.0, s_small[:, 5:6],
        ALU.mult, ALU.subtract), reads=(bsm,), writes=(bsm,))
    P.emit("S", lambda sc: sc.activation(s_small[:, 7:8], s_small[:, 6:7],
                                         AF.Ln, bias=0.0), reads=(bsm,), writes=(bsm,))
    P.emit("S", lambda sc: sc.activation(s_small[:, 8:9], s_small[:, 7:8],
                                         AF.Exp, bias=0.0, scale=K_sc[:, 1:2]),
           reads=(bsm, kb_sc), writes=(bsm,))
    P.emit("V", lambda v: v.scalar_tensor_tensor(
        s_small[:, 9:10], s_small[:, 4:5], -1.0, s_small[:, 8:9],
        ALU.mult, ALU.mult), reads=(bsm,), writes=(bsm,))
    P.emit("V", lambda v, j=j: v.tensor_scalar(
        R[:, j, :], apply_in, s_small[:, 8:9], s_small[:, 9:10],
        ALU.mult, ALU.add), reads=tuple(apply_bufs) + (bsm,), writes=(bR[j],))


def emit_stage_B(P, x_in=None, dbg=False):
    nc = P.nc
    dbg_outs = {}
    def DUMP(name, shape, dtype, ap_fn, rbufs):
        if not dbg:
            return
        d = P.nc.dram_tensor(name, shape, dtype, kind="ExternalOutput")
        P.emit("Y", lambda y: y.dma_start(d.ap(), ap_fn()), reads=rbufs,
               writes=(), dma=True)
        dbg_outs[name] = d
    R = P.sb("R", [128, NT, 1024], F32, OFF_R)
    Bt = P.sb("Bt", [128, 8, 2048], BF16, OFF_B)
    A_kt = P.sb("A_kt", [128, 8, 2048], BF16, OFF_A)
    A_w1 = P.sb("A_w1", [128, 8, 2048], BF16, OFF_A)
    C_v = P.sb("C_v", [128, 16, 528], BF16, OFF_C)
    C_ao = P.sb("C_ao", [128, 4, 2048], BF16, OFF_C + 16896)
    C_w2 = P.sb("C_w2", [128, 16, 1024], BF16, OFF_C)
    Wa = P.sb("Wa", [128, 8, 512], BF16, OFF_W)
    Wst = P.sb("Wst", [128, 8, 1024], BF16, OFF_W)
    Wb = P.sb("Wb", [128, 8, 512], BF16, OFF_W + 8192)
    Wo2 = P.sb("Wo2", [128, 4, 1024], BF16, OFF_W + 8192)
    s_qt = P.sb("s_qt", [128, 512], BF16, OFF_S)
    s_rec = P.sb("s_rec", [128, 1024], BF16, OFF_S + 1024)
    s_exp = [P.sb("s_exp0", [128, 512], BF16, OFF_S + 3072)] + \
            [P.sb(f"s_exp{1 + i}", [128, 512], BF16, OFF_S + 10240 + i * 1024) for i in range(3)]
    s_X = P.sb("s_X", [128, 1024], F32, OFF_S + 4096)
    s_dump = P.sb("s_dump", [128, 1024], BF16, OFF_S + 8192)
    s_mT = P.sb("s_mT", [128, 1024], F32, OFF_S + 10240)
    s_small = P.sb("s_small", [128, 64], F32, OFF_S + 14336)
    s_h1 = [P.sb(f"s_h1{i}", [128, 256], BF16, OFF_S + 14592 + i * 512) for i in range(2)]
    s_rr = P.sb("s_rr", [128, 16], F32, OFF_S + 15616)
    s_cb = P.sb("s_cb", [128, 8], BF16, OFF_S + 15680)
    K_id, K_idf, K_ut, K_on, K_onf, K_sc, ps, pb, psB, pbB, kbufs0 = setup_common(P)
    KZ = OFF_K + 2048
    K_qbT = P.sb("K_qbT", [128, 8], F32, KZ)
    K_kbT = P.sb("K_kbT", [128, 8], F32, KZ + 32)
    K_bv = P.sb("K_bv", [128, 512], BF16, KZ + 64)
    K_bo = P.sb("K_bo", [128, 1024], BF16, KZ + 1088)
    K_rw = P.sb("K_rw", [128, 8, 8], F32, KZ + 3136)
    K_rb = P.sb("K_rb", [128, 8], F32, KZ + 3392)
    K_b1T = P.sb("K_b1T", [128, 8, 16], F32, KZ + 3424)
    K_b2 = P.sb("K_b2", [8, 1024], BF16, KZ + 3936)
    K_cT = P.sb("K_cT", [8, 2048], BF16, KZ + 5984)
    K_c3 = P.sb("K_c3", [128, 16, 8], F32, KZ + 10080)
    kbufs = dict(kbufs0)
    kbufs.update(load_zone(P, [
        ("B_qbT", K_qbT, [128, 8]), ("B_kbT", K_kbT, [128, 8]),
        ("B_vbR", K_bv, [128, 512]), ("B_obR", K_bo, [128, 1024]),
        ("B_rw", K_rw, [1024, 8], "(k p) n -> p k n"), ("B_rbR", K_rb, [128, 8]),
        ("B_b1T", K_b1T, [128, 8, 16]), ("B_b2", K_b2, [8, 1024]),
    ]))
    dkw = P.din("B_kw", [1024, 512], F32)
    dvw = P.din("B_vw", [1024, 512], F32)
    dqw = P.din("B_qw", [1024, 512], F32)
    dow = P.din("B_ow", [512, 1024], F32)
    dw1 = P.din("B_w1", [8, 1024, 2048], F32)
    dw2 = P.din("B_w2", [8, 2048, 1024], F32)
    bR = [P.buf(("R", j)) for j in range(NT)]
    bBt = P.buf("bBt"); bA = P.buf("bA"); bC = P.buf("bC")
    bWa = P.buf("bWa"); bWb = P.buf("bWb")
    bc3 = P.buf("bc3"); bcT = P.buf("bcT")
    bsm = P.buf("s_small")
    bdump0 = P.buf("s_dump")
    if x_in is not None:
        _load_R_f32(P, x_in, R, bR)
        emit_xT_from_R(P, R, bR, Bt, bBt, K_id, kbufs["IDN"], pb, pbB,
                       s_dump, bdump0)

    g_load_ch(P, Wa, dkw.ap().rearrange("(k p) n -> p k n", p=128), "bWa", 4)
    g_load_ch(P, Wb, dvw.ap().rearrange("(k p) n -> p k n", p=128), "bWb", 4)
    for hp in range(8):
        for w in range(4):
            wc = slice(w * 512, (w + 1) * 512)
            pi = (hp * 4 + w) % 2
            grp = []
            for hh in range(2):
                r0 = hh * 64
                for cc in range(8):
                    grp.append(lambda t, cc=cc, hp=hp, hh=hh, r0=r0, wc=wc, pi=pi:
                               t.matmul(
                        ps[pi][r0:r0 + 32, 0:512],
                        Wa[:, cc, hp * 64 + hh * 32:hp * 64 + hh * 32 + 32],
                        Bt[:, cc, wc], start=(cc == 0), stop=(cc == 7),
                        tile_position=(0, r0)))
            P.emit("T", grp, reads=(bWa, bBt, psB[pi]), writes=(psB[pi],))
            for r0 in (0, 64):
                P.emit("V", lambda v, hp=hp, wc=wc, pi=pi, r0=r0: v.tensor_scalar(
                    A_kt[r0:r0 + 32, hp, wc], ps[pi][r0:r0 + 32, 0:512],
                    K_kbT[r0:r0 + 32, hp:hp + 1], None, ALU.add),
                    reads=(psB[pi], kbufs["B_kbT"]), writes=(bA,))
    Cv4 = C_v[:, :, :].rearrange("p t (h e) -> p t h e", e=33)
    P.emit("V", lambda v: v.memset(Cv4[:, :, :, 32:33], 1.0), reads=(bC,), writes=(bC,))
    for j in range(NT):
        pi = j % 2
        P.emit("T", [
            (lambda t, cc=cc, j=j, pi=pi: t.matmul(
                ps[pi][:, 0:512], Bt[:, cc, j * 128:(j + 1) * 128],
                Wb[:, cc, 0:512], start=(cc == 0), stop=(cc == 7)))
            for cc in range(8)],
            reads=(bWb, bBt, psB[pi]), writes=(psB[pi],))
        P.emit("V", lambda v, j=j, pi=pi: v.tensor_add(
            Cv4[:, j, :, 0:32],
            ps[pi][:, 0:512].rearrange("p (h e) -> p h e", e=32),
            K_bv[:, :].rearrange("p (h e) -> p h e", e=32)),
            reads=(psB[pi], kbufs["B_vbR"]), writes=(bC,))

    DUMP("d_kt", [128, 8, 2048], BF16, lambda: A_kt[:, :, :], (bA,))
    DUMP("d_v", [128, 16, 528], BF16, lambda: C_v[:, :, :], (bC,))
    g_load_ch(P, Wa, dqw.ap().rearrange("(k p) n -> p k n", p=128), "bWa", 4)
    bAO = P.buf("C_ao")
    for hp in range(8):
        for w in range(4):
            wc = slice(w * 512, (w + 1) * 512)
            grp = []
            for hh in range(2):
                r0 = hh * 64
                for cc in range(8):
                    grp.append(lambda t, cc=cc, hp=hp, hh=hh, r0=r0, wc=wc: t.matmul(
                        ps[0][r0:r0 + 32, 0:512],
                        Wa[:, cc, hp * 64 + hh * 32:hp * 64 + hh * 32 + 32],
                        Bt[:, cc, wc], start=(cc == 0), stop=(cc == 7),
                        tile_position=(0, r0)))
            P.emit("T", grp, reads=(bWa, bBt, psB[0]), writes=(psB[0],))
            bqt = P.buf("s_qt")
            for r0 in (0, 64):
                P.emit("V", lambda v, hp=hp, r0=r0: v.tensor_scalar(
                    s_qt[r0:r0 + 32, 0:512], ps[0][r0:r0 + 32, 0:512],
                    K_qbT[r0:r0 + 32, hp:hp + 1], None, ALU.add),
                    reads=(psB[0], kbufs["B_qbT"]), writes=(bqt,))
            ROT2 = ((4, 5), (0, 1))
            def emit_av(sck, hp=hp, w=w):
                for hh in range(2):
                    h = hp * 2 + hh
                    pc = 2 + hh
                    be = P.buf(("exp", hh * 2 + sck % 2))
                    P.emit("T", lambda t, sck=sck, pc=pc, h=h, hh=hh: t.matmul(
                        ps[pc][0:33, 0:512], C_v[:, sck, h * 33:(h + 1) * 33],
                        s_exp[hh * 2 + sck % 2][:, 0:512],
                        start=(sck == 0), stop=(sck == 15)),
                        reads=(bC, be, psB[pc]), writes=(psB[pc],))
            for sck in range(16):
                pr = ROT2[sck % 2]
                P.emit("T", [
                    (lambda t, sck=sck, hp=hp, pr=pr, hh=hh, r0=r0: t.matmul(
                        ps[pr[hh]][:, 0:512],
                        A_kt[r0:r0 + 32, hp, sck * 128:(sck + 1) * 128],
                        s_qt[r0:r0 + 32, 0:512], start=True, stop=True,
                        tile_position=(r0, 0)))
                    for hh, r0 in ((0, 0), (1, 64))],
                    reads=(bA, P.buf("s_qt"), psB[pr[0]], psB[pr[1]]),
                    writes=(psB[pr[0]], psB[pr[1]]))
                for hh in range(2):
                    be = P.buf(("exp", hh * 2 + sck % 2))
                    P.emit("S", lambda sc, sck=sck, pe=pr[hh], hh=hh: sc.activation(
                        s_exp[hh * 2 + sck % 2][:, 0:512], ps[pe][:, 0:512], AF.Exp,
                        bias=0.0, scale=K_sc[:, 2:3]),
                        reads=(psB[pr[hh]], kbufs["SCAL"]), writes=(be,))
                if sck >= 1:
                    emit_av(sck - 1)
            emit_av(15)
            for hh in range(2):
                h = hp * 2 + hh
                r0 = hh * 64
                pc = 2 + hh
                bpc = psB[pc]
                brec = P.buf("s_rec")
                P.emit("S", lambda sc, pc=pc: sc.activation(
                    s_rec[32:33, 512:1024], ps[pc][32:33, 0:512], AF.Ln, bias=0.0),
                    reads=(bpc,), writes=(brec,))
                P.emit("S", lambda sc: sc.activation(
                    s_rec[32:33, 0:512], s_rec[32:33, 512:1024], AF.Exp,
                    bias=0.0, scale=K_sc[32:33, 8:9]),
                    reads=(brec, kbufs["SCAL"]), writes=(brec,))
                P.emit("T", lambda t: t.matmul(
                    ps[5][0:32, 0:512], K_on[32:33, 0:32], s_rec[32:33, 0:512],
                    start=True, stop=True),
                    reads=(brec, kbufs["ONEBF"], psB[5]), writes=(psB[5],))
                P.emit("V", lambda v: v.tensor_copy(s_rec[0:32, 0:512],
                                                    ps[5][0:32, 0:512]),
                    reads=(psB[5],), writes=(brec,))
                P.emit("V", lambda v, h=h, wc=wc, pc=pc: v.tensor_mul(
                    C_ao[(h % 4) * 32:(h % 4) * 32 + 32, h // 4, wc],
                    ps[pc][0:32, 0:512], s_rec[0:32, 0:512]),
                    reads=(bpc, brec), writes=(bAO,))

    g_load_ch(P, Wo2, dow.ap().rearrange("(k p) n -> p k n", p=128), "bWb", 2)
    bX = P.buf("s_X"); bmT = P.buf("s_mT"); brr = P.buf("s_rr")
    bdump = P.buf("s_dump")
    P.alias(bmT, P.buf(("exp", 1)), P.buf(("exp", 2)), P.buf(("exp", 3)))
    for j in range(NT):
        for half in range(2):
            pi = half
            P.emit("T", [
                (lambda t, cc=cc, j=j, half=half, pi=pi: t.matmul(
                    ps[pi][:, 0:512], C_ao[:, cc, j * 128:(j + 1) * 128],
                    Wo2[:, cc, half * 512:(half + 1) * 512],
                    start=(cc == 0), stop=(cc == 3)))
                for cc in range(4)],
                reads=(bWb, bAO, psB[pi]), writes=(psB[pi],))
            P.emit("V", lambda v, half=half, pi=pi: v.tensor_add(
                s_X[:, half * 512:(half + 1) * 512], ps[pi][:, 0:512],
                K_bo[:, half * 512:(half + 1) * 512]),
                reads=(psB[pi], kbufs["B_obR"]), writes=(bX,))
        P.emit("S", lambda sc: sc.copy(s_dump[:, 0:1024], s_X[:, 0:1024]),
               reads=(bX,), writes=(bdump,))
        for cc in range(8):
            pbi = cc % 2
            P.emit("T", lambda t, cc=cc, pbi=pbi: t.transpose(
                pb[pbi][:, 0:128], s_dump[:, cc * 128:(cc + 1) * 128], K_id[:, :]),
                reads=(bdump, kbufs["IDN"], pbB[pbi]), writes=(pbB[pbi],))
            P.emit("V", lambda v, j=j, cc=cc, pbi=pbi: v.tensor_copy(
                Bt[:, cc, j * 128:(j + 1) * 128], pb[pbi][:, 0:128]),
                reads=(pbB[pbi],), writes=(bBt,))
        for cc in range(8):
            pi = 2 + cc % 2
            P.emit("T", lambda t, cc=cc, pi=pi: t.transpose(
                ps[pi][:, 0:128], s_X[:, cc * 128:(cc + 1) * 128], K_idf[:, :]),
                reads=(bX, kbufs["IDF"], psB[pi]), writes=(psB[pi],))
            P.emit("V", lambda v, cc=cc, pi=pi: v.tensor_copy(
                s_mT[:, cc * 128:(cc + 1) * 128], ps[pi][:, 0:128]),
                reads=(psB[pi],), writes=(bmT,))
        P.emit("T", [
            (lambda t, cc=cc: t.matmul(
                ps[4][:, 0:8], s_mT[:, cc * 128:(cc + 1) * 128],
                K_rw[:, cc, :], start=(cc == 0), stop=(cc == 7)))
            for cc in range(8)],
            reads=(bmT, kbufs["B_rw"], psB[4]), writes=(psB[4],))
        P.emit("V", lambda v: v.tensor_add(s_rr[:, 0:8], ps[4][:, 0:8], K_rb[:, 0:8]),
               reads=(psB[4], kbufs["B_rbR"]), writes=(brr,))
        P.emit("S", lambda sc: sc.activation(s_small[:, 16:24], s_rr[:, 0:8], AF.Exp),
               reads=(brr,), writes=(bsm,))
        P.emit("V", lambda v: v.tensor_reduce(s_small[:, 0:1], s_small[:, 16:24],
                                              AX.X, ALU.add), reads=(bsm,), writes=(bsm,))
        P.emit("V", lambda v: v.reciprocal(s_small[:, 1:2], s_small[:, 0:1]),
               reads=(bsm,), writes=(bsm,))
        P.emit("V", lambda v: v.tensor_scalar(s_small[:, 16:24], s_small[:, 16:24],
                                              s_small[:, 1:2], None, ALU.mult),
               reads=(bsm,), writes=(bsm,))
        P.emit("V", lambda v: v.tensor_reduce(s_small[:, 2:3], s_small[:, 16:24],
                                              AX.X, ALU.max), reads=(bsm,), writes=(bsm,))
        P.emit("V", lambda v: v.tensor_single_scalar(s_small[:, 24:32], s_small[:, 16:24],
                                                     s_small[:, 2:3], ALU.subtract),
               reads=(bsm,), writes=(bsm,))
        P.emit("S", lambda sc: sc.activation(s_small[:, 32:40], s_small[:, 24:32],
                                             AF.Exp, bias=0.0, scale=K_sc[:, 5:6]),
               reads=(bsm, kbufs["SCAL"]), writes=(bsm,))
        P.emit("V", lambda v: v.scalar_tensor_tensor(
            s_small[:, 40:48], s_small[:, 32:40], -1e30, s_small[:, 16:24],
            ALU.mult, ALU.add), reads=(bsm,), writes=(bsm,))
        P.emit("V", lambda v: v.tensor_reduce(s_small[:, 3:4], s_small[:, 40:48],
                                              AX.X, ALU.max), reads=(bsm,), writes=(bsm,))
        P.emit("V", lambda v: v.tensor_single_scalar(s_small[:, 24:32], s_small[:, 16:24],
                                                     s_small[:, 3:4], ALU.subtract),
               reads=(bsm,), writes=(bsm,))
        P.emit("V", lambda v: v.tensor_scalar_min(s_small[:, 24:32], s_small[:, 24:32],
                                                  0.0), reads=(bsm,), writes=(bsm,))
        P.emit("S", lambda sc: sc.activation(s_small[:, 32:40], s_small[:, 24:32],
                                             AF.Exp, bias=0.0, scale=K_sc[:, 5:6]),
               reads=(bsm, kbufs["SCAL"]), writes=(bsm,))
        P.emit("V", lambda v: v.tensor_mul(s_small[:, 40:48], s_small[:, 16:24],
                                           s_small[:, 32:40]), reads=(bsm,), writes=(bsm,))
        P.emit("V", lambda v: v.tensor_reduce(s_small[:, 4:5], s_small[:, 40:48],
                                              AX.X, ALU.add), reads=(bsm,), writes=(bsm,))
        P.emit("V", lambda v: v.tensor_scalar_add(s_small[:, 4:5], s_small[:, 4:5],
                                                  1e-8), reads=(bsm,), writes=(bsm,))
        P.emit("V", lambda v: v.reciprocal(s_small[:, 5:6], s_small[:, 4:5]),
               reads=(bsm,), writes=(bsm,))
        P.emit("V", lambda v, j=j: v.tensor_scalar(
            K_c3[:, j, :], s_small[:, 40:48], s_small[:, 5:6], K_sc[:, 4:5],
            ALU.mult, ALU.mult), reads=(bsm, kbufs["SCAL"]), writes=(bc3,))
        P.emit("V", lambda v, j=j: v.tensor_copy(s_cb[:, 0:8], K_c3[:, j, :]),
               reads=(bc3,), writes=(P.buf("s_cb"),))
        P.emit("T", lambda t: t.transpose(pb[0][0:8, 0:128], s_cb[:, 0:8], K_id[:, :]),
               reads=(P.buf("s_cb"), kbufs["IDN"], pbB[0]), writes=(pbB[0],))
        P.emit("V", lambda v, j=j: v.tensor_copy(K_cT[0:8, j * 128:(j + 1) * 128],
                                                 pb[0][0:8, 0:128]),
               reads=(pbB[0],), writes=(bcT,))

    DUMP("d_ao", [128, 4, 2048], BF16, lambda: C_ao[:, :, :], (bAO,))
    DUMP("d_mlaT", [128, 8, 2048], BF16, lambda: Bt[:, :, :], (bBt,))
    DUMP("d_c3", [128, 16, 8], F32, lambda: K_c3[:, :, :], (bc3,))
    import os as _os
    NEXP = int(_os.environ.get("KDBG_NEXP", "8"))
    NOB2 = bool(_os.environ.get("KDBG_NOB2"))
    NFC = int(_os.environ.get("KDBG_NFC", "16"))
    P.alias(P.buf("A_w1"), bA)
    P.alias(P.buf("C_w2"), bC, P.buf("C_ao"))
    bw1 = P.buf("A_w1"); bw2 = P.buf("C_w2")
    for e in range(NEXP):
        w1src = dw1.ap()[e].rearrange("(k p) n -> p k n", p=128)
        half0 = (lambda cc, fs: Wst[:, cc, fs]) if e % 2 == 1 else \
            (lambda cc, fs: A_w1[:, cc, fs])
        b_half0 = P.buf(("w1h0", e % 2))
        if e == 0:
            P.alias(b_half0, bw1)
            for c0 in range(0, 8, 2):
                P.emit("G", lambda g, c0=c0, w1src=w1src: g.dma_start(
                    A_w1[:, c0:c0 + 2, 0:1024], w1src[:, c0:c0 + 2, 0:1024]),
                    reads=(), writes=(b_half0, bw1), dma=True)
        bc1 = P.buf("w1h1")
        P.alias(bc1, bw1)
        for c0 in range(0, 8, 2):
            P.emit("G", lambda g, c0=c0, w1src=w1src: g.dma_start(
                A_w1[:, c0:c0 + 2, 1024:2048], w1src[:, c0:c0 + 2, 1024:2048]),
                reads=(), writes=(bc1, bw1), dma=True)
        g_load_ch(P, C_w2, dw2.ap()[e].rearrange("(k p) n -> p k n", p=128), "C_w2", 8)
        for tw in range(8):
            tc = slice(tw * 256, (tw + 1) * 256)
            def emit_h1(fc, tc=tc, e=e, half0=half0, b_half0=b_half0, bc1=bc1):
                pi = fc % 2
                bh = P.buf(("h1", fc % 2))
                if fc < 8:
                    lsrc = lambda cc, fc=fc: half0(cc, slice(fc * 128, (fc + 1) * 128))
                    bsrc = b_half0
                else:
                    lsrc = lambda cc, fc=fc: A_w1[:, cc, fc * 128:(fc + 1) * 128]
                    bsrc = bc1
                P.emit("T", [
                    (lambda t, cc=cc, fc=fc, tc=tc, pi=pi, lsrc=lsrc: t.matmul(
                        ps[pi][:, 0:256], lsrc(cc),
                        Bt[:, cc, tc], start=(cc == 0), stop=(cc == 7)))
                    for cc in range(8)],
                    reads=(bsrc, bBt, psB[pi]), writes=(psB[pi],))
                P.emit("S", lambda sc, fc=fc, pi=pi, e=e: sc.activation(
                    s_h1[fc % 2][:, 0:256], ps[pi][:, 0:256], AF.Gelu,
                    bias=K_b1T[:, e, fc:fc + 1]),
                    reads=(psB[pi], kbufs["B_b1T"]), writes=(bh,))
            if tw == 4 and e + 1 < 8:
                nsrc = dw1.ap()[e + 1].rearrange("(k p) n -> p k n", p=128)
                ndst = (lambda c0: Wst[:, c0:c0 + 2, :]) if (e + 1) % 2 == 1 else \
                    (lambda c0: A_w1[:, c0:c0 + 2, 0:1024])
                bnx = P.buf(("w1h0", (e + 1) % 2))
                if (e + 1) % 2 == 1:
                    P.alias(bnx, P.buf("bWa"), P.buf("bWb"))
                else:
                    P.alias(bnx, bw1)
                for c0 in range(0, 8, 2):
                    P.emit("G", lambda g, c0=c0, nsrc=nsrc, ndst=ndst: g.dma_start(
                        ndst(c0), nsrc[:, c0:c0 + 2, 0:1024]),
                        reads=(), writes=(bnx,), dma=True)
            emit_h1(0)
            for fc in range(NFC):
                if fc + 1 < NFC:
                    emit_h1(fc + 1)
                bh = P.buf(("h1", fc % 2))
                for tt in range(2):
                    for dg in range(2):
                        pj = 2 + tt * 2 + dg
                        P.emit("T", lambda t, fc=fc, tt=tt, dg=dg, pj=pj: t.matmul(
                            ps[pj][:, 0:512], s_h1[fc % 2][:, tt * 128:(tt + 1) * 128],
                            C_w2[:, fc, dg * 512:(dg + 1) * 512],
                            start=(fc == 0), stop=(fc == NFC - 1)),
                            reads=(bh, bw2, psB[pj]), writes=(psB[pj],))
            for tt in range(2):
                j = tw * 2 + tt
                for dg in range(2):
                    pj = 2 + tt * 2 + dg
                    P.emit("V", lambda v, j=j, dg=dg, pj=pj, e=e: v.tensor_scalar(
                        s_X[:, dg * 512:(dg + 1) * 512], ps[pj][:, 0:512],
                        K_c3[:, j, e:e + 1], None, ALU.mult),
                        reads=(psB[pj], bc3), writes=(bX, psB[pj]))
                    P.emit("V", lambda v, j=j, dg=dg: v.tensor_add(
                        R[:, j, dg * 512:(dg + 1) * 512],
                        R[:, j, dg * 512:(dg + 1) * 512],
                        s_X[:, dg * 512:(dg + 1) * 512]),
                        reads=(bX, bR[j]), writes=(bR[j],))
    for j in (range(0) if NOB2 else range(NT)):
        for dg in range(2):
            pi = dg % 2
            P.emit("T", lambda t, j=j, dg=dg, pi=pi: t.matmul(
                ps[pi][:, 0:512], K_cT[0:8, j * 128:(j + 1) * 128],
                K_b2[0:8, dg * 512:(dg + 1) * 512], start=True, stop=True),
                reads=(bcT, kbufs["B_b2"], psB[pi]), writes=(psB[pi],))
            P.emit("V", lambda v, j=j, dg=dg, pi=pi: v.tensor_add(
                R[:, j, dg * 512:(dg + 1) * 512],
                R[:, j, dg * 512:(dg + 1) * 512], ps[pi][:, 0:512]),
                reads=(psB[pi], bR[j]), writes=(bR[j], psB[pi]))
    if dbg:
        dR = P.nc.dram_tensor("d_R", [T, CD], F32, kind="ExternalOutput")
        drr2 = dR.ap().rearrange("(j p) c -> p j c", p=128)
        for j in range(NT):
            P.emit("Y", lambda y, j=j: y.dma_start(drr2[:, j, :], R[:, j, :]),
                   reads=(bR[j],), writes=(), dma=True)
    bdump2 = P.buf("s_dump")
    for j in range(NT):
        P.emit("V", lambda v, j=j: v.scalar_tensor_tensor(
            s_dump[:, 0:1024], R[:, j, :], 0.0, R[:, j, :], ALU.add, ALU.mult,
            accum_out=s_small[:, 2:3]), reads=(bR[j],), writes=(bdump2, bsm))
        P.emit("V", lambda v, j=j: v.tensor_reduce(
            s_small[:, 0:1], R[:, j, :], AX.X, ALU.add), reads=(bR[j],), writes=(bsm,))
        P.emit("V", lambda v: v.memset(s_small[:, 1:2], 0.0), reads=(bsm,), writes=(bsm,))
        emit_ln_tiles(P, R, bR, j, None, None, s_small, bsm, K_sc, kbufs["SCAL"],
                      R[:, j, :], (bR[j],))
    emit_xT_from_R(P, R, bR, Bt, bBt, K_id, kbufs["IDN"], pb, pbB, s_dump, bdump2)
    return R, Bt, bR, bBt, (K_id, kbufs, pb, pbB, ps, psB, K_sc, dbg_outs)


def emit_stage_C(P, x_in=None, dbg=False):
    nc = P.nc
    def DUMP(name, shape, dtype, ap_fn, rbufs):
        if not dbg:
            return
        d = P.nc.dram_tensor(name, shape, dtype, kind="ExternalOutput")
        P.emit("Y", lambda y: y.dma_start(d.ap(), ap_fn()), reads=rbufs,
               writes=(), dma=True)
    R = P.sb("R", [128, NT, 1024], F32, OFF_R)
    Bt = P.sb("Bt", [128, 8, 2048], BF16, OFF_B)
    B_hnT = P.sb("B_hnT", [128, 4, 2048], BF16, OFF_B)
    A_q = P.sb("A_q", [128, 16, 512], BF16, OFF_A)
    A_k = P.sb("A_k", [128, 16, 512], BF16, OFF_A + 16384)
    C_v = P.sb("C_v", [128, 16, 520], BF16, OFF_C)
    C_og = P.sb("C_og", [128, 16, 512], BF16, OFF_C + 16640)
    Wt = P.sb("Wt", [128, 8, 512], BF16, OFF_W)
    Wt2 = P.sb("Wt2", [128, 8, 512], BF16, OFF_W + 8192)
    Wout = P.sb("Wout", [128, 4, 1024], BF16, OFF_W + 8192)
    s_g = P.sb("s_g", [128, 16, 16], BF16, OFF_S)
    s_eq = P.sb("s_eq", [128, 8], F32, OFF_S + 512)
    s_ek = P.sb("s_ek", [128, 8], F32, OFF_S + 544)
    s_dd = P.sb("s_dd", [128, 8], F32, OFF_S + 576)
    s_F = P.sb("s_F", [128, 8], F32, OFF_S + 608)
    s_qp = P.sb("s_qp", [128, 512], BF16, OFF_S + 640)
    s_kp = P.sb("s_kp", [128, 512], BF16, OFF_S + 1664)
    s_qpT = P.sb("s_qpT", [128, 4, 128], BF16, OFF_S + 2688)
    s_kpT = P.sb("s_kpT", [128, 4, 128], BF16, OFF_S + 3712)
    s_AT = [P.sb(f"s_AT{i}", [128, 128], BF16, OFF_S + 4736 + 256 * i) for i in range(2)]
    stf = [P.sb(f"stf{i}", [128, 4, 65], F32, OFF_S + 5248 + 1056 * i) for i in range(2)]
    stb = [P.sb(f"stb{i}", [128, 4, 65], BF16, OFF_S + 7360 + 544 * i) for i in range(2)]
    s_dn = P.sb("s_dn", [128, 8], F32, OFF_S + 8448)
    s_X = P.sb("s_X", [128, 1024], F32, OFF_S + 8480)
    s_dump = P.sb("s_dump", [128, 1024], BF16, OFF_S + 12576)
    s_small = P.sb("s_small", [128, 64], F32, OFF_S + 14624)
    s_hn = P.sb("s_hn", [128, 512], F32, OFF_S + 640)
    K_id, K_idf, K_ut, K_on, K_onf, K_sc, ps, pb, psB, pbB, kbufs0 = setup_common(P)
    KZ = OFF_K + 2048
    K_wif = P.sb("K_wif", [128, 8, 16], BF16, KZ)
    K_bif = P.sb("K_bif", [128, 16], F32, KZ + 512)
    K_bo = P.sb("K_bo", [128, 512], BF16, KZ + 576)
    K_lng = P.sb("K_lng", [128, 512], BF16, KZ + 1600)
    K_lnb = P.sb("K_lnb", [128, 512], BF16, KZ + 2624)
    kbufs = dict(kbufs0)
    kbufs.update(load_zone(P, [
        ("C_WifT", K_wif, [1024, 16], "(k p) n -> p k n"), ("C_bifR", K_bif, [128, 16]),
        ("C_boR", K_bo, [128, 512]),
        ("C_lngR", K_lng, [128, 512]), ("C_lnbR", K_lnb, [128, 512]),
    ]))
    dq = P.din("C_WqT", [1024, 512], F32)
    dk = P.din("C_WkT", [1024, 512], F32)
    dv = P.din("C_WvT", [1024, 512], F32)
    do = P.din("C_WoT", [1024, 512], F32)
    dwo = P.din("C_WoutT", [512, 1024], F32)
    bR = [P.buf(("R", j)) for j in range(NT)]
    bBt = P.buf("bBt"); bA = P.buf("bA"); bC = P.buf("bC")
    bW = P.buf("bW"); bW2 = P.buf("bW2")
    bsm = P.buf("s_small")
    if x_in is not None:
        _load_R_f32(P, x_in, R, bR)
        emit_xT_from_R(P, R, bR, Bt, bBt, K_id, kbufs["IDN"], pb, pbB,
                       s_dump, P.buf("s_dump"))

    g_load_ch(P, Wt, dq.ap().rearrange("(k p) n -> p k n", p=128), "bW", 4)
    g_load_ch(P, Wt2, dk.ap().rearrange("(k p) n -> p k n", p=128), "bW2", 4)
    Cv4 = C_v[:, :, :].rearrange("p t (h e) -> p t h e", e=65)
    P.emit("V", lambda v: v.memset(Cv4[:, :, :, 64:65], 1.0), reads=(bC,), writes=(bC,))
    for j in range(NT):
        P.emit("T", [
            (lambda t, cc=cc, j=j: t.matmul(
                ps[0][:, 0:512], Bt[:, cc, j * 128:(j + 1) * 128],
                Wt[:, cc, :], start=(cc == 0), stop=(cc == 7)))
            for cc in range(8)],
            reads=(bW, bBt, psB[0]), writes=(psB[0],))
        P.emit("S", lambda sc, j=j: sc.copy(A_q[:, j, :], ps[0][:, 0:512]),
               reads=(psB[0],), writes=(bA,))
        P.emit("T", [
            (lambda t, cc=cc, j=j: t.matmul(
                ps[1][:, 0:512], Bt[:, cc, j * 128:(j + 1) * 128],
                Wt2[:, cc, :], start=(cc == 0), stop=(cc == 7)))
            for cc in range(8)],
            reads=(bW2, bBt, psB[1]), writes=(psB[1],))
        P.emit("S", lambda sc, j=j: sc.activation(A_k[:, j, :], ps[1][:, 0:512],
                                                  AF.Copy, bias=0.0, scale=K_sc[:, 0:1]),
               reads=(psB[1], kbufs["SCAL"]), writes=(bA,))
    g_load_ch(P, Wt, dv.ap().rearrange("(k p) n -> p k n", p=128), "bW", 4)
    g_load_ch(P, Wt2, do.ap().rearrange("(k p) n -> p k n", p=128), "bW2", 4)
    bXc = P.buf("s_X")
    for j in range(NT):
        P.emit("T", [
            (lambda t, cc=cc, j=j: t.matmul(
                ps[0][:, 0:512], Bt[:, cc, j * 128:(j + 1) * 128],
                Wt[:, cc, :], start=(cc == 0), stop=(cc == 7)))
            for cc in range(8)],
            reads=(bW, bBt, psB[0]), writes=(psB[0],))
        P.emit("V", lambda v, j=j: v.tensor_copy(
            Cv4[:, j, :, 0:64], ps[0][:, 0:512].rearrange("p (h e) -> p h e", e=64)),
            reads=(psB[0],), writes=(bC,))
        P.emit("T", [
            (lambda t, cc=cc, j=j: t.matmul(
                ps[1][:, 0:512], Bt[:, cc, j * 128:(j + 1) * 128],
                Wt2[:, cc, :], start=(cc == 0), stop=(cc == 7)))
            for cc in range(8)],
            reads=(bW2, bBt, psB[1]), writes=(psB[1],))
        P.emit("V", lambda v, j=j: v.tensor_add(s_X[:, 0:512], ps[1][:, 0:512],
                                                K_bo[:, :]),
               reads=(psB[1], kbufs["C_boR"]), writes=(bXc,))
        P.emit("S", lambda sc, j=j: sc.activation(C_og[:, j, :], s_X[:, 0:512],
                                                  AF.Sigmoid),
               reads=(bXc,), writes=(bC,))
        P.emit("T", [
            (lambda t, cc=cc, j=j: t.matmul(
                ps[2][:, 0:16], Bt[:, cc, j * 128:(j + 1) * 128],
                K_wif[:, cc, :], start=(cc == 0), stop=(cc == 7)))
            for cc in range(8)],
            reads=(kbufs["C_WifT"], bBt, psB[2]), writes=(psB[2],))
        P.emit("V", lambda v, j=j: v.tensor_add(s_g[:, j, :], ps[2][:, 0:16],
                                                K_bif[:, :]),
               reads=(psB[2], kbufs["C_bifR"]), writes=(P.buf("s_g"),))

    DUMP("d_q", [128, 16, 512], BF16, lambda: A_q[:, :, :], (bA,))
    DUMP("d_k", [128, 16, 512], BF16, lambda: A_k[:, :, :], (bA,))
    DUMP("d_og", [128, 16, 512], BF16, lambda: C_og[:, :, :], (bC,))
    DUMP("d_g", [128, 16, 16], BF16, lambda: s_g[:, :, :], (P.buf("s_g"),))
    bstf = [[P.buf(("stf", par, h)) for h in range(8)] for par in range(2)]
    bstb = [[P.buf(("stb", par, h)) for h in range(8)] for par in range(2)]
    P.emit("V", lambda v: v.memset(stf[0][:, :, :], 0.0),
           reads=(), writes=tuple(bstf[0]))
    P.emit("V", lambda v: v.memset(stb[0][:, :, :], 0.0),
           reads=(), writes=tuple(bstb[0]))
    bg = P.buf("s_g"); beq = P.buf("s_eq"); bek = P.buf("s_ek")
    bdd = P.buf("s_dd"); bF = P.buf("s_F")
    bqp = P.buf("s_qp"); bkp = P.buf("s_kp")
    bqpT = P.buf("s_qpT"); bkpT = P.buf("s_kpT")
    for m in range(NT):
        par = m % 2
        P.emit("T", lambda t, m=m: t.matmul(
            ps[0][:, 0:8], K_ut[:, :], s_g[:, m, 8:16], start=True, stop=True),
            reads=(kbufs["UTRI"], bg, psB[0]), writes=(psB[0],))
        P.emit("S", lambda sc: sc.activation(s_eq[:, 0:8], ps[0][:, 0:8], AF.Exp),
               reads=(psB[0],), writes=(beq,))
        P.emit("V", lambda v, m=m: v.tensor_sub(s_dd[:, 0:8], s_g[:, m, 0:8],
                                                ps[0][:, 0:8]),
               reads=(bg, psB[0]), writes=(bdd, psB[0]))
        P.emit("S", lambda sc: sc.activation(s_ek[:, 0:8], s_dd[:, 0:8], AF.Exp),
               reads=(bdd,), writes=(bek,))
        P.emit("T", lambda t, m=m: t.matmul(ps[0][0:1, 8:16], K_on[:, 0:1],
                                            s_g[:, m, 8:16], start=True, stop=True),
               reads=(kbufs["ONEBF"], bg, psB[0]), writes=(psB[0],))
        P.emit("S", lambda sc: sc.activation(s_F[0:1, 0:8], ps[0][0:1, 8:16], AF.Exp),
               reads=(psB[0],), writes=(bF,))
        P.emit("T", lambda t: t.matmul(ps[0][:, 8:16], K_onf[0:1, 0:128],
                                       s_F[0:1, 0:8], start=True, stop=True),
               reads=(bF, kbufs["ONEF"], psB[0]), writes=(psB[0],))
        P.emit("V", lambda v: v.tensor_copy(s_F[:, 0:8], ps[0][:, 8:16]),
               reads=(psB[0],), writes=(bF,))
        def prep_head(h, m=m):
            hp = h % 2
            r0 = hp * 64
            P.emit("V", lambda v, m=m, h=h: v.tensor_scalar(
                s_qp[:, h * 64:(h + 1) * 64], A_q[:, m, h * 64:(h + 1) * 64],
                s_eq[:, h:h + 1], None, ALU.mult),
                reads=(bA, beq), writes=(bqp,))
            P.emit("V", lambda v, m=m, h=h: v.tensor_scalar(
                s_kp[:, h * 64:(h + 1) * 64], A_k[:, m, h * 64:(h + 1) * 64],
                s_ek[:, h:h + 1], None, ALU.mult),
                reads=(bA, bek), writes=(bkp,))
            P.emit("T", lambda t, h=h: t.transpose(
                pb[0][0:64, 0:128], s_qp[:, h * 64:(h + 1) * 64], K_id[:, :]),
                reads=(bqp, kbufs["IDN"], pbB[0]), writes=(pbB[0],))
            P.emit("V", lambda v, h=h, r0=r0: v.tensor_copy(
                s_qpT[r0:r0 + 64, h // 2, :], pb[0][0:64, 0:128]),
                reads=(pbB[0],), writes=(bqpT,))
            P.emit("T", lambda t, h=h: t.transpose(
                pb[1][0:64, 0:128], s_kp[:, h * 64:(h + 1) * 64], K_id[:, :]),
                reads=(bkp, kbufs["IDN"], pbB[1]), writes=(pbB[1],))
            P.emit("V", lambda v, h=h, r0=r0: v.tensor_copy(
                s_kpT[r0:r0 + 64, h // 2, :], pb[1][0:64, 0:128]),
                reads=(pbB[1],), writes=(bkpT,))
        def work_head(h, m=m, par=par):
            hp = h % 2
            r0 = hp * 64
            pe = 1 + (h % 2)
            P.emit("T", lambda t, h=h, r0=r0, pe=pe: t.matmul(
                ps[pe][:, 0:128], s_kpT[r0:r0 + 64, h // 2, :],
                s_qpT[r0:r0 + 64, h // 2, :], start=True, stop=True),
                reads=(bkpT, bqpT, psB[pe]), writes=(psB[pe],))
            bat = P.buf(("AT", h % 2))
            P.emit("V", lambda v, h=h, pe=pe: v.tensor_mul(
                s_AT[h % 2][:, :], ps[pe][:, 0:128], K_ut[:, :]),
                reads=(psB[pe], kbufs["UTRI"]), writes=(bat,))
            ph = 3 + (h % 2)
            P.emit("T", lambda t, h=h, m=m: t.matmul(
                ps[3 + (h % 2)][:, 0:65], s_AT[h % 2][:, :],
                C_v[:, m, h * 65:(h + 1) * 65], start=True, stop=False),
                reads=(bat, bC, psB[ph]), writes=(psB[ph],))
            P.emit("T", lambda t, h=h, r0=r0, par=par: t.matmul(
                ps[3 + (h % 2)][:, 0:65], s_qpT[r0:r0 + 64, h // 2, :],
                stb[par][r0:r0 + 64, h // 2, :], start=False, stop=True),
                reads=(bqpT, bstb[par][h], psB[ph]), writes=(psB[ph],))
            P.emit("T", lambda t, h=h, m=m: t.matmul(
                ps[5][0:64, 0:65], s_kp[:, h * 64:(h + 1) * 64],
                C_v[:, m, h * 65:(h + 1) * 65], start=True, stop=True),
                reads=(bkp, bC, psB[5]), writes=(psB[5],))
            P.emit("S", lambda sc, h=h, ph=ph: sc.activation(
                s_dn[:, h % 4:h % 4 + 1], ps[ph][:, 64:65], AF.Abs),
                reads=(psB[ph],), writes=(P.buf("s_dn"),))
            P.emit("V", lambda v, h=h: v.tensor_scalar_max(
                s_dn[:, h % 4:h % 4 + 1], s_dn[:, h % 4:h % 4 + 1], 1.0),
                reads=(P.buf("s_dn"),), writes=(P.buf("s_dn"),))
            P.emit("V", lambda v, h=h: v.reciprocal(
                s_dn[:, 4 + h % 4:5 + h % 4], s_dn[:, h % 4:h % 4 + 1]),
                reads=(P.buf("s_dn"),), writes=(P.buf("s_dn"),))
            P.emit("V", lambda v, h=h, m=m, ph=ph: v.scalar_tensor_tensor(
                C_v[:, m, h * 64:(h + 1) * 64], ps[ph][:, 0:64],
                s_dn[:, 4 + h % 4:5 + h % 4], C_og[:, m, h * 64:(h + 1) * 64],
                ALU.mult, ALU.mult),
                reads=(psB[ph], P.buf("s_dn"), bC), writes=(bC, psB[ph]))
            P.emit("V", lambda v, h=h, r0=r0, par=par: v.tensor_add(
                stf[1 - par][r0:r0 + 64, h // 2, :], stf[par][r0:r0 + 64, h // 2, :],
                ps[5][0:64, 0:65]),
                reads=(bstf[par][h], psB[5]), writes=(bstf[1 - par][h], psB[5]))
            P.emit("V", lambda v, h=h, r0=r0, par=par: v.tensor_scalar(
                stf[1 - par][r0:r0 + 64, h // 2, :], stf[1 - par][r0:r0 + 64, h // 2, :],
                s_F[r0:r0 + 64, h:h + 1], None, ALU.mult),
                reads=(bstf[1 - par][h], bF), writes=(bstf[1 - par][h],))
            P.emit("V", lambda v, h=h, r0=r0, par=par: v.tensor_copy(
                stb[1 - par][r0:r0 + 64, h // 2, :],
                stf[1 - par][r0:r0 + 64, h // 2, :]),
                reads=(bstf[1 - par][h],), writes=(bstb[1 - par][h],))
        prep_head(0)
        for h in range(8):
            if h + 1 < 8:
                prep_head(h + 1)
            work_head(h)

    DUMP("d_hs", [128, 16, 520], BF16, lambda: C_v[:, :, :], (bC,))
    g_load_ch(P, Wout, dwo.ap().rearrange("(k p) n -> p k n", p=128), "bW2", 2)
    bhn = P.buf("s_hn"); bdump = P.buf("s_dump")
    P.alias(bhn, P.buf("s_qp"), P.buf("s_kp"))
    for m in range(NT):
        P.emit("V", lambda v, m=m: v.scalar_tensor_tensor(
            s_dump[:, 0:512], C_v[:, m, 0:512], 0.0, C_v[:, m, 0:512],
            ALU.add, ALU.mult, accum_out=s_small[:, 2:3]),
            reads=(bC,), writes=(bdump, bsm))
        P.emit("V", lambda v, m=m: v.tensor_reduce(
            s_small[:, 0:1], C_v[:, m, 0:512], AX.X, ALU.add),
            reads=(bC,), writes=(bsm,))
        P.emit("V", lambda v: v.tensor_scalar_mul(s_small[:, 4:5], s_small[:, 0:1],
                                                  1.0 / 512.0), reads=(bsm,), writes=(bsm,))
        P.emit("V", lambda v: v.tensor_scalar(s_small[:, 5:6], s_small[:, 4:5],
                                              s_small[:, 4:5], -1e-5,
                                              ALU.mult, ALU.add), reads=(bsm,), writes=(bsm,))
        P.emit("V", lambda v: v.scalar_tensor_tensor(
            s_small[:, 6:7], s_small[:, 2:3], 1.0 / 512.0, s_small[:, 5:6],
            ALU.mult, ALU.subtract), reads=(bsm,), writes=(bsm,))
        P.emit("S", lambda sc: sc.activation(s_small[:, 7:8], s_small[:, 6:7],
                                             AF.Ln, bias=0.0), reads=(bsm,), writes=(bsm,))
        P.emit("S", lambda sc: sc.activation(s_small[:, 8:9], s_small[:, 7:8],
                                             AF.Exp, bias=0.0, scale=K_sc[:, 1:2]),
               reads=(bsm, kbufs["SCAL"]), writes=(bsm,))
        P.emit("V", lambda v: v.scalar_tensor_tensor(
            s_small[:, 9:10], s_small[:, 4:5], -1.0, s_small[:, 8:9],
            ALU.mult, ALU.mult), reads=(bsm,), writes=(bsm,))
        P.emit("V", lambda v, m=m: v.tensor_scalar(
            s_hn[:, 0:512], C_v[:, m, 0:512], s_small[:, 8:9], s_small[:, 9:10],
            ALU.mult, ALU.add), reads=(bC, bsm), writes=(bhn,))
        P.emit("V", lambda v, m=m: v.tensor_mul(s_hn[:, 0:512], s_hn[:, 0:512],
                                                K_lng[:, :]),
               reads=(bhn, kbufs["C_lngR"]), writes=(bhn,))
        P.emit("V", lambda v, m=m: v.tensor_add(s_dump[:, 0:512], s_hn[:, 0:512],
                                                K_lnb[:, :]),
               reads=(bhn, kbufs["C_lnbR"]), writes=(bdump,))
        for cc in range(4):
            pbi = cc % 2
            P.emit("T", lambda t, cc=cc, pbi=pbi: t.transpose(
                pb[pbi][:, 0:128], s_dump[:, cc * 128:(cc + 1) * 128], K_id[:, :]),
                reads=(bdump, kbufs["IDN"], pbB[pbi]), writes=(pbB[pbi],))
            P.emit("V", lambda v, m=m, cc=cc, pbi=pbi: v.tensor_copy(
                B_hnT[:, cc, m * 128:(m + 1) * 128], pb[pbi][:, 0:128]),
                reads=(pbB[pbi],), writes=(bBt,))
    bX = P.buf("s_X")
    for j in range(NT):
        for half in range(2):
            pi = half
            P.emit("T", [
                (lambda t, cc=cc, j=j, half=half, pi=pi: t.matmul(
                    ps[pi][:, 0:512], B_hnT[:, cc, j * 128:(j + 1) * 128],
                    Wout[:, cc, half * 512:(half + 1) * 512],
                    start=(cc == 0), stop=(cc == 3)))
                for cc in range(4)],
                reads=(bW2, bBt, psB[pi]), writes=(psB[pi],))
            P.emit("V", lambda v, j=j, half=half, pi=pi: v.scalar_tensor_tensor(
                s_X[:, half * 512:(half + 1) * 512], ps[pi][:, 0:512], 0.0,
                R[:, j, half * 512:(half + 1) * 512], ALU.add, ALU.add,
                accum_out=s_small[:, half:half + 1]),
                reads=(psB[pi], bR[j]), writes=(bX, bsm))
        P.emit("V", lambda v: v.scalar_tensor_tensor(
            s_dump[:, 0:1024], s_X[:, 0:1024], 0.0, s_X[:, 0:1024], ALU.add,
            ALU.mult, accum_out=s_small[:, 2:3]),
            reads=(bX,), writes=(bdump, bsm))
        emit_ln_tiles(P, R, bR, j, None, None, s_small, bsm, K_sc, kbufs["SCAL"],
                      s_X[:, 0:1024], (bX,))
    return R, bR


def emit_stage_D(P, out_dram, x_in=None, dbg=False):
    nc = P.nc
    def DUMP(name, shape, dtype, ap_fn, rbufs):
        if not dbg:
            return
        d = P.nc.dram_tensor(name, shape, dtype, kind="ExternalOutput")
        P.emit("Y", lambda y: y.dma_start(d.ap(), ap_fn()), reads=rbufs,
               writes=(), dma=True)
    R = P.sb("R", [128, NT, 1024], F32, OFF_R)
    A_xnT = P.sb("A_xnT", [128, 8, 2048], BF16, OFF_A)
    B_kk = P.sb("B_kk", [128, 8, 2048], BF16, OFF_B)
    C_v = P.sb("C_v", [128, 16, 1040], BF16, OFF_C)
    C_mla = P.sb("C_mla", [128, 16, 1024], BF16, OFF_C)
    C_wd = P.sb("C_wd", [128, 8, 1024], BF16, OFF_C)
    W_kvd = P.sb("W_kvd", [128, 8, 256], BF16, OFF_W)
    W_qd = P.sb("W_qd", [128, 8, 256], BF16, OFF_W + 4096)
    W_kvT = P.sb("W_kvT", [128, 2, 2048], BF16, OFF_W + 8192)
    W_ku = P.sb("W_ku", [128, 2, 512], BF16, OFF_W)
    W_qu = P.sb("W_qu", [128, 2, 512], BF16, OFF_W + 2048)
    W_vu = P.sb("W_vu", [128, 2, 1024], BF16, OFF_W + 4096)
    W_rq = P.sb("W_rq", [128, 2, 512], BF16, OFF_W + 4096)
    W_wo = P.sb("W_wo", [128, 8, 1024], BF16, OFF_W)
    WstD = P.sb("WstD", [128, 8, 1024], BF16, OFF_W)
    s_y0 = P.sb("s_y0", [128, 2048], BF16, OFF_S)
    s_qt = P.sb("s_qt", [128, 512], BF16, OFF_S + 4096)
    s_rec = P.sb("s_rec", [128, 1024], BF16, OFF_S + 5120)
    s_exp = [P.sb(f"s_exp{i}", [128, 512], BF16, OFF_S + 6144 + i * 1024) for i in range(2)] + \
            [P.sb(f"s_exp{2 + i}", [128, 512], BF16, OFF_W + 8192 + i * 1024) for i in range(2)]
    S_qlT = P.sb("S_qlT", [128, 2, 2048], BF16, OFF_S + 8192)
    s_rkw = [P.sb(f"s_rkw{i}", [128, 8, 64], BF16, OFF_S + 4096 + i * 1024) for i in range(2)]
    s_rkwp = [P.sb(f"s_rkwp{i}", [128, 8, 64], BF16, OFF_S + 6144 + i * 1024) for i in range(2)]
    s_X = P.sb("s_X", [128, 1024], F32, OFF_S)
    s_mT = P.sb("s_mT", [128, 1024], F32, OFF_S + 4096)
    s_dump = P.sb("s_dump", [128, 1024], BF16, OFF_S + 8192)
    K_id, K_idf, K_ut, K_on, K_onf, K_sc, ps, pb, psB, pbB, kbufs0 = setup_common(P)
    KZ = OFF_K + 2048
    K_cos = P.sb("K_cosD", [128, 2048], BF16, KZ)
    K_sin = P.sb("K_sinD", [128, 2048], BF16, KZ + 4096)
    K_wln1 = P.sb("K_wln1", [128, 1024], BF16, KZ + 8192)
    K_wln2 = P.sb("K_wln2", [128, 1024], BF16, KZ + 10240)
    K_rw = P.sb("K_rw7", [128, 8, 7], F32, KZ + 12288)
    K_rb = P.sb("K_rb7", [128, 7], F32, KZ + 12512)
    s_t1 = P.sb("s_t1D", [128, 512], BF16, KZ + 12576)
    s_t1k = P.sb("s_t1k", [128, 512], BF16, OFF_S)
    W_rqp = P.sb("W_rqp", [128, 2, 512], BF16, OFF_W + 6144)
    s_small = P.sb("s_small", [128, 64], F32, OFF_S + 10240)
    s_wts = P.sb("s_wts", [128, 16, 8], F32, OFF_S + 10496)
    s_hm = [P.sb(f"s_hm{i}", [128, 256], BF16, OFF_S + 11008 + i * 512) for i in range(2)]
    s_sg = P.sb("s_sg", [128, 512], BF16, OFF_S + 12032)
    kbufs = dict(kbufs0)
    kbufs.update(load_zone(P, [
        ("COSD", K_cos, [128, 2048]), ("SIND", K_sin, [128, 2048]),
        ("D_wln1R", K_wln1, [128, 1024]), ("D_wln2R", K_wln2, [128, 1024]),
        ("D_rw", K_rw, [1024, 7], "(k p) n -> p k n"), ("D_rbR", K_rb, [128, 7]),
    ], native=("COSD", "SIND")))
    dkvd = P.din("D_kvd", [1024, 256], F32)
    dqd = P.din("D_qd", [1024, 256], F32)
    dku = P.din("D_ku", [256, 512], F32)
    dqu = P.din("D_qu", [256, 512], F32)
    dvu = P.din("D_vu", [256, 1024], F32)
    drq = P.din("D_rq", [256, 512], F32)
    drqp = P.din("D_rqp", [256, 512], F32)
    drk = P.din("D_rk", [1024, 512], F32)
    drkp = P.din("D_rkp", [1024, 512], F32)
    dwo = P.din("D_wo", [1024, 1024], F32)
    dgs = P.din("D_gs", [1024, 1024], F32)
    dus = P.din("D_us", [1024, 1024], F32)
    dds = P.din("D_ds", [1024, 1024], F32)
    dgr = P.din("D_gr", [7, 1024, 1024], F32)
    dur = P.din("D_ur", [7, 1024, 1024], F32)
    ddr = P.din("D_dr", [7, 1024, 1024], F32)
    bR = [P.buf(("R", j)) for j in range(NT)]
    bA = P.buf("bA"); bB = P.buf("bB"); bC = P.buf("bC")
    bsm = P.buf("s_small")
    bdump = P.buf("s_dump")
    if x_in is not None:
        _load_R_f32(P, x_in, R, bR)

    for j in range(NT):
        P.emit("V", lambda v, j=j: v.scalar_tensor_tensor(
            s_dump[:, 0:1024], R[:, j, :], 0.0, R[:, j, :], ALU.add, ALU.mult,
            accum_out=s_small[:, 2:3]), reads=(bR[j],), writes=(bdump, bsm))
        P.emit("S", lambda sc: sc.activation(s_small[:, 7:8], s_small[:, 2:3],
                                             AF.Ln, bias=0.0), reads=(bsm,), writes=(bsm,))
        P.emit("S", lambda sc: sc.activation(s_small[:, 8:9], s_small[:, 7:8],
                                             AF.Exp, bias=K_sc[:, 7:8], scale=K_sc[:, 6:7]),
               reads=(bsm, kbufs["SCAL"]), writes=(bsm,))
        P.emit("V", lambda v: v.tensor_scalar_add(s_small[:, 9:10], s_small[:, 8:9],
                                                  1e-5), reads=(bsm,), writes=(bsm,))
        P.emit("V", lambda v: v.reciprocal(s_small[:, 10:11], s_small[:, 9:10]),
               reads=(bsm,), writes=(bsm,))
        P.emit("V", lambda v, j=j: v.scalar_tensor_tensor(
            s_dump[:, 0:1024], R[:, j, :], s_small[:, 10:11], K_wln1[:, :],
            ALU.mult, ALU.mult), reads=(bR[j], bsm, kbufs["D_wln1R"]), writes=(bdump,))
        for cc in range(8):
            pbi = cc % 2
            P.emit("T", lambda t, cc=cc, pbi=pbi: t.transpose(
                pb[pbi][:, 0:128], s_dump[:, cc * 128:(cc + 1) * 128], K_id[:, :]),
                reads=(bdump, kbufs["IDN"], pbB[pbi]), writes=(pbB[pbi],))
            P.emit("V", lambda v, j=j, cc=cc, pbi=pbi: v.tensor_copy(
                A_xnT[:, cc, j * 128:(j + 1) * 128], pb[pbi][:, 0:128]),
                reads=(pbB[pbi],), writes=(bA,))

    g_load_ch(P, W_kvd, dkvd.ap().rearrange("(k p) n -> p k n", p=128), "bkvd")
    g_load_ch(P, W_qd, dqd.ap().rearrange("(k p) n -> p k n", p=128), "bqd")
    bkvT = P.buf("W_kvT"); bqlT = P.buf("S_qlT")
    P.alias(bqlT, bdump, bsm)
    for lc in range(2):
        for w in range(4):
            wc = slice(w * 512, (w + 1) * 512)
            pi = (lc * 4 + w) % 2
            P.emit("T", [
                (lambda t, cc=cc, lc=lc, wc=wc, pi=pi: t.matmul(
                    ps[pi][:, 0:512], W_kvd[:, cc, lc * 128:(lc + 1) * 128],
                    A_xnT[:, cc, wc], start=(cc == 0), stop=(cc == 7)))
                for cc in range(8)],
                reads=(P.buf("bkvd"), bA, psB[pi]), writes=(psB[pi],))
            P.emit("S", lambda sc, lc=lc, wc=wc, pi=pi: sc.copy(
                W_kvT[:, lc, wc], ps[pi][:, 0:512]), reads=(psB[pi],), writes=(bkvT,))
            P.emit("T", [
                (lambda t, cc=cc, lc=lc, wc=wc, pi=pi: t.matmul(
                    ps[2 + pi][:, 0:512], W_qd[:, cc, lc * 128:(lc + 1) * 128],
                    A_xnT[:, cc, wc], start=(cc == 0), stop=(cc == 7)))
                for cc in range(8)],
                reads=(P.buf("bqd"), bA, psB[2 + pi]), writes=(psB[2 + pi],))
            P.emit("S", lambda sc, lc=lc, wc=wc, pi=pi: sc.copy(
                S_qlT[:, lc, wc], ps[2 + pi][:, 0:512]), reads=(psB[2 + pi],), writes=(bqlT,))

    P.alias(P.buf("W_ku"), P.buf("bkvd"))
    P.alias(P.buf("W_qu"), P.buf("bkvd"))
    P.alias(P.buf("W_vu"), P.buf("bqd"))
    P.alias(P.buf("W_rq"), P.buf("bqd"))
    g_load(P, W_ku[:, :, :], dku.ap().rearrange("(k p) n -> p k n", p=128), "W_ku")
    g_load(P, W_qu[:, :, :], dqu.ap().rearrange("(k p) n -> p k n", p=128), "W_qu")
    g_load(P, W_vu[:, :, :], dvu.ap().rearrange("(k p) n -> p k n", p=128), "W_vu")
    Cv4 = C_v[:, :, :].rearrange("p t (h e) -> p t h e", e=65)
    P.emit("V", lambda v: v.memset(Cv4[:, :, :, 64:65], 1.0), reads=(bC,), writes=(bC,))
    bt1 = P.buf("s_t1D"); bt1k = P.buf("s_t1k")
    for j in range(NT):
        for half in range(2):
            pi = 2 + half
            P.emit("T", [
                (lambda t, lc=lc, j=j, half=half, pi=pi: t.matmul(
                    ps[pi][:, 0:512], W_kvT[:, lc, j * 128:(j + 1) * 128],
                    W_vu[:, lc, half * 512:(half + 1) * 512],
                    start=(lc == 0), stop=(lc == 1)))
                for lc in range(2)],
                reads=(bkvT, P.buf("W_vu"), psB[pi]), writes=(psB[pi],))
            P.emit("V", lambda v, j=j, half=half, pi=pi: v.tensor_copy(
                Cv4[:, j, half * 8:(half + 1) * 8, 0:64],
                ps[pi][:, 0:512].rearrange("p (h e) -> p h e", e=64)),
                reads=(psB[pi],), writes=(bC,))

    P.alias(P.buf("W_rq"), P.buf("W_vu"))
    P.alias(P.buf("W_rqp"), P.buf("W_vu"))
    g_load(P, W_rq[:, :, :], drq.ap().rearrange("(k p) n -> p k n", p=128), "W_rq")
    g_load(P, W_rqp[:, :, :], drqp.ap().rearrange("(k p) n -> p k n", p=128), "W_rqp")
    for hp in range(8):
        kwb = s_rkw[hp % 2]; kwpb = s_rkwp[hp % 2]
        bkw = P.buf(("rkw", hp % 2)); bkwp = P.buf(("rkwp", hp % 2))
        for c0 in range(0, 8, 2):
            P.emit("G", lambda g, hp=hp, kwb=kwb, c0=c0: g.dma_start(
                kwb[:, c0:c0 + 2, :], drk.ap().rearrange("(k p) n -> p k n", p=128)
                [:, c0:c0 + 2, hp * 64:(hp + 1) * 64]),
                reads=(bkw,), writes=(bkw,), dma=True)
            P.emit("G", lambda g, hp=hp, kwpb=kwpb, c0=c0: g.dma_start(
                kwpb[:, c0:c0 + 2, :], drkp.ap().rearrange("(k p) n -> p k n", p=128)
                [:, c0:c0 + 2, hp * 64:(hp + 1) * 64]),
                reads=(bkwp,), writes=(bkwp,), dma=True)
        for w in range(4):
            wc = slice(w * 512, (w + 1) * 512)
            grp = []
            for hh in range(2):
                h = hp * 2 + hh
                b0 = hh * 64
                grp.append(lambda t, h=h, b0=b0, wc=wc: t.matmul(
                    ps[0][b0:b0 + 32, 0:512], W_ku[:, 0, h * 32:(h + 1) * 32],
                    W_kvT[:, 0, wc], start=True, stop=False,
                    tile_position=(0, b0)))
                grp.append(lambda t, h=h, b0=b0, wc=wc: t.matmul(
                    ps[0][b0:b0 + 32, 0:512], W_ku[:, 1, h * 32:(h + 1) * 32],
                    W_kvT[:, 1, wc], start=False, stop=True,
                    tile_position=(0, b0)))
            for hh in range(2):
                b0 = hh * 64 + 32
                for cc in range(8):
                    grp.append(lambda t, hh=hh, b0=b0, cc=cc, wc=wc, kwb=kwb: t.matmul(
                        ps[0][b0:b0 + 32, 0:512], kwb[:, cc, hh * 32:(hh + 1) * 32],
                        A_xnT[:, cc, wc], start=(cc == 0), stop=(cc == 7),
                        tile_position=(0, b0)))
            P.emit("T", grp, reads=(P.buf("W_ku"), bkvT, bkw, bA, psB[0]),
                   writes=(psB[0],))
            grp2 = []
            for hh in range(2):
                b0 = hh * 64 + 32
                for cc in range(8):
                    grp2.append(lambda t, hh=hh, b0=b0, cc=cc, wc=wc, kwpb=kwpb: t.matmul(
                        ps[1][b0:b0 + 32, 0:512], kwpb[:, cc, hh * 32:(hh + 1) * 32],
                        A_xnT[:, cc, wc], start=(cc == 0), stop=(cc == 7),
                        tile_position=(0, b0)))
            P.emit("T", grp2, reads=(bkwp, bA, psB[1]), writes=(psB[1],))
            for hh in range(2):
                b0 = hh * 64
                P.emit("V", lambda v, hp=hp, b0=b0, wc=wc: v.tensor_copy(
                    B_kk[b0:b0 + 32, hp, wc], ps[0][b0:b0 + 32, 0:512]),
                    reads=(psB[0],), writes=(bB,))
                rb = b0 + 32
                P.emit("V", lambda v, rb=rb, wc=wc: v.tensor_mul(
                    s_t1k[rb:rb + 32, 0:512], ps[0][rb:rb + 32, 0:512],
                    K_cos[rb:rb + 32, wc]),
                    reads=(psB[0], kbufs["COSD"]), writes=(bt1k,))
                P.emit("V", lambda v, hp=hp, rb=rb, wc=wc: v.tensor_mul(
                    B_kk[rb:rb + 32, hp, wc], ps[1][rb:rb + 32, 0:512],
                    K_sin[rb:rb + 32, wc]),
                    reads=(psB[1], kbufs["SIND"]), writes=(bB,))
                P.emit("G", lambda g, hp=hp, rb=rb, wc=wc: g.tensor_add(
                    B_kk[rb:rb + 32, hp, wc], B_kk[rb:rb + 32, hp, wc],
                    s_t1k[rb:rb + 32, 0:512]),
                    reads=(bt1k, bB), writes=(bB,))
    DUMP("d_xnT", [128, 8, 2048], BF16, lambda: A_xnT[:, :, :], (bA,))
    DUMP("d_kk", [128, 8, 2048], BF16, lambda: B_kk[:, :, :], (bB,))
    DUMP("d_v", [128, 16, 1040], BF16, lambda: C_v[:, :, :], (bC,))
    by0 = P.buf("s_y0")
    P.alias(by0, P.buf("s_t1k"))
    P.alias(P.buf("s_qt"), P.buf(("rkw", 0)))
    P.alias(P.buf("s_rec"), P.buf(("rkw", 1)), P.buf(("rkwp", 0)))
    P.alias(P.buf(("exp", 0)), P.buf(("rkwp", 0)), P.buf(("rkwp", 1)))
    P.alias(P.buf(("exp", 1)), P.buf("W_kvT"))
    P.alias(P.buf(("exp", 2)), P.buf("W_kvT"))
    P.alias(P.buf(("exp", 3)), P.buf("W_kvT"))
    for hp in range(8):
        for w in range(4):
            wc = slice(w * 512, (w + 1) * 512)
            for hh in range(2):
                h = hp * 2 + hh
                r0 = hh * 64
                grp = []
                for lc in range(2):
                    grp.append(lambda t, lc=lc, h=h, r0=r0, wc=wc: t.matmul(
                        ps[0][r0:r0 + 32, 0:512], W_qu[:, lc, h * 32:(h + 1) * 32],
                        S_qlT[:, lc, wc], start=(lc == 0), stop=(lc == 1),
                        tile_position=(0, r0)))
                for lc in range(2):
                    grp.append(lambda t, lc=lc, h=h, r0=r0, wc=wc: t.matmul(
                        ps[0][r0 + 32:r0 + 64, 0:512], W_rq[:, lc, h * 32:(h + 1) * 32],
                        S_qlT[:, lc, wc], start=(lc == 0), stop=(lc == 1),
                        tile_position=(0, r0 + 32)))
                P.emit("T", grp, reads=(P.buf("W_qu"), P.buf("W_rq"), bqlT, psB[0]),
                       writes=(psB[0],))
                P.emit("T", [
                    (lambda t, lc=lc, h=h, r0=r0, wc=wc: t.matmul(
                        ps[1][r0 + 32:r0 + 64, 0:512], W_rqp[:, lc, h * 32:(h + 1) * 32],
                        S_qlT[:, lc, wc], start=(lc == 0), stop=(lc == 1),
                        tile_position=(0, r0 + 32)))
                    for lc in range(2)],
                    reads=(P.buf("W_rqp"), bqlT, psB[1]), writes=(psB[1],))
                bqt = P.buf("s_qt")
                P.emit("V", lambda v, r0=r0: v.tensor_copy(
                    s_qt[r0:r0 + 32, 0:512], ps[0][r0:r0 + 32, 0:512]),
                    reads=(psB[0],), writes=(bqt,))
                rb = r0 + 32
                P.emit("V", lambda v, rb=rb, wc=wc: v.tensor_mul(
                    s_t1[rb:rb + 32, 0:512], ps[0][rb:rb + 32, 0:512],
                    K_cos[rb:rb + 32, wc]),
                    reads=(psB[0], kbufs["COSD"]), writes=(bt1,))
                P.emit("V", lambda v, rb=rb, wc=wc: v.tensor_mul(
                    s_qt[rb:rb + 32, 0:512], ps[1][rb:rb + 32, 0:512],
                    K_sin[rb:rb + 32, wc]),
                    reads=(psB[1], kbufs["SIND"]), writes=(bqt,))
                P.emit("V", lambda v, rb=rb: v.tensor_add(
                    s_qt[rb:rb + 32, 0:512], s_qt[rb:rb + 32, 0:512],
                    s_t1[rb:rb + 32, 0:512]),
                    reads=(bt1, bqt), writes=(bqt,))
            nS = 4 * w + 4
            j0s = [max(0, sck * 128 - w * 512) for sck in range(nS)]
            ROT2 = ((4, 5), (2, 3))
            def emit_av(sck, j0, hp=hp, w=w, nS=nS):
                for hh in range(2):
                    h = hp * 2 + hh
                    pc = 6 + hh
                    pc = hh
                    be = P.buf(("exp", hh * 2 + sck % 2))
                    P.emit("T", lambda t, sck=sck, j0=j0, pc=pc, h=h, hh=hh, nS=nS: t.matmul(
                        ps[pc][0:65, j0:512], C_v[:, sck, h * 65:(h + 1) * 65],
                        s_exp[hh * 2 + sck % 2][:, j0:512],
                        start=(sck == 0), stop=(sck == nS - 1)),
                        reads=(bC, be, psB[pc]), writes=(psB[pc],))
            for sck in range(nS):
                j0 = j0s[sck]
                pr = ROT2[sck % 2]
                P.emit("T", [
                    (lambda t, sck=sck, j0=j0, hp=hp, pr=pr, hh=hh, r0=r0: t.matmul(
                        ps[pr[hh]][:, j0:512],
                        B_kk[r0:r0 + 64, hp, sck * 128:(sck + 1) * 128],
                        s_qt[r0:r0 + 64, j0:512], start=True, stop=True,
                        tile_position=(r0, 0)))
                    for hh, r0 in ((0, 0), (1, 64))],
                    reads=(bB, P.buf("s_qt"), psB[pr[0]], psB[pr[1]]),
                    writes=(psB[pr[0]], psB[pr[1]]))
                for hh in range(2):
                    be = P.buf(("exp", hh * 2 + sck % 2))
                    P.emit("S", lambda sc, sck=sck, j0=j0, pe=pr[hh], hh=hh: sc.activation(
                        s_exp[hh * 2 + sck % 2][:, j0:512], ps[pe][:, j0:512], AF.Exp,
                        bias=0.0, scale=K_sc[:, 0:1]),
                        reads=(psB[pr[hh]], kbufs["SCAL"]), writes=(be,))
                    if sck * 128 >= w * 512:
                        P.emit("V", lambda v, sck=sck, j0=j0, hh=hh: v.tensor_mul(
                            s_exp[hh * 2 + sck % 2][:, j0:j0 + 128],
                            s_exp[hh * 2 + sck % 2][:, j0:j0 + 128], K_ut[:, :]),
                            reads=(be, kbufs["UTRI"]), writes=(be,))
                if sck >= 1:
                    emit_av(sck - 1, j0s[sck - 1])
            emit_av(nS - 1, j0s[nS - 1])
            for hh in range(2):
                h = hp * 2 + hh
                r0 = hh * 64
                pc = hh
                bpc = psB[pc]
                brec = P.buf("s_rec")
                P.emit("S", lambda sc, pc=pc: sc.activation(
                    s_rec[64:65, 512:1024], ps[pc][64:65, 0:512], AF.Ln, bias=0.0),
                    reads=(bpc,), writes=(brec,))
                P.emit("S", lambda sc: sc.activation(
                    s_rec[64:65, 0:512], s_rec[64:65, 512:1024], AF.Exp,
                    bias=0.0, scale=K_sc[64:65, 8:9]),
                    reads=(brec, kbufs["SCAL"]), writes=(brec,))
                P.emit("T", lambda t: t.matmul(
                    ps[5][0:64, 0:512], K_on[64:65, 0:64], s_rec[64:65, 0:512],
                    start=True, stop=True),
                    reads=(brec, kbufs["ONEBF"], psB[5]), writes=(psB[5],))
                P.emit("V", lambda v: v.tensor_copy(s_rec[0:64, 0:512],
                                                    ps[5][0:64, 0:512]),
                    reads=(psB[5],), writes=(brec,))
                if hp == 0:
                    dst = lambda wc=wc, r0=r0: s_y0[r0:r0 + 64, wc]
                    dstb = by0
                else:
                    dst = lambda wc=wc, r0=r0, hp=hp: B_kk[r0:r0 + 64, hp - 1, wc]
                    dstb = bB
                P.emit("V", lambda v, dst=dst, pc=pc: v.tensor_mul(
                    dst(), ps[pc][0:64, 0:512], s_rec[0:64, 0:512]),
                    reads=(bpc, brec), writes=(dstb,))

    P.alias(P.buf("W_wo"), P.buf("W_ku"), P.buf("W_qu"), P.buf("W_vu"),
            P.buf("W_rq"), P.buf("bkvd"), P.buf("bqd"), bkvT,
            P.buf(("exp", 2)), P.buf(("exp", 3)))
    g_load_ch(P, W_wo, dwo.ap().rearrange("(k p) n -> p k n", p=128), "W_wo", 4)
    P.alias(P.buf("C_mla"), bC)
    bmla = P.buf("C_mla")
    for j in range(NT):
        for half in range(2):
            pi = half
            def mml(cc, j=j, half=half, pi=pi):
                def f(t):
                    src = s_y0[:, j * 128:(j + 1) * 128] if cc == 0 else \
                        B_kk[:, cc - 1, j * 128:(j + 1) * 128]
                    return t.matmul(ps[pi][:, 0:512], src,
                                    W_wo[:, cc, half * 512:(half + 1) * 512],
                                    start=(cc == 0), stop=(cc == 7))
                return f
            P.emit("T", [mml(cc) for cc in range(8)],
                   reads=(P.buf("W_wo"), bB, by0, psB[pi]), writes=(psB[pi],))
            P.emit("V", lambda v, j=j, half=half, pi=pi: v.scalar_tensor_tensor(
                R[:, j, half * 512:(half + 1) * 512], ps[pi][:, 0:512], 1.0,
                R[:, j, half * 512:(half + 1) * 512], ALU.mult, ALU.add),
                reads=(psB[pi], bR[j]), writes=(bR[j],))
            P.emit("S", lambda sc, j=j, half=half, pi=pi: sc.copy(
                C_mla[:, j, half * 512:(half + 1) * 512], ps[pi][:, 0:512]),
                reads=(psB[pi],), writes=(bmla, psB[pi]))

    if dbg:
        dU = P.nc.dram_tensor("d_u", [T, CD], F32, kind="ExternalOutput")
        dur2 = dU.ap().rearrange("(j p) c -> p j c", p=128)
        for j in range(NT):
            P.emit("Y", lambda y, j=j: y.dma_start(dur2[:, j, :], R[:, j, :]),
                   reads=(bR[j],), writes=(), dma=True)
        DUMP("d_mla", [128, 16, 1024], BF16, lambda: C_mla[:, :, :], (bmla,))
    P.alias(P.buf("s_X2"), by0)
    P.alias(P.buf("s_mT2"), P.buf("s_qt"), P.buf("s_rec"), P.buf(("exp", 0)),
            P.buf(("exp", 1)))
    P.alias(bdump, bqlT)
    P.alias(bsm, bqlT)
    P.alias(P.buf("s_wts"), bqlT)
    P.alias(P.buf(("hm", 0)), bqlT)
    P.alias(P.buf(("hm", 1)), bqlT)
    P.alias(P.buf("s_sg"), bqlT)
    bX = P.buf("s_X2"); bmT = P.buf("s_mT2")
    bwts = P.buf("s_wts")
    for j in range(NT):
        P.emit("V", lambda v, j=j: v.scalar_tensor_tensor(
            s_dump[:, 0:1024], R[:, j, :], 0.0, R[:, j, :], ALU.add, ALU.mult,
            accum_out=s_small[:, 2:3]), reads=(bR[j],), writes=(bdump, bsm))
        P.emit("S", lambda sc: sc.activation(s_small[:, 7:8], s_small[:, 2:3],
                                             AF.Ln, bias=0.0), reads=(bsm,), writes=(bsm,))
        P.emit("S", lambda sc: sc.activation(s_small[:, 8:9], s_small[:, 7:8],
                                             AF.Exp, bias=K_sc[:, 7:8], scale=K_sc[:, 6:7]),
               reads=(bsm, kbufs["SCAL"]), writes=(bsm,))
        P.emit("V", lambda v: v.tensor_scalar_add(s_small[:, 9:10], s_small[:, 8:9],
                                                  1e-5), reads=(bsm,), writes=(bsm,))
        P.emit("V", lambda v: v.reciprocal(s_small[:, 10:11], s_small[:, 9:10]),
               reads=(bsm,), writes=(bsm,))
        P.emit("V", lambda v, j=j: v.scalar_tensor_tensor(
            s_X[:, 0:1024], R[:, j, :], s_small[:, 10:11], K_wln2[:, :],
            ALU.mult, ALU.mult), reads=(bR[j], bsm, kbufs["D_wln2R"]), writes=(bX,))
        P.emit("S", lambda sc: sc.copy(s_dump[:, 0:1024], s_X[:, 0:1024]),
               reads=(bX,), writes=(bdump,))
        for cc in range(8):
            pbi = cc % 2
            P.emit("T", lambda t, cc=cc, pbi=pbi: t.transpose(
                pb[pbi][:, 0:128], s_dump[:, cc * 128:(cc + 1) * 128], K_id[:, :]),
                reads=(bdump, kbufs["IDN"], pbB[pbi]), writes=(pbB[pbi],))
            P.emit("V", lambda v, j=j, cc=cc, pbi=pbi: v.tensor_copy(
                A_xnT[:, cc, j * 128:(j + 1) * 128], pb[pbi][:, 0:128]),
                reads=(pbB[pbi],), writes=(bA,))
        for cc in range(8):
            pi = cc % 2
            P.emit("T", lambda t, cc=cc, pi=pi: t.transpose(
                ps[pi][:, 0:128], s_X[:, cc * 128:(cc + 1) * 128], K_idf[:, :]),
                reads=(bX, kbufs["IDF"], psB[pi]), writes=(psB[pi],))
            P.emit("V", lambda v, cc=cc, pi=pi: v.tensor_copy(
                s_mT[:, cc * 128:(cc + 1) * 128], ps[pi][:, 0:128]),
                reads=(psB[pi],), writes=(bmT,))
        P.emit("T", [
            (lambda t, cc=cc: t.matmul(
                ps[4][:, 0:7], s_mT[:, cc * 128:(cc + 1) * 128],
                K_rw[:, cc, :], start=(cc == 0), stop=(cc == 7)))
            for cc in range(8)],
            reads=(bmT, kbufs["D_rw"], psB[4]), writes=(psB[4],))
        P.emit("V", lambda v: v.tensor_add(s_small[:, 16:23], ps[4][:, 0:7],
                                           K_rb[:, 0:7]),
               reads=(psB[4], kbufs["D_rbR"]), writes=(bsm, psB[4]))
        P.emit("S", lambda sc: sc.activation(s_small[:, 16:23], s_small[:, 16:23],
                                             AF.Sigmoid), reads=(bsm,), writes=(bsm,))
        P.emit("V", lambda v: v.tensor_reduce(s_small[:, 2:3], s_small[:, 16:23],
                                              AX.X, ALU.max), reads=(bsm,), writes=(bsm,))
        P.emit("V", lambda v: v.tensor_single_scalar(s_small[:, 24:31], s_small[:, 16:23],
                                                     s_small[:, 2:3], ALU.subtract),
               reads=(bsm,), writes=(bsm,))
        P.emit("S", lambda sc: sc.activation(s_small[:, 32:39], s_small[:, 24:31],
                                             AF.Exp, bias=0.0, scale=K_sc[:, 5:6]),
               reads=(bsm, kbufs["SCAL"]), writes=(bsm,))
        P.emit("V", lambda v: v.scalar_tensor_tensor(
            s_small[:, 40:47], s_small[:, 32:39], -1e30, s_small[:, 16:23],
            ALU.mult, ALU.add), reads=(bsm,), writes=(bsm,))
        P.emit("V", lambda v: v.tensor_reduce(s_small[:, 3:4], s_small[:, 40:47],
                                              AX.X, ALU.max), reads=(bsm,), writes=(bsm,))
        P.emit("V", lambda v: v.tensor_single_scalar(s_small[:, 24:31], s_small[:, 16:23],
                                                     s_small[:, 3:4], ALU.subtract),
               reads=(bsm,), writes=(bsm,))
        P.emit("V", lambda v: v.tensor_scalar_min(s_small[:, 24:31], s_small[:, 24:31],
                                                  0.0), reads=(bsm,), writes=(bsm,))
        P.emit("S", lambda sc: sc.activation(s_small[:, 32:39], s_small[:, 24:31],
                                             AF.Exp, bias=0.0, scale=K_sc[:, 5:6]),
               reads=(bsm, kbufs["SCAL"]), writes=(bsm,))
        P.emit("V", lambda v: v.tensor_mul(s_small[:, 40:47], s_small[:, 16:23],
                                           s_small[:, 32:39]), reads=(bsm,), writes=(bsm,))
        P.emit("V", lambda v: v.tensor_reduce(s_small[:, 4:5], s_small[:, 40:47],
                                              AX.X, ALU.add), reads=(bsm,), writes=(bsm,))
        P.emit("V", lambda v: v.reciprocal(s_small[:, 5:6], s_small[:, 4:5]),
               reads=(bsm,), writes=(bsm,))
        P.emit("V", lambda v, j=j: v.tensor_scalar(
            s_wts[:, j, 0:7], s_small[:, 40:47], s_small[:, 5:6], None,
            ALU.mult), reads=(bsm,), writes=(bwts,))
        P.emit("V", lambda v, j=j: v.scalar_tensor_tensor(
            R[:, j, :], R[:, j, :], 2.0, C_mla[:, j, :], ALU.mult, ALU.subtract),
            reads=(bR[j], bmla), writes=(bR[j],))

    P.alias(bB, by0)
    P.alias(P.buf("C_wd"), bmla, bC)
    bwd = P.buf("C_wd")
    bsg = P.buf("s_sg")
    for e in range(-1, 7):
        gsrc = dgs.ap() if e < 0 else dgr.ap()[e]
        usrc = dus.ap() if e < 0 else dur.ap()[e]
        dsrc = dds.ap() if e < 0 else ddr.ap()[e]
        par = (e + 1) % 2
        if par == 1:
            gh0 = lambda cc, isl: WstD[:, cc, isl]
            uh0 = lambda cc, isl: WstD[:, cc, slice(512 + isl.start, 512 + isl.stop)]
        else:
            gh0 = lambda cc, isl: B_kk[:, cc, isl]
            uh0 = lambda cc, isl: B_kk[:, cc, slice(1024 + isl.start, 1024 + isl.stop)]
        b_h0 = P.buf(("Dgu0", par))
        if e == -1:
            P.alias(b_h0, bB)
            for c0 in range(0, 8, 2):
                P.emit("G", lambda g, c0=c0, gsrc=gsrc: g.dma_start(
                    B_kk[:, c0:c0 + 2, 0:512],
                    gsrc.rearrange("(k p) n -> p k n", p=128)[:, c0:c0 + 2, 0:512]),
                    reads=(), writes=(b_h0, bB), dma=True)
                P.emit("G", lambda g, c0=c0, usrc=usrc: g.dma_start(
                    B_kk[:, c0:c0 + 2, 1024:1536],
                    usrc.rearrange("(k p) n -> p k n", p=128)[:, c0:c0 + 2, 0:512]),
                    reads=(), writes=(b_h0, bB), dma=True)
        b_h1 = P.buf("Dgu1")
        P.alias(b_h1, bB)
        for c0 in range(0, 8, 2):
            P.emit("G", lambda g, c0=c0, gsrc=gsrc: g.dma_start(
                B_kk[:, c0:c0 + 2, 512:1024],
                gsrc.rearrange("(k p) n -> p k n", p=128)[:, c0:c0 + 2, 512:1024]),
                reads=(), writes=(b_h1, bB), dma=True)
            P.emit("G", lambda g, c0=c0, usrc=usrc: g.dma_start(
                B_kk[:, c0:c0 + 2, 1536:2048],
                usrc.rearrange("(k p) n -> p k n", p=128)[:, c0:c0 + 2, 512:1024]),
                reads=(), writes=(b_h1, bB), dma=True)
        for c0 in range(0, 8, 2):
            P.emit("G", lambda g, c0=c0, dsrc=dsrc: g.dma_start(
                C_wd[:, c0:c0 + 2, :],
                dsrc.rearrange("(k p) n -> p k n", p=128)[:, c0:c0 + 2, :]),
                reads=(), writes=(bwd,), dma=True)
        for tw in range(8):
            tc = slice(tw * 256, (tw + 1) * 256)
            def emit_gu(ic, tc=tc, gh0=gh0, uh0=uh0, b_h0=b_h0, b_h1=b_h1):
                if ic < 4:
                    gs = lambda cc, ic=ic: gh0(cc, slice(ic * 128, (ic + 1) * 128))
                    us = lambda cc, ic=ic: uh0(cc, slice(ic * 128, (ic + 1) * 128))
                    bsrc = b_h0
                else:
                    gs = lambda cc, ic=ic: B_kk[:, cc, ic * 128:(ic + 1) * 128]
                    us = lambda cc, ic=ic: B_kk[:, cc, 1024 + ic * 128:1024 + (ic + 1) * 128]
                    bsrc = b_h1
                P.emit("T", [
                    (lambda t, cc=cc, ic=ic, tc=tc, gs=gs: t.matmul(
                        ps[0][:, 0:256], gs(cc),
                        A_xnT[:, cc, tc], start=(cc == 0), stop=(cc == 7)))
                    for cc in range(8)],
                    reads=(bsrc, bA, psB[0]), writes=(psB[0],))
                P.emit("T", [
                    (lambda t, cc=cc, ic=ic, tc=tc, us=us: t.matmul(
                        ps[1][:, 0:256], us(cc),
                        A_xnT[:, cc, tc], start=(cc == 0), stop=(cc == 7)))
                    for cc in range(8)],
                    reads=(bsrc, bA, psB[1]), writes=(psB[1],))
                P.emit("S", lambda sc, ic=ic: sc.activation(
                    s_sg[:, 0:256], ps[0][:, 0:256], AF.Silu),
                       reads=(psB[0],), writes=(bsg,))
                bh = P.buf(("hm", ic % 2))
                P.emit("V", lambda v, ic=ic: v.tensor_mul(
                    s_hm[ic % 2][:, 0:256], s_sg[:, 0:256], ps[1][:, 0:256]),
                    reads=(bsg, psB[1]), writes=(bh,))
            if tw == 4 and e + 1 < 7:
                npar = (e + 2) % 2
                ngsrc = dgr.ap()[e + 1]
                nusrc = dur.ap()[e + 1]
                bnx = P.buf(("Dgu0", npar))
                if npar == 1:
                    P.alias(bnx, P.buf("W_wo"))
                    for c0 in range(0, 8, 2):
                        P.emit("G", lambda g, c0=c0, ngsrc=ngsrc: g.dma_start(
                            WstD[:, c0:c0 + 2, 0:512],
                            ngsrc.rearrange("(k p) n -> p k n", p=128)
                            [:, c0:c0 + 2, 0:512]),
                            reads=(), writes=(bnx,), dma=True)
                        P.emit("G", lambda g, c0=c0, nusrc=nusrc: g.dma_start(
                            WstD[:, c0:c0 + 2, 512:1024],
                            nusrc.rearrange("(k p) n -> p k n", p=128)
                            [:, c0:c0 + 2, 0:512]),
                            reads=(), writes=(bnx,), dma=True)
                else:
                    P.alias(bnx, bB)
                    for c0 in range(0, 8, 2):
                        P.emit("G", lambda g, c0=c0, ngsrc=ngsrc: g.dma_start(
                            B_kk[:, c0:c0 + 2, 0:512],
                            ngsrc.rearrange("(k p) n -> p k n", p=128)
                            [:, c0:c0 + 2, 0:512]),
                            reads=(), writes=(bnx, bB), dma=True)
                        P.emit("G", lambda g, c0=c0, nusrc=nusrc: g.dma_start(
                            B_kk[:, c0:c0 + 2, 1024:1536],
                            nusrc.rearrange("(k p) n -> p k n", p=128)
                            [:, c0:c0 + 2, 0:512]),
                            reads=(), writes=(bnx, bB), dma=True)
            emit_gu(0)
            for ic in range(8):
                if ic + 1 < 8:
                    emit_gu(ic + 1)
                bh = P.buf(("hm", ic % 2))
                for tt in range(2):
                    for dg in range(2):
                        pj = 2 + tt * 2 + dg
                        P.emit("T", lambda t, ic=ic, tt=tt, dg=dg, pj=pj: t.matmul(
                            ps[pj][:, 0:512], s_hm[ic % 2][:, tt * 128:(tt + 1) * 128],
                            C_wd[:, ic, dg * 512:(dg + 1) * 512],
                            start=(ic == 0), stop=(ic == 7)),
                            reads=(bh, bwd, psB[pj]), writes=(psB[pj],))
            for tt in range(2):
                j = tw * 2 + tt
                for dg in range(2):
                    pj = 2 + tt * 2 + dg
                    if e < 0:
                        P.emit("V", lambda v, j=j, dg=dg, pj=pj: v.scalar_tensor_tensor(
                            R[:, j, dg * 512:(dg + 1) * 512], ps[pj][:, 0:512], 1.0,
                            R[:, j, dg * 512:(dg + 1) * 512], ALU.mult, ALU.add),
                            reads=(psB[pj], bR[j]), writes=(bR[j], psB[pj]))
                    else:
                        P.emit("V", lambda v, j=j, dg=dg, pj=pj, e=e: v.scalar_tensor_tensor(
                            R[:, j, dg * 512:(dg + 1) * 512], ps[pj][:, 0:512],
                            s_wts[:, j, e:e + 1], R[:, j, dg * 512:(dg + 1) * 512],
                            ALU.mult, ALU.add),
                            reads=(psB[pj], bwts, bR[j]), writes=(bR[j], psB[pj]))
    DUMP("d_wts", [128, 16, 8], F32, lambda: s_wts[:, :, :], (P.buf("s_wts"),))
    if dbg:
        dP = P.nc.dram_tensor("d_preln", [T, CD], F32, kind="ExternalOutput")
        dpr2 = dP.ap().rearrange("(j p) c -> p j c", p=128)
        for j in range(NT):
            P.emit("Y", lambda y, j=j: y.dma_start(dpr2[:, j, :], R[:, j, :]),
                   reads=(bR[j],), writes=(), dma=True)
    orr = out_dram.ap().rearrange("(j p) c -> p j c", p=128)
    for j in range(NT):
        P.emit("V", lambda v, j=j: v.scalar_tensor_tensor(
            s_dump[:, 0:1024], R[:, j, :], 0.0, R[:, j, :], ALU.add, ALU.mult,
            accum_out=s_small[:, 2:3]), reads=(bR[j],), writes=(bdump, bsm))
        P.emit("V", lambda v, j=j: v.tensor_reduce(
            s_small[:, 0:1], R[:, j, :], AX.X, ALU.add), reads=(bR[j],), writes=(bsm,))
        P.emit("V", lambda v: v.memset(s_small[:, 1:2], 0.0), reads=(bsm,), writes=(bsm,))
        emit_ln_tiles(P, R, bR, j, None, None, s_small, bsm, K_sc, kbufs["SCAL"],
                      R[:, j, :], (bR[j],))
        P.emit("Y", lambda y, j=j: y.dma_start(orr[:, j, :], R[:, j, :]),
               reads=(bR[j],), writes=(), dma=True)
    return R, bR


def _in_maps(P, consts, per_core):
    maps = []
    for ci in range(NCORES):
        m = {}
        for k in P.inputs:
            if k in per_core:
                m[k] = per_core[k][ci]
            else:
                m[k] = consts[k]
        maps.append(m)
    return maps

def _load_R_f32(P, x_dram, R, bR):
    xr = x_dram.ap().rearrange("(j p) c -> p j c", p=128)
    for j in range(NT):
        P.emit("Y", lambda y, j=j: y.dma_start(R[:, j, :], xr[:, j, :]),
               reads=(), writes=(bR[j],), dma=True)

def _store_R(P, x_dram, R, bR):
    orr = x_dram.ap().rearrange("(j p) c -> p j c", p=128)
    for j in range(NT):
        P.emit("Y", lambda y, j=j: y.dma_start(orr[:, j, :], R[:, j, :]),
               reads=(bR[j],), writes=(), dma=True)

def run_stage_A(x_slices, consts):
    P = Prog("stageA")
    x_in = P.din("x_in", [T, CD], F16)
    x_out = P.nc.dram_tensor("x_out", [T, CD], F32, kind="ExternalOutput")
    R, Bt, bR, bB = emit_stage_A(P, x_in)
    _store_R(P, x_out, R, bR)
    P.final_wait()
    nc = P.build()
    res = run_bass_kernel_spmd(nc, _in_maps(P, consts, {"x_in": x_slices}),
                               core_ids=list(range(NCORES)))
    return [r["x_out"] for r in res.results]

def _prologue_xT(P, x_in):
    R = P.sb("Rpre", [128, NT, 1024], F32, OFF_R)
    Bt = P.sb("Btpre", [128, 8, 2048], BF16, OFF_B)
    bR = [P.buf(("R", j)) for j in range(NT)]
    bB = P.buf("bBt")
    _load_R_f32(P, x_in, R, bR)
    return R, Bt, bR, bB

def run_stage_B(x_full, consts):
    P = Prog("stageB")
    x_in = P.din("x_in", [T, CD], F32)
    x_out = P.nc.dram_tensor("x_out", [T, CD], F32, kind="ExternalOutput")
    import os
    dbg = bool(os.environ.get("KDBG"))
    R, Bt, bR, bBt, extra = emit_stage_B(P, x_in=x_in, dbg=dbg)
    _store_R(P, x_out, R, bR)
    P.final_wait()
    nc = P.build()
    res = run_bass_kernel_spmd(nc, _in_maps(P, consts, {"x_in": x_full}),
                               core_ids=list(range(NCORES)))
    if dbg:
        import pickle
        with open("/tmp/dbgB.pkl", "wb") as f:
            pickle.dump({k: np.asarray(v) for k, v in res.results[0].items()}, f)
    return [r["x_out"] for r in res.results]

def run_stage_C(x_full, consts):
    P = Prog("stageC")
    x_in = P.din("x_in", [T, CD], F32)
    x_out = P.nc.dram_tensor("x_out", [T, CD], F32, kind="ExternalOutput")
    import os
    dbg = bool(os.environ.get("KDBG"))
    R, bR = emit_stage_C(P, x_in=x_in, dbg=dbg)
    _store_R(P, x_out, R, bR)
    P.final_wait()
    nc = P.build()
    res = run_bass_kernel_spmd(nc, _in_maps(P, consts, {"x_in": x_full}),
                               core_ids=list(range(NCORES)))
    if dbg:
        import pickle
        with open("/tmp/dbgC.pkl", "wb") as f:
            pickle.dump({k: np.asarray(v) for k, v in res.results[0].items()}, f)
    return [r["x_out"] for r in res.results]

def run_stage_D(x_full, consts):
    P = Prog("stageD")
    x_in = P.din("x_in", [T, CD], F32)
    x_out = P.nc.dram_tensor("x_out", [T, CD], F32, kind="ExternalOutput")
    import os
    dbg = bool(os.environ.get("KDBG"))
    emit_stage_D(P, x_out, x_in=x_in, dbg=dbg)
    P.final_wait()
    nc = P.build()
    res = run_bass_kernel_spmd(nc, _in_maps(P, consts, {"x_in": x_full}),
                               core_ids=list(range(NCORES)))
    if dbg:
        import pickle
        with open("/tmp/dbgD.pkl", "wb") as f:
            pickle.dump({k: np.asarray(v) for k, v in res.results[0].items()}, f)
    return [r["x_out"] for r in res.results]

def run_fused(x_slices, consts):
    P = Prog("mk")
    x_in = P.din("x_in", [T, CD], F16)
    out = P.nc.dram_tensor("out", [T, CD], F32, kind="ExternalOutput")
    emit_stage_A(P, x_in)
    P.fence()
    emit_stage_B(P)
    P.fence()
    emit_stage_C(P)
    P.fence()
    emit_stage_D(P, out)
    P.final_wait()
    nc = P.build()
    res = run_bass_kernel_spmd(nc, _in_maps(P, consts, {"x_in": x_slices}),
                               core_ids=list(range(NCORES)))
    return [r["out"] for r in res.results]

def kernel(**inputs):
    consts = mk_consts(inputs)
    x = np.ascontiguousarray(inputs["L0_attn_x"])
    x_slices = [np.ascontiguousarray(x[i]) for i in range(NCORES)]
    if FUSED:
        return np.stack(run_fused(x_slices, consts), axis=0)
    x1 = run_stage_A(x_slices, consts)
    x2 = run_stage_B(x1, consts)
    x3 = run_stage_C(x2, consts)
    x4 = run_stage_D(x3, consts)
    return np.stack(x4, axis=0)
